# version 03
# baseline (speedup 1.0000x reference)
import numpy as np
from contextlib import ExitStack

import concourse.bass as bass
import concourse.mybir as mybir
from concourse.bass_utils import run_bass_kernel_spmd
from concourse.aot_env import allow_ici_all_to_all

F32 = mybir.dt.float32
BF16 = mybir.dt.bfloat16
AT = mybir.ActivationFunctionType
OP = mybir.AluOpType
AX = mybir.AxisListType

E = 1024
DIN = 2048
DH = 1024
NST = 16
TF = 4096
TH = 2048
TC = 512
NCH = TF // TC
NDT = DH // 128
NKT = E // 128
XOW = TH + 1
HS_COUNT = 4.0 * TF * DIN * NST

ENGS = ["sync", "gpsimd", "vector", "scalar", "tensor"]


class Cnt:
    def __init__(self, h, step=1):
        self.h = h
        self.n = 0
        self.step = step

    def res(self, k=None):
        self.n += self.step if k is None else k
        return self.n


class Prog:
    def __init__(self):
        self.q = {en: [] for en in ENGS}

    def add(self, en, f):
        self.q[en].append(f)

    def emit(self, block):
        for en in ENGS:
            fs = self.q[en]

            def runner(eng, fs=fs):
                for f in fs:
                    f(eng)
            getattr(block, en)(runner)


def build_nc():
    nc = bass.Bass("TRN2", target_bir_lowering=False)

    xin = nc.dram_tensor("xin", [E, XOW], F32, kind="ExternalInput").ap()
    wr_t = nc.dram_tensor("wr_t", [E, E], F32, kind="ExternalInput").ap()
    wk_t = nc.dram_tensor("wk_t", [E, E], F32, kind="ExternalInput").ap()
    wv_t = nc.dram_tensor("wv_t", [E, E], F32, kind="ExternalInput").ap()
    mu_r = nc.dram_tensor("mu_r", [E, 1], F32, kind="ExternalInput").ap()
    mu_k = nc.dram_tensor("mu_k", [E, 1], F32, kind="ExternalInput").ap()
    g1in = nc.dram_tensor("g1in", [16, 1024], F32, kind="ExternalInput").ap()
    g2in = nc.dram_tensor("g2in", [16, 2048], F32, kind="ExternalInput").ap()
    g1x = nc.dram_tensor("g1x", [16, 128], F32, kind="ExternalInput").ap()
    g2x = nc.dram_tensor("g2x", [16, 768], F32, kind="ExternalInput").ap()
    g1dt = nc.dram_tensor("g1dt", [16, 128], F32, kind="ExternalInput").ap()
    g2dt = nc.dram_tensor("g2dt", [16, 512], F32, kind="ExternalInput").ap()
    g1out = nc.dram_tensor("g1out", [16, 512], F32, kind="ExternalInput").ap()
    g2out = nc.dram_tensor("g2out", [16, 2048], F32, kind="ExternalInput").ap()
    cfin = nc.dram_tensor("cfin", [DH, 4], F32, kind="ExternalInput").ap()
    cbin = nc.dram_tensor("cbin", [DH, 1], F32, kind="ExternalInput").ap()
    alog = nc.dram_tensor("alog", [DH, NST], F32, kind="ExternalInput").ap()
    dvin = nc.dram_tensor("dvin", [DH, 1], F32, kind="ExternalInput").ap()
    maskp = nc.dram_tensor("maskp", [E, 1], F32, kind="ExternalInput").ap()
    masku = nc.dram_tensor("masku", [E, 1], F32, kind="ExternalInput").ap()
    out_t = nc.dram_tensor("out", [E, TH], F32, kind="ExternalOutput").ap()

    h_wtmp = nc.dram_tensor("h_wtmp", [2048, 2048], BF16).ap()
    h_wtmpx = nc.dram_tensor("h_wtmpx", [128, 768], BF16).ap()
    h_wtmpdt = nc.dram_tensor("h_wtmpdt", [128, 512], BF16).ap()
    h_ln2own = nc.dram_tensor("h_ln2own", [E, TH], BF16).ap()
    h_ln2full = nc.dram_tensor("h_ln2full", [2 * E, TH], BF16).ap()
    h_y = nc.dram_tensor("h_y", [DH, TF], BF16).ap()
    h_dtbc_in = nc.dram_tensor("h_dtbc_in", [96, TF], BF16).ap()
    h_dtbc = nc.dram_tensor("h_dtbc", [96, TF], BF16).ap()
    h_ys = nc.dram_tensor("h_ys", [DH, TF], BF16).ap()
    h_op = nc.dram_tensor("h_op", [TF, E], BF16).ap()
    h_od = nc.dram_tensor("h_od", [TH, E], BF16).ap()
    h_hs_in = nc.dram_tensor("h_hs_in", [1, 1032], F32).ap()
    h_hs = nc.dram_tensor("h_hs", [1, 1032], F32).ap()
    h_halo_in = nc.dram_tensor("h_halo_in", [E, 1], F32).ap()
    h_halo = nc.dram_tensor("h_halo", [E, 1], F32).ap()
    h_stat = nc.dram_tensor("h_stat", [2, XOW], BF16).ap()
    h_mb = nc.dram_tensor("h_mb", [1, E], F32).ap()

    st = ExitStack()
    e = st.enter_context

    BARs = e(nc.semaphore("BAR"))
    bar_n = [0]
    ld = Cnt(e(nc.semaphore("ld")), 16)
    lx = Cnt(e(nc.semaphore("lx")), 16)
    tp = Cnt(e(nc.semaphore("tp")), 1)
    sp = Cnt(e(nc.semaphore("sp")), 1)
    vp = Cnt(e(nc.semaphore("vp")), 1)
    ccs = Cnt(e(nc.semaphore("ccs")), 1)
    d_slab = [Cnt(e(nc.semaphore(f"dslab{i}")), 16) for i in range(2)]
    d_u = [Cnt(e(nc.semaphore(f"du{i}")), 16) for i in range(2)]
    d_ysw = [Cnt(e(nc.semaphore(f"dysw{i}")), 16) for i in range(2)]
    d_op = [Cnt(e(nc.semaphore(f"dop{i}")), 16) for i in range(2)]
    d_dod = [Cnt(e(nc.semaphore(f"ddod{i}")), 16) for i in range(2)]
    d_wb = [Cnt(e(nc.semaphore(f"dwb{i}")), 16) for i in range(2)]
    d_bc = Cnt(e(nc.semaphore("dbc")), 16)
    d_y = Cnt(e(nc.semaphore("dy")), 16)
    d_stat = Cnt(e(nc.semaphore("dstat")), 16)

    SYNC_SEMS = [lx] + d_slab + d_u + d_dod
    GP_SEMS = [ld, d_bc, d_y, d_stat] + d_ysw + d_op + d_wb

    P = Prog()

    def barrier():
        bar_n[0] += 1
        tgt = 5 * bar_n[0]
        waits = {"sync": [(c, c.n) for c in SYNC_SEMS],
                 "gpsimd": [(c, c.n) for c in GP_SEMS],
                 "vector": [(vp, vp.n)],
                 "scalar": [(sp, sp.n)],
                 "tensor": [(tp, tp.n)]}
        for en in ENGS:
            ws = waits[en]

            def f(eng, ws=ws, tgt=tgt):
                for c, v in ws:
                    if v > 0:
                        eng.wait_ge(c.h, v)
                eng.sem_inc(BARs, 1)
                eng.wait_ge(BARs, tgt)
            P.add(en, f)

    def dma(en, dst, src, cnt, slow=False):
        v = cnt.res()

        def f(eng, dst=dst, src=src, slow=slow):
            if slow:
                with nc.allow_non_contiguous_dma(reason="tiny gather"):
                    eng.dma_start(out=dst, in_=src).then_inc(cnt.h, cnt.step)
            else:
                eng.dma_start(out=dst, in_=src).then_inc(cnt.h, cnt.step)
        P.add(en, f)
        return v

    def wait(en, cnt, v):
        if v <= 0:
            return

        def f(eng, cnt=cnt, v=v):
            eng.wait_ge(cnt.h, v)
        P.add(en, f)

    def bump(en, cnt):
        def f(eng, cnt=cnt):
            eng.maybe_drain_then_inc((cnt.h, cnt.step))
        P.add(en, f)
        return cnt.res()

    xo = e(nc.sbuf_tensor("xo", [128, NKT * XOW], F32))

    def xov(kt, c0, c1):
        return xo[:, kt * XOW + c0: kt * XOW + c1]

    mu_r_sb = e(nc.sbuf_tensor("mu_r_sb", [128, NKT], F32))
    mu_k_sb = e(nc.sbuf_tensor("mu_k_sb", [128, NKT], F32))
    mu_r1_sb = e(nc.sbuf_tensor("mu_r1_sb", [128, NKT], F32))
    mu_k1_sb = e(nc.sbuf_tensor("mu_k1_sb", [128, NKT], F32))
    msku_sb = e(nc.sbuf_tensor("msku_sb", [128, NKT], F32))
    mskp_sb = e(nc.sbuf_tensor("mskp_sb", [128, NKT], F32))
    ones_c = e(nc.sbuf_tensor("ones_c", [128, 1], BF16))
    ones_f = e(nc.sbuf_tensor("ones_f", [128, 1], F32))
    pp = [e(nc.psum_tensor(f"pp{i}", [128, 512], F32)) for i in range(8)]

    for kt in range(NKT):
        dma("sync", xov(kt, 0, XOW), xin[kt * 128:(kt + 1) * 128, :], lx)
    dma("gpsimd", mu_r_sb[:, :].rearrange("p (a b) -> p a b", b=1), mu_r.rearrange("(a p) b -> p a b", p=128), ld, slow=True)
    dma("gpsimd", mu_k_sb[:, :].rearrange("p (a b) -> p a b", b=1), mu_k.rearrange("(a p) b -> p a b", p=128), ld, slow=True)
    dma("gpsimd", msku_sb[:, :].rearrange("p (a b) -> p a b", b=1), masku.rearrange("(a p) b -> p a b", p=128), ld, slow=True)
    dma("gpsimd", mskp_sb[:, :].rearrange("p (a b) -> p a b", b=1), maskp.rearrange("(a p) b -> p a b", p=128), ld, slow=True)

    ld_mu = ld.n

    def f(v, ld_mu=ld_mu):
        v.memset(ones_c[:, :], 1.0)
        v.memset(ones_f[:, :], 1.0)
        v.wait_ge(ld.h, ld_mu)
        v.tensor_scalar(mu_r1_sb[:, :], mu_r_sb[:, :], -1.0, 1.0, OP.mult, OP.add)
        v.tensor_scalar(mu_k1_sb[:, :], mu_k_sb[:, :], -1.0, 1.0, OP.mult, OP.add)
    P.add("vector", f)
    bump("vector", vp)

    barrier()

    def emit_ln(xview, lnview, chunks):
        stL = ExitStack()
        sqbuf = stL.enter_context(nc.sbuf_tensor(None, [128, 2 * 512], BF16))
        s_m = stL.enter_context(nc.sbuf_tensor(None, [1, XOW], F32))
        s_v = stL.enter_context(nc.sbuf_tensor(None, [1, XOW], F32))
        s_i = stL.enter_context(nc.sbuf_tensor(None, [1, XOW], F32))
        invb_ = stL.enter_context(nc.sbuf_tensor(None, [128, XOW], BF16))
        msbb_ = stL.enter_context(nc.sbuf_tensor(None, [128, XOW], BF16))
        c_lo = chunks[0][0]
        c_hi = chunks[-1][1]
        wdt = c_hi - c_lo
        wait("scalar", vp, vp.n)

        def f(s):
            for kt in range(NKT):
                s.activation(out=lnview(kt, c_lo, c_hi), in_=xview(kt, c_lo, c_hi), func=AT.Copy)
        P.add("scalar", f)
        sp_cast = bump("scalar", sp)
        wait("vector", sp, sp_cast)

        tp_pair = {}
        gix = 0
        for (c0, c1) in chunks:
            wd = c1 - c0
            for kt in range(NKT):
                gix += 1
                def f(v, kt=kt, c0=c0, c1=c1, wd=wd, tpg=tp_pair.get(gix - 2, 0)):
                    if tpg:
                        v.wait_ge(tp.h, tpg)
                    v.tensor_tensor(sqbuf[:, (gix % 2) * 512:(gix % 2) * 512 + wd] if False else
                                    sqbuf[:, (kt % 2) * 512:(kt % 2) * 512 + wd],
                                    lnview(kt, c0, c1), lnview(kt, c0, c1), OP.mult)
                P.add("vector", f)
                vpm = bump("vector", vp)

                def f(t, kt=kt, c0=c0, c1=c1, wd=wd, vpm=vpm):
                    t.wait_ge(vp.h, vpm)
                    t.matmul(pp[6][0:1, 0:wd], ones_c[:, :], lnview(kt, c0, c1),
                             start=(kt == 0), stop=(kt == NKT - 1))
                    t.matmul(pp[7][0:1, 0:wd], ones_c[:, :],
                             sqbuf[:, (kt % 2) * 512:(kt % 2) * 512 + wd],
                             start=(kt == 0), stop=(kt == NKT - 1))
                P.add("tensor", f)
                tp_pair[gix] = bump("tensor", tp)
            tpm = tp_pair[gix]

            def f(s, c0=c0, c1=c1, wd=wd, tpm=tpm):
                s.wait_ge(tp.h, tpm)
                s.copy(s_m[0:1, c0 - c_lo:c1 - c_lo], pp[6][0:1, 0:wd])
                s.copy(s_v[0:1, c0 - c_lo:c1 - c_lo], pp[7][0:1, 0:wd])
            P.add("scalar", f)
            spm = bump("scalar", sp)
            wait("tensor", sp, spm)
        sp_ev = sp.n

        def f(v, sp_ev=sp_ev, wdt=wdt):
            v.wait_ge(sp.h, sp_ev)
            v.tensor_scalar(s_m[0:1, 0:wdt], s_m[0:1, 0:wdt], 1.0 / E, None, OP.mult)
            v.tensor_scalar(s_v[0:1, 0:wdt], s_v[0:1, 0:wdt], 1.0 / E, None, OP.mult)
            v.drain()
            v.tensor_tensor(s_i[0:1, 0:wdt], s_m[0:1, 0:wdt], s_m[0:1, 0:wdt], OP.mult)
            v.drain()
            v.tensor_tensor(s_v[0:1, 0:wdt], s_v[0:1, 0:wdt], s_i[0:1, 0:wdt], OP.subtract)
            v.drain()
            v.tensor_scalar(s_v[0:1, 0:wdt], s_v[0:1, 0:wdt], 1e-5, None, OP.add)
        P.add("vector", f)
        vp_var = bump("vector", vp)

        def f(s, vp_var=vp_var, wdt=wdt):
            s.wait_ge(vp.h, vp_var)
            s.activation(out=s_i[0:1, 0:wdt], in_=s_v[0:1, 0:wdt], func=AT.Ln)
        P.add("scalar", f)
        sp_inv = bump("scalar", sp)

        def f(v, sp_inv=sp_inv, wdt=wdt):
            v.wait_ge(sp.h, sp_inv)
            v.tensor_scalar(s_i[0:1, 0:wdt], s_i[0:1, 0:wdt], -0.5, None, OP.mult)
        P.add("vector", f)
        vp_l2 = bump("vector", vp)

        def f(s, vp_l2=vp_l2, wdt=wdt):
            s.wait_ge(vp.h, vp_l2)
            s.activation(out=s_i[0:1, 0:wdt], in_=s_i[0:1, 0:wdt], func=AT.Exp)
        P.add("scalar", f)
        sp_inv2 = bump("scalar", sp)

        def f(v, sp_inv2=sp_inv2, wdt=wdt):
            v.wait_ge(sp.h, sp_inv2)
            v.tensor_tensor(s_v[0:1, 0:wdt], s_m[0:1, 0:wdt], s_i[0:1, 0:wdt], OP.mult)
        P.add("vector", f)
        vp_ms = bump("vector", vp)

        wait("gpsimd", vp, vp_ms)
        dma("gpsimd", h_stat[0:1, 0:wdt], s_i[0:1, 0:wdt], d_stat)
        w2 = dma("gpsimd", h_stat[1:2, 0:wdt], s_v[0:1, 0:wdt], d_stat)
        wait("gpsimd", d_stat, w2)
        dma("gpsimd", invb_[:, 0:wdt], h_stat[0:1, 0:wdt].broadcast_to([128, wdt]), d_stat)
        wbc = dma("gpsimd", msbb_[:, 0:wdt], h_stat[1:2, 0:wdt].broadcast_to([128, wdt]), d_stat)
        wait("vector", d_stat, wbc)

        def f(v):
            for kt in range(NKT):
                v.tensor_tensor(lnview(kt, c_lo, c_hi), lnview(kt, c_lo, c_hi), invb_[:, 0:wdt], OP.mult)
            v.drain()
            for kt in range(NKT):
                v.tensor_tensor(lnview(kt, c_lo, c_hi), lnview(kt, c_lo, c_hi), msbb_[:, 0:wdt], OP.subtract)
        P.add("vector", f)
        r = bump("vector", vp)
        barrier()
        stL.close()
        return r

    def emit_ffn(lnview, bufs):
        (mixb, rbuf, kabuf, wfbuf, vstage) = bufs
        wait("gpsimd", ld, ld.n)
        for kt in range(NKT):
            dma("gpsimd", wfbuf[0][:, kt * 1024:(kt + 1) * 1024],
                wk_t[kt * 128:(kt + 1) * 128, :], ld)
        wk_done = ld.n

        def mix(mu_sb, mu1_sb):
            def f(v, mu_sb=mu_sb, mu1_sb=mu1_sb):
                for kt in range(NKT):
                    v.tensor_scalar(mixb[:, kt * TH:(kt + 1) * TH], lnview(kt, 1, TH + 1),
                                    mu_sb[:, kt:kt + 1], None, OP.mult)
                v.drain()
                for kt in range(NKT):
                    v.scalar_tensor_tensor(mixb[:, kt * TH:(kt + 1) * TH], lnview(kt, 0, TH),
                                           mu1_sb[:, kt:kt + 1], mixb[:, kt * TH:(kt + 1) * TH],
                                           OP.mult, OP.add)
            P.add("vector", f)
            return bump("vector", vp)

        mix(mu_k_sb, mu_k1_sb)

        def proj(wbuf_idx, store, rhs_buf):
            wait("tensor", vp, vp.n)
            for mt in range(NKT):
                for tc in range(4):
                    bank = (mt * 4 + tc) % 2

                    def f(t, mt=mt, tc=tc, bank=bank, wbuf_idx=wbuf_idx, rhs_buf=rhs_buf):
                        for kt in range(NKT):
                            t.matmul(pp[bank][:, :],
                                     wfbuf[wbuf_idx][:, (kt * NKT + mt) * 128:(kt * NKT + mt + 1) * 128],
                                     rhs_buf[:, kt * TH + tc * 512:kt * TH + (tc + 1) * 512],
                                     start=(kt == 0), stop=(kt == NKT - 1))
                    P.add("tensor", f)
                    tpm = bump("tensor", tp)

                    def f(s, mt=mt, tc=tc, bank=bank, tpm=tpm):
                        s.wait_ge(tp.h, tpm)
                        store(mt, tc, pp[bank], s)
                    P.add("scalar", f)
                    spm = bump("scalar", sp)
                    wait("tensor", sp, spm)

        wait("tensor", ld, wk_done)

        def store_k(mt, tc, ps, s):
            s.activation(out=kabuf[:, mt * TH + tc * 512:mt * TH + (tc + 1) * 512],
                         in_=ps[:, :], func=AT.Relu)
        proj(0, store_k, mixb)
        sp_k = sp.n
        tp_k = tp.n

        wait("gpsimd", ld, ld.n)
        for kt in range(NKT):
            dma("gpsimd", wfbuf[1][:, kt * 1024:(kt + 1) * 1024],
                wr_t[kt * 128:(kt + 1) * 128, :], ld)
        wr_done = ld.n

        wait("vector", sp, sp_k)

        def f(v):
            for kt in range(NKT):
                v.tensor_tensor(kabuf[:, kt * TH:(kt + 1) * TH], kabuf[:, kt * TH:(kt + 1) * TH],
                                kabuf[:, kt * TH:(kt + 1) * TH], OP.mult)
        P.add("vector", f)
        wait("vector", tp, tp_k)
        mix(mu_r_sb, mu_r1_sb)

        wait("gpsimd", tp, tp_k)
        wait("gpsimd", ld, ld.n)
        for kt in range(NKT):
            dma("gpsimd", wfbuf[0][:, kt * 1024:(kt + 1) * 1024],
                wv_t[kt * 128:(kt + 1) * 128, :], ld)
        wv_done = ld.n
        wait("tensor", ld, wr_done)
        wait("tensor", ld, wv_done)
        wait("tensor", vp, vp.n)
        vp_hist = {}
        itf = 0
        for mt in range(NKT):
            for tc in range(4):
                itf += 1
                rb = (2 * itf) % 4
                vb = (2 * itf + 1) % 4

                def f(t, mt=mt, tc=tc, rb=rb, vb=vb, g=vp_hist.get(itf - 2, 0)):
                    if g:
                        t.wait_ge(vp.h, g)
                    for kt in range(NKT):
                        t.matmul(pp[rb][:, :],
                                 wfbuf[1][:, (kt * NKT + mt) * 128:(kt * NKT + mt + 1) * 128],
                                 mixb[:, kt * TH + tc * 512:kt * TH + (tc + 1) * 512],
                                 start=(kt == 0), stop=(kt == NKT - 1))
                    for kt in range(NKT):
                        t.matmul(pp[vb][:, :],
                                 wfbuf[0][:, (kt * NKT + mt) * 128:(kt * NKT + mt + 1) * 128],
                                 kabuf[:, kt * TH + tc * 512:kt * TH + (tc + 1) * 512],
                                 start=(kt == 0), stop=(kt == NKT - 1))
                P.add("tensor", f)
                tpm = bump("tensor", tp)

                def f(s, rb=rb, tpm=tpm, g=vp_hist.get(itf - 1, 0)):
                    s.wait_ge(tp.h, tpm)
                    if g:
                        s.wait_ge(vp.h, g)
                    s.activation(out=rbuf, in_=pp[rb][:, :], func=AT.Sigmoid)
                P.add("scalar", f)
                spm = bump("scalar", sp)

                def f(v, mt=mt, tc=tc, vb=vb, spm=spm):
                    v.wait_ge(sp.h, spm)
                    v.tensor_tensor(vstage, pp[vb][:, :], rbuf, OP.mult)
                    v.drain()
                    v.scalar_tensor_tensor(xov(mt, 1 + tc * 512, 1 + (tc + 1) * 512),
                                           vstage, 0.5,
                                           xov(mt, 1 + tc * 512, 1 + (tc + 1) * 512),
                                           OP.mult, OP.add)
                P.add("vector", f)
                vp_hist[itf] = bump("vector", vp)
        return bump("vector", vp)

    LN1_CH = [(0, 512), (512, 1024), (1024, 1536), (1536, 2048), (2048, 2049)]
    LN2_CH = [(1, 513), (513, 1025), (1025, 1537), (1537, 2049)]

    stA = ExitStack()
    eA = stA.enter_context
    lnb = eA(nc.sbuf_tensor("lnb", [128, NKT * XOW], BF16))

    def lnv(kt, c0, c1):
        return lnb[:, kt * XOW + c0: kt * XOW + c1]

    ffnsm = eA(nc.sbuf_tensor("ffnsm", [128, 1024], BF16))

    stG = ExitStack()
    g_sb = stG.enter_context(nc.sbuf_tensor("g_sb", [16, 7680], BF16))
    wstage = stG.enter_context(nc.sbuf_tensor("wstage", [128, 2048], BF16))
    go = {}
    _off = 0
    for nm, wd in [("g1in", 1024), ("g2in", 2048), ("g1x", 128), ("g2x", 768),
                   ("g1dt", 128), ("g2dt", 512), ("g1out", 512), ("g2out", 2048)]:
        go[nm] = (_off, _off + wd)
        _off += wd

    def gvw(nm):
        a, b_ = go[nm]
        return g_sb[:, a:b_]

    for nm, gsrc in [("g1in", g1in), ("g2in", g2in), ("g1x", g1x), ("g2x", g2x),
                     ("g1dt", g1dt), ("g2dt", g2dt), ("g1out", g1out), ("g2out", g2out)]:
        dma("gpsimd", gvw(nm), gsrc, ld)
    ld_g = ld.n
    wait("tensor", ld, ld_g)

    emit_ln(xov, lnv, LN1_CH)

    def build_wtmp(g1name, g2name, npt, fchunks, rowbase, dst_hbm):
        for pt in range(npt):
            for ci, (c0, c1) in enumerate(fchunks):
                wd = c1 - c0
                bank = (pt * len(fchunks) + ci) % 2

                def f(t, pt=pt, c0=c0, wd=wd, bank=bank, g1name=g1name, g2name=g2name):
                    t.matmul(pp[4 + bank][:, 0:wd], gvw(g1name)[:, pt * 128:(pt + 1) * 128],
                             gvw(g2name)[:, c0:c0 + wd], start=True, stop=True)
                P.add("tensor", f)
                tpm = bump("tensor", tp)

                def f(s, tpm=tpm, wd=wd, bank=bank):
                    s.wait_ge(tp.h, tpm)
                    s.copy(wstage[:, bank * 512:bank * 512 + wd], pp[4 + bank][:, 0:wd])
                P.add("scalar", f)
                spm = bump("scalar", sp)
                wait("tensor", sp, spm)
                wait("gpsimd", sp, spm)
                wdm = dma("gpsimd", dst_hbm[rowbase + pt * 128: rowbase + (pt + 1) * 128, c0:c0 + wd],
                          wstage[:, bank * 512:bank * 512 + wd], d_wb[bank])
                wait("scalar", d_wb[bank], wdm)

    F4 = [(0, 512), (512, 1024), (1024, 1536), (1536, 2048)]
    build_wtmp("g1in", "g2in", 8, F4, 0, h_wtmp)
    build_wtmp("g1out", "g2out", 4, F4, 1024, h_wtmp)
    build_wtmp("g1x", "g2x", 1, [(0, 512), (512, 768)], 0, h_wtmpx)
    build_wtmp("g1dt", "g2dt", 1, [(0, 512)], 0, h_wtmpdt)
    barrier()
    stG.close()

    stF = ExitStack()
    mixb = stF.enter_context(nc.sbuf_tensor("mixb", [128, NKT * TH], BF16))
    kabuf = stF.enter_context(nc.sbuf_tensor("kabuf", [128, NKT * TH], BF16))
    wfbuf = [stF.enter_context(nc.sbuf_tensor(f"wf{i}", [128, NKT * NKT * 128], BF16)) for i in range(2)]
    rbuf = ffnsm[:, 0:512]
    vstage = ffnsm[:, 512:1024]

    emit_ffn(lnv, (mixb, rbuf, kabuf, wfbuf, vstage))
    barrier()
    stF.close()

    emit_ln(xov, lnv, LN2_CH)
    wait("sync", vp, vp.n)
    for kt in range(NKT):
        dma("sync", h_ln2own[kt * 128:(kt + 1) * 128, :], lnv(kt, 1, TH + 1), lx)

    barrier()
    stA.close()

    def f(gp):
        gp.collective_compute(
            "AllGather", OP.bypass,
            replica_groups=[[0, 1], [2, 3], [4, 5], [6, 7]],
            ins=[h_ln2own], outs=[h_ln2full],
        ).then_inc(ccs.h, 1)
    P.add("gpsimd", f)
    cc1 = ccs.res()
    wait("gpsimd", ccs, cc1)
    barrier()

    stB = ExitStack()
    eB = stB.enter_context
    wi_sb = eB(nc.sbuf_tensor("wi_sb", [128, NKT * 2048], BF16))
    wx_sb = eB(nc.sbuf_tensor("wx_sb", [128, NKT * 96], BF16))
    slab = [eB(nc.sbuf_tensor(f"slab{i}", [128, NKT * 512], BF16)) for i in range(2)]
    XCW = TF + 3
    xcp = eB(nc.sbuf_tensor("xcp", [128, XCW], BF16))
    zb = eB(nc.sbuf_tensor("zb", [128, TF], BF16))
    cacc = eB(nc.sbuf_tensor("cacc", [128, TF], BF16))
    ctmp = eB(nc.sbuf_tensor("ctmp", [128, TF], BF16))
    convd = eB(nc.sbuf_tensor("convd", [128, TF], BF16))
    ytile = eB(nc.sbuf_tensor("ytile", [128, TF], BF16))
    cf_sb = eB(nc.sbuf_tensor("cf_sb", [128, NDT * 4], F32))
    cb_sb = eB(nc.sbuf_tensor("cb_sb", [128, NDT], F32))
    dtbc_sb = eB(nc.sbuf_tensor("dtbc_sb", [96, TF], BF16))

    for kt in range(NKT):
        for ai in range(4):
            a = 4 * kt + ai
            srcv = h_wtmp[a * 32: a * 32 + 32, :].rearrange("i (b j) -> b i j", j=64)
            dstv = wi_sb[ai * 32:(ai + 1) * 32, kt * 2048:(kt + 1) * 2048].rearrange(
                "b (i j) -> b i j", j=64)
            dma("gpsimd", dstv, srcv, ld)
    for kt in range(NKT):
        for ai in range(2):
            a_loc = 2 * kt + ai
            srcv = h_wtmpx[a_loc * 8: a_loc * 8 + 8, :].rearrange("i (b j) -> b i j", j=12)
            dstv = wx_sb[ai * 64:(ai + 1) * 64, kt * 96:(kt + 1) * 96].rearrange(
                "b (i j) -> b i j", j=12)
            dma("gpsimd", dstv, srcv, ld)
    dma("gpsimd", cf_sb[:, :].rearrange("p (a b) -> p a b", b=4), cfin.rearrange("(a p) b -> p a b", p=128), ld, slow=True)
    ld_wi = dma("gpsimd", cb_sb[:, :].rearrange("p (a b) -> p a b", b=1), cbin.rearrange("(a p) b -> p a b", p=128), ld, slow=True)
    wait("tensor", ld, ld_wi)

    vp_y_prev = 0
    ld_y_prev = 0
    for j in range(NDT):
        for tc in range(NCH):
            sl = tc % 2
            src = h_ln2full.rearrange("(s p) t -> s p t", s=2)[tc // 4, :, (tc % 4) * 512:(tc % 4 + 1) * 512]
            srcv = src.rearrange("(a p) t -> p a t", p=128)
            lxm = dma("sync", slab[sl][:, :].rearrange("p (a t) -> p a t", a=NKT), srcv, d_slab[sl])
            for which in range(2):
                mcol = (0 if which == 0 else 1024) + j * 128
                bank = (tc * 2 + which) % 2

                def f(t, mcol=mcol, bank=bank, sl=sl, lxm=lxm):
                    t.wait_ge(d_slab[sl].h, lxm)
                    for kt in range(NKT):
                        t.matmul(pp[bank][:, :],
                                 wi_sb[:, kt * 2048 + mcol: kt * 2048 + mcol + 128],
                                 slab[sl][:, kt * 512:(kt + 1) * 512],
                                 start=(kt == 0), stop=(kt == NKT - 1))
                P.add("tensor", f)
                tpm = bump("tensor", tp)

                def f(s, which=which, tc=tc, bank=bank, tpm=tpm, vp_y_prev=vp_y_prev):
                    s.wait_ge(tp.h, tpm)
                    if tc == 0 and which == 0 and vp_y_prev:
                        s.wait_ge(vp.h, vp_y_prev)
                    if which == 0:
                        s.copy(xcp[:, 1 + tc * 512: 1 + (tc + 1) * 512], pp[bank][:, :])
                    else:
                        s.copy(zb[:, tc * 512:(tc + 1) * 512], pp[bank][:, :])
                P.add("scalar", f)
                spm = bump("scalar", sp)
                wait("tensor", sp, spm)
            wait("sync", tp, tp.n)
        sp_xz = sp.n

        def f(v, j=j, sp_xz=sp_xz, ld_y_prev=ld_y_prev):
            v.wait_ge(sp.h, sp_xz)
            if ld_y_prev:
                v.wait_ge(d_y.h, ld_y_prev)
            v.memset(xcp[:, 0:1], 0.0)
            v.memset(xcp[:, TF + 1:TF + 3], 0.0)
            v.drain()
            v.tensor_scalar(cacc[:, :], xcp[:, 0:TF], cf_sb[:, j * 4:j * 4 + 1], None, OP.mult)
            v.tensor_scalar(ctmp[:, :], xcp[:, 1:1 + TF], cf_sb[:, j * 4 + 1:j * 4 + 2], None, OP.mult)
            v.tensor_scalar(ytile[:, :], xcp[:, 2:2 + TF], cf_sb[:, j * 4 + 2:j * 4 + 3], None, OP.mult)
            v.tensor_scalar(convd[:, :], xcp[:, 3:3 + TF], cf_sb[:, j * 4 + 3:j * 4 + 4], None, OP.mult)
            v.drain()
            v.tensor_tensor(cacc[:, :], cacc[:, :], ctmp[:, :], OP.add)
            v.tensor_tensor(ytile[:, :], ytile[:, :], convd[:, :], OP.add)
            v.drain()
            v.tensor_tensor(cacc[:, :], cacc[:, :], ytile[:, :], OP.add)
        P.add("vector", f)
        vp_conv = bump("vector", vp)

        def f(s, j=j, vp_conv=vp_conv):
            s.wait_ge(vp.h, vp_conv)
            s.activation(out=cacc[:, :], in_=cacc[:, :], func=AT.Silu, bias=cb_sb[:, j:j + 1])
        P.add("scalar", f)
        sp_silu = bump("scalar", sp)

        def f(v, sp_silu=sp_silu):
            v.wait_ge(sp.h, sp_silu)
            v.tensor_tensor(ytile[:, :], cacc[:, :], zb[:, :], OP.mult)
        P.add("vector", f)
        vp_y = bump("vector", vp)
        wait("gpsimd", vp, vp_y)
        ld_y = dma("gpsimd", h_y[j * 128:(j + 1) * 128, :], ytile[:, :], d_y)
        vp_y_prev = vp_y
        ld_y_prev = ld_y

    wait("sync", d_y, d_y.n)
    for tc in range(NCH):
        sl = tc % 2
        lxm = dma("sync", slab[sl][:, :].rearrange("p (a t) -> p a t", a=NDT),
                  h_y.rearrange("(a p) t -> p a t", p=128)[:, :, tc * 512:(tc + 1) * 512], d_slab[sl])
        bank = tc % 2

        def f(t, bank=bank, sl=sl, lxm=lxm):
            t.wait_ge(d_slab[sl].h, lxm)
            for kt in range(NDT):
                t.matmul(pp[bank][0:96, :],
                         wx_sb[:, kt * 96:(kt + 1) * 96],
                         slab[sl][:, kt * 512:(kt + 1) * 512],
                         start=(kt == 0), stop=(kt == NDT - 1))
        P.add("tensor", f)
        tpm = bump("tensor", tp)

        def f(s, tc=tc, bank=bank, tpm=tpm):
            s.wait_ge(tp.h, tpm)
            s.copy(dtbc_sb[:, tc * 512:(tc + 1) * 512], pp[bank][0:96, :])
        P.add("scalar", f)
        spm = bump("scalar", sp)
        wait("tensor", sp, spm)
        wait("sync", sp, spm)
    wait("gpsimd", sp, sp.n)
    dma("gpsimd", h_dtbc_in[:, :], dtbc_sb[:, :], ld)

    barrier()
    stB.close()

    def f(gp):
        gp.collective_compute(
            "AllReduce", OP.add,
            replica_groups=[[0, 1], [2, 3], [4, 5], [6, 7]],
            ins=[h_dtbc_in], outs=[h_dtbc],
        ).then_inc(ccs.h, 1)
    P.add("gpsimd", f)
    cc2 = ccs.res()
    wait("gpsimd", ccs, cc2)
    barrier()

    stW = ExitStack()
    wo_sb = stW.enter_context(nc.sbuf_tensor("wo_sb", [128, NKT * 1024], BF16))
    for kt in range(NKT):
        for ai in range(2):
            a_loc = 2 * kt + ai
            srcv = h_wtmp[1024 + a_loc * 32: 1024 + a_loc * 32 + 32, :].rearrange(
                "i (b j) -> b i j", j=32)
            dstv = wo_sb[ai * 64:(ai + 1) * 64, kt * 1024:(kt + 1) * 1024].rearrange(
                "b (i j) -> b i j", j=32)
            dma("gpsimd", dstv, srcv, ld)
    ld_wo = ld.n

    stC = ExitStack()
    eC = stC.enter_context
    bbc = eC(nc.sbuf_tensor("bbc", [128, NST * TC], BF16))
    cbc = eC(nc.sbuf_tensor("cbc", [128, NST * TC], BF16))
    argb = eC(nc.sbuf_tensor("argb", [128, NST * TC], BF16))
    dab = eC(nc.sbuf_tensor("dab", [128, NST * TC], BF16))
    bsb = eC(nc.sbuf_tensor("bsb", [128, NST * TC], BF16))
    hsb = eC(nc.sbuf_tensor("hsb", [128, NST * TC], BF16))
    dtun_b = eC(nc.sbuf_tensor("dtun_b", [64, TF], BF16))
    wdt_sb = eC(nc.sbuf_tensor("wdt_sb", [64, DH], BF16))
    dts = eC(nc.sbuf_tensor("dts", [128, TC], BF16))
    dus = eC(nc.sbuf_tensor("dus", [128, TC], BF16))
    spf = eC(nc.sbuf_tensor("spf", [128, TC], F32))
    usb = [eC(nc.sbuf_tensor(f"usb{i}", [128, TC], BF16)) for i in range(2)]
    yssb = [eC(nc.sbuf_tensor(f"yssb{i}", [128, TC], BF16)) for i in range(2)]
    acol = eC(nc.sbuf_tensor("acol", [128, NDT * NST], F32))
    atmp = eC(nc.sbuf_tensor("atmp", [128, NDT * NST], F32))
    dcol = eC(nc.sbuf_tensor("dcol", [128, NDT], F32))
    carry = eC(nc.sbuf_tensor("carry", [128, NDT * NST], BF16))
    fold16 = eC(nc.sbuf_tensor("fold16", [128, NST], BF16))
    hsumc = eC(nc.sbuf_tensor("hsumc", [128, NCH * NDT], F32))
    hsred = eC(nc.sbuf_tensor("hsred", [128, 1], F32))
    hspad = eC(nc.sbuf_tensor("hspad", [1, 1032], F32))

    ld_dtun = dma("gpsimd", dtun_b[:, :], h_dtbc[0:64, :], ld)
    ld_alog = dma("gpsimd", atmp[:, :].rearrange("p (a n) -> p a n", a=NDT),
                  alog.rearrange("(a p) n -> p a n", p=128), ld)
    ld_dv = dma("gpsimd", dcol[:, :].rearrange("p (a b) -> p a b", b=1), dvin.rearrange("(a p) b -> p a b", p=128), ld, slow=True)
    for a in range(8):
        srcv = h_wtmpdt[a * 16:a * 16 + 16, :].rearrange("i (b j) -> b i j", j=64)
        dstv = wdt_sb[a * 8:(a + 1) * 8, :].rearrange("b (i j) -> b i j", j=64)
        dma("gpsimd", dstv, srcv, ld)
    ld_wdt = ld.n

    def f(s, w=None):
        s.wait_ge(ld.h, ld_wdt)
        s.activation(out=atmp[:, :], in_=atmp[:, :], func=AT.Exp)
    P.add("scalar", f)
    sp_aexp = bump("scalar", sp)

    def f(v):
        v.wait_ge(sp.h, sp_aexp)
        v.tensor_scalar(acol[:, :], atmp[:, :], -1.0, None, OP.mult)
        v.memset(carry[:, :], 0.0)
        v.memset(hspad[0:1, :], 0.0)
    P.add("vector", f)
    bump("vector", vp)
    wait("tensor", ld, ld.n)
    wait("vector", ld, ld.n)

    wait("tensor", ld, ld.n)
    wait("tensor", vp, vp.n)
    for ec in range(2):
        def f(t, ec=ec):
            for kt in range(NKT):
                t.matmul(pp[6][0:1, :], ones_c[:, :],
                         wo_sb[:, kt * 1024 + ec * 512: kt * 1024 + (ec + 1) * 512],
                         start=(kt == 0), stop=(kt == NKT - 1))
        P.add("tensor", f)
        tpm = bump("tensor", tp)

        def f(s, ec=ec, tpm=tpm):
            s.wait_ge(tp.h, tpm)
            s.copy(hspad[0:1, 8 + ec * 512: 8 + (ec + 1) * 512], pp[6][0:1, :])
        P.add("scalar", f)
        spm = bump("scalar", sp)
        wait("tensor", sp, spm)

    it = 0
    vp_ys_hist = {}
    ysw_hist = {}
    for ci in range(NCH):
        if ci > 0:
            wait("gpsimd", vp, vp_ys_hist[(ci - 1) * NDT + NDT])
        bsrc = h_dtbc[64:80, ci * TC:(ci + 1) * TC].unsqueeze(0).broadcast_to([128, NST, TC])
        csrc = h_dtbc[80:96, ci * TC:(ci + 1) * TC].unsqueeze(0).broadcast_to([128, NST, TC])
        dma("gpsimd", bbc[:, :].rearrange("p (n t) -> p n t", n=NST), bsrc, d_bc)
        cbm = dma("gpsimd", cbc[:, :].rearrange("p (n t) -> p n t", n=NST), csrc, d_bc)

        for j in range(NDT):
            it += 1
            bank = it % 2

            def f(t, ci=ci, j=j, bank=bank):
                t.matmul(pp[bank][:, :], wdt_sb[:, j * 128:(j + 1) * 128],
                         dtun_b[:, ci * TC:(ci + 1) * TC], start=True, stop=True)
            P.add("tensor", f)
            tp_dt = bump("tensor", tp)

            def f(s, bank=bank, tp_dt=tp_dt):
                s.wait_ge(tp.h, tp_dt)
                s.activation(out=spf[:, :], in_=pp[bank][:, :], func=AT.Exp)
                s.drain()
                s.activation(out=dts[:, :], in_=spf[:, :], func=AT.Ln, bias=1.0)
            P.add("scalar", f)
            sp_dts = bump("scalar", sp)
            wait("tensor", sp, sp_dts)

            ub = it % 2
            if it > 2:
                wait("sync", vp, vp_ys_hist[it - 2])
            lx_u = dma("sync", usb[ub][:, :], h_y[j * 128:(j + 1) * 128, ci * TC:(ci + 1) * TC], d_u[ub])

            def f(v, j=j, ub=ub, sp_dts=sp_dts, lx_u=lx_u):
                v.wait_ge(sp.h, sp_dts)
                v.wait_ge(d_u[ub].h, lx_u)
                v.tensor_tensor(dus[:, :], dts[:, :], usb[ub][:, :], OP.mult)
                for n in range(NST):
                    v.tensor_scalar(argb[:, n * TC:(n + 1) * TC], dts[:, :],
                                    acol[:, j * NST + n:j * NST + n + 1], None, OP.mult)
            P.add("vector", f)
            vp_arg = bump("vector", vp)

            def f(s, vp_arg=vp_arg):
                s.wait_ge(vp.h, vp_arg)
                s.activation(out=dab[:, :], in_=argb[:, :], func=AT.Exp)
            P.add("scalar", f)
            sp_da = bump("scalar", sp)

            ysb = it % 2
            dview = dus[:, :].unsqueeze(1).broadcast_to([128, NST, TC])

            def f(v, ci=ci, j=j, sp_da=sp_da, cbm=cbm, dview=dview):
                v.wait_ge(d_bc.h, cbm)
                v.wait_ge(sp.h, sp_da)
                da0 = dab[:, :].rearrange("p (n t) -> p n t", n=NST)[:, :, 0]
                bb0 = bsb[:, :].rearrange("p (n t) -> p n t", n=NST)[:, :, 0]
                v.tensor_tensor(fold16[:, :], da0, carry[:, j * NST:(j + 1) * NST], OP.mult)
                v.tensor_tensor(bsb[:, :].rearrange("p (n t) -> p n t", n=NST), dview,
                                bbc[:, :].rearrange("p (n t) -> p n t", n=NST), OP.mult)
                v.drain()
                v.memset(da0, 0.0)
                v.drain()
                v.tensor_tensor(bb0, bb0, fold16[:, :], OP.add)
                v.drain()
                v.tensor_tensor_scan(hsb[:, :], dab[:, :], bsb[:, :], 0.0, OP.mult, OP.add)
            P.add("vector", f)
            vp_scan = bump("vector", vp)

            def f(s, ci=ci, j=j, vp_scan=vp_scan):
                s.wait_ge(vp.h, vp_scan)
                s.activation(out=dab[:, :], in_=hsb[:, :], func=AT.Copy,
                             accum_out=hsumc[:, ci * NDT + j: ci * NDT + j + 1])
            P.add("scalar", f)
            sp_hs = bump("scalar", sp)

            def f(v, j=j, ub=ub, ysb=ysb, sp_hs=sp_hs, it=it):
                if it > 2 and (it - 2) in ysw_hist:
                    v.wait_ge(d_ysw[it % 2].h, ysw_hist[it - 2])
                hl = hsb[:, :].rearrange("p (n t) -> p n t", n=NST)[:, :, TC - 1]
                v.tensor_tensor(argb[:, :], hsb[:, :], cbc[:, :], OP.mult)
                v.tensor_copy(carry[:, j * NST:(j + 1) * NST], hl)
                v.drain()
                v.tensor_tensor(argb[:, 0:8 * TC], argb[:, 0:8 * TC], argb[:, 8 * TC:16 * TC], OP.add)
                v.drain()
                v.tensor_tensor(argb[:, 0:4 * TC], argb[:, 0:4 * TC], argb[:, 4 * TC:8 * TC], OP.add)
                v.drain()
                v.tensor_tensor(argb[:, 0:2 * TC], argb[:, 0:2 * TC], argb[:, 2 * TC:4 * TC], OP.add)
                v.drain()
                v.tensor_tensor(yssb[ysb][:, :], argb[:, 0:TC], argb[:, TC:2 * TC], OP.add)
                v.drain()
                v.scalar_tensor_tensor(yssb[ysb][:, :], usb[ub][:, :], dcol[:, j:j + 1],
                                       yssb[ysb][:, :], OP.mult, OP.add)
                v.wait_ge(sp.h, sp_hs)
            P.add("vector", f)
            vp_ys = bump("vector", vp)
            vp_ys_hist[it] = vp_ys

            wait("gpsimd", vp, vp_ys)
            ysw = dma("gpsimd", h_ys[j * 128:(j + 1) * 128, ci * TC:(ci + 1) * TC],
                      yssb[ysb][:, :], d_ysw[ysb])
            ysw_hist[it] = ysw

    def f(v):
        v.tensor_reduce(hsred[:, 0:1], hsumc[:, :], AX.X, OP.add)
    P.add("vector", f)
    vp_hsr = bump("vector", vp)

    def f(t, vp_hsr=vp_hsr):
        t.wait_ge(vp.h, vp_hsr)
        t.matmul(pp[6][0:1, 0:1], ones_f[:, :], hsred[:, 0:1], start=True, stop=True)
    P.add("tensor", f)
    tp_hs = bump("tensor", tp)

    def f(s, tp_hs=tp_hs):
        s.wait_ge(tp.h, tp_hs)
        s.copy(hspad[0:1, 0:1], pp[6][0:1, 0:1])
    P.add("scalar", f)
    sp_hsf = bump("scalar", sp)
    wait("gpsimd", sp, sp_hsf)
    dma("gpsimd", h_hs_in[:, :], hspad[0:1, :], ld)

    barrier()
    stC.close()

    def f(gp):
        gp.collective_compute(
            "AllReduce", OP.add,
            replica_groups=[[0, 1, 2, 3, 4, 5, 6, 7]],
            ins=[h_hs_in], outs=[h_hs],
        ).then_inc(ccs.h, 1)
    P.add("gpsimd", f)
    cc4 = ccs.res()

    stD = ExitStack()
    eD = stD.enter_context
    yslab = [eD(nc.sbuf_tensor(f"yslab{i}", [128, NKT * 128], BF16)) for i in range(2)]
    opst = [eD(nc.sbuf_tensor(f"opst{i}", [128, 512], BF16)) for i in range(2)]

    op_w_hist = {}
    for tt in range(TF // 128):
        sl = tt % 2
        if tt >= 2:
            wait("sync", tp, op_w_hist[tt - 2])
        lxm = dma("sync", yslab[sl][:, :].rearrange("p (a t) -> p a t", a=NKT),
                  h_ys.rearrange("(a p) t -> p a t", p=128)[:, :, tt * 128:(tt + 1) * 128], d_slab[sl])
        for ec in range(2):
            bank = (tt * 2 + ec) % 4

            def f(t, tt=tt, ec=ec, bank=bank, sl=sl, lxm=lxm):
                t.wait_ge(d_slab[sl].h, lxm)
                for kt in range(NKT):
                    t.matmul(pp[bank][:, :],
                             yslab[sl][:, kt * 128:(kt + 1) * 128],
                             wo_sb[:, kt * 1024 + ec * 512:(kt) * 1024 + (ec + 1) * 512],
                             start=(kt == 0), stop=(kt == NKT - 1))
            P.add("tensor", f)
            tpm = bump("tensor", tp)
            if ec == 1:
                op_w_hist[tt] = tpm

            def f(s, ec=ec, bank=bank, tpm=tpm, st_i=(tt * 2 + ec) % 2, g=ld.n):
                s.wait_ge(tp.h, tpm)
                s.copy(opst[st_i][:, :], pp[bank][:, :])
            P.add("scalar", f)
            spm = bump("scalar", sp)
            wait("tensor", sp, spm)
            wait("gpsimd", sp, spm)
            stb = (tt * 2 + ec) % 2
            opw = dma("gpsimd", h_op[tt * 128:(tt + 1) * 128, ec * 512:(ec + 1) * 512],
                      opst[stb][:, :], d_op[stb])
            wait("scalar", d_op[stb], opw)

    barrier()

    def f(gp):
        gp.collective_compute(
            "ReduceScatter", OP.add,
            replica_groups=[[0, 1], [2, 3], [4, 5], [6, 7]],
            ins=[h_op], outs=[h_od],
        ).then_inc(ccs.h, 1)
    P.add("gpsimd", f)
    cc_rs = ccs.res()
    wait("gpsimd", ccs, cc_rs)
    wait("gpsimd", ccs, cc4)
    barrier()
    stD.close()
    stW.close()

    stE = ExitStack()
    eE = stE.enter_context
    lnb3 = eE(nc.sbuf_tensor("lnb3", [128, NKT * XOW], BF16))

    def lnv3(kt, c0, c1):
        return lnb3[:, kt * XOW + c0: kt * XOW + c1]

    ffnsm3 = eE(nc.sbuf_tensor("ffnsm3", [128, 1024], BF16))
    halc = eE(nc.sbuf_tensor("halc", [128, NKT], F32))
    stT = ExitStack()
    hrow = stT.enter_context(nc.sbuf_tensor("hrow", [1, 1032], F32))
    mrow = stT.enter_context(nc.sbuf_tensor("mrow", [1, E], F32))
    mbc = stT.enter_context(nc.sbuf_tensor("mbc", [128, NKT], F32))
    dodb = [stT.enter_context(nc.sbuf_tensor(f"dodb{i}", [128, TH], BF16)) for i in range(2)]

    ld_hrow = dma("gpsimd", hrow[0:1, :], h_hs[0:1, :], ld)

    def f(v, ld_hrow=ld_hrow):
        v.wait_ge(ld.h, ld_hrow)
        v.tensor_scalar(hrow[0:1, 0:1], hrow[0:1, 0:1], 1e-8 / (4.0 * HS_COUNT), None, OP.mult)
        v.drain()
        v.tensor_scalar(mrow[0:1, :], hrow[0:1, 8:8 + E], hrow[0:1, 0:1], None, OP.mult)
    P.add("vector", f)
    vp_mb = bump("vector", vp)
    wait("gpsimd", vp, vp_mb)
    wmb = dma("gpsimd", h_mb[:, :], mrow[0:1, :], ld)
    wait("gpsimd", ld, wmb)
    for kt in range(NKT):
        dma("gpsimd", mbc[:, kt:kt + 1],
            h_mb[0:1, kt * 128:(kt + 1) * 128].rearrange("a b -> b a"), ld, slow=True)
    ld_mbc = ld.n

    dod_hist = {}
    for et in range(NKT):
        sl = et % 2
        if et >= 2:
            wait("sync", vp, dod_hist[et - 2])
        v_tr = d_dod[sl].res()

        def f(eng, sl=sl, et=et):
            eng.dma_start_transpose(dodb[sl][:, :],
                                    h_od[:, et * 128:(et + 1) * 128]).then_inc(d_dod[sl].h, 16)
        P.add("sync", f)

        def f(v, sl=sl, et=et, v_tr=v_tr, ld_mbc=ld_mbc):
            v.wait_ge(d_dod[sl].h, v_tr)
            if et == 0:
                v.wait_ge(ld.h, ld_mbc)
            v.scalar_tensor_tensor(xov(et, 1, TH + 1), dodb[sl][:, :],
                                   mbc[:, et:et + 1], xov(et, 1, TH + 1),
                                   OP.add, OP.add)
        P.add("vector", f)
        dod_hist[et] = bump("vector", vp)
    barrier()
    stT.close()

    emit_ln(xov, lnv3, LN2_CH)

    def f(v):
        for kt in range(NKT):
            v.tensor_scalar(halc[:, kt:kt + 1], lnv3(kt, TH, TH + 1),
                            mskp_sb[:, kt:kt + 1], None, OP.mult)
    P.add("vector", f)
    vp_pl = bump("vector", vp)
    wait("gpsimd", vp, vp_pl)
    wait("gpsimd", ld, ld.n)
    wpl = dma("gpsimd", h_halo_in.rearrange("(a p) b -> p a b", p=128), halc[:, :].rearrange("p (a b) -> p a b", b=1), ld, slow=True)
    wait("gpsimd", ld, wpl)

    def f(gp):
        gp.collective_compute(
            "AllReduce", OP.add,
            replica_groups=[[0, 1], [2, 3], [4, 5], [6, 7]],
            ins=[h_halo_in], outs=[h_halo],
        ).then_inc(ccs.h, 1)
    P.add("gpsimd", f)
    cc5 = ccs.res()
    wait("gpsimd", ccs, cc5)
    ld_hal = dma("gpsimd", halc[:, :].rearrange("p (a b) -> p a b", b=1), h_halo.rearrange("(a p) b -> p a b", p=128), ld, slow=True)
    wait("vector", ld, ld_hal)

    def f(v):
        for kt in range(NKT):
            v.tensor_scalar(lnv3(kt, 0, 1), halc[:, kt:kt + 1],
                            msku_sb[:, kt:kt + 1], None, OP.mult)
    P.add("vector", f)
    bump("vector", vp)

    stF3 = ExitStack()
    mixb3 = stF3.enter_context(nc.sbuf_tensor("mixb3", [128, NKT * TH], BF16))
    kabuf3 = stF3.enter_context(nc.sbuf_tensor("kabuf3", [128, NKT * TH], BF16))
    wfbuf3 = [stF3.enter_context(nc.sbuf_tensor(f"wf3{i}", [128, NKT * NKT * 128], BF16)) for i in range(2)]
    rbuf3 = ffnsm3[:, 0:512]
    vstage3 = ffnsm3[:, 512:1024]
    emit_ffn(lnv3, (mixb3, rbuf3, kabuf3, wfbuf3, vstage3))
    stF3.close()

    wait("sync", vp, vp.n)
    for kt in range(NKT):
        dma("sync", out_t[kt * 128:(kt + 1) * 128, :], xov(kt, 1, TH + 1), lx)
    lx_fin = lx.n
    wait("sync", lx, lx_fin)
    barrier()
    stE.close()

    block = e(nc.Block())
    P.emit(block)
    st.close()
    return nc


def _host_maps(inputs):
    x = np.asarray(inputs["L0_attn_x"])
    in_g1 = np.asarray(inputs["L0_attn_in_g1"], np.float32)
    in_g2 = np.asarray(inputs["L0_attn_in_g2"], np.float32)
    x_g1 = np.asarray(inputs["L0_attn_x_g1"], np.float32)
    x_g2 = np.asarray(inputs["L0_attn_x_g2"], np.float32)
    dt_g1 = np.asarray(inputs["L0_attn_dt_g1"], np.float32)
    dt_g2 = np.asarray(inputs["L0_attn_dt_g2"], np.float32)
    out_g1 = np.asarray(inputs["L0_attn_out_g1"], np.float32)
    out_g2 = np.asarray(inputs["L0_attn_out_g2"], np.float32)
    cf = np.asarray(inputs["L0_attn_conv_filter"], np.float32)
    cb = np.asarray(inputs["L0_attn_conv_bias"], np.float32)
    alog_a = np.asarray(inputs["L0_attn_A_log"], np.float32)
    dv = np.asarray(inputs["L0_attn_D"], np.float32)
    wr = np.asarray(inputs["L0_ffn_W_r"], np.float32)
    wk = np.asarray(inputs["L0_ffn_W_k"], np.float32)
    wv = np.asarray(inputs["L0_ffn_W_v"], np.float32)
    mur = np.asarray(inputs["L0_ffn_mu_r"], np.float32)
    muk = np.asarray(inputs["L0_ffn_mu_k"], np.float32)

    C = np.ascontiguousarray
    wr_t = C(wr.T)
    wk_t = C(wk.T)
    wv_t = C(wv.T)
    g2in_m = C(in_g2.reshape(16, 2048))
    g2x_m = C(x_g2.reshape(16, 768))
    g2dt_m = C(dt_g2.reshape(16, 512))
    g2out_m = C(out_g2.reshape(16, 2048))

    maps = []
    for c in range(8):
        b, h = c // 2, c % 2
        xs = x[b, h * TH:(h + 1) * TH, :].T.astype(np.float32)
        halo = (x[b, h * TH - 1:h * TH, :].T.astype(np.float32) if h == 1
                else np.zeros((E, 1), np.float32))
        xin_m = C(np.concatenate([halo, xs], axis=1))

        iperm = list(range(h * 16, (h + 1) * 16)) + list(range(32 + h * 16, 32 + (h + 1) * 16))
        g1in_m = C(in_g1[:, iperm, :].reshape(32 * 32, 16).T)
        aperm = list(range(h * 16, (h + 1) * 16))
        g1x_m = C(x_g1[aperm].reshape(16 * 8, 16).T)
        ipermd = list(range(h * 16, (h + 1) * 16))
        g1dt_m = C(dt_g1[:, ipermd, :].reshape(8 * 16, 16).T)
        g1out_m = C(out_g1[aperm].reshape(16 * 32, 16).T)

        dh = slice(h * DH, (h + 1) * DH)
        maps.append(dict(
            xin=xin_m,
            wr_t=wr_t, wk_t=wk_t, wv_t=wv_t,
            mu_r=C(mur.reshape(E, 1)), mu_k=C(muk.reshape(E, 1)),
            g1in=g1in_m, g2in=g2in_m,
            g1x=g1x_m, g2x=g2x_m,
            g1dt=g1dt_m, g2dt=g2dt_m,
            g1out=g1out_m, g2out=g2out_m,
            cfin=C(cf[:, 0, dh].T),
            cbin=C(cb[dh].reshape(DH, 1)),
            alog=C(alog_a[dh]),
            dvin=C(dv[dh].reshape(DH, 1)),
            maskp=np.full((E, 1), 1.0 - h, np.float32),
            masku=np.full((E, 1), float(h), np.float32),
        ))
    return maps


def kernel(**inputs):
    maps = _host_maps(inputs)
    nc = build_nc()
    res = run_bass_kernel_spmd(nc, maps, core_ids=list(range(8)))
    out = np.empty((4, TF, E), np.float32)
    for c in range(8):
        b, h = c // 2, c % 2
        out[b, h * TH:(h + 1) * TH, :] = res.results[c]["out"].T
    return out


# version 04
# speedup vs baseline: 1.1049x; 1.1049x over previous
import numpy as np
from contextlib import ExitStack

import concourse.bass as bass
import concourse.mybir as mybir
from concourse.bass_utils import run_bass_kernel_spmd
from concourse.aot_env import allow_ici_all_to_all

F32 = mybir.dt.float32
BF16 = mybir.dt.bfloat16
AT = mybir.ActivationFunctionType
OP = mybir.AluOpType
AX = mybir.AxisListType

E = 1024
DIN = 2048
DH = 1024
NST = 16
TF = 4096
TH = 2048
TC = 512
NCH = TF // TC
NDT = DH // 128
NKT = E // 128
XOW = TH + 1
HS_COUNT = 4.0 * TF * DIN * NST

ENGS = ["sync", "gpsimd", "vector", "scalar", "tensor"]


class Cnt:
    def __init__(self, h, step=1):
        self.h = h
        self.n = 0
        self.step = step

    def res(self, k=None):
        self.n += self.step if k is None else k
        return self.n


class Prog:
    def __init__(self):
        self.q = {en: [] for en in ENGS}

    def add(self, en, f):
        self.q[en].append(f)

    def emit(self, block):
        for en in ENGS:
            fs = self.q[en]

            def runner(eng, fs=fs):
                for f in fs:
                    f(eng)
            getattr(block, en)(runner)


def build_nc():
    nc = bass.Bass("TRN2", target_bir_lowering=False)

    xin = nc.dram_tensor("xin", [E, XOW], F32, kind="ExternalInput").ap()
    wr_t = nc.dram_tensor("wr_t", [E, E], F32, kind="ExternalInput").ap()
    wk_t = nc.dram_tensor("wk_t", [E, E], F32, kind="ExternalInput").ap()
    wv_t = nc.dram_tensor("wv_t", [E, E], F32, kind="ExternalInput").ap()
    mu_r = nc.dram_tensor("mu_r", [E, 1], F32, kind="ExternalInput").ap()
    mu_k = nc.dram_tensor("mu_k", [E, 1], F32, kind="ExternalInput").ap()
    g1in = nc.dram_tensor("g1in", [16, 1024], F32, kind="ExternalInput").ap()
    g2in = nc.dram_tensor("g2in", [16, 2048], F32, kind="ExternalInput").ap()
    g1x = nc.dram_tensor("g1x", [16, 128], F32, kind="ExternalInput").ap()
    g2x = nc.dram_tensor("g2x", [16, 768], F32, kind="ExternalInput").ap()
    g1dt = nc.dram_tensor("g1dt", [16, 128], F32, kind="ExternalInput").ap()
    g2dt = nc.dram_tensor("g2dt", [16, 512], F32, kind="ExternalInput").ap()
    g1out = nc.dram_tensor("g1out", [16, 512], F32, kind="ExternalInput").ap()
    g2out = nc.dram_tensor("g2out", [16, 2048], F32, kind="ExternalInput").ap()
    cfin = nc.dram_tensor("cfin", [DH, 4], F32, kind="ExternalInput").ap()
    cbin = nc.dram_tensor("cbin", [DH, 1], F32, kind="ExternalInput").ap()
    alog = nc.dram_tensor("alog", [DH, NST], F32, kind="ExternalInput").ap()
    dvin = nc.dram_tensor("dvin", [DH, 1], F32, kind="ExternalInput").ap()
    maskp = nc.dram_tensor("maskp", [E, 1], F32, kind="ExternalInput").ap()
    masku = nc.dram_tensor("masku", [E, 1], F32, kind="ExternalInput").ap()
    out_t = nc.dram_tensor("out", [E, TH], F32, kind="ExternalOutput").ap()

    h_wtmp = nc.dram_tensor("h_wtmp", [2048, 2048], BF16).ap()
    h_wtmpx = nc.dram_tensor("h_wtmpx", [128, 768], BF16).ap()
    h_wtmpdt = nc.dram_tensor("h_wtmpdt", [128, 512], BF16).ap()
    h_ln2own = nc.dram_tensor("h_ln2own", [E, TH], BF16).ap()
    h_ln2full = nc.dram_tensor("h_ln2full", [2 * E, TH], BF16).ap()
    h_y = nc.dram_tensor("h_y", [DH, TF], BF16).ap()
    h_dtbc_in = nc.dram_tensor("h_dtbc_in", [96, TF], BF16).ap()
    h_dtbc = nc.dram_tensor("h_dtbc", [96, TF], BF16).ap()
    h_ys = nc.dram_tensor("h_ys", [DH, TF], BF16).ap()
    h_op = nc.dram_tensor("h_op", [TF, E], BF16).ap()
    h_od = nc.dram_tensor("h_od", [TH, E], BF16).ap()
    h_hs_in = nc.dram_tensor("h_hs_in", [1, 1032], F32).ap()
    h_hs = nc.dram_tensor("h_hs", [1, 1032], F32).ap()
    h_halo_in = nc.dram_tensor("h_halo_in", [E, 1], F32).ap()
    h_halo = nc.dram_tensor("h_halo", [E, 1], F32).ap()
    h_stat = nc.dram_tensor("h_stat", [2, XOW], BF16).ap()
    h_mb = nc.dram_tensor("h_mb", [1, E], F32).ap()

    st = ExitStack()
    e = st.enter_context

    BARs = e(nc.semaphore("BAR"))
    bar_n = [0]
    ld = Cnt(e(nc.semaphore("ld")), 16)
    lx = Cnt(e(nc.semaphore("lx")), 16)
    tp = Cnt(e(nc.semaphore("tp")), 1)
    sp = Cnt(e(nc.semaphore("sp")), 1)
    vp = Cnt(e(nc.semaphore("vp")), 1)
    ccs = Cnt(e(nc.semaphore("ccs")), 1)
    d_slab = [Cnt(e(nc.semaphore(f"dslab{i}")), 16) for i in range(2)]
    d_u = [Cnt(e(nc.semaphore(f"du{i}")), 16) for i in range(2)]
    d_ysw = [Cnt(e(nc.semaphore(f"dysw{i}")), 16) for i in range(2)]
    d_op = [Cnt(e(nc.semaphore(f"dop{i}")), 16) for i in range(2)]
    d_dod = [Cnt(e(nc.semaphore(f"ddod{i}")), 16) for i in range(2)]
    d_wb = [Cnt(e(nc.semaphore(f"dwb{i}")), 16) for i in range(2)]
    d_bc = Cnt(e(nc.semaphore("dbc")), 16)
    d_y = Cnt(e(nc.semaphore("dy")), 16)
    d_stat = Cnt(e(nc.semaphore("dstat")), 16)

    SYNC_SEMS = [lx] + d_slab + d_u + d_dod
    GP_SEMS = [ld, d_bc, d_y, d_stat] + d_ysw + d_op + d_wb

    P = Prog()

    def barrier():
        bar_n[0] += 1
        tgt = 5 * bar_n[0]
        waits = {"sync": [(c, c.n) for c in SYNC_SEMS],
                 "gpsimd": [(c, c.n) for c in GP_SEMS],
                 "vector": [(vp, vp.n)],
                 "scalar": [(sp, sp.n)],
                 "tensor": [(tp, tp.n)]}
        for en in ENGS:
            ws = waits[en]

            def f(eng, ws=ws, tgt=tgt):
                for c, v in ws:
                    if v > 0:
                        eng.wait_ge(c.h, v)
                eng.sem_inc(BARs, 1)
                eng.wait_ge(BARs, tgt)
            P.add(en, f)

    def dma(en, dst, src, cnt, slow=False):
        v = cnt.res()

        def f(eng, dst=dst, src=src, slow=slow):
            if slow:
                with nc.allow_non_contiguous_dma(reason="tiny gather"):
                    eng.dma_start(out=dst, in_=src).then_inc(cnt.h, cnt.step)
            else:
                eng.dma_start(out=dst, in_=src).then_inc(cnt.h, cnt.step)
        P.add(en, f)
        return v

    def wait(en, cnt, v):
        if v <= 0:
            return

        def f(eng, cnt=cnt, v=v):
            eng.wait_ge(cnt.h, v)
        P.add(en, f)

    cur_scope = {en: None for en in ENGS}

    def phase(name):
        for en in ENGS:
            def f(eng, en=en, name=name):
                if cur_scope[en] is not None:
                    pn, pid = cur_scope[en]
                    nc.leave_named_scope(pn, pid, False)
                sid, _ = nc.enter_named_scope(name, False)
                cur_scope[en] = (name, sid)
            P.add(en, f)

    def phase_end():
        for en in ENGS:
            def f(eng, en=en):
                if cur_scope[en] is not None:
                    pn, pid = cur_scope[en]
                    nc.leave_named_scope(pn, pid, False)
                    cur_scope[en] = None
            P.add(en, f)

    def bump(en, cnt):
        def f(eng, cnt=cnt):
            eng.maybe_drain_then_inc((cnt.h, cnt.step))
        P.add(en, f)
        return cnt.res()

    xo = e(nc.sbuf_tensor("xo", [128, NKT * XOW], F32))

    def xov(kt, c0, c1):
        return xo[:, kt * XOW + c0: kt * XOW + c1]

    mu_r_sb = e(nc.sbuf_tensor("mu_r_sb", [128, NKT], F32))
    mu_k_sb = e(nc.sbuf_tensor("mu_k_sb", [128, NKT], F32))
    mu_r1_sb = e(nc.sbuf_tensor("mu_r1_sb", [128, NKT], F32))
    mu_k1_sb = e(nc.sbuf_tensor("mu_k1_sb", [128, NKT], F32))
    msku_sb = e(nc.sbuf_tensor("msku_sb", [128, NKT], F32))
    mskp_sb = e(nc.sbuf_tensor("mskp_sb", [128, NKT], F32))
    ones_c = e(nc.sbuf_tensor("ones_c", [128, 1], BF16))
    ones_f = e(nc.sbuf_tensor("ones_f", [128, 1], F32))
    pp = [e(nc.psum_tensor(f"pp{i}", [128, 512], F32)) for i in range(8)]

    phase("p0_load")
    for kt in range(NKT):
        dma("sync", xov(kt, 0, XOW), xin[kt * 128:(kt + 1) * 128, :], lx)
    dma("gpsimd", mu_r_sb[:, :].rearrange("p (a b) -> p a b", b=1), mu_r.rearrange("(a p) b -> p a b", p=128), ld, slow=True)
    dma("gpsimd", mu_k_sb[:, :].rearrange("p (a b) -> p a b", b=1), mu_k.rearrange("(a p) b -> p a b", p=128), ld, slow=True)
    dma("gpsimd", msku_sb[:, :].rearrange("p (a b) -> p a b", b=1), masku.rearrange("(a p) b -> p a b", p=128), ld, slow=True)
    dma("gpsimd", mskp_sb[:, :].rearrange("p (a b) -> p a b", b=1), maskp.rearrange("(a p) b -> p a b", p=128), ld, slow=True)

    ld_mu = ld.n

    def f(v, ld_mu=ld_mu):
        v.memset(ones_c[:, :], 1.0)
        v.memset(ones_f[:, :], 1.0)
        v.wait_ge(ld.h, ld_mu)
        v.tensor_scalar(mu_r1_sb[:, :], mu_r_sb[:, :], -1.0, 1.0, OP.mult, OP.add)
        v.tensor_scalar(mu_k1_sb[:, :], mu_k_sb[:, :], -1.0, 1.0, OP.mult, OP.add)
    P.add("vector", f)
    bump("vector", vp)

    barrier()

    def emit_ln(xview, lnview, chunks):
        stL = ExitStack()
        sqbuf = stL.enter_context(nc.sbuf_tensor(None, [128, 2 * 512], BF16))
        s_m = stL.enter_context(nc.sbuf_tensor(None, [1, XOW], F32))
        s_v = stL.enter_context(nc.sbuf_tensor(None, [1, XOW], F32))
        s_i = stL.enter_context(nc.sbuf_tensor(None, [1, XOW], F32))
        invb_ = stL.enter_context(nc.sbuf_tensor(None, [128, XOW], BF16))
        msbb_ = stL.enter_context(nc.sbuf_tensor(None, [128, XOW], BF16))
        c_lo = chunks[0][0]
        c_hi = chunks[-1][1]
        wdt = c_hi - c_lo
        wait("scalar", vp, vp.n)

        def f(s):
            for kt in range(NKT):
                s.activation(out=lnview(kt, c_lo, c_hi), in_=xview(kt, c_lo, c_hi), func=AT.Copy)
        P.add("scalar", f)
        sp_cast = bump("scalar", sp)
        wait("vector", sp, sp_cast)

        tp_pair = {}
        gix = 0
        for (c0, c1) in chunks:
            wd = c1 - c0
            for kt in range(NKT):
                gix += 1
                def f(v, kt=kt, c0=c0, c1=c1, wd=wd, tpg=tp_pair.get(gix - 2, 0)):
                    if tpg:
                        v.wait_ge(tp.h, tpg)
                    v.tensor_tensor(sqbuf[:, (gix % 2) * 512:(gix % 2) * 512 + wd] if False else
                                    sqbuf[:, (kt % 2) * 512:(kt % 2) * 512 + wd],
                                    lnview(kt, c0, c1), lnview(kt, c0, c1), OP.mult)
                P.add("vector", f)
                vpm = bump("vector", vp)

                def f(t, kt=kt, c0=c0, c1=c1, wd=wd, vpm=vpm):
                    t.wait_ge(vp.h, vpm)
                    t.matmul(pp[6][0:1, 0:wd], ones_c[:, :], lnview(kt, c0, c1),
                             start=(kt == 0), stop=(kt == NKT - 1))
                    t.matmul(pp[7][0:1, 0:wd], ones_c[:, :],
                             sqbuf[:, (kt % 2) * 512:(kt % 2) * 512 + wd],
                             start=(kt == 0), stop=(kt == NKT - 1))
                P.add("tensor", f)
                tp_pair[gix] = bump("tensor", tp)
            tpm = tp_pair[gix]

            def f(s, c0=c0, c1=c1, wd=wd, tpm=tpm):
                s.wait_ge(tp.h, tpm)
                s.copy(s_m[0:1, c0 - c_lo:c1 - c_lo], pp[6][0:1, 0:wd])
                s.copy(s_v[0:1, c0 - c_lo:c1 - c_lo], pp[7][0:1, 0:wd])
            P.add("scalar", f)
            spm = bump("scalar", sp)
            wait("tensor", sp, spm)
        sp_ev = sp.n

        def f(v, sp_ev=sp_ev, wdt=wdt):
            v.wait_ge(sp.h, sp_ev)
            v.tensor_scalar(s_m[0:1, 0:wdt], s_m[0:1, 0:wdt], 1.0 / E, None, OP.mult)
            v.tensor_scalar(s_v[0:1, 0:wdt], s_v[0:1, 0:wdt], 1.0 / E, None, OP.mult)
            v.drain()
            v.tensor_tensor(s_i[0:1, 0:wdt], s_m[0:1, 0:wdt], s_m[0:1, 0:wdt], OP.mult)
            v.drain()
            v.tensor_tensor(s_v[0:1, 0:wdt], s_v[0:1, 0:wdt], s_i[0:1, 0:wdt], OP.subtract)
            v.drain()
            v.tensor_scalar(s_v[0:1, 0:wdt], s_v[0:1, 0:wdt], 1e-5, None, OP.add)
        P.add("vector", f)
        vp_var = bump("vector", vp)

        def f(s, vp_var=vp_var, wdt=wdt):
            s.wait_ge(vp.h, vp_var)
            s.activation(out=s_i[0:1, 0:wdt], in_=s_v[0:1, 0:wdt], func=AT.Ln)
        P.add("scalar", f)
        sp_inv = bump("scalar", sp)

        def f(v, sp_inv=sp_inv, wdt=wdt):
            v.wait_ge(sp.h, sp_inv)
            v.tensor_scalar(s_i[0:1, 0:wdt], s_i[0:1, 0:wdt], -0.5, None, OP.mult)
        P.add("vector", f)
        vp_l2 = bump("vector", vp)

        def f(s, vp_l2=vp_l2, wdt=wdt):
            s.wait_ge(vp.h, vp_l2)
            s.activation(out=s_i[0:1, 0:wdt], in_=s_i[0:1, 0:wdt], func=AT.Exp)
        P.add("scalar", f)
        sp_inv2 = bump("scalar", sp)

        def f(v, sp_inv2=sp_inv2, wdt=wdt):
            v.wait_ge(sp.h, sp_inv2)
            v.tensor_tensor(s_v[0:1, 0:wdt], s_m[0:1, 0:wdt], s_i[0:1, 0:wdt], OP.mult)
        P.add("vector", f)
        vp_ms = bump("vector", vp)

        wait("gpsimd", vp, vp_ms)
        dma("gpsimd", h_stat[0:1, 0:wdt], s_i[0:1, 0:wdt], d_stat)
        w2 = dma("gpsimd", h_stat[1:2, 0:wdt], s_v[0:1, 0:wdt], d_stat)
        wait("gpsimd", d_stat, w2)
        dma("gpsimd", invb_[:, 0:wdt], h_stat[0:1, 0:wdt].broadcast_to([128, wdt]), d_stat)
        wbc = dma("gpsimd", msbb_[:, 0:wdt], h_stat[1:2, 0:wdt].broadcast_to([128, wdt]), d_stat)
        wait("vector", d_stat, wbc)

        def f(v):
            for kt in range(NKT):
                v.tensor_tensor(lnview(kt, c_lo, c_hi), lnview(kt, c_lo, c_hi), invb_[:, 0:wdt], OP.mult)
            v.drain()
            for kt in range(NKT):
                v.tensor_tensor(lnview(kt, c_lo, c_hi), lnview(kt, c_lo, c_hi), msbb_[:, 0:wdt], OP.subtract)
        P.add("vector", f)
        r = bump("vector", vp)
        barrier()
        stL.close()
        return r

    def emit_ffn(lnview, bufs):
        (mixb, rbuf, kabuf, wfbuf, vstage) = bufs
        wait("gpsimd", ld, ld.n)
        for kt in range(NKT):
            dma("gpsimd", wfbuf[0][:, kt * 1024:(kt + 1) * 1024],
                wk_t[kt * 128:(kt + 1) * 128, :], ld)
        wk_done = ld.n

        def mix(mu_sb, mu1_sb):
            def f(v, mu_sb=mu_sb, mu1_sb=mu1_sb):
                for kt in range(NKT):
                    v.tensor_scalar(mixb[:, kt * TH:(kt + 1) * TH], lnview(kt, 1, TH + 1),
                                    mu_sb[:, kt:kt + 1], None, OP.mult)
                v.drain()
                for kt in range(NKT):
                    v.scalar_tensor_tensor(mixb[:, kt * TH:(kt + 1) * TH], lnview(kt, 0, TH),
                                           mu1_sb[:, kt:kt + 1], mixb[:, kt * TH:(kt + 1) * TH],
                                           OP.mult, OP.add)
            P.add("vector", f)
            return bump("vector", vp)

        mix(mu_k_sb, mu_k1_sb)

        def proj(wbuf_idx, store, rhs_buf):
            wait("tensor", vp, vp.n)
            pr_sph = {}
            for mt in range(NKT):
                for tc in range(4):
                    ipr = mt * 4 + tc
                    bank = ipr % 4
                    if ipr >= 4:
                        wait("tensor", sp, pr_sph[ipr - 4])

                    def f(t, mt=mt, tc=tc, bank=bank, wbuf_idx=wbuf_idx, rhs_buf=rhs_buf):
                        for kt in range(NKT):
                            t.matmul(pp[bank][:, :],
                                     wfbuf[wbuf_idx][:, (kt * NKT + mt) * 128:(kt * NKT + mt + 1) * 128],
                                     rhs_buf[:, kt * TH + tc * 512:kt * TH + (tc + 1) * 512],
                                     start=(kt == 0), stop=(kt == NKT - 1))
                    P.add("tensor", f)
                    tpm = bump("tensor", tp)

                    def f(s, mt=mt, tc=tc, bank=bank, tpm=tpm):
                        s.wait_ge(tp.h, tpm)
                        store(mt, tc, pp[bank], s)
                    P.add("scalar", f)
                    pr_sph[ipr] = bump("scalar", sp)

        wait("tensor", ld, wk_done)

        def store_k(mt, tc, ps, s):
            s.activation(out=kabuf[:, mt * TH + tc * 512:mt * TH + (tc + 1) * 512],
                         in_=ps[:, :], func=AT.Relu)
        proj(0, store_k, mixb)
        sp_k = sp.n
        tp_k = tp.n

        wait("gpsimd", ld, ld.n)
        for kt in range(NKT):
            dma("gpsimd", wfbuf[1][:, kt * 1024:(kt + 1) * 1024],
                wr_t[kt * 128:(kt + 1) * 128, :], ld)
        wr_done = ld.n

        wait("vector", sp, sp_k)

        def f(v):
            for kt in range(NKT):
                v.tensor_tensor(kabuf[:, kt * TH:(kt + 1) * TH], kabuf[:, kt * TH:(kt + 1) * TH],
                                kabuf[:, kt * TH:(kt + 1) * TH], OP.mult)
        P.add("vector", f)
        wait("vector", tp, tp_k)
        mix(mu_r_sb, mu_r1_sb)

        wait("gpsimd", tp, tp_k)
        wait("gpsimd", ld, ld.n)
        for kt in range(NKT):
            dma("gpsimd", wfbuf[0][:, kt * 1024:(kt + 1) * 1024],
                wv_t[kt * 128:(kt + 1) * 128, :], ld)
        wv_done = ld.n
        wait("tensor", ld, wr_done)
        wait("tensor", ld, wv_done)
        wait("tensor", vp, vp.n)
        vp_hist = {}
        itf = 0
        for mt in range(NKT):
            for tc in range(4):
                itf += 1
                rb = (2 * itf) % 4
                vb = (2 * itf + 1) % 4

                def f(t, mt=mt, tc=tc, rb=rb, vb=vb, g=vp_hist.get(itf - 2, 0)):
                    if g:
                        t.wait_ge(vp.h, g)
                    for kt in range(NKT):
                        t.matmul(pp[rb][:, :],
                                 wfbuf[1][:, (kt * NKT + mt) * 128:(kt * NKT + mt + 1) * 128],
                                 mixb[:, kt * TH + tc * 512:kt * TH + (tc + 1) * 512],
                                 start=(kt == 0), stop=(kt == NKT - 1))
                    for kt in range(NKT):
                        t.matmul(pp[vb][:, :],
                                 wfbuf[0][:, (kt * NKT + mt) * 128:(kt * NKT + mt + 1) * 128],
                                 kabuf[:, kt * TH + tc * 512:kt * TH + (tc + 1) * 512],
                                 start=(kt == 0), stop=(kt == NKT - 1))
                P.add("tensor", f)
                tpm = bump("tensor", tp)

                def f(s, rb=rb, tpm=tpm, g=vp_hist.get(itf - 1, 0)):
                    s.wait_ge(tp.h, tpm)
                    if g:
                        s.wait_ge(vp.h, g)
                    s.activation(out=rbuf, in_=pp[rb][:, :], func=AT.Sigmoid)
                P.add("scalar", f)
                spm = bump("scalar", sp)

                def f(v, mt=mt, tc=tc, vb=vb, spm=spm):
                    v.wait_ge(sp.h, spm)
                    v.tensor_tensor(vstage, pp[vb][:, :], rbuf, OP.mult)
                    v.drain()
                    v.scalar_tensor_tensor(xov(mt, 1 + tc * 512, 1 + (tc + 1) * 512),
                                           vstage, 0.5,
                                           xov(mt, 1 + tc * 512, 1 + (tc + 1) * 512),
                                           OP.mult, OP.add)
                P.add("vector", f)
                vp_hist[itf] = bump("vector", vp)
        return bump("vector", vp)

    LN1_CH = [(0, 512), (512, 1024), (1024, 1536), (1536, 2048), (2048, 2049)]
    LN2_CH = [(1, 513), (513, 1025), (1025, 1537), (1537, 2049)]

    stA = ExitStack()
    eA = stA.enter_context
    lnb = eA(nc.sbuf_tensor("lnb", [128, NKT * XOW], BF16))

    def lnv(kt, c0, c1):
        return lnb[:, kt * XOW + c0: kt * XOW + c1]

    ffnsm = eA(nc.sbuf_tensor("ffnsm", [128, 1024], BF16))

    stG = ExitStack()
    g_sb = stG.enter_context(nc.sbuf_tensor("g_sb", [16, 7680], BF16))
    wstage = stG.enter_context(nc.sbuf_tensor("wstage", [128, 2048], BF16))
    go = {}
    _off = 0
    for nm, wd in [("g1in", 1024), ("g2in", 2048), ("g1x", 128), ("g2x", 768),
                   ("g1dt", 128), ("g2dt", 512), ("g1out", 512), ("g2out", 2048)]:
        go[nm] = (_off, _off + wd)
        _off += wd

    def gvw(nm):
        a, b_ = go[nm]
        return g_sb[:, a:b_]

    for nm, gsrc in [("g1in", g1in), ("g2in", g2in), ("g1x", g1x), ("g2x", g2x),
                     ("g1dt", g1dt), ("g2dt", g2dt), ("g1out", g1out), ("g2out", g2out)]:
        dma("gpsimd", gvw(nm), gsrc, ld)
    ld_g = ld.n
    wait("tensor", ld, ld_g)

    phase("ln1")
    emit_ln(xov, lnv, LN1_CH)

    def build_wtmp(g1name, g2name, npt, fchunks, rowbase, dst_hbm):
        for pt in range(npt):
            for ci, (c0, c1) in enumerate(fchunks):
                wd = c1 - c0
                bank = (pt * len(fchunks) + ci) % 2

                def f(t, pt=pt, c0=c0, wd=wd, bank=bank, g1name=g1name, g2name=g2name):
                    t.matmul(pp[4 + bank][:, 0:wd], gvw(g1name)[:, pt * 128:(pt + 1) * 128],
                             gvw(g2name)[:, c0:c0 + wd], start=True, stop=True)
                P.add("tensor", f)
                tpm = bump("tensor", tp)

                def f(s, tpm=tpm, wd=wd, bank=bank):
                    s.wait_ge(tp.h, tpm)
                    s.copy(wstage[:, bank * 512:bank * 512 + wd], pp[4 + bank][:, 0:wd])
                P.add("scalar", f)
                spm = bump("scalar", sp)
                wait("tensor", sp, spm)
                wait("gpsimd", sp, spm)
                wdm = dma("gpsimd", dst_hbm[rowbase + pt * 128: rowbase + (pt + 1) * 128, c0:c0 + wd],
                          wstage[:, bank * 512:bank * 512 + wd], d_wb[bank])
                wait("scalar", d_wb[bank], wdm)

    phase("wbuild")
    F4 = [(0, 512), (512, 1024), (1024, 1536), (1536, 2048)]
    build_wtmp("g1in", "g2in", 8, F4, 0, h_wtmp)
    build_wtmp("g1out", "g2out", 4, F4, 1024, h_wtmp)
    build_wtmp("g1x", "g2x", 1, [(0, 512), (512, 768)], 0, h_wtmpx)
    build_wtmp("g1dt", "g2dt", 1, [(0, 512)], 0, h_wtmpdt)
    barrier()
    stG.close()

    stF = ExitStack()
    mixb = stF.enter_context(nc.sbuf_tensor("mixb", [128, NKT * TH], BF16))
    kabuf = stF.enter_context(nc.sbuf_tensor("kabuf", [128, NKT * TH], BF16))
    wfbuf = [stF.enter_context(nc.sbuf_tensor(f"wf{i}", [128, NKT * NKT * 128], BF16)) for i in range(2)]
    rbuf = ffnsm[:, 0:512]
    vstage = ffnsm[:, 512:1024]

    phase("ffn1")
    emit_ffn(lnv, (mixb, rbuf, kabuf, wfbuf, vstage))
    barrier()
    stF.close()

    phase("ln2")
    emit_ln(xov, lnv, LN2_CH)
    wait("sync", vp, vp.n)
    for kt in range(NKT):
        dma("sync", h_ln2own[kt * 128:(kt + 1) * 128, :], lnv(kt, 1, TH + 1), lx)

    barrier()
    stA.close()

    phase("e1")
    def f(gp):
        gp.collective_compute(
            "AllGather", OP.bypass,
            replica_groups=[[0, 1], [2, 3], [4, 5], [6, 7]],
            ins=[h_ln2own], outs=[h_ln2full],
        ).then_inc(ccs.h, 1)
    P.add("gpsimd", f)
    cc1 = ccs.res()
    wait("gpsimd", ccs, cc1)
    barrier()

    phase("front")
    stB = ExitStack()
    eB = stB.enter_context
    wi_sb = eB(nc.sbuf_tensor("wi_sb", [128, NKT * 2048], BF16))
    wx_sb = eB(nc.sbuf_tensor("wx_sb", [128, NKT * 96], BF16))
    slab = [eB(nc.sbuf_tensor(f"slab{i}", [128, NKT * 512], BF16)) for i in range(2)]
    XCW = TF + 3
    xcp = eB(nc.sbuf_tensor("xcp", [128, XCW], BF16))
    zb = eB(nc.sbuf_tensor("zb", [128, TF], BF16))
    cacc = eB(nc.sbuf_tensor("cacc", [128, TF], BF16))
    ctmp = eB(nc.sbuf_tensor("ctmp", [128, TF], BF16))
    convd = eB(nc.sbuf_tensor("convd", [128, TF], BF16))
    ytile = eB(nc.sbuf_tensor("ytile", [128, TF], BF16))
    cf_sb = eB(nc.sbuf_tensor("cf_sb", [128, NDT * 4], F32))
    cb_sb = eB(nc.sbuf_tensor("cb_sb", [128, NDT], F32))
    dtbc_sb = eB(nc.sbuf_tensor("dtbc_sb", [96, TF], BF16))

    for kt in range(NKT):
        for ai in range(4):
            a = 4 * kt + ai
            srcv = h_wtmp[a * 32: a * 32 + 32, :].rearrange("i (b j) -> b i j", j=64)
            dstv = wi_sb[ai * 32:(ai + 1) * 32, kt * 2048:(kt + 1) * 2048].rearrange(
                "b (i j) -> b i j", j=64)
            dma("gpsimd", dstv, srcv, ld)
    for kt in range(NKT):
        for ai in range(2):
            a_loc = 2 * kt + ai
            srcv = h_wtmpx[a_loc * 8: a_loc * 8 + 8, :].rearrange("i (b j) -> b i j", j=12)
            dstv = wx_sb[ai * 64:(ai + 1) * 64, kt * 96:(kt + 1) * 96].rearrange(
                "b (i j) -> b i j", j=12)
            dma("gpsimd", dstv, srcv, ld)
    dma("gpsimd", cf_sb[:, :].rearrange("p (a b) -> p a b", b=4), cfin.rearrange("(a p) b -> p a b", p=128), ld, slow=True)
    ld_wi = dma("gpsimd", cb_sb[:, :].rearrange("p (a b) -> p a b", b=1), cbin.rearrange("(a p) b -> p a b", p=128), ld, slow=True)
    wait("tensor", ld, ld_wi)

    vp_y_prev = 0
    ld_y_prev = 0
    ip_sph = {}
    ip_tph = {}
    for j in range(NDT):
        for tc in range(NCH):
            sl = tc % 2
            gi = j * NCH + tc
            if gi >= 2:
                wait("sync", tp, ip_tph[gi - 2])
            src = h_ln2full.rearrange("(s p) t -> s p t", s=2)[tc // 4, :, (tc % 4) * 512:(tc % 4 + 1) * 512]
            srcv = src.rearrange("(a p) t -> p a t", p=128)
            lxm = dma("sync", slab[sl][:, :].rearrange("p (a t) -> p a t", a=NKT), srcv, d_slab[sl])
            for which in range(2):
                mcol = (0 if which == 0 else 1024) + j * 128
                iw = gi * 2 + which
                bank = iw % 4
                if iw >= 4:
                    wait("tensor", sp, ip_sph[iw - 4])

                def f(t, mcol=mcol, bank=bank, sl=sl, lxm=lxm):
                    t.wait_ge(d_slab[sl].h, lxm)
                    for kt in range(NKT):
                        t.matmul(pp[bank][:, :],
                                 wi_sb[:, kt * 2048 + mcol: kt * 2048 + mcol + 128],
                                 slab[sl][:, kt * 512:(kt + 1) * 512],
                                 start=(kt == 0), stop=(kt == NKT - 1))
                P.add("tensor", f)
                tpm = bump("tensor", tp)

                def f(s, which=which, tc=tc, bank=bank, tpm=tpm, vp_y_prev=vp_y_prev):
                    s.wait_ge(tp.h, tpm)
                    if tc == 0 and which == 0 and vp_y_prev:
                        s.wait_ge(vp.h, vp_y_prev)
                    if which == 0:
                        s.copy(xcp[:, 1 + tc * 512: 1 + (tc + 1) * 512], pp[bank][:, :])
                    else:
                        s.copy(zb[:, tc * 512:(tc + 1) * 512], pp[bank][:, :])
                P.add("scalar", f)
                ip_sph[iw] = bump("scalar", sp)
            ip_tph[gi] = tp.n
        sp_xz = sp.n

        def f(v, j=j, sp_xz=sp_xz, ld_y_prev=ld_y_prev):
            v.wait_ge(sp.h, sp_xz)
            if ld_y_prev:
                v.wait_ge(d_y.h, ld_y_prev)
            v.memset(xcp[:, 0:1], 0.0)
            v.memset(xcp[:, TF + 1:TF + 3], 0.0)
            v.drain()
            v.tensor_scalar(cacc[:, :], xcp[:, 0:TF], cf_sb[:, j * 4:j * 4 + 1], None, OP.mult)
            v.tensor_scalar(ctmp[:, :], xcp[:, 1:1 + TF], cf_sb[:, j * 4 + 1:j * 4 + 2], None, OP.mult)
            v.tensor_scalar(ytile[:, :], xcp[:, 2:2 + TF], cf_sb[:, j * 4 + 2:j * 4 + 3], None, OP.mult)
            v.tensor_scalar(convd[:, :], xcp[:, 3:3 + TF], cf_sb[:, j * 4 + 3:j * 4 + 4], None, OP.mult)
            v.drain()
            v.tensor_tensor(cacc[:, :], cacc[:, :], ctmp[:, :], OP.add)
            v.tensor_tensor(ytile[:, :], ytile[:, :], convd[:, :], OP.add)
            v.drain()
            v.tensor_tensor(cacc[:, :], cacc[:, :], ytile[:, :], OP.add)
        P.add("vector", f)
        vp_conv = bump("vector", vp)

        def f(s, j=j, vp_conv=vp_conv):
            s.wait_ge(vp.h, vp_conv)
            s.activation(out=cacc[:, :], in_=cacc[:, :], func=AT.Silu, bias=cb_sb[:, j:j + 1])
        P.add("scalar", f)
        sp_silu = bump("scalar", sp)

        def f(v, sp_silu=sp_silu):
            v.wait_ge(sp.h, sp_silu)
            v.tensor_tensor(ytile[:, :], cacc[:, :], zb[:, :], OP.mult)
        P.add("vector", f)
        vp_y = bump("vector", vp)
        wait("gpsimd", vp, vp_y)
        ld_y = dma("gpsimd", h_y[j * 128:(j + 1) * 128, :], ytile[:, :], d_y)
        vp_y_prev = vp_y
        ld_y_prev = ld_y

    phase("xproj")
    wait("sync", d_y, d_y.n)
    xp_sph = {}
    xp_tph = {}
    for tc in range(NCH):
        sl = tc % 2
        if tc >= 2:
            wait("sync", tp, xp_tph[tc - 2])
        lxm = dma("sync", slab[sl][:, :].rearrange("p (a t) -> p a t", a=NDT),
                  h_y.rearrange("(a p) t -> p a t", p=128)[:, :, tc * 512:(tc + 1) * 512], d_slab[sl])
        bank = tc % 4

        def f(t, bank=bank, sl=sl, lxm=lxm):
            t.wait_ge(d_slab[sl].h, lxm)
            for kt in range(NDT):
                t.matmul(pp[bank][0:96, :],
                         wx_sb[:, kt * 96:(kt + 1) * 96],
                         slab[sl][:, kt * 512:(kt + 1) * 512],
                         start=(kt == 0), stop=(kt == NDT - 1))
        P.add("tensor", f)
        tpm = bump("tensor", tp)

        def f(s, tc=tc, bank=bank, tpm=tpm):
            s.wait_ge(tp.h, tpm)
            s.copy(dtbc_sb[:, tc * 512:(tc + 1) * 512], pp[bank][0:96, :])
        P.add("scalar", f)
        spm = bump("scalar", sp)
        xp_sph[tc] = spm
        xp_tph[tc] = tp.n
        if tc >= 3:
            wait("tensor", sp, xp_sph[tc - 3])
    wait("gpsimd", sp, sp.n)
    dma("gpsimd", h_dtbc_in[:, :], dtbc_sb[:, :], ld)

    barrier()
    stB.close()

    phase("e2")
    def f(gp):
        gp.collective_compute(
            "AllReduce", OP.add,
            replica_groups=[[0, 1], [2, 3], [4, 5], [6, 7]],
            ins=[h_dtbc_in], outs=[h_dtbc],
        ).then_inc(ccs.h, 1)
    P.add("gpsimd", f)
    cc2 = ccs.res()
    wait("gpsimd", ccs, cc2)
    barrier()

    phase("scanprep")
    stW = ExitStack()
    wo_sb = stW.enter_context(nc.sbuf_tensor("wo_sb", [128, NKT * 1024], BF16))
    for kt in range(NKT):
        for ai in range(2):
            a_loc = 2 * kt + ai
            srcv = h_wtmp[1024 + a_loc * 32: 1024 + a_loc * 32 + 32, :].rearrange(
                "i (b j) -> b i j", j=32)
            dstv = wo_sb[ai * 64:(ai + 1) * 64, kt * 1024:(kt + 1) * 1024].rearrange(
                "b (i j) -> b i j", j=32)
            dma("gpsimd", dstv, srcv, ld)
    ld_wo = ld.n

    stC = ExitStack()
    eC = stC.enter_context
    bbc = eC(nc.sbuf_tensor("bbc", [128, NST * TC], BF16))
    cbc = eC(nc.sbuf_tensor("cbc", [128, NST * TC], BF16))
    argb = eC(nc.sbuf_tensor("argb", [128, NST * TC], BF16))
    dab = eC(nc.sbuf_tensor("dab", [128, NST * TC], BF16))
    bsb = eC(nc.sbuf_tensor("bsb", [128, NST * TC], BF16))
    hsb = eC(nc.sbuf_tensor("hsb", [128, NST * TC], BF16))
    dtun_b = eC(nc.sbuf_tensor("dtun_b", [64, TF], BF16))
    wdt_sb = eC(nc.sbuf_tensor("wdt_sb", [64, DH], BF16))
    dts = eC(nc.sbuf_tensor("dts", [128, TC], BF16))
    dus = eC(nc.sbuf_tensor("dus", [128, TC], BF16))
    spf = eC(nc.sbuf_tensor("spf", [128, TC], F32))
    usb = [eC(nc.sbuf_tensor(f"usb{i}", [128, TC], BF16)) for i in range(2)]
    yssb = [eC(nc.sbuf_tensor(f"yssb{i}", [128, TC], BF16)) for i in range(2)]
    acol = eC(nc.sbuf_tensor("acol", [128, NDT * NST], F32))
    atmp = eC(nc.sbuf_tensor("atmp", [128, NDT * NST], F32))
    dcol = eC(nc.sbuf_tensor("dcol", [128, NDT], F32))
    carry = eC(nc.sbuf_tensor("carry", [128, NDT * NST], BF16))
    fold16 = eC(nc.sbuf_tensor("fold16", [128, NST], BF16))
    hsumc = eC(nc.sbuf_tensor("hsumc", [128, NCH * NDT], F32))
    hsred = eC(nc.sbuf_tensor("hsred", [128, 1], F32))
    hspad = eC(nc.sbuf_tensor("hspad", [1, 1032], F32))

    ld_dtun = dma("gpsimd", dtun_b[:, :], h_dtbc[0:64, :], ld)
    ld_alog = dma("gpsimd", atmp[:, :].rearrange("p (a n) -> p a n", a=NDT),
                  alog.rearrange("(a p) n -> p a n", p=128), ld)
    ld_dv = dma("gpsimd", dcol[:, :].rearrange("p (a b) -> p a b", b=1), dvin.rearrange("(a p) b -> p a b", p=128), ld, slow=True)
    for a in range(8):
        srcv = h_wtmpdt[a * 16:a * 16 + 16, :].rearrange("i (b j) -> b i j", j=64)
        dstv = wdt_sb[a * 8:(a + 1) * 8, :].rearrange("b (i j) -> b i j", j=64)
        dma("gpsimd", dstv, srcv, ld)
    ld_wdt = ld.n

    def f(s, w=None):
        s.wait_ge(ld.h, ld_wdt)
        s.activation(out=atmp[:, :], in_=atmp[:, :], func=AT.Exp)
    P.add("scalar", f)
    sp_aexp = bump("scalar", sp)

    def f(v):
        v.wait_ge(sp.h, sp_aexp)
        v.tensor_scalar(acol[:, :], atmp[:, :], -1.0, None, OP.mult)
        v.memset(carry[:, :], 0.0)
        v.memset(hspad[0:1, :], 0.0)
    P.add("vector", f)
    bump("vector", vp)
    wait("tensor", ld, ld.n)
    wait("vector", ld, ld.n)

    wait("tensor", ld, ld.n)
    wait("tensor", vp, vp.n)
    for ec in range(2):
        def f(t, ec=ec):
            for kt in range(NKT):
                t.matmul(pp[6][0:1, :], ones_c[:, :],
                         wo_sb[:, kt * 1024 + ec * 512: kt * 1024 + (ec + 1) * 512],
                         start=(kt == 0), stop=(kt == NKT - 1))
        P.add("tensor", f)
        tpm = bump("tensor", tp)

        def f(s, ec=ec, tpm=tpm):
            s.wait_ge(tp.h, tpm)
            s.copy(hspad[0:1, 8 + ec * 512: 8 + (ec + 1) * 512], pp[6][0:1, :])
        P.add("scalar", f)
        spm = bump("scalar", sp)
        wait("tensor", sp, spm)

    phase("scan")
    it = 0
    vp_ys_hist = {}
    ysw_hist = {}
    for ci in range(NCH):
        if ci > 0:
            wait("gpsimd", vp, vp_ys_hist[(ci - 1) * NDT + NDT])
        bsrc = h_dtbc[64:80, ci * TC:(ci + 1) * TC].unsqueeze(0).broadcast_to([128, NST, TC])
        csrc = h_dtbc[80:96, ci * TC:(ci + 1) * TC].unsqueeze(0).broadcast_to([128, NST, TC])
        dma("gpsimd", bbc[:, :].rearrange("p (n t) -> p n t", n=NST), bsrc, d_bc)
        cbm = dma("gpsimd", cbc[:, :].rearrange("p (n t) -> p n t", n=NST), csrc, d_bc)

        for j in range(NDT):
            it += 1
            bank = it % 2

            def f(t, ci=ci, j=j, bank=bank):
                t.matmul(pp[bank][:, :], wdt_sb[:, j * 128:(j + 1) * 128],
                         dtun_b[:, ci * TC:(ci + 1) * TC], start=True, stop=True)
            P.add("tensor", f)
            tp_dt = bump("tensor", tp)

            def f(s, bank=bank, tp_dt=tp_dt):
                s.wait_ge(tp.h, tp_dt)
                s.activation(out=spf[:, :], in_=pp[bank][:, :], func=AT.Exp)
                s.drain()
                s.activation(out=dts[:, :], in_=spf[:, :], func=AT.Ln, bias=1.0)
            P.add("scalar", f)
            sp_dts = bump("scalar", sp)
            wait("tensor", sp, sp_dts)

            ub = it % 2
            if it > 2:
                wait("sync", vp, vp_ys_hist[it - 2])
            lx_u = dma("sync", usb[ub][:, :], h_y[j * 128:(j + 1) * 128, ci * TC:(ci + 1) * TC], d_u[ub])

            def f(v, j=j, ub=ub, sp_dts=sp_dts, lx_u=lx_u):
                v.wait_ge(sp.h, sp_dts)
                v.wait_ge(d_u[ub].h, lx_u)
                v.tensor_tensor(dus[:, :], dts[:, :], usb[ub][:, :], OP.mult)
                for n in range(NST):
                    v.tensor_scalar(argb[:, n * TC:(n + 1) * TC], dts[:, :],
                                    acol[:, j * NST + n:j * NST + n + 1], None, OP.mult)
            P.add("vector", f)
            vp_arg = bump("vector", vp)

            def f(s, vp_arg=vp_arg):
                s.wait_ge(vp.h, vp_arg)
                s.activation(out=dab[:, :], in_=argb[:, :], func=AT.Exp)
            P.add("scalar", f)
            sp_da = bump("scalar", sp)

            ysb = it % 2
            dview = dus[:, :].unsqueeze(1).broadcast_to([128, NST, TC])

            def f(v, ci=ci, j=j, sp_da=sp_da, cbm=cbm, dview=dview):
                v.wait_ge(d_bc.h, cbm)
                v.wait_ge(sp.h, sp_da)
                da0 = dab[:, :].rearrange("p (n t) -> p n t", n=NST)[:, :, 0]
                bb0 = bsb[:, :].rearrange("p (n t) -> p n t", n=NST)[:, :, 0]
                v.tensor_tensor(fold16[:, :], da0, carry[:, j * NST:(j + 1) * NST], OP.mult)
                v.tensor_tensor(bsb[:, :].rearrange("p (n t) -> p n t", n=NST), dview,
                                bbc[:, :].rearrange("p (n t) -> p n t", n=NST), OP.mult)
                v.drain()
                v.memset(da0, 0.0)
                v.drain()
                v.tensor_tensor(bb0, bb0, fold16[:, :], OP.add)
                v.drain()
                v.tensor_tensor_scan(hsb[:, :], dab[:, :], bsb[:, :], 0.0, OP.mult, OP.add)
            P.add("vector", f)
            vp_scan = bump("vector", vp)

            def f(s, ci=ci, j=j, vp_scan=vp_scan):
                s.wait_ge(vp.h, vp_scan)
                s.activation(out=dab[:, :], in_=hsb[:, :], func=AT.Copy,
                             accum_out=hsumc[:, ci * NDT + j: ci * NDT + j + 1])
            P.add("scalar", f)
            sp_hs = bump("scalar", sp)

            def f(v, j=j, ub=ub, ysb=ysb, sp_hs=sp_hs, it=it):
                if it > 2 and (it - 2) in ysw_hist:
                    v.wait_ge(d_ysw[it % 2].h, ysw_hist[it - 2])
                hl = hsb[:, :].rearrange("p (n t) -> p n t", n=NST)[:, :, TC - 1]
                v.tensor_tensor(argb[:, :], hsb[:, :], cbc[:, :], OP.mult)
                v.tensor_copy(carry[:, j * NST:(j + 1) * NST], hl)
                v.drain()
                v.tensor_tensor(argb[:, 0:8 * TC], argb[:, 0:8 * TC], argb[:, 8 * TC:16 * TC], OP.add)
                v.drain()
                v.tensor_tensor(argb[:, 0:4 * TC], argb[:, 0:4 * TC], argb[:, 4 * TC:8 * TC], OP.add)
                v.drain()
                v.tensor_tensor(argb[:, 0:2 * TC], argb[:, 0:2 * TC], argb[:, 2 * TC:4 * TC], OP.add)
                v.drain()
                v.tensor_tensor(yssb[ysb][:, :], argb[:, 0:TC], argb[:, TC:2 * TC], OP.add)
                v.drain()
                v.scalar_tensor_tensor(yssb[ysb][:, :], usb[ub][:, :], dcol[:, j:j + 1],
                                       yssb[ysb][:, :], OP.mult, OP.add)
                v.wait_ge(sp.h, sp_hs)
            P.add("vector", f)
            vp_ys = bump("vector", vp)
            vp_ys_hist[it] = vp_ys

            wait("gpsimd", vp, vp_ys)
            ysw = dma("gpsimd", h_ys[j * 128:(j + 1) * 128, ci * TC:(ci + 1) * TC],
                      yssb[ysb][:, :], d_ysw[ysb])
            ysw_hist[it] = ysw

    phase("hsfin")
    def f(v):
        v.tensor_reduce(hsred[:, 0:1], hsumc[:, :], AX.X, OP.add)
    P.add("vector", f)
    vp_hsr = bump("vector", vp)

    def f(t, vp_hsr=vp_hsr):
        t.wait_ge(vp.h, vp_hsr)
        t.matmul(pp[6][0:1, 0:1], ones_f[:, :], hsred[:, 0:1], start=True, stop=True)
    P.add("tensor", f)
    tp_hs = bump("tensor", tp)

    def f(s, tp_hs=tp_hs):
        s.wait_ge(tp.h, tp_hs)
        s.copy(hspad[0:1, 0:1], pp[6][0:1, 0:1])
    P.add("scalar", f)
    sp_hsf = bump("scalar", sp)
    wait("gpsimd", sp, sp_hsf)
    dma("gpsimd", h_hs_in[:, :], hspad[0:1, :], ld)

    barrier()
    stC.close()

    phase("e4")
    def f(gp):
        gp.collective_compute(
            "AllReduce", OP.add,
            replica_groups=[[0, 1, 2, 3, 4, 5, 6, 7]],
            ins=[h_hs_in], outs=[h_hs],
        ).then_inc(ccs.h, 1)
    P.add("gpsimd", f)
    cc4 = ccs.res()

    phase("outproj")
    stD = ExitStack()
    eD = stD.enter_context
    yslab = [eD(nc.sbuf_tensor(f"yslab{i}", [128, NKT * 128], BF16)) for i in range(2)]
    opst = [eD(nc.sbuf_tensor(f"opst{i}", [128, 512], BF16)) for i in range(2)]

    op_w_hist = {}
    op_sph = {}
    op_dwh = {}
    for tt in range(TF // 128):
        sl = tt % 2
        if tt >= 2:
            wait("sync", tp, op_w_hist[tt - 2])
        lxm = dma("sync", yslab[sl][:, :].rearrange("p (a t) -> p a t", a=NKT),
                  h_ys.rearrange("(a p) t -> p a t", p=128)[:, :, tt * 128:(tt + 1) * 128], d_slab[sl])
        for ec in range(2):
            bank = (tt * 2 + ec) % 4

            def f(t, tt=tt, ec=ec, bank=bank, sl=sl, lxm=lxm):
                t.wait_ge(d_slab[sl].h, lxm)
                for kt in range(NKT):
                    t.matmul(pp[bank][:, :],
                             yslab[sl][:, kt * 128:(kt + 1) * 128],
                             wo_sb[:, kt * 1024 + ec * 512:(kt) * 1024 + (ec + 1) * 512],
                             start=(kt == 0), stop=(kt == NKT - 1))
            P.add("tensor", f)
            tpm = bump("tensor", tp)
            if ec == 1:
                op_w_hist[tt] = tpm

            def f(s, ec=ec, bank=bank, tpm=tpm, st_i=(tt * 2 + ec) % 2,
                  g=op_dwh.get(tt * 2 + ec - 2, 0), gs=(tt * 2 + ec) % 2):
                s.wait_ge(tp.h, tpm)
                if g:
                    s.wait_ge(d_op[gs].h, g)
                s.copy(opst[st_i][:, :], pp[bank][:, :])
            P.add("scalar", f)
            spm = bump("scalar", sp)
            op_sph[tt * 2 + ec] = spm
            if tt * 2 + ec >= 3:
                wait("tensor", sp, op_sph[tt * 2 + ec - 3])
            wait("gpsimd", sp, spm)
            stb = (tt * 2 + ec) % 2
            opw = dma("gpsimd", h_op[tt * 128:(tt + 1) * 128, ec * 512:(ec + 1) * 512],
                      opst[stb][:, :], d_op[stb])
            op_dwh[tt * 2 + ec] = opw

    barrier()

    phase("rs")
    def f(gp):
        gp.collective_compute(
            "ReduceScatter", OP.add,
            replica_groups=[[0, 1], [2, 3], [4, 5], [6, 7]],
            ins=[h_op], outs=[h_od],
        ).then_inc(ccs.h, 1)
    P.add("gpsimd", f)
    cc_rs = ccs.res()
    wait("gpsimd", ccs, cc_rs)
    wait("gpsimd", ccs, cc4)
    barrier()
    stD.close()
    stW.close()

    phase("grpE_prep")
    stE = ExitStack()
    eE = stE.enter_context
    lnb3 = eE(nc.sbuf_tensor("lnb3", [128, NKT * XOW], BF16))

    def lnv3(kt, c0, c1):
        return lnb3[:, kt * XOW + c0: kt * XOW + c1]

    ffnsm3 = eE(nc.sbuf_tensor("ffnsm3", [128, 1024], BF16))
    halc = eE(nc.sbuf_tensor("halc", [128, NKT], F32))
    stT = ExitStack()
    hrow = stT.enter_context(nc.sbuf_tensor("hrow", [1, 1032], F32))
    mrow = stT.enter_context(nc.sbuf_tensor("mrow", [1, E], F32))
    mbc = stT.enter_context(nc.sbuf_tensor("mbc", [128, NKT], F32))
    dodb = [stT.enter_context(nc.sbuf_tensor(f"dodb{i}", [128, TH], BF16)) for i in range(2)]

    ld_hrow = dma("gpsimd", hrow[0:1, :], h_hs[0:1, :], ld)

    def f(v, ld_hrow=ld_hrow):
        v.wait_ge(ld.h, ld_hrow)
        v.tensor_scalar(hrow[0:1, 0:1], hrow[0:1, 0:1], 1e-8 / (4.0 * HS_COUNT), None, OP.mult)
        v.drain()
        v.tensor_scalar(mrow[0:1, :], hrow[0:1, 8:8 + E], hrow[0:1, 0:1], None, OP.mult)
    P.add("vector", f)
    vp_mb = bump("vector", vp)
    wait("gpsimd", vp, vp_mb)
    wmb = dma("gpsimd", h_mb[:, :], mrow[0:1, :], ld)
    wait("gpsimd", ld, wmb)
    for kt in range(NKT):
        dma("gpsimd", mbc[:, kt:kt + 1],
            h_mb[0:1, kt * 128:(kt + 1) * 128].rearrange("a b -> b a"), ld, slow=True)
    ld_mbc = ld.n

    dod_hist = {}
    for et in range(NKT):
        sl = et % 2
        if et >= 2:
            wait("sync", vp, dod_hist[et - 2])
        v_tr = d_dod[sl].res()

        def f(eng, sl=sl, et=et):
            eng.dma_start_transpose(dodb[sl][:, :],
                                    h_od[:, et * 128:(et + 1) * 128]).then_inc(d_dod[sl].h, 16)
        P.add("sync", f)

        def f(v, sl=sl, et=et, v_tr=v_tr, ld_mbc=ld_mbc):
            v.wait_ge(d_dod[sl].h, v_tr)
            if et == 0:
                v.wait_ge(ld.h, ld_mbc)
            v.scalar_tensor_tensor(xov(et, 1, TH + 1), dodb[sl][:, :],
                                   mbc[:, et:et + 1], xov(et, 1, TH + 1),
                                   OP.add, OP.add)
        P.add("vector", f)
        dod_hist[et] = bump("vector", vp)
    barrier()
    stT.close()

    phase("ln3")
    emit_ln(xov, lnv3, LN2_CH)

    phase("e5")
    def f(v):
        for kt in range(NKT):
            v.tensor_scalar(halc[:, kt:kt + 1], lnv3(kt, TH, TH + 1),
                            mskp_sb[:, kt:kt + 1], None, OP.mult)
    P.add("vector", f)
    vp_pl = bump("vector", vp)
    wait("gpsimd", vp, vp_pl)
    wait("gpsimd", ld, ld.n)
    wpl = dma("gpsimd", h_halo_in.rearrange("(a p) b -> p a b", p=128), halc[:, :].rearrange("p (a b) -> p a b", b=1), ld, slow=True)
    wait("gpsimd", ld, wpl)

    def f(gp):
        gp.collective_compute(
            "AllReduce", OP.add,
            replica_groups=[[0, 1], [2, 3], [4, 5], [6, 7]],
            ins=[h_halo_in], outs=[h_halo],
        ).then_inc(ccs.h, 1)
    P.add("gpsimd", f)
    cc5 = ccs.res()
    wait("gpsimd", ccs, cc5)
    ld_hal = dma("gpsimd", halc[:, :].rearrange("p (a b) -> p a b", b=1), h_halo.rearrange("(a p) b -> p a b", p=128), ld, slow=True)
    wait("vector", ld, ld_hal)

    def f(v):
        for kt in range(NKT):
            v.tensor_scalar(lnv3(kt, 0, 1), halc[:, kt:kt + 1],
                            msku_sb[:, kt:kt + 1], None, OP.mult)
    P.add("vector", f)
    bump("vector", vp)

    phase("ffn3")
    stF3 = ExitStack()
    mixb3 = stF3.enter_context(nc.sbuf_tensor("mixb3", [128, NKT * TH], BF16))
    kabuf3 = stF3.enter_context(nc.sbuf_tensor("kabuf3", [128, NKT * TH], BF16))
    wfbuf3 = [stF3.enter_context(nc.sbuf_tensor(f"wf3{i}", [128, NKT * NKT * 128], BF16)) for i in range(2)]
    rbuf3 = ffnsm3[:, 0:512]
    vstage3 = ffnsm3[:, 512:1024]
    emit_ffn(lnv3, (mixb3, rbuf3, kabuf3, wfbuf3, vstage3))
    stF3.close()

    phase("outstore")
    wait("sync", vp, vp.n)
    for kt in range(NKT):
        dma("sync", out_t[kt * 128:(kt + 1) * 128, :], xov(kt, 1, TH + 1), lx)
    lx_fin = lx.n
    wait("sync", lx, lx_fin)
    barrier()
    stE.close()

    phase_end()
    block = e(nc.Block())
    P.emit(block)
    st.close()
    return nc


def _host_maps(inputs):
    x = np.asarray(inputs["L0_attn_x"])
    in_g1 = np.asarray(inputs["L0_attn_in_g1"], np.float32)
    in_g2 = np.asarray(inputs["L0_attn_in_g2"], np.float32)
    x_g1 = np.asarray(inputs["L0_attn_x_g1"], np.float32)
    x_g2 = np.asarray(inputs["L0_attn_x_g2"], np.float32)
    dt_g1 = np.asarray(inputs["L0_attn_dt_g1"], np.float32)
    dt_g2 = np.asarray(inputs["L0_attn_dt_g2"], np.float32)
    out_g1 = np.asarray(inputs["L0_attn_out_g1"], np.float32)
    out_g2 = np.asarray(inputs["L0_attn_out_g2"], np.float32)
    cf = np.asarray(inputs["L0_attn_conv_filter"], np.float32)
    cb = np.asarray(inputs["L0_attn_conv_bias"], np.float32)
    alog_a = np.asarray(inputs["L0_attn_A_log"], np.float32)
    dv = np.asarray(inputs["L0_attn_D"], np.float32)
    wr = np.asarray(inputs["L0_ffn_W_r"], np.float32)
    wk = np.asarray(inputs["L0_ffn_W_k"], np.float32)
    wv = np.asarray(inputs["L0_ffn_W_v"], np.float32)
    mur = np.asarray(inputs["L0_ffn_mu_r"], np.float32)
    muk = np.asarray(inputs["L0_ffn_mu_k"], np.float32)

    C = np.ascontiguousarray
    wr_t = C(wr.T)
    wk_t = C(wk.T)
    wv_t = C(wv.T)
    g2in_m = C(in_g2.reshape(16, 2048))
    g2x_m = C(x_g2.reshape(16, 768))
    g2dt_m = C(dt_g2.reshape(16, 512))
    g2out_m = C(out_g2.reshape(16, 2048))

    maps = []
    for c in range(8):
        b, h = c // 2, c % 2
        xs = x[b, h * TH:(h + 1) * TH, :].T.astype(np.float32)
        halo = (x[b, h * TH - 1:h * TH, :].T.astype(np.float32) if h == 1
                else np.zeros((E, 1), np.float32))
        xin_m = C(np.concatenate([halo, xs], axis=1))

        iperm = list(range(h * 16, (h + 1) * 16)) + list(range(32 + h * 16, 32 + (h + 1) * 16))
        g1in_m = C(in_g1[:, iperm, :].reshape(32 * 32, 16).T)
        aperm = list(range(h * 16, (h + 1) * 16))
        g1x_m = C(x_g1[aperm].reshape(16 * 8, 16).T)
        ipermd = list(range(h * 16, (h + 1) * 16))
        g1dt_m = C(dt_g1[:, ipermd, :].reshape(8 * 16, 16).T)
        g1out_m = C(out_g1[aperm].reshape(16 * 32, 16).T)

        dh = slice(h * DH, (h + 1) * DH)
        maps.append(dict(
            xin=xin_m,
            wr_t=wr_t, wk_t=wk_t, wv_t=wv_t,
            mu_r=C(mur.reshape(E, 1)), mu_k=C(muk.reshape(E, 1)),
            g1in=g1in_m, g2in=g2in_m,
            g1x=g1x_m, g2x=g2x_m,
            g1dt=g1dt_m, g2dt=g2dt_m,
            g1out=g1out_m, g2out=g2out_m,
            cfin=C(cf[:, 0, dh].T),
            cbin=C(cb[dh].reshape(DH, 1)),
            alog=C(alog_a[dh]),
            dvin=C(dv[dh].reshape(DH, 1)),
            maskp=np.full((E, 1), 1.0 - h, np.float32),
            masku=np.full((E, 1), float(h), np.float32),
        ))
    return maps


def kernel(**inputs):
    maps = _host_maps(inputs)
    nc = build_nc()
    res = run_bass_kernel_spmd(nc, maps, core_ids=list(range(8)))
    out = np.empty((4, TF, E), np.float32)
    for c in range(8):
        b, h = c // 2, c % 2
        out[b, h * TH:(h + 1) * TH, :] = res.results[c]["out"].T
    return out


# version 05
# speedup vs baseline: 1.3036x; 1.1798x over previous
import numpy as np
from contextlib import ExitStack

import concourse.bass as bass
import concourse.mybir as mybir
from concourse.bass_utils import run_bass_kernel_spmd
from concourse.aot_env import allow_ici_all_to_all

F32 = mybir.dt.float32
BF16 = mybir.dt.bfloat16
AT = mybir.ActivationFunctionType
OP = mybir.AluOpType
AX = mybir.AxisListType

E = 1024
DIN = 2048
DH = 1024
NST = 16
TF = 4096
TH = 2048
TC = 512
NCH = TF // TC
NDT = DH // 128
NKT = E // 128
XOW = TH + 1
HS_COUNT = 4.0 * TF * DIN * NST

ENGS = ["sync", "gpsimd", "vector", "scalar", "tensor"]


class Cnt:
    def __init__(self, h, step=1):
        self.h = h
        self.n = 0
        self.step = step

    def res(self, k=None):
        self.n += self.step if k is None else k
        return self.n


class Prog:
    def __init__(self):
        self.q = {en: [] for en in ENGS}

    def add(self, en, f):
        self.q[en].append(f)

    def emit(self, block):
        for en in ENGS:
            fs = self.q[en]

            def runner(eng, fs=fs):
                for f in fs:
                    f(eng)
            getattr(block, en)(runner)


def build_nc():
    nc = bass.Bass("TRN2", target_bir_lowering=False)

    xin = nc.dram_tensor("xin", [E, XOW], F32, kind="ExternalInput").ap()
    wr_t = nc.dram_tensor("wr_t", [E, E], F32, kind="ExternalInput").ap()
    wk_t = nc.dram_tensor("wk_t", [E, E], F32, kind="ExternalInput").ap()
    wv_t = nc.dram_tensor("wv_t", [E, E], F32, kind="ExternalInput").ap()
    mu_r = nc.dram_tensor("mu_r", [E, 1], F32, kind="ExternalInput").ap()
    mu_k = nc.dram_tensor("mu_k", [E, 1], F32, kind="ExternalInput").ap()
    g1in = nc.dram_tensor("g1in", [16, 1024], F32, kind="ExternalInput").ap()
    g2in = nc.dram_tensor("g2in", [16, 2048], F32, kind="ExternalInput").ap()
    g1x = nc.dram_tensor("g1x", [16, 128], F32, kind="ExternalInput").ap()
    g2x = nc.dram_tensor("g2x", [16, 768], F32, kind="ExternalInput").ap()
    g1dt = nc.dram_tensor("g1dt", [16, 128], F32, kind="ExternalInput").ap()
    g2dt = nc.dram_tensor("g2dt", [16, 512], F32, kind="ExternalInput").ap()
    g1out = nc.dram_tensor("g1out", [16, 512], F32, kind="ExternalInput").ap()
    g2out = nc.dram_tensor("g2out", [16, 2048], F32, kind="ExternalInput").ap()
    cfin = nc.dram_tensor("cfin", [DH, 4], F32, kind="ExternalInput").ap()
    cbin = nc.dram_tensor("cbin", [DH, 1], F32, kind="ExternalInput").ap()
    alog = nc.dram_tensor("alog", [DH, NST], F32, kind="ExternalInput").ap()
    dvin = nc.dram_tensor("dvin", [DH, 1], F32, kind="ExternalInput").ap()
    maskp = nc.dram_tensor("maskp", [E, 1], F32, kind="ExternalInput").ap()
    masku = nc.dram_tensor("masku", [E, 1], F32, kind="ExternalInput").ap()
    out_t = nc.dram_tensor("out", [E, TH], F32, kind="ExternalOutput").ap()

    h_wtmp = nc.dram_tensor("h_wtmp", [2048, 2048], BF16).ap()
    h_wtmpx = nc.dram_tensor("h_wtmpx", [128, 768], BF16).ap()
    h_wtmpdt = nc.dram_tensor("h_wtmpdt", [128, 512], BF16).ap()
    h_ln2own = nc.dram_tensor("h_ln2own", [E, TH], BF16).ap()
    h_ln2full = nc.dram_tensor("h_ln2full", [2 * E, TH], BF16).ap()
    h_y = nc.dram_tensor("h_y", [DH, TF], BF16).ap()
    h_dtbc_in = nc.dram_tensor("h_dtbc_in", [96, TF], BF16).ap()
    h_dtbc = nc.dram_tensor("h_dtbc", [96, TF], BF16).ap()
    h_ys = nc.dram_tensor("h_ys", [DH, TF], BF16).ap()
    h_op = nc.dram_tensor("h_op", [TF, E], BF16).ap()
    h_od = nc.dram_tensor("h_od", [TH, E], BF16).ap()
    h_hs_in = nc.dram_tensor("h_hs_in", [1, 1032], F32).ap()
    h_hs = nc.dram_tensor("h_hs", [1, 1032], F32).ap()
    h_halo_in = nc.dram_tensor("h_halo_in", [E, 1], F32).ap()
    h_halo = nc.dram_tensor("h_halo", [E, 1], F32).ap()
    h_stat = nc.dram_tensor("h_stat", [2, XOW], BF16).ap()
    h_mb = nc.dram_tensor("h_mb", [1, E], F32).ap()

    st = ExitStack()
    e = st.enter_context

    BARs = e(nc.semaphore("BAR"))
    bar_n = [0]
    ld = Cnt(e(nc.semaphore("ld")), 16)
    lx = Cnt(e(nc.semaphore("lx")), 16)
    tp = Cnt(e(nc.semaphore("tp")), 1)
    sp = Cnt(e(nc.semaphore("sp")), 1)
    vp = Cnt(e(nc.semaphore("vp")), 1)
    ccs = Cnt(e(nc.semaphore("ccs")), 1)
    d_slab = [Cnt(e(nc.semaphore(f"dslab{i}")), 16) for i in range(2)]
    d_u = [Cnt(e(nc.semaphore(f"du{i}")), 16) for i in range(3)]
    d_ysw = [Cnt(e(nc.semaphore(f"dysw{i}")), 16) for i in range(2)]
    d_op = [Cnt(e(nc.semaphore(f"dop{i}")), 16) for i in range(2)]
    d_dod = [Cnt(e(nc.semaphore(f"ddod{i}")), 16) for i in range(2)]
    d_wb = [Cnt(e(nc.semaphore(f"dwb{i}")), 16) for i in range(2)]
    d_bc = Cnt(e(nc.semaphore("dbc")), 16)
    d_bc2 = Cnt(e(nc.semaphore("dbc2")), 16)
    d_dt = Cnt(e(nc.semaphore("ddt")), 16)
    d_y = Cnt(e(nc.semaphore("dy")), 16)
    d_stat = Cnt(e(nc.semaphore("dstat")), 16)

    SYNC_SEMS = [lx] + d_slab + d_u + d_dod
    GP_SEMS = [ld, d_bc, d_bc2, d_dt, d_y, d_stat] + d_ysw + d_op + d_wb

    P = Prog()

    def barrier():
        bar_n[0] += 1
        tgt = 5 * bar_n[0]
        waits = {"sync": [(c, c.n) for c in SYNC_SEMS],
                 "gpsimd": [(c, c.n) for c in GP_SEMS],
                 "vector": [(vp, vp.n)],
                 "scalar": [(sp, sp.n)],
                 "tensor": [(tp, tp.n)]}
        for en in ENGS:
            ws = waits[en]

            def f(eng, ws=ws, tgt=tgt):
                for c, v in ws:
                    if v > 0:
                        eng.wait_ge(c.h, v)
                eng.sem_inc(BARs, 1)
                eng.wait_ge(BARs, tgt)
            P.add(en, f)

    def dma(en, dst, src, cnt, slow=False):
        v = cnt.res()

        def f(eng, dst=dst, src=src, slow=slow):
            if slow:
                with nc.allow_non_contiguous_dma(reason="tiny gather"):
                    eng.dma_start(out=dst, in_=src).then_inc(cnt.h, cnt.step)
            else:
                eng.dma_start(out=dst, in_=src).then_inc(cnt.h, cnt.step)
        P.add(en, f)
        return v

    def wait(en, cnt, v):
        if v <= 0:
            return

        def f(eng, cnt=cnt, v=v):
            eng.wait_ge(cnt.h, v)
        P.add(en, f)

    cur_scope = {en: None for en in ENGS}

    def phase(name):
        for en in ENGS:
            def f(eng, en=en, name=name):
                if cur_scope[en] is not None:
                    pn, pid = cur_scope[en]
                    nc.leave_named_scope(pn, pid, False)
                sid, _ = nc.enter_named_scope(name, False)
                cur_scope[en] = (name, sid)
            P.add(en, f)

    def phase_end():
        for en in ENGS:
            def f(eng, en=en):
                if cur_scope[en] is not None:
                    pn, pid = cur_scope[en]
                    nc.leave_named_scope(pn, pid, False)
                    cur_scope[en] = None
            P.add(en, f)

    def bump(en, cnt):
        def f(eng, cnt=cnt):
            eng.maybe_drain_then_inc((cnt.h, cnt.step))
        P.add(en, f)
        return cnt.res()

    xo = e(nc.sbuf_tensor("xo", [128, NKT * XOW], F32))

    def xov(kt, c0, c1):
        return xo[:, kt * XOW + c0: kt * XOW + c1]

    mu_r_sb = e(nc.sbuf_tensor("mu_r_sb", [128, NKT], F32))
    mu_k_sb = e(nc.sbuf_tensor("mu_k_sb", [128, NKT], F32))
    mu_r1_sb = e(nc.sbuf_tensor("mu_r1_sb", [128, NKT], F32))
    mu_k1_sb = e(nc.sbuf_tensor("mu_k1_sb", [128, NKT], F32))
    msku_sb = e(nc.sbuf_tensor("msku_sb", [128, NKT], F32))
    mskp_sb = e(nc.sbuf_tensor("mskp_sb", [128, NKT], F32))
    ones_c = e(nc.sbuf_tensor("ones_c", [128, 1], BF16))
    ones_f = e(nc.sbuf_tensor("ones_f", [128, 1], F32))
    pp = [e(nc.psum_tensor(f"pp{i}", [128, 512], F32)) for i in range(8)]

    phase("p0_load")
    for kt in range(NKT):
        dma("sync", xov(kt, 0, XOW), xin[kt * 128:(kt + 1) * 128, :], lx)
    dma("gpsimd", mu_r_sb[:, :].rearrange("p (a b) -> p a b", b=1), mu_r.rearrange("(a p) b -> p a b", p=128), ld, slow=True)
    dma("gpsimd", mu_k_sb[:, :].rearrange("p (a b) -> p a b", b=1), mu_k.rearrange("(a p) b -> p a b", p=128), ld, slow=True)
    dma("gpsimd", msku_sb[:, :].rearrange("p (a b) -> p a b", b=1), masku.rearrange("(a p) b -> p a b", p=128), ld, slow=True)
    dma("gpsimd", mskp_sb[:, :].rearrange("p (a b) -> p a b", b=1), maskp.rearrange("(a p) b -> p a b", p=128), ld, slow=True)

    ld_mu = ld.n

    def f(v, ld_mu=ld_mu):
        v.memset(ones_c[:, :], 1.0)
        v.memset(ones_f[:, :], 1.0)
        v.wait_ge(ld.h, ld_mu)
        v.tensor_scalar(mu_r1_sb[:, :], mu_r_sb[:, :], -1.0, 1.0, OP.mult, OP.add)
        v.tensor_scalar(mu_k1_sb[:, :], mu_k_sb[:, :], -1.0, 1.0, OP.mult, OP.add)
    P.add("vector", f)
    bump("vector", vp)

    barrier()

    def emit_ln(xview, lnview, chunks):
        stL = ExitStack()
        sqbuf = stL.enter_context(nc.sbuf_tensor(None, [128, 2 * 512], BF16))
        s_m = stL.enter_context(nc.sbuf_tensor(None, [1, XOW], F32))
        s_v = stL.enter_context(nc.sbuf_tensor(None, [1, XOW], F32))
        s_i = stL.enter_context(nc.sbuf_tensor(None, [1, XOW], F32))
        invb_ = stL.enter_context(nc.sbuf_tensor(None, [128, XOW], BF16))
        msbb_ = stL.enter_context(nc.sbuf_tensor(None, [128, XOW], BF16))
        c_lo = chunks[0][0]
        c_hi = chunks[-1][1]
        wdt = c_hi - c_lo
        wait("scalar", vp, vp.n)

        def f(s):
            for kt in range(NKT):
                s.activation(out=lnview(kt, c_lo, c_hi), in_=xview(kt, c_lo, c_hi), func=AT.Copy)
        P.add("scalar", f)
        sp_cast = bump("scalar", sp)
        wait("vector", sp, sp_cast)

        tp_pair = {}
        gix = 0
        for (c0, c1) in chunks:
            wd = c1 - c0
            for kt in range(NKT):
                gix += 1
                def f(v, kt=kt, c0=c0, c1=c1, wd=wd, tpg=tp_pair.get(gix - 2, 0)):
                    if tpg:
                        v.wait_ge(tp.h, tpg)
                    v.tensor_tensor(sqbuf[:, (gix % 2) * 512:(gix % 2) * 512 + wd] if False else
                                    sqbuf[:, (kt % 2) * 512:(kt % 2) * 512 + wd],
                                    lnview(kt, c0, c1), lnview(kt, c0, c1), OP.mult)
                P.add("vector", f)
                vpm = bump("vector", vp)

                def f(t, kt=kt, c0=c0, c1=c1, wd=wd, vpm=vpm):
                    t.wait_ge(vp.h, vpm)
                    t.matmul(pp[6][0:1, 0:wd], ones_c[:, :], lnview(kt, c0, c1),
                             start=(kt == 0), stop=(kt == NKT - 1))
                    t.matmul(pp[7][0:1, 0:wd], ones_c[:, :],
                             sqbuf[:, (kt % 2) * 512:(kt % 2) * 512 + wd],
                             start=(kt == 0), stop=(kt == NKT - 1))
                P.add("tensor", f)
                tp_pair[gix] = bump("tensor", tp)
            tpm = tp_pair[gix]

            def f(s, c0=c0, c1=c1, wd=wd, tpm=tpm):
                s.wait_ge(tp.h, tpm)
                s.copy(s_m[0:1, c0 - c_lo:c1 - c_lo], pp[6][0:1, 0:wd])
                s.copy(s_v[0:1, c0 - c_lo:c1 - c_lo], pp[7][0:1, 0:wd])
            P.add("scalar", f)
            spm = bump("scalar", sp)
            wait("tensor", sp, spm)
        sp_ev = sp.n

        def f(v, sp_ev=sp_ev, wdt=wdt):
            v.wait_ge(sp.h, sp_ev)
            v.tensor_scalar(s_m[0:1, 0:wdt], s_m[0:1, 0:wdt], 1.0 / E, None, OP.mult)
            v.tensor_scalar(s_v[0:1, 0:wdt], s_v[0:1, 0:wdt], 1.0 / E, None, OP.mult)
            v.drain()
            v.tensor_tensor(s_i[0:1, 0:wdt], s_m[0:1, 0:wdt], s_m[0:1, 0:wdt], OP.mult)
            v.drain()
            v.tensor_tensor(s_v[0:1, 0:wdt], s_v[0:1, 0:wdt], s_i[0:1, 0:wdt], OP.subtract)
            v.drain()
            v.tensor_scalar(s_v[0:1, 0:wdt], s_v[0:1, 0:wdt], 1e-5, None, OP.add)
        P.add("vector", f)
        vp_var = bump("vector", vp)

        def f(s, vp_var=vp_var, wdt=wdt):
            s.wait_ge(vp.h, vp_var)
            s.activation(out=s_i[0:1, 0:wdt], in_=s_v[0:1, 0:wdt], func=AT.Ln)
        P.add("scalar", f)
        sp_inv = bump("scalar", sp)

        def f(v, sp_inv=sp_inv, wdt=wdt):
            v.wait_ge(sp.h, sp_inv)
            v.tensor_scalar(s_i[0:1, 0:wdt], s_i[0:1, 0:wdt], -0.5, None, OP.mult)
        P.add("vector", f)
        vp_l2 = bump("vector", vp)

        def f(s, vp_l2=vp_l2, wdt=wdt):
            s.wait_ge(vp.h, vp_l2)
            s.activation(out=s_i[0:1, 0:wdt], in_=s_i[0:1, 0:wdt], func=AT.Exp)
        P.add("scalar", f)
        sp_inv2 = bump("scalar", sp)

        def f(v, sp_inv2=sp_inv2, wdt=wdt):
            v.wait_ge(sp.h, sp_inv2)
            v.tensor_tensor(s_v[0:1, 0:wdt], s_m[0:1, 0:wdt], s_i[0:1, 0:wdt], OP.mult)
        P.add("vector", f)
        vp_ms = bump("vector", vp)

        wait("gpsimd", vp, vp_ms)
        dma("gpsimd", h_stat[0:1, 0:wdt], s_i[0:1, 0:wdt], d_stat)
        w2 = dma("gpsimd", h_stat[1:2, 0:wdt], s_v[0:1, 0:wdt], d_stat)
        wait("gpsimd", d_stat, w2)
        dma("gpsimd", invb_[:, 0:wdt], h_stat[0:1, 0:wdt].broadcast_to([128, wdt]), d_stat)
        wbc = dma("gpsimd", msbb_[:, 0:wdt], h_stat[1:2, 0:wdt].broadcast_to([128, wdt]), d_stat)
        wait("vector", d_stat, wbc)

        def f(v):
            for kt in range(NKT):
                v.tensor_tensor(lnview(kt, c_lo, c_hi), lnview(kt, c_lo, c_hi), invb_[:, 0:wdt], OP.mult)
            v.drain()
            for kt in range(NKT):
                v.tensor_tensor(lnview(kt, c_lo, c_hi), lnview(kt, c_lo, c_hi), msbb_[:, 0:wdt], OP.subtract)
        P.add("vector", f)
        r = bump("vector", vp)
        barrier()
        stL.close()
        return r

    def emit_ffn(lnview, bufs):
        (mixb, rbuf, kabuf, wfbuf, vstage) = bufs
        wait("gpsimd", ld, ld.n)
        for kt in range(NKT):
            dma("gpsimd", wfbuf[0][:, kt * 1024:(kt + 1) * 1024],
                wk_t[kt * 128:(kt + 1) * 128, :], ld)
        wk_done = ld.n

        def mix(mu_sb, mu1_sb):
            def f(v, mu_sb=mu_sb, mu1_sb=mu1_sb):
                for kt in range(NKT):
                    v.tensor_scalar(mixb[:, kt * TH:(kt + 1) * TH], lnview(kt, 1, TH + 1),
                                    mu_sb[:, kt:kt + 1], None, OP.mult)
                v.drain()
                for kt in range(NKT):
                    v.scalar_tensor_tensor(mixb[:, kt * TH:(kt + 1) * TH], lnview(kt, 0, TH),
                                           mu1_sb[:, kt:kt + 1], mixb[:, kt * TH:(kt + 1) * TH],
                                           OP.mult, OP.add)
            P.add("vector", f)
            return bump("vector", vp)

        mix(mu_k_sb, mu_k1_sb)

        def proj(wbuf_idx, store, rhs_buf):
            wait("tensor", vp, vp.n)
            pr_sph = {}
            for mt in range(NKT):
                for tc in range(4):
                    ipr = mt * 4 + tc
                    bank = ipr % 4
                    if ipr >= 4:
                        wait("tensor", sp, pr_sph[ipr - 4])

                    def f(t, mt=mt, tc=tc, bank=bank, wbuf_idx=wbuf_idx, rhs_buf=rhs_buf):
                        for kt in range(NKT):
                            t.matmul(pp[bank][:, :],
                                     wfbuf[wbuf_idx][:, (kt * NKT + mt) * 128:(kt * NKT + mt + 1) * 128],
                                     rhs_buf[:, kt * TH + tc * 512:kt * TH + (tc + 1) * 512],
                                     start=(kt == 0), stop=(kt == NKT - 1))
                    P.add("tensor", f)
                    tpm = bump("tensor", tp)

                    def f(s, mt=mt, tc=tc, bank=bank, tpm=tpm):
                        s.wait_ge(tp.h, tpm)
                        store(mt, tc, pp[bank], s)
                    P.add("scalar", f)
                    pr_sph[ipr] = bump("scalar", sp)

        wait("tensor", ld, wk_done)

        def store_k(mt, tc, ps, s):
            s.activation(out=kabuf[:, mt * TH + tc * 512:mt * TH + (tc + 1) * 512],
                         in_=ps[:, :], func=AT.Relu)
        proj(0, store_k, mixb)
        sp_k = sp.n
        tp_k = tp.n

        wait("gpsimd", ld, ld.n)
        for kt in range(NKT):
            dma("gpsimd", wfbuf[1][:, kt * 1024:(kt + 1) * 1024],
                wr_t[kt * 128:(kt + 1) * 128, :], ld)
        wr_done = ld.n

        wait("vector", sp, sp_k)

        def f(v):
            for kt in range(NKT):
                v.tensor_tensor(kabuf[:, kt * TH:(kt + 1) * TH], kabuf[:, kt * TH:(kt + 1) * TH],
                                kabuf[:, kt * TH:(kt + 1) * TH], OP.mult)
        P.add("vector", f)
        wait("vector", tp, tp_k)
        mix(mu_r_sb, mu_r1_sb)

        wait("gpsimd", tp, tp_k)
        wait("gpsimd", ld, ld.n)
        for kt in range(NKT):
            dma("gpsimd", wfbuf[0][:, kt * 1024:(kt + 1) * 1024],
                wv_t[kt * 128:(kt + 1) * 128, :], ld)
        wv_done = ld.n
        wait("tensor", ld, wr_done)
        wait("tensor", ld, wv_done)
        wait("tensor", vp, vp.n)
        vp_hist = {}
        itf = 0
        for mt in range(NKT):
            for tc in range(4):
                itf += 1
                rb = (2 * itf) % 4
                vb = (2 * itf + 1) % 4

                def f(t, mt=mt, tc=tc, rb=rb, vb=vb, g=vp_hist.get(itf - 2, 0)):
                    if g:
                        t.wait_ge(vp.h, g)
                    for kt in range(NKT):
                        t.matmul(pp[rb][:, :],
                                 wfbuf[1][:, (kt * NKT + mt) * 128:(kt * NKT + mt + 1) * 128],
                                 mixb[:, kt * TH + tc * 512:kt * TH + (tc + 1) * 512],
                                 start=(kt == 0), stop=(kt == NKT - 1))
                    for kt in range(NKT):
                        t.matmul(pp[vb][:, :],
                                 wfbuf[0][:, (kt * NKT + mt) * 128:(kt * NKT + mt + 1) * 128],
                                 kabuf[:, kt * TH + tc * 512:kt * TH + (tc + 1) * 512],
                                 start=(kt == 0), stop=(kt == NKT - 1))
                P.add("tensor", f)
                tpm = bump("tensor", tp)

                def f(s, rb=rb, tpm=tpm, g=vp_hist.get(itf - 1, 0)):
                    s.wait_ge(tp.h, tpm)
                    if g:
                        s.wait_ge(vp.h, g)
                    s.activation(out=rbuf, in_=pp[rb][:, :], func=AT.Sigmoid)
                P.add("scalar", f)
                spm = bump("scalar", sp)

                def f(v, mt=mt, tc=tc, vb=vb, spm=spm):
                    v.wait_ge(sp.h, spm)
                    v.tensor_tensor(vstage, pp[vb][:, :], rbuf, OP.mult)
                    v.drain()
                    v.scalar_tensor_tensor(xov(mt, 1 + tc * 512, 1 + (tc + 1) * 512),
                                           vstage, 0.5,
                                           xov(mt, 1 + tc * 512, 1 + (tc + 1) * 512),
                                           OP.mult, OP.add)
                P.add("vector", f)
                vp_hist[itf] = bump("vector", vp)
        return bump("vector", vp)

    LN1_CH = [(0, 512), (512, 1024), (1024, 1536), (1536, 2048), (2048, 2049)]
    LN2_CH = [(1, 513), (513, 1025), (1025, 1537), (1537, 2049)]

    stA = ExitStack()
    eA = stA.enter_context
    lnb = eA(nc.sbuf_tensor("lnb", [128, NKT * XOW], BF16))

    def lnv(kt, c0, c1):
        return lnb[:, kt * XOW + c0: kt * XOW + c1]

    ffnsm = eA(nc.sbuf_tensor("ffnsm", [128, 1024], BF16))

    stG = ExitStack()
    g_sb = stG.enter_context(nc.sbuf_tensor("g_sb", [16, 7680], BF16))
    wstage = stG.enter_context(nc.sbuf_tensor("wstage", [128, 2048], BF16))
    go = {}
    _off = 0
    for nm, wd in [("g1in", 1024), ("g2in", 2048), ("g1x", 128), ("g2x", 768),
                   ("g1dt", 128), ("g2dt", 512), ("g1out", 512), ("g2out", 2048)]:
        go[nm] = (_off, _off + wd)
        _off += wd

    def gvw(nm):
        a, b_ = go[nm]
        return g_sb[:, a:b_]

    for nm, gsrc in [("g1in", g1in), ("g2in", g2in), ("g1x", g1x), ("g2x", g2x),
                     ("g1dt", g1dt), ("g2dt", g2dt), ("g1out", g1out), ("g2out", g2out)]:
        dma("gpsimd", gvw(nm), gsrc, ld)
    ld_g = ld.n
    wait("tensor", ld, ld_g)

    phase("ln1")
    emit_ln(xov, lnv, LN1_CH)

    def build_wtmp(g1name, g2name, npt, fchunks, rowbase, dst_hbm):
        for pt in range(npt):
            for ci, (c0, c1) in enumerate(fchunks):
                wd = c1 - c0
                bank = (pt * len(fchunks) + ci) % 2

                def f(t, pt=pt, c0=c0, wd=wd, bank=bank, g1name=g1name, g2name=g2name):
                    t.matmul(pp[4 + bank][:, 0:wd], gvw(g1name)[:, pt * 128:(pt + 1) * 128],
                             gvw(g2name)[:, c0:c0 + wd], start=True, stop=True)
                P.add("tensor", f)
                tpm = bump("tensor", tp)

                def f(s, tpm=tpm, wd=wd, bank=bank):
                    s.wait_ge(tp.h, tpm)
                    s.copy(wstage[:, bank * 512:bank * 512 + wd], pp[4 + bank][:, 0:wd])
                P.add("scalar", f)
                spm = bump("scalar", sp)
                wait("tensor", sp, spm)
                wait("gpsimd", sp, spm)
                wdm = dma("gpsimd", dst_hbm[rowbase + pt * 128: rowbase + (pt + 1) * 128, c0:c0 + wd],
                          wstage[:, bank * 512:bank * 512 + wd], d_wb[bank])
                wait("scalar", d_wb[bank], wdm)

    phase("wbuild")
    F4 = [(0, 512), (512, 1024), (1024, 1536), (1536, 2048)]
    build_wtmp("g1in", "g2in", 8, F4, 0, h_wtmp)
    build_wtmp("g1out", "g2out", 4, F4, 1024, h_wtmp)
    build_wtmp("g1x", "g2x", 1, [(0, 512), (512, 768)], 0, h_wtmpx)
    build_wtmp("g1dt", "g2dt", 1, [(0, 512)], 0, h_wtmpdt)
    barrier()
    stG.close()

    stF = ExitStack()
    mixb = stF.enter_context(nc.sbuf_tensor("mixb", [128, NKT * TH], BF16))
    kabuf = stF.enter_context(nc.sbuf_tensor("kabuf", [128, NKT * TH], BF16))
    wfbuf = [stF.enter_context(nc.sbuf_tensor(f"wf{i}", [128, NKT * NKT * 128], BF16)) for i in range(2)]
    rbuf = ffnsm[:, 0:512]
    vstage = ffnsm[:, 512:1024]

    phase("ffn1")
    emit_ffn(lnv, (mixb, rbuf, kabuf, wfbuf, vstage))
    barrier()
    stF.close()

    phase("ln2")
    emit_ln(xov, lnv, LN2_CH)
    wait("sync", vp, vp.n)
    for kt in range(NKT):
        dma("sync", h_ln2own[kt * 128:(kt + 1) * 128, :], lnv(kt, 1, TH + 1), lx)

    barrier()
    stA.close()

    phase("e1")
    def f(gp):
        gp.collective_compute(
            "AllGather", OP.bypass,
            replica_groups=[[0, 1], [2, 3], [4, 5], [6, 7]],
            ins=[h_ln2own], outs=[h_ln2full],
        ).then_inc(ccs.h, 1)
    P.add("gpsimd", f)
    cc1 = ccs.res()
    wait("gpsimd", ccs, cc1)
    barrier()

    phase("front")
    stB = ExitStack()
    eB = stB.enter_context
    wi_sb = eB(nc.sbuf_tensor("wi_sb", [128, NKT * 2048], BF16))
    wx_sb = eB(nc.sbuf_tensor("wx_sb", [128, NKT * 96], BF16))
    slab = [eB(nc.sbuf_tensor(f"slab{i}", [128, NKT * 512], BF16)) for i in range(2)]
    XCW = TF + 3
    xcp = eB(nc.sbuf_tensor("xcp", [128, XCW], BF16))
    zb = eB(nc.sbuf_tensor("zb", [128, TF], BF16))
    cacc = eB(nc.sbuf_tensor("cacc", [128, TF], BF16))
    ctmp = eB(nc.sbuf_tensor("ctmp", [128, TF], BF16))
    convd = eB(nc.sbuf_tensor("convd", [128, TF], BF16))
    ytile = eB(nc.sbuf_tensor("ytile", [128, TF], BF16))
    cf_sb = eB(nc.sbuf_tensor("cf_sb", [128, NDT * 4], F32))
    cb_sb = eB(nc.sbuf_tensor("cb_sb", [128, NDT], F32))
    dtbc_sb = eB(nc.sbuf_tensor("dtbc_sb", [96, TF], BF16))

    for kt in range(NKT):
        for ai in range(4):
            a = 4 * kt + ai
            srcv = h_wtmp[a * 32: a * 32 + 32, :].rearrange("i (b j) -> b i j", j=64)
            dstv = wi_sb[ai * 32:(ai + 1) * 32, kt * 2048:(kt + 1) * 2048].rearrange(
                "b (i j) -> b i j", j=64)
            dma("gpsimd", dstv, srcv, ld)
    for kt in range(NKT):
        for ai in range(2):
            a_loc = 2 * kt + ai
            srcv = h_wtmpx[a_loc * 8: a_loc * 8 + 8, :].rearrange("i (b j) -> b i j", j=12)
            dstv = wx_sb[ai * 64:(ai + 1) * 64, kt * 96:(kt + 1) * 96].rearrange(
                "b (i j) -> b i j", j=12)
            dma("gpsimd", dstv, srcv, ld)
    dma("gpsimd", cf_sb[:, :].rearrange("p (a b) -> p a b", b=4), cfin.rearrange("(a p) b -> p a b", p=128), ld, slow=True)
    ld_wi = dma("gpsimd", cb_sb[:, :].rearrange("p (a b) -> p a b", b=1), cbin.rearrange("(a p) b -> p a b", p=128), ld, slow=True)
    wait("tensor", ld, ld_wi)

    vp_y_prev = 0
    ld_y_prev = 0
    ip_sph = {}
    ip_tph = {}
    for j in range(NDT):
        for tc in range(NCH):
            sl = tc % 2
            gi = j * NCH + tc
            if gi >= 2:
                wait("sync", tp, ip_tph[gi - 2])
            src = h_ln2full.rearrange("(s p) t -> s p t", s=2)[tc // 4, :, (tc % 4) * 512:(tc % 4 + 1) * 512]
            srcv = src.rearrange("(a p) t -> p a t", p=128)
            lxm = dma("sync", slab[sl][:, :].rearrange("p (a t) -> p a t", a=NKT), srcv, d_slab[sl])
            for which in range(2):
                mcol = (0 if which == 0 else 1024) + j * 128
                iw = gi * 2 + which
                bank = iw % 4
                if iw >= 4:
                    wait("tensor", sp, ip_sph[iw - 4])

                def f(t, mcol=mcol, bank=bank, sl=sl, lxm=lxm):
                    t.wait_ge(d_slab[sl].h, lxm)
                    for kt in range(NKT):
                        t.matmul(pp[bank][:, :],
                                 wi_sb[:, kt * 2048 + mcol: kt * 2048 + mcol + 128],
                                 slab[sl][:, kt * 512:(kt + 1) * 512],
                                 start=(kt == 0), stop=(kt == NKT - 1))
                P.add("tensor", f)
                tpm = bump("tensor", tp)

                def f(s, which=which, tc=tc, bank=bank, tpm=tpm, vp_y_prev=vp_y_prev):
                    s.wait_ge(tp.h, tpm)
                    if tc == 0 and which == 0 and vp_y_prev:
                        s.wait_ge(vp.h, vp_y_prev)
                    if which == 0:
                        s.copy(xcp[:, 1 + tc * 512: 1 + (tc + 1) * 512], pp[bank][:, :])
                    else:
                        s.copy(zb[:, tc * 512:(tc + 1) * 512], pp[bank][:, :])
                P.add("scalar", f)
                ip_sph[iw] = bump("scalar", sp)
            ip_tph[gi] = tp.n
        sp_xz = sp.n

        def f(v, j=j, sp_xz=sp_xz, ld_y_prev=ld_y_prev):
            v.wait_ge(sp.h, sp_xz)
            if ld_y_prev:
                v.wait_ge(d_y.h, ld_y_prev)
            v.memset(xcp[:, 0:1], 0.0)
            v.memset(xcp[:, TF + 1:TF + 3], 0.0)
            v.drain()
            v.tensor_scalar(cacc[:, :], xcp[:, 0:TF], cf_sb[:, j * 4:j * 4 + 1], None, OP.mult)
            v.tensor_scalar(ctmp[:, :], xcp[:, 1:1 + TF], cf_sb[:, j * 4 + 1:j * 4 + 2], None, OP.mult)
            v.tensor_scalar(ytile[:, :], xcp[:, 2:2 + TF], cf_sb[:, j * 4 + 2:j * 4 + 3], None, OP.mult)
            v.tensor_scalar(convd[:, :], xcp[:, 3:3 + TF], cf_sb[:, j * 4 + 3:j * 4 + 4], None, OP.mult)
            v.drain()
            v.tensor_tensor(cacc[:, :], cacc[:, :], ctmp[:, :], OP.add)
            v.tensor_tensor(ytile[:, :], ytile[:, :], convd[:, :], OP.add)
            v.drain()
            v.tensor_tensor(cacc[:, :], cacc[:, :], ytile[:, :], OP.add)
        P.add("vector", f)
        vp_conv = bump("vector", vp)

        def f(s, j=j, vp_conv=vp_conv):
            s.wait_ge(vp.h, vp_conv)
            s.activation(out=cacc[:, :], in_=cacc[:, :], func=AT.Silu, bias=cb_sb[:, j:j + 1])
        P.add("scalar", f)
        sp_silu = bump("scalar", sp)

        def f(v, sp_silu=sp_silu):
            v.wait_ge(sp.h, sp_silu)
            v.tensor_tensor(ytile[:, :], cacc[:, :], zb[:, :], OP.mult)
        P.add("vector", f)
        vp_y = bump("vector", vp)
        wait("gpsimd", vp, vp_y)
        ld_y = dma("gpsimd", h_y[j * 128:(j + 1) * 128, :], ytile[:, :], d_y)
        vp_y_prev = vp_y
        ld_y_prev = ld_y

    phase("xproj")
    wait("sync", d_y, d_y.n)
    xp_sph = {}
    xp_tph = {}
    for tc in range(NCH):
        sl = tc % 2
        if tc >= 2:
            wait("sync", tp, xp_tph[tc - 2])
        lxm = dma("sync", slab[sl][:, :].rearrange("p (a t) -> p a t", a=NDT),
                  h_y.rearrange("(a p) t -> p a t", p=128)[:, :, tc * 512:(tc + 1) * 512], d_slab[sl])
        bank = tc % 4

        def f(t, bank=bank, sl=sl, lxm=lxm):
            t.wait_ge(d_slab[sl].h, lxm)
            for kt in range(NDT):
                t.matmul(pp[bank][0:96, :],
                         wx_sb[:, kt * 96:(kt + 1) * 96],
                         slab[sl][:, kt * 512:(kt + 1) * 512],
                         start=(kt == 0), stop=(kt == NDT - 1))
        P.add("tensor", f)
        tpm = bump("tensor", tp)

        def f(s, tc=tc, bank=bank, tpm=tpm):
            s.wait_ge(tp.h, tpm)
            s.copy(dtbc_sb[:, tc * 512:(tc + 1) * 512], pp[bank][0:96, :])
        P.add("scalar", f)
        spm = bump("scalar", sp)
        xp_sph[tc] = spm
        xp_tph[tc] = tp.n
        if tc >= 3:
            wait("tensor", sp, xp_sph[tc - 3])
    wait("gpsimd", sp, sp.n)
    dma("gpsimd", h_dtbc_in[:, :], dtbc_sb[:, :], ld)

    barrier()
    stB.close()

    phase("e2")
    def f(gp):
        gp.collective_compute(
            "AllReduce", OP.add,
            replica_groups=[[0, 1], [2, 3], [4, 5], [6, 7]],
            ins=[h_dtbc_in], outs=[h_dtbc],
        ).then_inc(ccs.h, 1)
    P.add("gpsimd", f)
    cc2 = ccs.res()
    wait("gpsimd", ccs, cc2)
    barrier()

    phase("scanprep")
    stW = ExitStack()
    wo_sb = stW.enter_context(nc.sbuf_tensor("wo_sb", [128, NKT * 1024], BF16))
    for kt in range(NKT):
        for ai in range(2):
            a_loc = 2 * kt + ai
            srcv = h_wtmp[1024 + a_loc * 32: 1024 + a_loc * 32 + 32, :].rearrange(
                "i (b j) -> b i j", j=32)
            dstv = wo_sb[ai * 64:(ai + 1) * 64, kt * 1024:(kt + 1) * 1024].rearrange(
                "b (i j) -> b i j", j=32)
            dma("gpsimd", dstv, srcv, ld)
    ld_wo = ld.n

    stC = ExitStack()
    eC = stC.enter_context
    bbc = eC(nc.sbuf_tensor("bbc", [128, NST * TC], BF16))
    cbc = eC(nc.sbuf_tensor("cbc", [128, NST * TC], BF16))
    dab2 = [eC(nc.sbuf_tensor(f"dab{i}", [128, NST * TC], BF16)) for i in range(2)]
    bsb = eC(nc.sbuf_tensor("bsb", [128, NST * TC], BF16))
    hsb = eC(nc.sbuf_tensor("hsb", [128, NST * TC], BF16))
    dtun2 = [eC(nc.sbuf_tensor(f"dtun{i}", [64, TC], BF16)) for i in range(2)]
    wdt_sb = eC(nc.sbuf_tensor("wdt_sb", [64, DH], BF16))
    dts2 = [eC(nc.sbuf_tensor(f"dts{i}", [128, TC], BF16)) for i in range(2)]
    dus2 = [eC(nc.sbuf_tensor(f"dus{i}", [128, TC], BF16)) for i in range(2)]
    spf = eC(nc.sbuf_tensor("spf", [128, TC], F32))
    usb = [eC(nc.sbuf_tensor(f"usb{i}", [128, TC], BF16)) for i in range(3)]
    yssb = [eC(nc.sbuf_tensor(f"yssb{i}", [128, TC], BF16)) for i in range(2)]
    acol = eC(nc.sbuf_tensor("acol", [128, NDT * NST], F32))
    dcol = eC(nc.sbuf_tensor("dcol", [128, NDT], F32))
    carry = eC(nc.sbuf_tensor("carry", [128, NDT * NST], BF16))
    fold16 = eC(nc.sbuf_tensor("fold16", [128, NST], BF16))
    hsumc = eC(nc.sbuf_tensor("hsumc", [128, NCH * NDT], F32))
    hsred = eC(nc.sbuf_tensor("hsred", [128, 1], F32))
    hspad = eC(nc.sbuf_tensor("hspad", [1, 1032], F32))

    dma("gpsimd", spf[:, 0:NDT * NST].rearrange("p (a n) -> p a n", a=NDT),
        alog.rearrange("(a p) n -> p a n", p=128), d_stat)
    dma("gpsimd", dcol[:, :].rearrange("p (a b) -> p a b", b=1),
        dvin.rearrange("(a p) b -> p a b", p=128), d_stat, slow=True)
    for a in range(8):
        srcv = h_wtmpdt[a * 16:a * 16 + 16, :].rearrange("i (b j) -> b i j", j=64)
        dstv = wdt_sb[a * 8:(a + 1) * 8, :].rearrange("b (i j) -> b i j", j=64)
        dma("gpsimd", dstv, srcv, d_stat)
    ds_all = d_stat.n

    NIT = NCH * NDT
    dt_ld = {}
    bc_ld = {}
    cc_ld = {}

    def bsrc_of(c):
        return h_dtbc[64:80, c * TC:(c + 1) * TC].unsqueeze(0).broadcast_to([128, NST, TC])

    def csrc_of(c):
        return h_dtbc[80:96, c * TC:(c + 1) * TC].unsqueeze(0).broadcast_to([128, NST, TC])

    dt_ld[0] = dma("gpsimd", dtun2[0][:, :], h_dtbc[0:64, 0:TC], d_dt)
    dt_ld[1] = dma("gpsimd", dtun2[1][:, :], h_dtbc[0:64, TC:2 * TC], d_dt)
    bc_ld[0] = dma("gpsimd", bbc[:, :].rearrange("p (n t) -> p n t", n=NST), bsrc_of(0), d_bc)
    cc_ld[0] = dma("gpsimd", cbc[:, :].rearrange("p (n t) -> p n t", n=NST), csrc_of(0), d_bc2)

    def f(s, g=ds_all):
        s.wait_ge(d_stat.h, g)
        s.activation(out=spf[:, 0:NDT * NST], in_=spf[:, 0:NDT * NST], func=AT.Exp)
    P.add("scalar", f)
    sp_aexp = bump("scalar", sp)

    def f(v, g=ds_all, gs=sp_aexp):
        v.wait_ge(d_stat.h, g)
        v.wait_ge(sp.h, gs)
        v.tensor_scalar(acol[:, :], spf[:, 0:NDT * NST], -1.0, None, OP.mult)
        v.memset(carry[:, :], 0.0)
        v.memset(hspad[0:1, :], 0.0)
    P.add("vector", f)
    vp_prep = bump("vector", vp)

    wait("tensor", ld, ld.n)
    wait("tensor", vp, vp.n)
    for ec in range(2):
        def f(t, ec=ec):
            for kt in range(NKT):
                t.matmul(pp[6][0:1, :], ones_c[:, :],
                         wo_sb[:, kt * 1024 + ec * 512: kt * 1024 + (ec + 1) * 512],
                         start=(kt == 0), stop=(kt == NKT - 1))
        P.add("tensor", f)
        tpm = bump("tensor", tp)

        def f(s, ec=ec, tpm=tpm):
            s.wait_ge(tp.h, tpm)
            s.copy(hspad[0:1, 8 + ec * 512: 8 + (ec + 1) * 512], pp[6][0:1, :])
        P.add("scalar", f)
        spm = bump("scalar", sp)
        wait("tensor", sp, spm)

    phase("scan")
    sp_sp = {}
    sp_da = {}
    sp_hs = {}
    vp_du = {}
    vp_c = {}
    vp_scanh = {}
    vp_ysh = {}
    tp_dt = {}
    ysw_h = {}
    u_ld = {}

    def jj(k):
        return k % NDT

    def cof(k):
        return k // NDT

    def f(t, g=ds_all, gd=dt_ld[1]):
        t.wait_ge(d_stat.h, g)
        t.wait_ge(d_dt.h, gd)
        t.matmul(pp[0][:, :], wdt_sb[:, 0:128], dtun2[0][:, :], start=True, stop=True)
    P.add("tensor", f)
    tp_dt[0] = bump("tensor", tp)

    def f(s, g=tp_dt[0]):
        s.wait_ge(tp.h, g)
        s.activation(out=spf[:, :], in_=pp[0][:, :], func=AT.Exp)
        s.drain()
        s.activation(out=dts2[0][:, :], in_=spf[:, :], func=AT.Ln, bias=1.0)
    P.add("scalar", f)
    sp_sp[0] = bump("scalar", sp)

    def f(t):
        t.matmul(pp[1][:, :], wdt_sb[:, 128:256], dtun2[0][:, :], start=True, stop=True)
    P.add("tensor", f)
    tp_dt[1] = bump("tensor", tp)

    u_ld[0] = dma("sync", usb[0][:, :], h_y[0:128, 0:TC], d_u[0])
    u_ld[1] = dma("sync", usb[1][:, :], h_y[128:256, 0:TC], d_u[1])

    for i in range(NIT + 1):
        p, q = i % 2, (i - 1) % 2
        k = i - 1
        if i < NIT:
            def f(v, i=i, p=p, gs=sp_sp[i], gu=u_ld[i]):
                v.wait_ge(sp.h, gs)
                v.wait_ge(d_u[i % 3].h, gu)
                v.tensor_tensor(dus2[p][:, :], dts2[p][:, :], usb[i % 3][:, :], OP.mult)
            P.add("vector", f)
            vp_du[i] = bump("vector", vp)
        if i >= 1:
            gsp = sp_hs[k - 1] if k >= 1 else sp_da[k]

            def f(v, k=k, kp=q, gsp=gsp, gbc=bc_ld[cof(k)], j=jj(k)):
                v.wait_ge(sp.h, gsp)
                v.wait_ge(d_bc.h, gbc)
                da0 = dab2[kp][:, :].rearrange("p (n t) -> p n t", n=NST)[:, :, 0]
                bb0 = bsb[:, :].rearrange("p (n t) -> p n t", n=NST)[:, :, 0]
                v.tensor_tensor(fold16[:, :], da0, carry[:, j * NST:(j + 1) * NST], OP.mult)
                v.tensor_tensor(bsb[:, :].rearrange("p (n t) -> p n t", n=NST),
                                dus2[kp][:, :].unsqueeze(1).broadcast_to([128, NST, TC]),
                                bbc[:, :].rearrange("p (n t) -> p n t", n=NST), OP.mult)
                v.drain()
                v.memset(da0, 0.0)
                v.drain()
                v.tensor_tensor(bb0, bb0, fold16[:, :], OP.add)
                v.drain()
                v.tensor_tensor_scan(hsb[:, :], dab2[kp][:, :], bsb[:, :], 0.0, OP.mult, OP.add)
            P.add("vector", f)
            vp_scanh[k] = bump("vector", vp)

            def f(v, k=k, kp=q, gcc=cc_ld[cof(k)], j=jj(k)):
                v.wait_ge(d_bc2.h, gcc)
                hl = hsb[:, :].rearrange("p (n t) -> p n t", n=NST)[:, :, TC - 1]
                v.tensor_tensor(bsb[:, :], hsb[:, :], cbc[:, :], OP.mult)
                v.tensor_copy(carry[:, j * NST:(j + 1) * NST], hl)
            P.add("vector", f)
            vp_c[k] = bump("vector", vp)

            def f(v, k=k, kp=q, gys=ysw_h.get(k - 2, 0), j=jj(k)):
                if gys:
                    v.wait_ge(d_ysw[kp].h, gys)
                v.tensor_tensor(bsb[:, 0:8 * TC], bsb[:, 0:8 * TC], bsb[:, 8 * TC:16 * TC], OP.add)
                v.drain()
                v.tensor_tensor(bsb[:, 0:4 * TC], bsb[:, 0:4 * TC], bsb[:, 4 * TC:8 * TC], OP.add)
                v.drain()
                v.tensor_tensor(bsb[:, 0:2 * TC], bsb[:, 0:2 * TC], bsb[:, 2 * TC:4 * TC], OP.add)
                v.drain()
                v.tensor_tensor(yssb[kp][:, :], bsb[:, 0:TC], bsb[:, TC:2 * TC], OP.add)
                v.drain()
                v.scalar_tensor_tensor(yssb[kp][:, :], usb[k % 3][:, :], dcol[:, j:j + 1],
                                       yssb[kp][:, :], OP.mult, OP.add)
            P.add("vector", f)
            vp_ysh[k] = bump("vector", vp)

        if i < NIT:
            if i + 1 < NIT:
                def f(s, b=q, gt=tp_dt[i + 1], gd=vp_du.get(i - 1, 0)):
                    s.wait_ge(tp.h, gt)
                    if gd:
                        s.wait_ge(vp.h, gd)
                    s.activation(out=spf[:, :], in_=pp[b][:, :], func=AT.Exp)
                    s.drain()
                    s.activation(out=dts2[b][:, :], in_=spf[:, :], func=AT.Ln, bias=1.0)
                P.add("scalar", f)
                sp_sp[i + 1] = bump("scalar", sp)

            gv = vp_prep if i == 0 else vp_scanh.get(i - 2, 0)

            def f(s, i=i, p=p, gv=gv, j=jj(i)):
                if gv:
                    s.wait_ge(vp.h, gv)
                s.drain()
                for n in range(NST):
                    s.activation(out=dab2[p][:, n * TC:(n + 1) * TC], in_=dts2[p][:, :],
                                 func=AT.Copy, scale=acol[:, j * NST + n: j * NST + n + 1])
                s.drain()
                s.activation(out=dab2[p][:, :], in_=dab2[p][:, :], func=AT.Exp)
            P.add("scalar", f)
            sp_da[i] = bump("scalar", sp)

        if i >= 1:
            def f(s, k=k, gv=vp_c[k]):
                s.wait_ge(vp.h, gv)
                s.drain()
                s.activation(out=hsb[:, :], in_=hsb[:, :], func=AT.Copy,
                             accum_out=hsumc[:, k:k + 1])
            P.add("scalar", f)
            sp_hs[k] = bump("scalar", sp)

        if i + 2 < NIT:
            wait("tensor", sp, sp_sp[i])
            c2 = cof(i + 2)

            def f(t, i=i, c2=c2, gd=dt_ld[max(c2, 1)], j=jj(i + 2)):
                t.wait_ge(d_dt.h, gd)
                t.matmul(pp[i % 2][:, :], wdt_sb[:, j * 128:(j + 1) * 128],
                         dtun2[c2 % 2][:, :], start=True, stop=True)
            P.add("tensor", f)
            tp_dt[i + 2] = bump("tensor", tp)

        if 2 <= i + 2 < NIT:
            wait("sync", vp, vp_ysh.get(i - 1, 0))
            u_ld[i + 2] = dma("sync", usb[(i + 2) % 3][:, :],
                              h_y[jj(i + 2) * 128:(jj(i + 2) + 1) * 128,
                                  cof(i + 2) * TC:(cof(i + 2) + 1) * TC], d_u[(i + 2) % 3])

        if i >= 1:
            if i % NDT == 0 and i // NDT >= 1 and i < NIT:
                c = i // NDT
                wait("gpsimd", vp, vp_scanh[i - 1])
                bc_ld[c] = dma("gpsimd", bbc[:, :].rearrange("p (n t) -> p n t", n=NST),
                               bsrc_of(c), d_bc)
                wait("gpsimd", vp, vp_ysh[i - 1])
                ysw_h[k] = dma("gpsimd", h_ys[jj(k) * 128:(jj(k) + 1) * 128,
                                              cof(k) * TC:(cof(k) + 1) * TC],
                               yssb[q][:, :], d_ysw[q])
                cc_ld[c] = dma("gpsimd", cbc[:, :].rearrange("p (n t) -> p n t", n=NST),
                               csrc_of(c), d_bc2)
                if c + 1 < NCH:
                    wait("gpsimd", tp, tp_dt[c * NDT - 1])
                    dt_ld[c + 1] = dma("gpsimd", dtun2[(c + 1) % 2][:, :],
                                       h_dtbc[0:64, (c + 1) * TC:(c + 2) * TC], d_dt)
            else:
                wait("gpsimd", vp, vp_ysh[i - 1])
                ysw_h[k] = dma("gpsimd", h_ys[jj(k) * 128:(jj(k) + 1) * 128,
                                              cof(k) * TC:(cof(k) + 1) * TC],
                               yssb[q][:, :], d_ysw[q])


    phase("hsfin")
    wait("vector", sp, sp_hs[NIT - 1])

    def f(v):
        v.tensor_reduce(hsred[:, 0:1], hsumc[:, :], AX.X, OP.add)
    P.add("vector", f)
    vp_hsr = bump("vector", vp)

    def f(t, vp_hsr=vp_hsr):
        t.wait_ge(vp.h, vp_hsr)
        t.matmul(pp[6][0:1, 0:1], ones_f[:, :], hsred[:, 0:1], start=True, stop=True)
    P.add("tensor", f)
    tp_hs = bump("tensor", tp)

    def f(s, tp_hs=tp_hs):
        s.wait_ge(tp.h, tp_hs)
        s.copy(hspad[0:1, 0:1], pp[6][0:1, 0:1])
    P.add("scalar", f)
    sp_hsf = bump("scalar", sp)
    wait("gpsimd", sp, sp_hsf)
    dma("gpsimd", h_hs_in[:, :], hspad[0:1, :], ld)

    barrier()
    stC.close()

    phase("e4")
    def f(gp):
        gp.collective_compute(
            "AllReduce", OP.add,
            replica_groups=[[0, 1, 2, 3, 4, 5, 6, 7]],
            ins=[h_hs_in], outs=[h_hs],
        ).then_inc(ccs.h, 1)
    P.add("gpsimd", f)
    cc4 = ccs.res()

    phase("outproj")
    stD = ExitStack()
    eD = stD.enter_context
    yslab = [eD(nc.sbuf_tensor(f"yslab{i}", [128, NKT * 128], BF16)) for i in range(2)]
    opst = [eD(nc.sbuf_tensor(f"opst{i}", [128, 512], BF16)) for i in range(2)]

    op_w_hist = {}
    op_sph = {}
    op_dwh = {}
    for tt in range(TF // 128):
        sl = tt % 2
        if tt >= 2:
            wait("sync", tp, op_w_hist[tt - 2])
        lxm = dma("sync", yslab[sl][:, :].rearrange("p (a t) -> p a t", a=NKT),
                  h_ys.rearrange("(a p) t -> p a t", p=128)[:, :, tt * 128:(tt + 1) * 128], d_slab[sl])
        for ec in range(2):
            bank = (tt * 2 + ec) % 4

            def f(t, tt=tt, ec=ec, bank=bank, sl=sl, lxm=lxm):
                t.wait_ge(d_slab[sl].h, lxm)
                for kt in range(NKT):
                    t.matmul(pp[bank][:, :],
                             yslab[sl][:, kt * 128:(kt + 1) * 128],
                             wo_sb[:, kt * 1024 + ec * 512:(kt) * 1024 + (ec + 1) * 512],
                             start=(kt == 0), stop=(kt == NKT - 1))
            P.add("tensor", f)
            tpm = bump("tensor", tp)
            if ec == 1:
                op_w_hist[tt] = tpm

            def f(s, ec=ec, bank=bank, tpm=tpm, st_i=(tt * 2 + ec) % 2,
                  g=op_dwh.get(tt * 2 + ec - 2, 0), gs=(tt * 2 + ec) % 2):
                s.wait_ge(tp.h, tpm)
                if g:
                    s.wait_ge(d_op[gs].h, g)
                s.copy(opst[st_i][:, :], pp[bank][:, :])
            P.add("scalar", f)
            spm = bump("scalar", sp)
            op_sph[tt * 2 + ec] = spm
            if tt * 2 + ec >= 3:
                wait("tensor", sp, op_sph[tt * 2 + ec - 3])
            wait("gpsimd", sp, spm)
            stb = (tt * 2 + ec) % 2
            opw = dma("gpsimd", h_op[tt * 128:(tt + 1) * 128, ec * 512:(ec + 1) * 512],
                      opst[stb][:, :], d_op[stb])
            op_dwh[tt * 2 + ec] = opw

    barrier()

    phase("rs")
    def f(gp):
        gp.collective_compute(
            "ReduceScatter", OP.add,
            replica_groups=[[0, 1], [2, 3], [4, 5], [6, 7]],
            ins=[h_op], outs=[h_od],
        ).then_inc(ccs.h, 1)
    P.add("gpsimd", f)
    cc_rs = ccs.res()
    wait("gpsimd", ccs, cc_rs)
    wait("gpsimd", ccs, cc4)
    barrier()
    stD.close()
    stW.close()

    phase("grpE_prep")
    stE = ExitStack()
    eE = stE.enter_context
    lnb3 = eE(nc.sbuf_tensor("lnb3", [128, NKT * XOW], BF16))

    def lnv3(kt, c0, c1):
        return lnb3[:, kt * XOW + c0: kt * XOW + c1]

    ffnsm3 = eE(nc.sbuf_tensor("ffnsm3", [128, 1024], BF16))
    halc = eE(nc.sbuf_tensor("halc", [128, NKT], F32))
    stT = ExitStack()
    hrow = stT.enter_context(nc.sbuf_tensor("hrow", [1, 1032], F32))
    mrow = stT.enter_context(nc.sbuf_tensor("mrow", [1, E], F32))
    mbc = stT.enter_context(nc.sbuf_tensor("mbc", [128, NKT], F32))
    dodb = [stT.enter_context(nc.sbuf_tensor(f"dodb{i}", [128, TH], BF16)) for i in range(2)]

    ld_hrow = dma("gpsimd", hrow[0:1, :], h_hs[0:1, :], ld)

    def f(v, ld_hrow=ld_hrow):
        v.wait_ge(ld.h, ld_hrow)
        v.tensor_scalar(hrow[0:1, 0:1], hrow[0:1, 0:1], 1e-8 / (4.0 * HS_COUNT), None, OP.mult)
        v.drain()
        v.tensor_scalar(mrow[0:1, :], hrow[0:1, 8:8 + E], hrow[0:1, 0:1], None, OP.mult)
    P.add("vector", f)
    vp_mb = bump("vector", vp)
    wait("gpsimd", vp, vp_mb)
    wmb = dma("gpsimd", h_mb[:, :], mrow[0:1, :], ld)
    wait("gpsimd", ld, wmb)
    for kt in range(NKT):
        dma("gpsimd", mbc[:, kt:kt + 1],
            h_mb[0:1, kt * 128:(kt + 1) * 128].rearrange("a b -> b a"), ld, slow=True)
    ld_mbc = ld.n

    dod_hist = {}
    for et in range(NKT):
        sl = et % 2
        if et >= 2:
            wait("sync", vp, dod_hist[et - 2])
        v_tr = d_dod[sl].res()

        def f(eng, sl=sl, et=et):
            eng.dma_start_transpose(dodb[sl][:, :],
                                    h_od[:, et * 128:(et + 1) * 128]).then_inc(d_dod[sl].h, 16)
        P.add("sync", f)

        def f(v, sl=sl, et=et, v_tr=v_tr, ld_mbc=ld_mbc):
            v.wait_ge(d_dod[sl].h, v_tr)
            if et == 0:
                v.wait_ge(ld.h, ld_mbc)
            v.scalar_tensor_tensor(xov(et, 1, TH + 1), dodb[sl][:, :],
                                   mbc[:, et:et + 1], xov(et, 1, TH + 1),
                                   OP.add, OP.add)
        P.add("vector", f)
        dod_hist[et] = bump("vector", vp)
    barrier()
    stT.close()

    phase("ln3")
    emit_ln(xov, lnv3, LN2_CH)

    phase("e5")
    def f(v):
        for kt in range(NKT):
            v.tensor_scalar(halc[:, kt:kt + 1], lnv3(kt, TH, TH + 1),
                            mskp_sb[:, kt:kt + 1], None, OP.mult)
    P.add("vector", f)
    vp_pl = bump("vector", vp)
    wait("gpsimd", vp, vp_pl)
    wait("gpsimd", ld, ld.n)
    wpl = dma("gpsimd", h_halo_in.rearrange("(a p) b -> p a b", p=128), halc[:, :].rearrange("p (a b) -> p a b", b=1), ld, slow=True)
    wait("gpsimd", ld, wpl)

    def f(gp):
        gp.collective_compute(
            "AllReduce", OP.add,
            replica_groups=[[0, 1], [2, 3], [4, 5], [6, 7]],
            ins=[h_halo_in], outs=[h_halo],
        ).then_inc(ccs.h, 1)
    P.add("gpsimd", f)
    cc5 = ccs.res()
    wait("gpsimd", ccs, cc5)
    ld_hal = dma("gpsimd", halc[:, :].rearrange("p (a b) -> p a b", b=1), h_halo.rearrange("(a p) b -> p a b", p=128), ld, slow=True)
    wait("vector", ld, ld_hal)

    def f(v):
        for kt in range(NKT):
            v.tensor_scalar(lnv3(kt, 0, 1), halc[:, kt:kt + 1],
                            msku_sb[:, kt:kt + 1], None, OP.mult)
    P.add("vector", f)
    bump("vector", vp)

    phase("ffn3")
    stF3 = ExitStack()
    mixb3 = stF3.enter_context(nc.sbuf_tensor("mixb3", [128, NKT * TH], BF16))
    kabuf3 = stF3.enter_context(nc.sbuf_tensor("kabuf3", [128, NKT * TH], BF16))
    wfbuf3 = [stF3.enter_context(nc.sbuf_tensor(f"wf3{i}", [128, NKT * NKT * 128], BF16)) for i in range(2)]
    rbuf3 = ffnsm3[:, 0:512]
    vstage3 = ffnsm3[:, 512:1024]
    emit_ffn(lnv3, (mixb3, rbuf3, kabuf3, wfbuf3, vstage3))
    stF3.close()

    phase("outstore")
    wait("sync", vp, vp.n)
    for kt in range(NKT):
        dma("sync", out_t[kt * 128:(kt + 1) * 128, :], xov(kt, 1, TH + 1), lx)
    lx_fin = lx.n
    wait("sync", lx, lx_fin)
    barrier()
    stE.close()

    phase_end()
    block = e(nc.Block())
    P.emit(block)
    st.close()
    return nc


def _host_maps(inputs):
    x = np.asarray(inputs["L0_attn_x"])
    in_g1 = np.asarray(inputs["L0_attn_in_g1"], np.float32)
    in_g2 = np.asarray(inputs["L0_attn_in_g2"], np.float32)
    x_g1 = np.asarray(inputs["L0_attn_x_g1"], np.float32)
    x_g2 = np.asarray(inputs["L0_attn_x_g2"], np.float32)
    dt_g1 = np.asarray(inputs["L0_attn_dt_g1"], np.float32)
    dt_g2 = np.asarray(inputs["L0_attn_dt_g2"], np.float32)
    out_g1 = np.asarray(inputs["L0_attn_out_g1"], np.float32)
    out_g2 = np.asarray(inputs["L0_attn_out_g2"], np.float32)
    cf = np.asarray(inputs["L0_attn_conv_filter"], np.float32)
    cb = np.asarray(inputs["L0_attn_conv_bias"], np.float32)
    alog_a = np.asarray(inputs["L0_attn_A_log"], np.float32)
    dv = np.asarray(inputs["L0_attn_D"], np.float32)
    wr = np.asarray(inputs["L0_ffn_W_r"], np.float32)
    wk = np.asarray(inputs["L0_ffn_W_k"], np.float32)
    wv = np.asarray(inputs["L0_ffn_W_v"], np.float32)
    mur = np.asarray(inputs["L0_ffn_mu_r"], np.float32)
    muk = np.asarray(inputs["L0_ffn_mu_k"], np.float32)

    C = np.ascontiguousarray
    wr_t = C(wr.T)
    wk_t = C(wk.T)
    wv_t = C(wv.T)
    g2in_m = C(in_g2.reshape(16, 2048))
    g2x_m = C(x_g2.reshape(16, 768))
    g2dt_m = C(dt_g2.reshape(16, 512))
    g2out_m = C(out_g2.reshape(16, 2048))

    maps = []
    for c in range(8):
        b, h = c // 2, c % 2
        xs = x[b, h * TH:(h + 1) * TH, :].T.astype(np.float32)
        halo = (x[b, h * TH - 1:h * TH, :].T.astype(np.float32) if h == 1
                else np.zeros((E, 1), np.float32))
        xin_m = C(np.concatenate([halo, xs], axis=1))

        iperm = list(range(h * 16, (h + 1) * 16)) + list(range(32 + h * 16, 32 + (h + 1) * 16))
        g1in_m = C(in_g1[:, iperm, :].reshape(32 * 32, 16).T)
        aperm = list(range(h * 16, (h + 1) * 16))
        g1x_m = C(x_g1[aperm].reshape(16 * 8, 16).T)
        ipermd = list(range(h * 16, (h + 1) * 16))
        g1dt_m = C(dt_g1[:, ipermd, :].reshape(8 * 16, 16).T)
        g1out_m = C(out_g1[aperm].reshape(16 * 32, 16).T)

        dh = slice(h * DH, (h + 1) * DH)
        maps.append(dict(
            xin=xin_m,
            wr_t=wr_t, wk_t=wk_t, wv_t=wv_t,
            mu_r=C(mur.reshape(E, 1)), mu_k=C(muk.reshape(E, 1)),
            g1in=g1in_m, g2in=g2in_m,
            g1x=g1x_m, g2x=g2x_m,
            g1dt=g1dt_m, g2dt=g2dt_m,
            g1out=g1out_m, g2out=g2out_m,
            cfin=C(cf[:, 0, dh].T),
            cbin=C(cb[dh].reshape(DH, 1)),
            alog=C(alog_a[dh]),
            dvin=C(dv[dh].reshape(DH, 1)),
            maskp=np.full((E, 1), 1.0 - h, np.float32),
            masku=np.full((E, 1), float(h), np.float32),
        ))
    return maps


def kernel(**inputs):
    maps = _host_maps(inputs)
    nc = build_nc()
    res = run_bass_kernel_spmd(nc, maps, core_ids=list(range(8)))
    out = np.empty((4, TF, E), np.float32)
    for c in range(8):
        b, h = c // 2, c % 2
        out[b, h * TH:(h + 1) * TH, :] = res.results[c]["out"].T
    return out


# version 07
# speedup vs baseline: 1.3098x; 1.0048x over previous
import numpy as np
from contextlib import ExitStack

import concourse.bass as bass
import concourse.mybir as mybir
from concourse.bass_utils import run_bass_kernel_spmd
from concourse.aot_env import allow_ici_all_to_all

F32 = mybir.dt.float32
BF16 = mybir.dt.bfloat16
AT = mybir.ActivationFunctionType
OP = mybir.AluOpType
AX = mybir.AxisListType

E = 1024
DIN = 2048
DH = 1024
NST = 16
TF = 4096
TH = 2048
TC = 512
NCH = TF // TC
NDT = DH // 128
NKT = E // 128
XOW = TH + 1
HS_COUNT = 4.0 * TF * DIN * NST

ENGS = ["sync", "gpsimd", "vector", "scalar", "tensor"]


class Cnt:
    def __init__(self, h, step=1):
        self.h = h
        self.n = 0
        self.step = step

    def res(self, k=None):
        self.n += self.step if k is None else k
        return self.n


class Prog:
    def __init__(self):
        self.q = {en: [] for en in ENGS}

    def add(self, en, f):
        self.q[en].append(f)

    def emit(self, block):
        for en in ENGS:
            fs = self.q[en]

            def runner(eng, fs=fs):
                for f in fs:
                    f(eng)
            getattr(block, en)(runner)


def build_nc():
    nc = bass.Bass("TRN2", target_bir_lowering=False)

    xin = nc.dram_tensor("xin", [E, XOW], F32, kind="ExternalInput").ap()
    wr_t = nc.dram_tensor("wr_t", [E, E], F32, kind="ExternalInput").ap()
    wk_t = nc.dram_tensor("wk_t", [E, E], F32, kind="ExternalInput").ap()
    wv_t = nc.dram_tensor("wv_t", [E, E], F32, kind="ExternalInput").ap()
    mu_r = nc.dram_tensor("mu_r", [E, 1], F32, kind="ExternalInput").ap()
    mu_k = nc.dram_tensor("mu_k", [E, 1], F32, kind="ExternalInput").ap()
    g1in = nc.dram_tensor("g1in", [16, 1024], F32, kind="ExternalInput").ap()
    g2in = nc.dram_tensor("g2in", [16, 2048], F32, kind="ExternalInput").ap()
    g1x = nc.dram_tensor("g1x", [16, 128], F32, kind="ExternalInput").ap()
    g2x = nc.dram_tensor("g2x", [16, 768], F32, kind="ExternalInput").ap()
    g1dt = nc.dram_tensor("g1dt", [16, 128], F32, kind="ExternalInput").ap()
    g2dt = nc.dram_tensor("g2dt", [16, 512], F32, kind="ExternalInput").ap()
    g1out = nc.dram_tensor("g1out", [16, 512], F32, kind="ExternalInput").ap()
    g2out = nc.dram_tensor("g2out", [16, 2048], F32, kind="ExternalInput").ap()
    cfin = nc.dram_tensor("cfin", [DH, 4], F32, kind="ExternalInput").ap()
    cbin = nc.dram_tensor("cbin", [DH, 1], F32, kind="ExternalInput").ap()
    alog = nc.dram_tensor("alog", [DH, NST], F32, kind="ExternalInput").ap()
    dvin = nc.dram_tensor("dvin", [DH, 1], F32, kind="ExternalInput").ap()
    maskp = nc.dram_tensor("maskp", [E, 1], F32, kind="ExternalInput").ap()
    masku = nc.dram_tensor("masku", [E, 1], F32, kind="ExternalInput").ap()
    out_t = nc.dram_tensor("out", [E, TH], F32, kind="ExternalOutput").ap()

    h_wtmp = nc.dram_tensor("h_wtmp", [2048, 2048], BF16).ap()
    h_wtmpx = nc.dram_tensor("h_wtmpx", [128, 768], BF16).ap()
    h_wtmpdt = nc.dram_tensor("h_wtmpdt", [128, 512], BF16).ap()
    h_ln2own = nc.dram_tensor("h_ln2own", [E, TH], BF16).ap()
    h_ln2full = nc.dram_tensor("h_ln2full", [2 * E, TH], BF16).ap()
    h_y = nc.dram_tensor("h_y", [DH, TF], BF16).ap()
    h_dtbc_in = nc.dram_tensor("h_dtbc_in", [96, TF], BF16).ap()
    h_dtbc = nc.dram_tensor("h_dtbc", [96, TF], BF16).ap()
    h_ys = nc.dram_tensor("h_ys", [DH, TF], BF16).ap()
    h_op = nc.dram_tensor("h_op", [TF, E], BF16).ap()
    h_od = nc.dram_tensor("h_od", [TH, E], BF16).ap()
    h_hs_in = nc.dram_tensor("h_hs_in", [1, 1032], F32).ap()
    h_hs = nc.dram_tensor("h_hs", [1, 1032], F32).ap()
    h_halo_in = nc.dram_tensor("h_halo_in", [E, 1], F32).ap()
    h_halo = nc.dram_tensor("h_halo", [E, 1], F32).ap()
    h_stat = nc.dram_tensor("h_stat", [2, XOW], BF16).ap()
    h_mb = nc.dram_tensor("h_mb", [1, E], F32).ap()

    st = ExitStack()
    e = st.enter_context

    BARs = e(nc.semaphore("BAR"))
    bar_n = [0]
    ld = Cnt(e(nc.semaphore("ld")), 16)
    lx = Cnt(e(nc.semaphore("lx")), 16)
    tp = Cnt(e(nc.semaphore("tp")), 1)
    sp = Cnt(e(nc.semaphore("sp")), 1)
    vp = Cnt(e(nc.semaphore("vp")), 1)
    ccs = Cnt(e(nc.semaphore("ccs")), 1)
    d_slab = [Cnt(e(nc.semaphore(f"dslab{i}")), 16) for i in range(2)]
    d_u = [Cnt(e(nc.semaphore(f"du{i}")), 16) for i in range(3)]
    d_ysw = [Cnt(e(nc.semaphore(f"dysw{i}")), 16) for i in range(2)]
    d_op = [Cnt(e(nc.semaphore(f"dop{i}")), 16) for i in range(2)]
    d_dod = [Cnt(e(nc.semaphore(f"ddod{i}")), 16) for i in range(2)]
    d_wb = [Cnt(e(nc.semaphore(f"dwb{i}")), 16) for i in range(2)]
    d_bc = Cnt(e(nc.semaphore("dbc")), 16)
    d_bc2 = Cnt(e(nc.semaphore("dbc2")), 16)
    d_dt = Cnt(e(nc.semaphore("ddt")), 16)
    d_y = Cnt(e(nc.semaphore("dy")), 16)
    d_stat = Cnt(e(nc.semaphore("dstat")), 16)

    SYNC_SEMS = [lx] + d_slab + d_u + d_dod
    GP_SEMS = [ld, d_bc, d_bc2, d_dt, d_y, d_stat] + d_ysw + d_op + d_wb

    P = Prog()

    def barrier():
        bar_n[0] += 1
        tgt = 5 * bar_n[0]
        waits = {"sync": [(c, c.n) for c in SYNC_SEMS],
                 "gpsimd": [(c, c.n) for c in GP_SEMS],
                 "vector": [(vp, vp.n)],
                 "scalar": [(sp, sp.n)],
                 "tensor": [(tp, tp.n)]}
        for en in ENGS:
            ws = waits[en]

            def f(eng, ws=ws, tgt=tgt):
                for c, v in ws:
                    if v > 0:
                        eng.wait_ge(c.h, v)
                eng.sem_inc(BARs, 1)
                eng.wait_ge(BARs, tgt)
            P.add(en, f)

    def dma(en, dst, src, cnt, slow=False):
        v = cnt.res()

        def f(eng, dst=dst, src=src, slow=slow):
            if slow:
                with nc.allow_non_contiguous_dma(reason="tiny gather"):
                    eng.dma_start(out=dst, in_=src).then_inc(cnt.h, cnt.step)
            else:
                eng.dma_start(out=dst, in_=src).then_inc(cnt.h, cnt.step)
        P.add(en, f)
        return v

    def wait(en, cnt, v):
        if v <= 0:
            return

        def f(eng, cnt=cnt, v=v):
            eng.wait_ge(cnt.h, v)
        P.add(en, f)

    cur_scope = {en: None for en in ENGS}

    def phase(name):
        for en in ENGS:
            def f(eng, en=en, name=name):
                if cur_scope[en] is not None:
                    pn, pid = cur_scope[en]
                    nc.leave_named_scope(pn, pid, False)
                sid, _ = nc.enter_named_scope(name, False)
                cur_scope[en] = (name, sid)
            P.add(en, f)

    def phase_end():
        for en in ENGS:
            def f(eng, en=en):
                if cur_scope[en] is not None:
                    pn, pid = cur_scope[en]
                    nc.leave_named_scope(pn, pid, False)
                    cur_scope[en] = None
            P.add(en, f)

    def bump(en, cnt):
        def f(eng, cnt=cnt):
            eng.maybe_drain_then_inc((cnt.h, cnt.step))
        P.add(en, f)
        return cnt.res()

    xo = e(nc.sbuf_tensor("xo", [128, NKT * XOW], F32))

    def xov(kt, c0, c1):
        return xo[:, kt * XOW + c0: kt * XOW + c1]

    mu_r_sb = e(nc.sbuf_tensor("mu_r_sb", [128, NKT], F32))
    mu_k_sb = e(nc.sbuf_tensor("mu_k_sb", [128, NKT], F32))
    mu_r1_sb = e(nc.sbuf_tensor("mu_r1_sb", [128, NKT], F32))
    mu_k1_sb = e(nc.sbuf_tensor("mu_k1_sb", [128, NKT], F32))
    msku_sb = e(nc.sbuf_tensor("msku_sb", [128, NKT], F32))
    mskp_sb = e(nc.sbuf_tensor("mskp_sb", [128, NKT], F32))
    ones_c = e(nc.sbuf_tensor("ones_c", [128, 1], BF16))
    ones_f = e(nc.sbuf_tensor("ones_f", [128, 1], F32))
    pp = [e(nc.psum_tensor(f"pp{i}", [128, 512], F32)) for i in range(8)]

    phase("p0_load")
    for kt in range(NKT):
        dma("sync", xov(kt, 0, XOW), xin[kt * 128:(kt + 1) * 128, :], lx)
    dma("gpsimd", mu_r_sb[:, :].rearrange("p (a b) -> p a b", b=1), mu_r.rearrange("(a p) b -> p a b", p=128), ld, slow=True)
    dma("gpsimd", mu_k_sb[:, :].rearrange("p (a b) -> p a b", b=1), mu_k.rearrange("(a p) b -> p a b", p=128), ld, slow=True)
    dma("gpsimd", msku_sb[:, :].rearrange("p (a b) -> p a b", b=1), masku.rearrange("(a p) b -> p a b", p=128), ld, slow=True)
    dma("gpsimd", mskp_sb[:, :].rearrange("p (a b) -> p a b", b=1), maskp.rearrange("(a p) b -> p a b", p=128), ld, slow=True)

    ld_mu = ld.n

    def f(v, ld_mu=ld_mu):
        v.memset(ones_c[:, :], 1.0)
        v.memset(ones_f[:, :], 1.0)
        v.wait_ge(ld.h, ld_mu)
        v.tensor_scalar(mu_r1_sb[:, :], mu_r_sb[:, :], -1.0, 1.0, OP.mult, OP.add)
        v.tensor_scalar(mu_k1_sb[:, :], mu_k_sb[:, :], -1.0, 1.0, OP.mult, OP.add)
    P.add("vector", f)
    bump("vector", vp)

    barrier()

    def emit_ln(xview, lnview, chunks):
        stL = ExitStack()
        sqbuf = stL.enter_context(nc.sbuf_tensor(None, [128, 2 * 512], BF16))
        s_m = stL.enter_context(nc.sbuf_tensor(None, [1, XOW], F32))
        s_v = stL.enter_context(nc.sbuf_tensor(None, [1, XOW], F32))
        s_i = stL.enter_context(nc.sbuf_tensor(None, [1, XOW], F32))
        invb_ = stL.enter_context(nc.sbuf_tensor(None, [128, XOW], BF16))
        msbb_ = stL.enter_context(nc.sbuf_tensor(None, [128, XOW], BF16))
        c_lo = chunks[0][0]
        c_hi = chunks[-1][1]
        wdt = c_hi - c_lo
        wait("scalar", vp, vp.n)

        def f(s):
            for kt in range(NKT):
                s.activation(out=lnview(kt, c_lo, c_hi), in_=xview(kt, c_lo, c_hi), func=AT.Copy)
        P.add("scalar", f)
        sp_cast = bump("scalar", sp)
        wait("vector", sp, sp_cast)

        tp_pair = {}
        gix = 0
        for (c0, c1) in chunks:
            wd = c1 - c0
            for kt in range(NKT):
                gix += 1
                def f(v, kt=kt, c0=c0, c1=c1, wd=wd, tpg=tp_pair.get(gix - 2, 0)):
                    if tpg:
                        v.wait_ge(tp.h, tpg)
                    v.tensor_tensor(sqbuf[:, (gix % 2) * 512:(gix % 2) * 512 + wd] if False else
                                    sqbuf[:, (kt % 2) * 512:(kt % 2) * 512 + wd],
                                    lnview(kt, c0, c1), lnview(kt, c0, c1), OP.mult)
                P.add("vector", f)
                vpm = bump("vector", vp)

                def f(t, kt=kt, c0=c0, c1=c1, wd=wd, vpm=vpm):
                    t.wait_ge(vp.h, vpm)
                    t.matmul(pp[6][0:1, 0:wd], ones_c[:, :], lnview(kt, c0, c1),
                             start=(kt == 0), stop=(kt == NKT - 1))
                    t.matmul(pp[7][0:1, 0:wd], ones_c[:, :],
                             sqbuf[:, (kt % 2) * 512:(kt % 2) * 512 + wd],
                             start=(kt == 0), stop=(kt == NKT - 1))
                P.add("tensor", f)
                tp_pair[gix] = bump("tensor", tp)
            tpm = tp_pair[gix]

            def f(s, c0=c0, c1=c1, wd=wd, tpm=tpm):
                s.wait_ge(tp.h, tpm)
                s.copy(s_m[0:1, c0 - c_lo:c1 - c_lo], pp[6][0:1, 0:wd])
                s.copy(s_v[0:1, c0 - c_lo:c1 - c_lo], pp[7][0:1, 0:wd])
            P.add("scalar", f)
            spm = bump("scalar", sp)
            wait("tensor", sp, spm)
        sp_ev = sp.n

        def f(v, sp_ev=sp_ev, wdt=wdt):
            v.wait_ge(sp.h, sp_ev)
            v.tensor_scalar(s_m[0:1, 0:wdt], s_m[0:1, 0:wdt], 1.0 / E, None, OP.mult)
            v.tensor_scalar(s_v[0:1, 0:wdt], s_v[0:1, 0:wdt], 1.0 / E, None, OP.mult)
            v.drain()
            v.tensor_tensor(s_i[0:1, 0:wdt], s_m[0:1, 0:wdt], s_m[0:1, 0:wdt], OP.mult)
            v.drain()
            v.tensor_tensor(s_v[0:1, 0:wdt], s_v[0:1, 0:wdt], s_i[0:1, 0:wdt], OP.subtract)
            v.drain()
            v.tensor_scalar(s_v[0:1, 0:wdt], s_v[0:1, 0:wdt], 1e-5, None, OP.add)
        P.add("vector", f)
        vp_var = bump("vector", vp)

        def f(s, vp_var=vp_var, wdt=wdt):
            s.wait_ge(vp.h, vp_var)
            s.activation(out=s_i[0:1, 0:wdt], in_=s_v[0:1, 0:wdt], func=AT.Ln)
        P.add("scalar", f)
        sp_inv = bump("scalar", sp)

        def f(v, sp_inv=sp_inv, wdt=wdt):
            v.wait_ge(sp.h, sp_inv)
            v.tensor_scalar(s_i[0:1, 0:wdt], s_i[0:1, 0:wdt], -0.5, None, OP.mult)
        P.add("vector", f)
        vp_l2 = bump("vector", vp)

        def f(s, vp_l2=vp_l2, wdt=wdt):
            s.wait_ge(vp.h, vp_l2)
            s.activation(out=s_i[0:1, 0:wdt], in_=s_i[0:1, 0:wdt], func=AT.Exp)
        P.add("scalar", f)
        sp_inv2 = bump("scalar", sp)

        def f(v, sp_inv2=sp_inv2, wdt=wdt):
            v.wait_ge(sp.h, sp_inv2)
            v.tensor_tensor(s_v[0:1, 0:wdt], s_m[0:1, 0:wdt], s_i[0:1, 0:wdt], OP.mult)
        P.add("vector", f)
        vp_ms = bump("vector", vp)

        wait("gpsimd", vp, vp_ms)
        dma("gpsimd", h_stat[0:1, 0:wdt], s_i[0:1, 0:wdt], d_stat)
        w2 = dma("gpsimd", h_stat[1:2, 0:wdt], s_v[0:1, 0:wdt], d_stat)
        wait("gpsimd", d_stat, w2)
        dma("gpsimd", invb_[:, 0:wdt], h_stat[0:1, 0:wdt].broadcast_to([128, wdt]), d_stat)
        wbc = dma("gpsimd", msbb_[:, 0:wdt], h_stat[1:2, 0:wdt].broadcast_to([128, wdt]), d_stat)
        wait("vector", d_stat, wbc)

        def f(v):
            for kt in range(NKT):
                v.tensor_tensor(lnview(kt, c_lo, c_hi), lnview(kt, c_lo, c_hi), invb_[:, 0:wdt], OP.mult)
            v.drain()
            for kt in range(NKT):
                v.tensor_tensor(lnview(kt, c_lo, c_hi), lnview(kt, c_lo, c_hi), msbb_[:, 0:wdt], OP.subtract)
        P.add("vector", f)
        r = bump("vector", vp)
        barrier()
        stL.close()
        return r

    def emit_ffn(lnview, bufs):
        (mixb, rbuf, kabuf, wfbuf, vstage) = bufs
        wait("gpsimd", ld, ld.n)
        for kt in range(NKT):
            dma("gpsimd", wfbuf[0][:, kt * 1024:(kt + 1) * 1024],
                wk_t[kt * 128:(kt + 1) * 128, :], ld)
        wk_done = ld.n

        def mix(mu_sb, mu1_sb):
            def f(v, mu_sb=mu_sb, mu1_sb=mu1_sb):
                for kt in range(NKT):
                    v.tensor_scalar(mixb[:, kt * TH:(kt + 1) * TH], lnview(kt, 1, TH + 1),
                                    mu_sb[:, kt:kt + 1], None, OP.mult)
                v.drain()
                for kt in range(NKT):
                    v.scalar_tensor_tensor(mixb[:, kt * TH:(kt + 1) * TH], lnview(kt, 0, TH),
                                           mu1_sb[:, kt:kt + 1], mixb[:, kt * TH:(kt + 1) * TH],
                                           OP.mult, OP.add)
            P.add("vector", f)
            return bump("vector", vp)

        mix(mu_k_sb, mu_k1_sb)

        def proj(wbuf_idx, store, rhs_buf):
            wait("tensor", vp, vp.n)
            pr_sph = {}
            for mt in range(NKT):
                for tc in range(4):
                    ipr = mt * 4 + tc
                    bank = ipr % 4
                    if ipr >= 4:
                        wait("tensor", sp, pr_sph[ipr - 4])

                    def f(t, mt=mt, tc=tc, bank=bank, wbuf_idx=wbuf_idx, rhs_buf=rhs_buf):
                        for kt in range(NKT):
                            t.matmul(pp[bank][:, :],
                                     wfbuf[wbuf_idx][:, (kt * NKT + mt) * 128:(kt * NKT + mt + 1) * 128],
                                     rhs_buf[:, kt * TH + tc * 512:kt * TH + (tc + 1) * 512],
                                     start=(kt == 0), stop=(kt == NKT - 1))
                    P.add("tensor", f)
                    tpm = bump("tensor", tp)

                    def f(s, mt=mt, tc=tc, bank=bank, tpm=tpm):
                        s.wait_ge(tp.h, tpm)
                        store(mt, tc, pp[bank], s)
                    P.add("scalar", f)
                    pr_sph[ipr] = bump("scalar", sp)

        wait("tensor", ld, wk_done)

        def store_k(mt, tc, ps, s):
            s.activation(out=kabuf[:, mt * TH + tc * 512:mt * TH + (tc + 1) * 512],
                         in_=ps[:, :], func=AT.Relu)
        proj(0, store_k, mixb)
        sp_k = sp.n
        tp_k = tp.n

        wait("gpsimd", ld, ld.n)
        for kt in range(NKT):
            dma("gpsimd", wfbuf[1][:, kt * 1024:(kt + 1) * 1024],
                wr_t[kt * 128:(kt + 1) * 128, :], ld)
        wr_done = ld.n

        wait("vector", sp, sp_k)

        def f(v):
            for kt in range(NKT):
                v.tensor_tensor(kabuf[:, kt * TH:(kt + 1) * TH], kabuf[:, kt * TH:(kt + 1) * TH],
                                kabuf[:, kt * TH:(kt + 1) * TH], OP.mult)
        P.add("vector", f)
        wait("vector", tp, tp_k)
        mix(mu_r_sb, mu_r1_sb)

        wait("gpsimd", tp, tp_k)
        wait("gpsimd", ld, ld.n)
        for kt in range(NKT):
            dma("gpsimd", wfbuf[0][:, kt * 1024:(kt + 1) * 1024],
                wv_t[kt * 128:(kt + 1) * 128, :], ld)
        wv_done = ld.n
        wait("tensor", ld, wr_done)
        wait("tensor", ld, wv_done)
        wait("tensor", vp, vp.n)
        vp_hist = {}
        itf = 0
        for mt in range(NKT):
            for tc in range(4):
                itf += 1
                rb = (2 * itf) % 4
                vb = (2 * itf + 1) % 4

                def f(t, mt=mt, tc=tc, rb=rb, vb=vb, g=vp_hist.get(itf - 2, 0)):
                    if g:
                        t.wait_ge(vp.h, g)
                    for kt in range(NKT):
                        t.matmul(pp[rb][:, :],
                                 wfbuf[1][:, (kt * NKT + mt) * 128:(kt * NKT + mt + 1) * 128],
                                 mixb[:, kt * TH + tc * 512:kt * TH + (tc + 1) * 512],
                                 start=(kt == 0), stop=(kt == NKT - 1))
                    for kt in range(NKT):
                        t.matmul(pp[vb][:, :],
                                 wfbuf[0][:, (kt * NKT + mt) * 128:(kt * NKT + mt + 1) * 128],
                                 kabuf[:, kt * TH + tc * 512:kt * TH + (tc + 1) * 512],
                                 start=(kt == 0), stop=(kt == NKT - 1))
                P.add("tensor", f)
                tpm = bump("tensor", tp)

                def f(s, rb=rb, tpm=tpm, g=vp_hist.get(itf - 1, 0)):
                    s.wait_ge(tp.h, tpm)
                    if g:
                        s.wait_ge(vp.h, g)
                    s.activation(out=rbuf, in_=pp[rb][:, :], func=AT.Sigmoid)
                P.add("scalar", f)
                spm = bump("scalar", sp)

                def f(v, mt=mt, tc=tc, vb=vb, spm=spm):
                    v.wait_ge(sp.h, spm)
                    v.tensor_tensor(vstage, pp[vb][:, :], rbuf, OP.mult)
                    v.drain()
                    v.scalar_tensor_tensor(xov(mt, 1 + tc * 512, 1 + (tc + 1) * 512),
                                           vstage, 0.5,
                                           xov(mt, 1 + tc * 512, 1 + (tc + 1) * 512),
                                           OP.mult, OP.add)
                P.add("vector", f)
                vp_hist[itf] = bump("vector", vp)
        return bump("vector", vp)

    LN1_CH = [(0, 512), (512, 1024), (1024, 1536), (1536, 2048), (2048, 2049)]
    LN2_CH = [(1, 513), (513, 1025), (1025, 1537), (1537, 2049)]

    stA = ExitStack()
    eA = stA.enter_context
    lnb = eA(nc.sbuf_tensor("lnb", [128, NKT * XOW], BF16))

    def lnv(kt, c0, c1):
        return lnb[:, kt * XOW + c0: kt * XOW + c1]

    ffnsm = eA(nc.sbuf_tensor("ffnsm", [128, 1024], BF16))

    stG = ExitStack()
    g_sb = stG.enter_context(nc.sbuf_tensor("g_sb", [16, 7680], BF16))
    wstage = stG.enter_context(nc.sbuf_tensor("wstage", [128, 2048], BF16))
    go = {}
    _off = 0
    for nm, wd in [("g1in", 1024), ("g2in", 2048), ("g1x", 128), ("g2x", 768),
                   ("g1dt", 128), ("g2dt", 512), ("g1out", 512), ("g2out", 2048)]:
        go[nm] = (_off, _off + wd)
        _off += wd

    def gvw(nm):
        a, b_ = go[nm]
        return g_sb[:, a:b_]

    for nm, gsrc in [("g1in", g1in), ("g2in", g2in), ("g1x", g1x), ("g2x", g2x),
                     ("g1dt", g1dt), ("g2dt", g2dt), ("g1out", g1out), ("g2out", g2out)]:
        dma("gpsimd", gvw(nm), gsrc, ld)
    ld_g = ld.n
    wait("tensor", ld, ld_g)

    phase("ln1")
    emit_ln(xov, lnv, LN1_CH)

    def build_wtmp(g1name, g2name, npt, fchunks, rowbase, dst_hbm):
        for pt in range(npt):
            for ci, (c0, c1) in enumerate(fchunks):
                wd = c1 - c0
                bank = (pt * len(fchunks) + ci) % 2

                def f(t, pt=pt, c0=c0, wd=wd, bank=bank, g1name=g1name, g2name=g2name):
                    t.matmul(pp[4 + bank][:, 0:wd], gvw(g1name)[:, pt * 128:(pt + 1) * 128],
                             gvw(g2name)[:, c0:c0 + wd], start=True, stop=True)
                P.add("tensor", f)
                tpm = bump("tensor", tp)

                def f(s, tpm=tpm, wd=wd, bank=bank):
                    s.wait_ge(tp.h, tpm)
                    s.copy(wstage[:, bank * 512:bank * 512 + wd], pp[4 + bank][:, 0:wd])
                P.add("scalar", f)
                spm = bump("scalar", sp)
                wait("tensor", sp, spm)
                wait("gpsimd", sp, spm)
                wdm = dma("gpsimd", dst_hbm[rowbase + pt * 128: rowbase + (pt + 1) * 128, c0:c0 + wd],
                          wstage[:, bank * 512:bank * 512 + wd], d_wb[bank])
                wait("scalar", d_wb[bank], wdm)

    phase("wbuild")
    F4 = [(0, 512), (512, 1024), (1024, 1536), (1536, 2048)]
    build_wtmp("g1in", "g2in", 8, F4, 0, h_wtmp)
    build_wtmp("g1out", "g2out", 4, F4, 1024, h_wtmp)
    build_wtmp("g1x", "g2x", 1, [(0, 512), (512, 768)], 0, h_wtmpx)
    build_wtmp("g1dt", "g2dt", 1, [(0, 512)], 0, h_wtmpdt)
    barrier()
    stG.close()

    stF = ExitStack()
    mixb = stF.enter_context(nc.sbuf_tensor("mixb", [128, NKT * TH], BF16))
    kabuf = stF.enter_context(nc.sbuf_tensor("kabuf", [128, NKT * TH], BF16))
    wfbuf = [stF.enter_context(nc.sbuf_tensor(f"wf{i}", [128, NKT * NKT * 128], BF16)) for i in range(2)]
    rbuf = ffnsm[:, 0:512]
    vstage = ffnsm[:, 512:1024]

    phase("ffn1")
    emit_ffn(lnv, (mixb, rbuf, kabuf, wfbuf, vstage))
    barrier()
    stF.close()

    phase("ln2")
    emit_ln(xov, lnv, LN2_CH)
    wait("sync", vp, vp.n)
    for kt in range(NKT):
        dma("sync", h_ln2own[kt * 128:(kt + 1) * 128, :], lnv(kt, 1, TH + 1), lx)

    barrier()
    stA.close()

    phase("e1")
    def f(gp):
        gp.collective_compute(
            "AllGather", OP.bypass,
            replica_groups=[[0, 1], [2, 3], [4, 5], [6, 7]],
            ins=[h_ln2own], outs=[h_ln2full],
        ).then_inc(ccs.h, 1)
    P.add("gpsimd", f)
    cc1 = ccs.res()
    wait("gpsimd", ccs, cc1)
    barrier()

    phase("front")
    stB = ExitStack()
    eB = stB.enter_context
    wi_sb = eB(nc.sbuf_tensor("wi_sb", [128, NKT * 2048], BF16))
    wx_sb = eB(nc.sbuf_tensor("wx_sb", [128, NKT * 96], BF16))
    slab = [eB(nc.sbuf_tensor(f"slab{i}", [128, NKT * 512], BF16)) for i in range(2)]
    XCW = TF + 3
    xcp = eB(nc.sbuf_tensor("xcp", [128, XCW], BF16))
    zb = eB(nc.sbuf_tensor("zb", [128, TF], BF16))
    cacc = eB(nc.sbuf_tensor("cacc", [128, TF], BF16))
    ctmp = eB(nc.sbuf_tensor("ctmp", [128, TF], BF16))
    convd = eB(nc.sbuf_tensor("convd", [128, TF], BF16))
    ytile = eB(nc.sbuf_tensor("ytile", [128, TF], BF16))
    cf_sb = eB(nc.sbuf_tensor("cf_sb", [128, NDT * 4], F32))
    cb_sb = eB(nc.sbuf_tensor("cb_sb", [128, NDT], F32))
    dtbc_sb = eB(nc.sbuf_tensor("dtbc_sb", [96, TF], BF16))

    for kt in range(NKT):
        for ai in range(4):
            a = 4 * kt + ai
            srcv = h_wtmp[a * 32: a * 32 + 32, :].rearrange("i (b j) -> b i j", j=64)
            dstv = wi_sb[ai * 32:(ai + 1) * 32, kt * 2048:(kt + 1) * 2048].rearrange(
                "b (i j) -> b i j", j=64)
            dma("gpsimd", dstv, srcv, ld)
    for kt in range(NKT):
        for ai in range(2):
            a_loc = 2 * kt + ai
            srcv = h_wtmpx[a_loc * 8: a_loc * 8 + 8, :].rearrange("i (b j) -> b i j", j=12)
            dstv = wx_sb[ai * 64:(ai + 1) * 64, kt * 96:(kt + 1) * 96].rearrange(
                "b (i j) -> b i j", j=12)
            dma("gpsimd", dstv, srcv, ld)
    dma("gpsimd", cf_sb[:, :].rearrange("p (a b) -> p a b", b=4), cfin.rearrange("(a p) b -> p a b", p=128), ld, slow=True)
    ld_wi = dma("gpsimd", cb_sb[:, :].rearrange("p (a b) -> p a b", b=1), cbin.rearrange("(a p) b -> p a b", p=128), ld, slow=True)
    wait("tensor", ld, ld_wi)

    vp_y_prev = 0
    ld_y_prev = 0
    ip_sph = {}
    ip_tph = {}
    for j in range(NDT):
        for tc in range(NCH):
            sl = tc % 2
            gi = j * NCH + tc
            if gi >= 2:
                wait("sync", tp, ip_tph[gi - 2])
            src = h_ln2full.rearrange("(s p) t -> s p t", s=2)[tc // 4, :, (tc % 4) * 512:(tc % 4 + 1) * 512]
            srcv = src.rearrange("(a p) t -> p a t", p=128)
            lxm = dma("sync", slab[sl][:, :].rearrange("p (a t) -> p a t", a=NKT), srcv, d_slab[sl])
            for which in range(2):
                mcol = (0 if which == 0 else 1024) + j * 128
                iw = gi * 2 + which
                bank = iw % 4
                if iw >= 4:
                    wait("tensor", sp, ip_sph[iw - 4])

                def f(t, mcol=mcol, bank=bank, sl=sl, lxm=lxm):
                    t.wait_ge(d_slab[sl].h, lxm)
                    for kt in range(NKT):
                        t.matmul(pp[bank][:, :],
                                 wi_sb[:, kt * 2048 + mcol: kt * 2048 + mcol + 128],
                                 slab[sl][:, kt * 512:(kt + 1) * 512],
                                 start=(kt == 0), stop=(kt == NKT - 1))
                P.add("tensor", f)
                tpm = bump("tensor", tp)

                def f(s, which=which, tc=tc, bank=bank, tpm=tpm, vp_y_prev=vp_y_prev):
                    s.wait_ge(tp.h, tpm)
                    if tc == 0 and which == 0 and vp_y_prev:
                        s.wait_ge(vp.h, vp_y_prev)
                    if which == 0:
                        s.copy(xcp[:, 1 + tc * 512: 1 + (tc + 1) * 512], pp[bank][:, :])
                    else:
                        s.copy(zb[:, tc * 512:(tc + 1) * 512], pp[bank][:, :])
                P.add("scalar", f)
                ip_sph[iw] = bump("scalar", sp)
            ip_tph[gi] = tp.n
        sp_xz = sp.n

        def f(v, j=j, sp_xz=sp_xz, ld_y_prev=ld_y_prev):
            v.wait_ge(sp.h, sp_xz)
            if ld_y_prev:
                v.wait_ge(d_y.h, ld_y_prev)
            v.memset(xcp[:, 0:1], 0.0)
            v.memset(xcp[:, TF + 1:TF + 3], 0.0)
            v.drain()
            v.tensor_scalar(cacc[:, :], xcp[:, 0:TF], cf_sb[:, j * 4:j * 4 + 1], None, OP.mult)
            v.tensor_scalar(ctmp[:, :], xcp[:, 1:1 + TF], cf_sb[:, j * 4 + 1:j * 4 + 2], None, OP.mult)
            v.tensor_scalar(ytile[:, :], xcp[:, 2:2 + TF], cf_sb[:, j * 4 + 2:j * 4 + 3], None, OP.mult)
            v.tensor_scalar(convd[:, :], xcp[:, 3:3 + TF], cf_sb[:, j * 4 + 3:j * 4 + 4], None, OP.mult)
            v.drain()
            v.tensor_tensor(cacc[:, :], cacc[:, :], ctmp[:, :], OP.add)
            v.tensor_tensor(ytile[:, :], ytile[:, :], convd[:, :], OP.add)
            v.drain()
            v.tensor_tensor(cacc[:, :], cacc[:, :], ytile[:, :], OP.add)
        P.add("vector", f)
        vp_conv = bump("vector", vp)

        def f(s, j=j, vp_conv=vp_conv):
            s.wait_ge(vp.h, vp_conv)
            s.activation(out=cacc[:, :], in_=cacc[:, :], func=AT.Silu, bias=cb_sb[:, j:j + 1])
        P.add("scalar", f)
        sp_silu = bump("scalar", sp)

        def f(v, sp_silu=sp_silu):
            v.wait_ge(sp.h, sp_silu)
            v.tensor_tensor(ytile[:, :], cacc[:, :], zb[:, :], OP.mult)
        P.add("vector", f)
        vp_y = bump("vector", vp)
        wait("gpsimd", vp, vp_y)
        ld_y = dma("gpsimd", h_y[j * 128:(j + 1) * 128, :], ytile[:, :], d_y)
        vp_y_prev = vp_y
        ld_y_prev = ld_y

    phase("xproj")
    wait("sync", d_y, d_y.n)
    xp_sph = {}
    xp_tph = {}
    for tc in range(NCH):
        sl = tc % 2
        if tc >= 2:
            wait("sync", tp, xp_tph[tc - 2])
        lxm = dma("sync", slab[sl][:, :].rearrange("p (a t) -> p a t", a=NDT),
                  h_y.rearrange("(a p) t -> p a t", p=128)[:, :, tc * 512:(tc + 1) * 512], d_slab[sl])
        bank = tc % 4

        def f(t, bank=bank, sl=sl, lxm=lxm):
            t.wait_ge(d_slab[sl].h, lxm)
            for kt in range(NDT):
                t.matmul(pp[bank][0:96, :],
                         wx_sb[:, kt * 96:(kt + 1) * 96],
                         slab[sl][:, kt * 512:(kt + 1) * 512],
                         start=(kt == 0), stop=(kt == NDT - 1))
        P.add("tensor", f)
        tpm = bump("tensor", tp)

        def f(s, tc=tc, bank=bank, tpm=tpm):
            s.wait_ge(tp.h, tpm)
            s.copy(dtbc_sb[:, tc * 512:(tc + 1) * 512], pp[bank][0:96, :])
        P.add("scalar", f)
        spm = bump("scalar", sp)
        xp_sph[tc] = spm
        xp_tph[tc] = tp.n
        if tc >= 3:
            wait("tensor", sp, xp_sph[tc - 3])
    wait("gpsimd", sp, sp.n)
    dma("gpsimd", h_dtbc_in[:, :], dtbc_sb[:, :], ld)

    barrier()
    stB.close()

    phase("e2")
    def f(gp):
        gp.collective_compute(
            "AllReduce", OP.add,
            replica_groups=[[0, 1], [2, 3], [4, 5], [6, 7]],
            ins=[h_dtbc_in], outs=[h_dtbc],
        ).then_inc(ccs.h, 1)
    P.add("gpsimd", f)
    cc2 = ccs.res()
    wait("gpsimd", ccs, cc2)
    barrier()

    phase("scanprep")
    stW = ExitStack()
    wo_sb = stW.enter_context(nc.sbuf_tensor("wo_sb", [128, NKT * 1024], BF16))
    for kt in range(NKT):
        for ai in range(2):
            a_loc = 2 * kt + ai
            srcv = h_wtmp[1024 + a_loc * 32: 1024 + a_loc * 32 + 32, :].rearrange(
                "i (b j) -> b i j", j=32)
            dstv = wo_sb[ai * 64:(ai + 1) * 64, kt * 1024:(kt + 1) * 1024].rearrange(
                "b (i j) -> b i j", j=32)
            dma("gpsimd", dstv, srcv, ld)
    ld_wo = ld.n

    stC = ExitStack()
    eC = stC.enter_context
    bbc = eC(nc.sbuf_tensor("bbc", [128, NST * TC], BF16))
    cbc = eC(nc.sbuf_tensor("cbc", [128, NST * TC], BF16))
    dab2 = [eC(nc.sbuf_tensor(f"dab{i}", [128, NST * TC], BF16)) for i in range(2)]
    bsb = eC(nc.sbuf_tensor("bsb", [128, NST * TC], BF16))
    hsb = eC(nc.sbuf_tensor("hsb", [128, NST * TC], BF16))
    dtun2 = [eC(nc.sbuf_tensor(f"dtun{i}", [64, TC], BF16)) for i in range(2)]
    wdt_sb = eC(nc.sbuf_tensor("wdt_sb", [64, DH], BF16))
    dts2 = [eC(nc.sbuf_tensor(f"dts{i}", [128, TC], BF16)) for i in range(2)]
    dus2 = [eC(nc.sbuf_tensor(f"dus{i}", [128, TC], BF16)) for i in range(2)]
    spf = eC(nc.sbuf_tensor("spf", [128, TC], F32))
    usb = [eC(nc.sbuf_tensor(f"usb{i}", [128, TC], BF16)) for i in range(3)]
    yssb = [eC(nc.sbuf_tensor(f"yssb{i}", [128, TC], BF16)) for i in range(2)]
    acol = eC(nc.sbuf_tensor("acol", [128, NDT * NST], F32))
    dcol = eC(nc.sbuf_tensor("dcol", [128, NDT], F32))
    carry = eC(nc.sbuf_tensor("carry", [128, NDT * NST], BF16))
    fold16 = eC(nc.sbuf_tensor("fold16", [128, NST], BF16))
    hsrow = eC(nc.sbuf_tensor("hsrow", [1, 512], F32))
    hspad = eC(nc.sbuf_tensor("hspad", [1, 1032], F32))

    dma("gpsimd", spf[:, 0:NDT * NST].rearrange("p (a n) -> p a n", a=NDT),
        alog.rearrange("(a p) n -> p a n", p=128), d_stat)
    dma("gpsimd", dcol[:, :].rearrange("p (a b) -> p a b", b=1),
        dvin.rearrange("(a p) b -> p a b", p=128), d_stat, slow=True)
    for a in range(8):
        srcv = h_wtmpdt[a * 16:a * 16 + 16, :].rearrange("i (b j) -> b i j", j=64)
        dstv = wdt_sb[a * 8:(a + 1) * 8, :].rearrange("b (i j) -> b i j", j=64)
        dma("gpsimd", dstv, srcv, d_stat)
    ds_all = d_stat.n

    NIT = NCH * NDT
    dt_ld = {}
    bc_ld = {}
    cc_ld = {}

    def bsrc_of(c):
        return h_dtbc[64:80, c * TC:(c + 1) * TC].unsqueeze(0).broadcast_to([128, NST, TC])

    def csrc_of(c):
        return h_dtbc[80:96, c * TC:(c + 1) * TC].unsqueeze(0).broadcast_to([128, NST, TC])

    dt_ld[0] = dma("gpsimd", dtun2[0][:, :], h_dtbc[0:64, 0:TC], d_dt)
    dt_ld[1] = dma("gpsimd", dtun2[1][:, :], h_dtbc[0:64, TC:2 * TC], d_dt)
    bc_ld[0] = dma("gpsimd", bbc[:, :].rearrange("p (n t) -> p n t", n=NST), bsrc_of(0), d_bc)
    cc_ld[0] = dma("gpsimd", cbc[:, :].rearrange("p (n t) -> p n t", n=NST), csrc_of(0), d_bc2)

    def f(s, g=ds_all):
        s.wait_ge(d_stat.h, g)
        s.activation(out=spf[:, 0:NDT * NST], in_=spf[:, 0:NDT * NST], func=AT.Exp)
    P.add("scalar", f)
    sp_aexp = bump("scalar", sp)

    def f(v, g=ds_all, gs=sp_aexp):
        v.wait_ge(d_stat.h, g)
        v.wait_ge(sp.h, gs)
        v.tensor_scalar(acol[:, :], spf[:, 0:NDT * NST], -1.0, None, OP.mult)
        v.memset(carry[:, :], 0.0)
        v.memset(hspad[0:1, :], 0.0)
    P.add("vector", f)
    vp_prep = bump("vector", vp)

    wait("tensor", ld, ld.n)
    wait("tensor", vp, vp.n)
    for ec in range(2):
        def f(t, ec=ec):
            for kt in range(NKT):
                t.matmul(pp[6][0:1, :], ones_c[:, :],
                         wo_sb[:, kt * 1024 + ec * 512: kt * 1024 + (ec + 1) * 512],
                         start=(kt == 0), stop=(kt == NKT - 1))
        P.add("tensor", f)
        tpm = bump("tensor", tp)

        def f(s, ec=ec, tpm=tpm):
            s.wait_ge(tp.h, tpm)
            s.copy(hspad[0:1, 8 + ec * 512: 8 + (ec + 1) * 512], pp[6][0:1, :])
        P.add("scalar", f)
        spm = bump("scalar", sp)
        wait("tensor", sp, spm)

    phase("scan")
    sp_sp = {}
    sp_da = {}
    tp_hs = {}
    vp_du = {}
    vp_c = {}
    vp_scanh = {}
    vp_ysh = {}
    tp_dt = {}
    ysw_h = {}
    u_ld = {}

    def jj(k):
        return k % NDT

    def cof(k):
        return k // NDT

    def f(t, g=ds_all, gd=dt_ld[1]):
        t.wait_ge(d_stat.h, g)
        t.wait_ge(d_dt.h, gd)
        t.matmul(pp[0][:, :], wdt_sb[:, 0:128], dtun2[0][:, :], start=True, stop=True)
    P.add("tensor", f)
    tp_dt[0] = bump("tensor", tp)

    def f(s, g=tp_dt[0]):
        s.wait_ge(tp.h, g)
        s.activation(out=spf[:, :], in_=pp[0][:, :], func=AT.Exp)
        s.drain()
        s.activation(out=dts2[0][:, :], in_=spf[:, :], func=AT.Ln, bias=1.0)
    P.add("scalar", f)
    sp_sp[0] = bump("scalar", sp)

    def f(t):
        t.matmul(pp[1][:, :], wdt_sb[:, 128:256], dtun2[0][:, :], start=True, stop=True)
    P.add("tensor", f)
    tp_dt[1] = bump("tensor", tp)

    u_ld[0] = dma("sync", usb[0][:, :], h_y[0:128, 0:TC], d_u[0])
    u_ld[1] = dma("sync", usb[1][:, :], h_y[128:256, 0:TC], d_u[1])

    for i in range(NIT + 1):
        p, q = i % 2, (i - 1) % 2
        k = i - 1
        if i < NIT:
            def f(v, i=i, p=p, gs=sp_sp[i], gu=u_ld[i]):
                v.wait_ge(sp.h, gs)
                v.wait_ge(d_u[i % 3].h, gu)
                v.tensor_tensor(dus2[p][:, :], dts2[p][:, :], usb[i % 3][:, :], OP.mult)
            P.add("vector", f)
            vp_du[i] = bump("vector", vp)
        if i >= 1:
            def f(v, k=k, kp=q, gsp=sp_da[k], gtp=tp_hs.get(k - 1, 0),
                  gbc=bc_ld[cof(k)], j=jj(k)):
                v.wait_ge(sp.h, gsp)
                if gtp:
                    v.wait_ge(tp.h, gtp)
                v.wait_ge(d_bc.h, gbc)
                da0 = dab2[kp][:, :].rearrange("p (n t) -> p n t", n=NST)[:, :, 0]
                bb0 = bsb[:, :].rearrange("p (n t) -> p n t", n=NST)[:, :, 0]
                v.tensor_tensor(fold16[:, :], da0, carry[:, j * NST:(j + 1) * NST], OP.mult)
                v.tensor_tensor(bsb[:, :].rearrange("p (n t) -> p n t", n=NST),
                                dus2[kp][:, :].unsqueeze(1).broadcast_to([128, NST, TC]),
                                bbc[:, :].rearrange("p (n t) -> p n t", n=NST), OP.mult)
                v.drain()
                v.memset(da0, 0.0)
                v.drain()
                v.tensor_tensor(bb0, bb0, fold16[:, :], OP.add)
                v.drain()
                v.tensor_tensor_scan(hsb[:, :], dab2[kp][:, :], bsb[:, :], 0.0, OP.mult, OP.add)
            P.add("vector", f)
            vp_scanh[k] = bump("vector", vp)

            def f(v, k=k, kp=q, gcc=cc_ld[cof(k)], j=jj(k)):
                v.wait_ge(d_bc2.h, gcc)
                hl = hsb[:, :].rearrange("p (n t) -> p n t", n=NST)[:, :, TC - 1]
                v.tensor_tensor(bsb[:, :], hsb[:, :], cbc[:, :], OP.mult)
                v.tensor_copy(carry[:, j * NST:(j + 1) * NST], hl)
            P.add("vector", f)
            vp_c[k] = bump("vector", vp)

            def f(v, k=k, kp=q, gys=ysw_h.get(k - 2, 0), j=jj(k)):
                if gys:
                    v.wait_ge(d_ysw[kp].h, gys)
                v.tensor_tensor(bsb[:, 0:8 * TC], bsb[:, 0:8 * TC], bsb[:, 8 * TC:16 * TC], OP.add)
                v.drain()
                v.tensor_tensor(bsb[:, 0:4 * TC], bsb[:, 0:4 * TC], bsb[:, 4 * TC:8 * TC], OP.add)
                v.drain()
                v.tensor_tensor(bsb[:, 0:2 * TC], bsb[:, 0:2 * TC], bsb[:, 2 * TC:4 * TC], OP.add)
                v.drain()
                v.tensor_tensor(yssb[kp][:, :], bsb[:, 0:TC], bsb[:, TC:2 * TC], OP.add)
                v.drain()
                v.scalar_tensor_tensor(yssb[kp][:, :], usb[k % 3][:, :], dcol[:, j:j + 1],
                                       yssb[kp][:, :], OP.mult, OP.add)
            P.add("vector", f)
            vp_ysh[k] = bump("vector", vp)

        if i < NIT:
            if i + 1 < NIT:
                def f(s, b=q, gt=tp_dt[i + 1], gd=vp_du.get(i - 1, 0)):
                    s.wait_ge(tp.h, gt)
                    if gd:
                        s.wait_ge(vp.h, gd)
                    s.activation(out=spf[:, :], in_=pp[b][:, :], func=AT.Exp)
                    s.drain()
                    s.activation(out=dts2[b][:, :], in_=spf[:, :], func=AT.Ln, bias=1.0)
                P.add("scalar", f)
                sp_sp[i + 1] = bump("scalar", sp)

            gv = vp_prep if i == 0 else vp_scanh.get(i - 2, 0)

            def f(s, i=i, p=p, gv=gv, j=jj(i)):
                if gv:
                    s.wait_ge(vp.h, gv)
                s.drain()
                for n in range(NST):
                    s.activation(out=dab2[p][:, n * TC:(n + 1) * TC], in_=dts2[p][:, :],
                                 func=AT.Exp, scale=acol[:, j * NST + n: j * NST + n + 1])
            P.add("scalar", f)
            sp_da[i] = bump("scalar", sp)

        if i >= 1:
            def f(t, k=k, gv=vp_scanh[k]):
                t.wait_ge(vp.h, gv)
                for n in range(NST):
                    t.matmul(pp[5][0:1, :], ones_c[:, :], hsb[:, n * TC:(n + 1) * TC],
                             start=(k == 0 and n == 0), stop=(k == NIT - 1 and n == NST - 1))
            P.add("tensor", f)
            tp_hs[k] = bump("tensor", tp)

        if i + 2 < NIT:
            wait("tensor", sp, sp_sp[i])
            c2 = cof(i + 2)

            def f(t, i=i, c2=c2, gd=dt_ld[max(c2, 1)], j=jj(i + 2)):
                t.wait_ge(d_dt.h, gd)
                t.matmul(pp[i % 2][:, :], wdt_sb[:, j * 128:(j + 1) * 128],
                         dtun2[c2 % 2][:, :], start=True, stop=True)
            P.add("tensor", f)
            tp_dt[i + 2] = bump("tensor", tp)

        if 2 <= i + 2 < NIT:
            wait("sync", vp, vp_ysh.get(i - 1, 0))
            u_ld[i + 2] = dma("sync", usb[(i + 2) % 3][:, :],
                              h_y[jj(i + 2) * 128:(jj(i + 2) + 1) * 128,
                                  cof(i + 2) * TC:(cof(i + 2) + 1) * TC], d_u[(i + 2) % 3])

        if i >= 1:
            if i % NDT == 0 and i // NDT >= 1 and i < NIT:
                c = i // NDT
                wait("gpsimd", vp, vp_scanh[i - 1])
                bc_ld[c] = dma("gpsimd", bbc[:, :].rearrange("p (n t) -> p n t", n=NST),
                               bsrc_of(c), d_bc)
                wait("gpsimd", vp, vp_ysh[i - 1])
                ysw_h[k] = dma("gpsimd", h_ys[jj(k) * 128:(jj(k) + 1) * 128,
                                              cof(k) * TC:(cof(k) + 1) * TC],
                               yssb[q][:, :], d_ysw[q])
                cc_ld[c] = dma("gpsimd", cbc[:, :].rearrange("p (n t) -> p n t", n=NST),
                               csrc_of(c), d_bc2)
                if c + 1 < NCH:
                    wait("gpsimd", tp, tp_dt[c * NDT - 1])
                    dt_ld[c + 1] = dma("gpsimd", dtun2[(c + 1) % 2][:, :],
                                       h_dtbc[0:64, (c + 1) * TC:(c + 2) * TC], d_dt)
            else:
                wait("gpsimd", vp, vp_ysh[i - 1])
                ysw_h[k] = dma("gpsimd", h_ys[jj(k) * 128:(jj(k) + 1) * 128,
                                              cof(k) * TC:(cof(k) + 1) * TC],
                               yssb[q][:, :], d_ysw[q])


    phase("hsfin")
    def f(s, g=tp_hs[NIT - 1]):
        s.wait_ge(tp.h, g)
        s.copy(hsrow[0:1, :], pp[5][0:1, :])
    P.add("scalar", f)
    sp_hsr = bump("scalar", sp)

    def f(v, sp_hsr=sp_hsr):
        v.wait_ge(sp.h, sp_hsr)
        v.tensor_reduce(hspad[0:1, 0:1], hsrow[0:1, :], AX.X, OP.add)
    P.add("vector", f)
    vp_hsr = bump("vector", vp)
    wait("gpsimd", vp, vp_hsr)
    dma("gpsimd", h_hs_in[:, :], hspad[0:1, :], ld)

    barrier()
    stC.close()

    phase("e4")
    def f(gp):
        gp.collective_compute(
            "AllReduce", OP.add,
            replica_groups=[[0, 1, 2, 3, 4, 5, 6, 7]],
            ins=[h_hs_in], outs=[h_hs],
        ).then_inc(ccs.h, 1)
    P.add("gpsimd", f)
    cc4 = ccs.res()

    phase("outproj")
    stD = ExitStack()
    eD = stD.enter_context
    yslab = [eD(nc.sbuf_tensor(f"yslab{i}", [128, NKT * 128], BF16)) for i in range(2)]
    opst = [eD(nc.sbuf_tensor(f"opst{i}", [128, 512], BF16)) for i in range(2)]

    op_w_hist = {}
    op_sph = {}
    op_dwh = {}
    for tt in range(TF // 128):
        sl = tt % 2
        if tt >= 2:
            wait("sync", tp, op_w_hist[tt - 2])
        lxm = dma("sync", yslab[sl][:, :].rearrange("p (a t) -> p a t", a=NKT),
                  h_ys.rearrange("(a p) t -> p a t", p=128)[:, :, tt * 128:(tt + 1) * 128], d_slab[sl])
        for ec in range(2):
            bank = (tt * 2 + ec) % 4

            def f(t, tt=tt, ec=ec, bank=bank, sl=sl, lxm=lxm):
                t.wait_ge(d_slab[sl].h, lxm)
                for kt in range(NKT):
                    t.matmul(pp[bank][:, :],
                             yslab[sl][:, kt * 128:(kt + 1) * 128],
                             wo_sb[:, kt * 1024 + ec * 512:(kt) * 1024 + (ec + 1) * 512],
                             start=(kt == 0), stop=(kt == NKT - 1))
            P.add("tensor", f)
            tpm = bump("tensor", tp)
            if ec == 1:
                op_w_hist[tt] = tpm

            def f(s, ec=ec, bank=bank, tpm=tpm, st_i=(tt * 2 + ec) % 2,
                  g=op_dwh.get(tt * 2 + ec - 2, 0), gs=(tt * 2 + ec) % 2):
                s.wait_ge(tp.h, tpm)
                if g:
                    s.wait_ge(d_op[gs].h, g)
                s.copy(opst[st_i][:, :], pp[bank][:, :])
            P.add("scalar", f)
            spm = bump("scalar", sp)
            op_sph[tt * 2 + ec] = spm
            if tt * 2 + ec >= 3:
                wait("tensor", sp, op_sph[tt * 2 + ec - 3])
            wait("gpsimd", sp, spm)
            stb = (tt * 2 + ec) % 2
            opw = dma("gpsimd", h_op[tt * 128:(tt + 1) * 128, ec * 512:(ec + 1) * 512],
                      opst[stb][:, :], d_op[stb])
            op_dwh[tt * 2 + ec] = opw

    barrier()

    phase("rs")
    def f(gp):
        gp.collective_compute(
            "ReduceScatter", OP.add,
            replica_groups=[[0, 1], [2, 3], [4, 5], [6, 7]],
            ins=[h_op], outs=[h_od],
        ).then_inc(ccs.h, 1)
    P.add("gpsimd", f)
    cc_rs = ccs.res()
    wait("gpsimd", ccs, cc_rs)
    wait("gpsimd", ccs, cc4)
    barrier()
    stD.close()
    stW.close()

    phase("grpE_prep")
    stE = ExitStack()
    eE = stE.enter_context
    lnb3 = eE(nc.sbuf_tensor("lnb3", [128, NKT * XOW], BF16))

    def lnv3(kt, c0, c1):
        return lnb3[:, kt * XOW + c0: kt * XOW + c1]

    ffnsm3 = eE(nc.sbuf_tensor("ffnsm3", [128, 1024], BF16))
    halc = eE(nc.sbuf_tensor("halc", [128, NKT], F32))
    stT = ExitStack()
    hrow = stT.enter_context(nc.sbuf_tensor("hrow", [1, 1032], F32))
    mrow = stT.enter_context(nc.sbuf_tensor("mrow", [1, E], F32))
    mbc = stT.enter_context(nc.sbuf_tensor("mbc", [128, NKT], F32))
    dodb = [stT.enter_context(nc.sbuf_tensor(f"dodb{i}", [128, TH], BF16)) for i in range(2)]

    ld_hrow = dma("gpsimd", hrow[0:1, :], h_hs[0:1, :], ld)

    def f(v, ld_hrow=ld_hrow):
        v.wait_ge(ld.h, ld_hrow)
        v.tensor_scalar(hrow[0:1, 0:1], hrow[0:1, 0:1], 1e-8 / (4.0 * HS_COUNT), None, OP.mult)
        v.drain()
        v.tensor_scalar(mrow[0:1, :], hrow[0:1, 8:8 + E], hrow[0:1, 0:1], None, OP.mult)
    P.add("vector", f)
    vp_mb = bump("vector", vp)
    wait("gpsimd", vp, vp_mb)
    wmb = dma("gpsimd", h_mb[:, :], mrow[0:1, :], ld)
    wait("gpsimd", ld, wmb)
    for kt in range(NKT):
        dma("gpsimd", mbc[:, kt:kt + 1],
            h_mb[0:1, kt * 128:(kt + 1) * 128].rearrange("a b -> b a"), ld, slow=True)
    ld_mbc = ld.n

    dod_hist = {}
    for et in range(NKT):
        sl = et % 2
        if et >= 2:
            wait("sync", vp, dod_hist[et - 2])
        v_tr = d_dod[sl].res()

        def f(eng, sl=sl, et=et):
            eng.dma_start_transpose(dodb[sl][:, :],
                                    h_od[:, et * 128:(et + 1) * 128]).then_inc(d_dod[sl].h, 16)
        P.add("sync", f)

        def f(v, sl=sl, et=et, v_tr=v_tr, ld_mbc=ld_mbc):
            v.wait_ge(d_dod[sl].h, v_tr)
            if et == 0:
                v.wait_ge(ld.h, ld_mbc)
            v.scalar_tensor_tensor(xov(et, 1, TH + 1), dodb[sl][:, :],
                                   mbc[:, et:et + 1], xov(et, 1, TH + 1),
                                   OP.add, OP.add)
        P.add("vector", f)
        dod_hist[et] = bump("vector", vp)
    barrier()
    stT.close()

    phase("ln3")
    emit_ln(xov, lnv3, LN2_CH)

    phase("e5")
    def f(v):
        for kt in range(NKT):
            v.tensor_scalar(halc[:, kt:kt + 1], lnv3(kt, TH, TH + 1),
                            mskp_sb[:, kt:kt + 1], None, OP.mult)
    P.add("vector", f)
    vp_pl = bump("vector", vp)
    wait("gpsimd", vp, vp_pl)
    wait("gpsimd", ld, ld.n)
    wpl = dma("gpsimd", h_halo_in.rearrange("(a p) b -> p a b", p=128), halc[:, :].rearrange("p (a b) -> p a b", b=1), ld, slow=True)
    wait("gpsimd", ld, wpl)

    def f(gp):
        gp.collective_compute(
            "AllReduce", OP.add,
            replica_groups=[[0, 1], [2, 3], [4, 5], [6, 7]],
            ins=[h_halo_in], outs=[h_halo],
        ).then_inc(ccs.h, 1)
    P.add("gpsimd", f)
    cc5 = ccs.res()
    wait("gpsimd", ccs, cc5)
    ld_hal = dma("gpsimd", halc[:, :].rearrange("p (a b) -> p a b", b=1), h_halo.rearrange("(a p) b -> p a b", p=128), ld, slow=True)
    wait("vector", ld, ld_hal)

    def f(v):
        for kt in range(NKT):
            v.tensor_scalar(lnv3(kt, 0, 1), halc[:, kt:kt + 1],
                            msku_sb[:, kt:kt + 1], None, OP.mult)
    P.add("vector", f)
    bump("vector", vp)

    phase("ffn3")
    stF3 = ExitStack()
    mixb3 = stF3.enter_context(nc.sbuf_tensor("mixb3", [128, NKT * TH], BF16))
    kabuf3 = stF3.enter_context(nc.sbuf_tensor("kabuf3", [128, NKT * TH], BF16))
    wfbuf3 = [stF3.enter_context(nc.sbuf_tensor(f"wf3{i}", [128, NKT * NKT * 128], BF16)) for i in range(2)]
    rbuf3 = ffnsm3[:, 0:512]
    vstage3 = ffnsm3[:, 512:1024]
    emit_ffn(lnv3, (mixb3, rbuf3, kabuf3, wfbuf3, vstage3))
    stF3.close()

    phase("outstore")
    wait("sync", vp, vp.n)
    for kt in range(NKT):
        dma("sync", out_t[kt * 128:(kt + 1) * 128, :], xov(kt, 1, TH + 1), lx)
    lx_fin = lx.n
    wait("sync", lx, lx_fin)
    barrier()
    stE.close()

    phase_end()
    block = e(nc.Block())
    P.emit(block)
    st.close()
    return nc


def _host_maps(inputs):
    x = np.asarray(inputs["L0_attn_x"])
    in_g1 = np.asarray(inputs["L0_attn_in_g1"], np.float32)
    in_g2 = np.asarray(inputs["L0_attn_in_g2"], np.float32)
    x_g1 = np.asarray(inputs["L0_attn_x_g1"], np.float32)
    x_g2 = np.asarray(inputs["L0_attn_x_g2"], np.float32)
    dt_g1 = np.asarray(inputs["L0_attn_dt_g1"], np.float32)
    dt_g2 = np.asarray(inputs["L0_attn_dt_g2"], np.float32)
    out_g1 = np.asarray(inputs["L0_attn_out_g1"], np.float32)
    out_g2 = np.asarray(inputs["L0_attn_out_g2"], np.float32)
    cf = np.asarray(inputs["L0_attn_conv_filter"], np.float32)
    cb = np.asarray(inputs["L0_attn_conv_bias"], np.float32)
    alog_a = np.asarray(inputs["L0_attn_A_log"], np.float32)
    dv = np.asarray(inputs["L0_attn_D"], np.float32)
    wr = np.asarray(inputs["L0_ffn_W_r"], np.float32)
    wk = np.asarray(inputs["L0_ffn_W_k"], np.float32)
    wv = np.asarray(inputs["L0_ffn_W_v"], np.float32)
    mur = np.asarray(inputs["L0_ffn_mu_r"], np.float32)
    muk = np.asarray(inputs["L0_ffn_mu_k"], np.float32)

    C = np.ascontiguousarray
    wr_t = C(wr.T)
    wk_t = C(wk.T)
    wv_t = C(wv.T)
    g2in_m = C(in_g2.reshape(16, 2048))
    g2x_m = C(x_g2.reshape(16, 768))
    g2dt_m = C(dt_g2.reshape(16, 512))
    g2out_m = C(out_g2.reshape(16, 2048))

    maps = []
    for c in range(8):
        b, h = c // 2, c % 2
        xs = x[b, h * TH:(h + 1) * TH, :].T.astype(np.float32)
        halo = (x[b, h * TH - 1:h * TH, :].T.astype(np.float32) if h == 1
                else np.zeros((E, 1), np.float32))
        xin_m = C(np.concatenate([halo, xs], axis=1))

        iperm = list(range(h * 16, (h + 1) * 16)) + list(range(32 + h * 16, 32 + (h + 1) * 16))
        g1in_m = C(in_g1[:, iperm, :].reshape(32 * 32, 16).T)
        aperm = list(range(h * 16, (h + 1) * 16))
        g1x_m = C(x_g1[aperm].reshape(16 * 8, 16).T)
        ipermd = list(range(h * 16, (h + 1) * 16))
        g1dt_m = C(dt_g1[:, ipermd, :].reshape(8 * 16, 16).T)
        g1out_m = C(out_g1[aperm].reshape(16 * 32, 16).T)

        dh = slice(h * DH, (h + 1) * DH)
        maps.append(dict(
            xin=xin_m,
            wr_t=wr_t, wk_t=wk_t, wv_t=wv_t,
            mu_r=C(mur.reshape(E, 1)), mu_k=C(muk.reshape(E, 1)),
            g1in=g1in_m, g2in=g2in_m,
            g1x=g1x_m, g2x=g2x_m,
            g1dt=g1dt_m, g2dt=g2dt_m,
            g1out=g1out_m, g2out=g2out_m,
            cfin=C(cf[:, 0, dh].T),
            cbin=C(cb[dh].reshape(DH, 1)),
            alog=C(alog_a[dh]),
            dvin=C(dv[dh].reshape(DH, 1)),
            maskp=np.full((E, 1), 1.0 - h, np.float32),
            masku=np.full((E, 1), float(h), np.float32),
        ))
    return maps


def kernel(**inputs):
    maps = _host_maps(inputs)
    nc = build_nc()
    res = run_bass_kernel_spmd(nc, maps, core_ids=list(range(8)))
    out = np.empty((4, TF, E), np.float32)
    for c in range(8):
        b, h = c // 2, c % 2
        out[b, h * TH:(h + 1) * TH, :] = res.results[c]["out"].T
    return out


# version 12
# speedup vs baseline: 1.3939x; 1.0642x over previous
import numpy as np
from contextlib import ExitStack

import concourse.bass as bass
import concourse.mybir as mybir
from concourse.bass_utils import run_bass_kernel_spmd
from concourse.aot_env import allow_ici_all_to_all

F32 = mybir.dt.float32
BF16 = mybir.dt.bfloat16
AT = mybir.ActivationFunctionType
OP = mybir.AluOpType
AX = mybir.AxisListType

E = 1024
DIN = 2048
DH = 1024
NST = 16
TF = 4096
TH = 2048
TC = 512
NCH = TF // TC
NDT = DH // 128
NKT = E // 128
XOW = TH + 1
HS_COUNT = 4.0 * TF * DIN * NST

ENGS = ["sync", "gpsimd", "vector", "scalar", "tensor"]


class Cnt:
    def __init__(self, h, step=1):
        self.h = h
        self.n = 0
        self.step = step

    def res(self, k=None):
        self.n += self.step if k is None else k
        return self.n


class Prog:
    def __init__(self):
        self.q = {en: [] for en in ENGS}

    def add(self, en, f):
        self.q[en].append(f)

    def emit(self, block):
        for en in ENGS:
            fs = self.q[en]

            def runner(eng, fs=fs):
                for f in fs:
                    f(eng)
            getattr(block, en)(runner)


def build_nc():
    nc = bass.Bass("TRN2", target_bir_lowering=False)

    xin = nc.dram_tensor("xin", [E, XOW], F32, kind="ExternalInput").ap()
    wr_t = nc.dram_tensor("wr_t", [E, E], F32, kind="ExternalInput").ap()
    wk_t = nc.dram_tensor("wk_t", [E, E], F32, kind="ExternalInput").ap()
    wv_t = nc.dram_tensor("wv_t", [E, E], F32, kind="ExternalInput").ap()
    mu_r = nc.dram_tensor("mu_r", [E, 1], F32, kind="ExternalInput").ap()
    mu_k = nc.dram_tensor("mu_k", [E, 1], F32, kind="ExternalInput").ap()
    g1in = nc.dram_tensor("g1in", [16, 1024], F32, kind="ExternalInput").ap()
    g2in = nc.dram_tensor("g2in", [16, 2048], F32, kind="ExternalInput").ap()
    g1x = nc.dram_tensor("g1x", [16, 128], F32, kind="ExternalInput").ap()
    g2x = nc.dram_tensor("g2x", [16, 768], F32, kind="ExternalInput").ap()
    g1dt = nc.dram_tensor("g1dt", [16, 128], F32, kind="ExternalInput").ap()
    g2dt = nc.dram_tensor("g2dt", [16, 512], F32, kind="ExternalInput").ap()
    g1out = nc.dram_tensor("g1out", [16, 512], F32, kind="ExternalInput").ap()
    g2out = nc.dram_tensor("g2out", [16, 2048], F32, kind="ExternalInput").ap()
    cfin = nc.dram_tensor("cfin", [DH, 4], F32, kind="ExternalInput").ap()
    cbin = nc.dram_tensor("cbin", [DH, 1], F32, kind="ExternalInput").ap()
    alog = nc.dram_tensor("alog", [DH, NST], F32, kind="ExternalInput").ap()
    dvin = nc.dram_tensor("dvin", [DH, 1], F32, kind="ExternalInput").ap()
    maskp = nc.dram_tensor("maskp", [E, 1], F32, kind="ExternalInput").ap()
    masku = nc.dram_tensor("masku", [E, 1], F32, kind="ExternalInput").ap()
    out_t = nc.dram_tensor("out", [E, TH], F32, kind="ExternalOutput").ap()

    h_wtmp = nc.dram_tensor("h_wtmp", [2048, 2048], BF16).ap()
    h_wtmpx = nc.dram_tensor("h_wtmpx", [128, 768], BF16).ap()
    h_wtmpdt = nc.dram_tensor("h_wtmpdt", [128, 512], BF16).ap()
    h_ln2own = nc.dram_tensor("h_ln2own", [E, TH], BF16).ap()
    h_ln2full = nc.dram_tensor("h_ln2full", [2 * E, TH], BF16).ap()
    h_y = nc.dram_tensor("h_y", [DH, TF], BF16).ap()
    h_dtbc_in = nc.dram_tensor("h_dtbc_in", [96, TF], BF16).ap()
    h_dtbc = nc.dram_tensor("h_dtbc", [96, TF], BF16).ap()
    h_ys = nc.dram_tensor("h_ys", [DH, TF], BF16).ap()
    h_op = nc.dram_tensor("h_op", [TF, E], BF16).ap()
    h_od = nc.dram_tensor("h_od", [TH, E], BF16).ap()
    h_hs_in = nc.dram_tensor("h_hs_in", [1, 1032], F32).ap()
    h_hs = nc.dram_tensor("h_hs", [1, 1032], F32).ap()
    h_halo_in = nc.dram_tensor("h_halo_in", [E, 1], F32).ap()
    h_halo = nc.dram_tensor("h_halo", [E, 1], F32).ap()
    h_stat = nc.dram_tensor("h_stat", [2, XOW], BF16).ap()
    h_mb = nc.dram_tensor("h_mb", [1, E], F32).ap()
    h_xsp = nc.dram_tensor("h_xsp", [E, TH], F32).ap()

    st = ExitStack()
    e = st.enter_context

    BARs = e(nc.semaphore("BAR"))
    bar_n = [0]
    ld = Cnt(e(nc.semaphore("ld")), 16)
    lx = Cnt(e(nc.semaphore("lx")), 16)
    tp = Cnt(e(nc.semaphore("tp")), 1)
    sp = Cnt(e(nc.semaphore("sp")), 1)
    vp = Cnt(e(nc.semaphore("vp")), 1)
    ccs = Cnt(e(nc.semaphore("ccs")), 1)
    d_slab = [Cnt(e(nc.semaphore(f"dslab{i}")), 16) for i in range(2)]
    d_u = [Cnt(e(nc.semaphore(f"du{i}")), 16) for i in range(3)]
    d_ysw = [Cnt(e(nc.semaphore(f"dysw{i}")), 16) for i in range(2)]
    d_op = [Cnt(e(nc.semaphore(f"dop{i}")), 16) for i in range(2)]
    d_dod = [Cnt(e(nc.semaphore(f"ddod{i}")), 16) for i in range(2)]
    d_wb = [Cnt(e(nc.semaphore(f"dwb{i}")), 16) for i in range(2)]
    d_bc = Cnt(e(nc.semaphore("dbc")), 16)
    d_bc2 = Cnt(e(nc.semaphore("dbc2")), 16)
    d_dt = Cnt(e(nc.semaphore("ddt")), 16)
    d_y = Cnt(e(nc.semaphore("dy")), 16)
    d_stat = Cnt(e(nc.semaphore("dstat")), 16)

    SYNC_SEMS = [lx] + d_slab + d_u + d_dod
    GP_SEMS = [ld, d_bc, d_bc2, d_dt, d_y, d_stat] + d_ysw + d_op + d_wb

    P = Prog()

    def barrier():
        bar_n[0] += 1
        tgt = 5 * bar_n[0]
        waits = {"sync": [(c, c.n) for c in SYNC_SEMS],
                 "gpsimd": [(c, c.n) for c in GP_SEMS],
                 "vector": [(vp, vp.n)],
                 "scalar": [(sp, sp.n)],
                 "tensor": [(tp, tp.n)]}
        for en in ENGS:
            ws = waits[en]

            def f(eng, ws=ws, tgt=tgt):
                for c, v in ws:
                    if v > 0:
                        eng.wait_ge(c.h, v)
                eng.sem_inc(BARs, 1)
                eng.wait_ge(BARs, tgt)
            P.add(en, f)

    def dma(en, dst, src, cnt, slow=False):
        v = cnt.res()

        def f(eng, dst=dst, src=src, slow=slow):
            if slow:
                with nc.allow_non_contiguous_dma(reason="tiny gather"):
                    eng.dma_start(out=dst, in_=src).then_inc(cnt.h, cnt.step)
            else:
                eng.dma_start(out=dst, in_=src).then_inc(cnt.h, cnt.step)
        P.add(en, f)
        return v

    def wait(en, cnt, v):
        if v <= 0:
            return

        def f(eng, cnt=cnt, v=v):
            eng.wait_ge(cnt.h, v)
        P.add(en, f)

    cur_scope = {en: None for en in ENGS}

    def phase(name):
        for en in ENGS:
            def f(eng, en=en, name=name):
                if cur_scope[en] is not None:
                    pn, pid = cur_scope[en]
                    nc.leave_named_scope(pn, pid, False)
                sid, _ = nc.enter_named_scope(name, False)
                cur_scope[en] = (name, sid)
            P.add(en, f)

    def phase_end():
        for en in ENGS:
            def f(eng, en=en):
                if cur_scope[en] is not None:
                    pn, pid = cur_scope[en]
                    nc.leave_named_scope(pn, pid, False)
                    cur_scope[en] = None
            P.add(en, f)

    def bump(en, cnt):
        def f(eng, cnt=cnt):
            eng.maybe_drain_then_inc((cnt.h, cnt.step))
        P.add(en, f)
        return cnt.res()

    mu_r_sb = e(nc.sbuf_tensor("mu_r_sb", [128, NKT], F32))
    mu_k_sb = e(nc.sbuf_tensor("mu_k_sb", [128, NKT], F32))
    mu_r1_sb = e(nc.sbuf_tensor("mu_r1_sb", [128, NKT], F32))
    mu_k1_sb = e(nc.sbuf_tensor("mu_k1_sb", [128, NKT], F32))
    msku_sb = e(nc.sbuf_tensor("msku_sb", [128, NKT], F32))
    mskp_sb = e(nc.sbuf_tensor("mskp_sb", [128, NKT], F32))
    ones_c = e(nc.sbuf_tensor("ones_c", [128, 1], BF16))
    ones_f = e(nc.sbuf_tensor("ones_f", [128, 1], F32))
    pp = [e(nc.psum_tensor(f"pp{i}", [128, 512], F32)) for i in range(8)]

    stX = ExitStack()
    xo = stX.enter_context(nc.sbuf_tensor("xo", [128, NKT * XOW], F32))

    def xov(kt, c0, c1):
        return xo[:, kt * XOW + c0: kt * XOW + c1]

    phase("p0_load")
    for kt in range(NKT):
        dma("sync", xov(kt, 0, XOW), xin[kt * 128:(kt + 1) * 128, :], lx)
    dma("gpsimd", mu_r_sb[:, :].rearrange("p (a b) -> p a b", b=1), mu_r.rearrange("(a p) b -> p a b", p=128), ld, slow=True)
    dma("gpsimd", mu_k_sb[:, :].rearrange("p (a b) -> p a b", b=1), mu_k.rearrange("(a p) b -> p a b", p=128), ld, slow=True)
    dma("gpsimd", msku_sb[:, :].rearrange("p (a b) -> p a b", b=1), masku.rearrange("(a p) b -> p a b", p=128), ld, slow=True)
    dma("gpsimd", mskp_sb[:, :].rearrange("p (a b) -> p a b", b=1), maskp.rearrange("(a p) b -> p a b", p=128), ld, slow=True)

    ld_mu = ld.n

    def f(v, ld_mu=ld_mu):
        v.memset(ones_c[:, :], 1.0)
        v.memset(ones_f[:, :], 1.0)
        v.wait_ge(ld.h, ld_mu)
        v.tensor_scalar(mu_r1_sb[:, :], mu_r_sb[:, :], -1.0, 1.0, OP.mult, OP.add)
        v.tensor_scalar(mu_k1_sb[:, :], mu_k_sb[:, :], -1.0, 1.0, OP.mult, OP.add)
    P.add("vector", f)
    bump("vector", vp)

    barrier()

    def emit_ln(xview, lnview, chunks):
        stL = ExitStack()
        sqbuf = stL.enter_context(nc.sbuf_tensor(None, [128, 2 * 512], BF16))
        s_m = stL.enter_context(nc.sbuf_tensor(None, [1, XOW], F32))
        s_v = stL.enter_context(nc.sbuf_tensor(None, [1, XOW], F32))
        s_i = stL.enter_context(nc.sbuf_tensor(None, [1, XOW], F32))
        invb_ = stL.enter_context(nc.sbuf_tensor(None, [128, XOW], BF16))
        msbb_ = stL.enter_context(nc.sbuf_tensor(None, [128, XOW], BF16))
        c_lo = chunks[0][0]
        c_hi = chunks[-1][1]
        wdt = c_hi - c_lo
        wait("scalar", vp, vp.n)

        def f(s):
            for kt in range(NKT):
                s.activation(out=lnview(kt, c_lo, c_hi), in_=xview(kt, c_lo, c_hi), func=AT.Copy)
        P.add("scalar", f)
        sp_cast = bump("scalar", sp)
        wait("vector", sp, sp_cast)

        tp_pair = {}
        gix = 0
        for (c0, c1) in chunks:
            wd = c1 - c0
            for kt in range(NKT):
                gix += 1
                def f(v, kt=kt, c0=c0, c1=c1, wd=wd, tpg=tp_pair.get(gix - 2, 0)):
                    if tpg:
                        v.wait_ge(tp.h, tpg)
                    v.tensor_tensor(sqbuf[:, (gix % 2) * 512:(gix % 2) * 512 + wd] if False else
                                    sqbuf[:, (kt % 2) * 512:(kt % 2) * 512 + wd],
                                    lnview(kt, c0, c1), lnview(kt, c0, c1), OP.mult)
                P.add("vector", f)
                vpm = bump("vector", vp)

                def f(t, kt=kt, c0=c0, c1=c1, wd=wd, vpm=vpm):
                    t.wait_ge(vp.h, vpm)
                    t.matmul(pp[6][0:1, 0:wd], ones_c[:, :], lnview(kt, c0, c1),
                             start=(kt == 0), stop=(kt == NKT - 1))
                    t.matmul(pp[7][0:1, 0:wd], ones_c[:, :],
                             sqbuf[:, (kt % 2) * 512:(kt % 2) * 512 + wd],
                             start=(kt == 0), stop=(kt == NKT - 1))
                P.add("tensor", f)
                tp_pair[gix] = bump("tensor", tp)
            tpm = tp_pair[gix]

            def f(s, c0=c0, c1=c1, wd=wd, tpm=tpm):
                s.wait_ge(tp.h, tpm)
                s.copy(s_m[0:1, c0 - c_lo:c1 - c_lo], pp[6][0:1, 0:wd])
                s.copy(s_v[0:1, c0 - c_lo:c1 - c_lo], pp[7][0:1, 0:wd])
            P.add("scalar", f)
            spm = bump("scalar", sp)
            wait("tensor", sp, spm)
        sp_ev = sp.n

        def f(v, sp_ev=sp_ev, wdt=wdt):
            v.wait_ge(sp.h, sp_ev)
            v.tensor_scalar(s_m[0:1, 0:wdt], s_m[0:1, 0:wdt], 1.0 / E, None, OP.mult)
            v.tensor_scalar(s_v[0:1, 0:wdt], s_v[0:1, 0:wdt], 1.0 / E, None, OP.mult)
            v.drain()
            v.tensor_tensor(s_i[0:1, 0:wdt], s_m[0:1, 0:wdt], s_m[0:1, 0:wdt], OP.mult)
            v.drain()
            v.tensor_tensor(s_v[0:1, 0:wdt], s_v[0:1, 0:wdt], s_i[0:1, 0:wdt], OP.subtract)
            v.drain()
            v.tensor_scalar(s_v[0:1, 0:wdt], s_v[0:1, 0:wdt], 1e-5, None, OP.add)
        P.add("vector", f)
        vp_var = bump("vector", vp)

        def f(s, vp_var=vp_var, wdt=wdt):
            s.wait_ge(vp.h, vp_var)
            s.activation(out=s_i[0:1, 0:wdt], in_=s_v[0:1, 0:wdt], func=AT.Ln)
        P.add("scalar", f)
        sp_inv = bump("scalar", sp)

        def f(v, sp_inv=sp_inv, wdt=wdt):
            v.wait_ge(sp.h, sp_inv)
            v.tensor_scalar(s_i[0:1, 0:wdt], s_i[0:1, 0:wdt], -0.5, None, OP.mult)
        P.add("vector", f)
        vp_l2 = bump("vector", vp)

        def f(s, vp_l2=vp_l2, wdt=wdt):
            s.wait_ge(vp.h, vp_l2)
            s.activation(out=s_i[0:1, 0:wdt], in_=s_i[0:1, 0:wdt], func=AT.Exp)
        P.add("scalar", f)
        sp_inv2 = bump("scalar", sp)

        def f(v, sp_inv2=sp_inv2, wdt=wdt):
            v.wait_ge(sp.h, sp_inv2)
            v.tensor_tensor(s_v[0:1, 0:wdt], s_m[0:1, 0:wdt], s_i[0:1, 0:wdt], OP.mult)
        P.add("vector", f)
        vp_ms = bump("vector", vp)

        wait("gpsimd", vp, vp_ms)
        dma("gpsimd", h_stat[0:1, 0:wdt], s_i[0:1, 0:wdt], d_stat)
        w2 = dma("gpsimd", h_stat[1:2, 0:wdt], s_v[0:1, 0:wdt], d_stat)
        wait("gpsimd", d_stat, w2)
        dma("gpsimd", invb_[:, 0:wdt], h_stat[0:1, 0:wdt].broadcast_to([128, wdt]), d_stat)
        wbc = dma("gpsimd", msbb_[:, 0:wdt], h_stat[1:2, 0:wdt].broadcast_to([128, wdt]), d_stat)
        wait("vector", d_stat, wbc)

        def f(v):
            for kt in range(NKT):
                v.tensor_tensor(lnview(kt, c_lo, c_hi), lnview(kt, c_lo, c_hi), invb_[:, 0:wdt], OP.mult)
            v.drain()
            for kt in range(NKT):
                v.tensor_tensor(lnview(kt, c_lo, c_hi), lnview(kt, c_lo, c_hi), msbb_[:, 0:wdt], OP.subtract)
        P.add("vector", f)
        r = bump("vector", vp)
        barrier()
        stL.close()
        return r

    def emit_ffn(lnview, bufs, xview):
        (mixb, rbuf, kabuf, wfbuf, vstage) = bufs
        wait("gpsimd", ld, ld.n)
        for kt in range(NKT):
            dma("gpsimd", wfbuf[0][:, kt * 1024:(kt + 1) * 1024],
                wk_t[kt * 128:(kt + 1) * 128, :], ld)
        wk_done = ld.n

        def mix(mu_sb, mu1_sb):
            def f(v, mu_sb=mu_sb, mu1_sb=mu1_sb):
                for kt in range(NKT):
                    v.tensor_scalar(mixb[:, kt * TH:(kt + 1) * TH], lnview(kt, 1, TH + 1),
                                    mu_sb[:, kt:kt + 1], None, OP.mult)
                v.drain()
                for kt in range(NKT):
                    v.scalar_tensor_tensor(mixb[:, kt * TH:(kt + 1) * TH], lnview(kt, 0, TH),
                                           mu1_sb[:, kt:kt + 1], mixb[:, kt * TH:(kt + 1) * TH],
                                           OP.mult, OP.add)
            P.add("vector", f)
            return bump("vector", vp)

        mix(mu_k_sb, mu_k1_sb)

        def proj(wbuf_idx, store, rhs_buf):
            wait("tensor", vp, vp.n)
            pr_sph = {}
            for mt in range(NKT):
                for tc in range(4):
                    ipr = mt * 4 + tc
                    bank = ipr % 4
                    if ipr >= 4:
                        wait("tensor", sp, pr_sph[ipr - 4])

                    def f(t, mt=mt, tc=tc, bank=bank, wbuf_idx=wbuf_idx, rhs_buf=rhs_buf):
                        for kt in range(NKT):
                            t.matmul(pp[bank][:, :],
                                     wfbuf[wbuf_idx][:, (kt * NKT + mt) * 128:(kt * NKT + mt + 1) * 128],
                                     rhs_buf[:, kt * TH + tc * 512:kt * TH + (tc + 1) * 512],
                                     start=(kt == 0), stop=(kt == NKT - 1))
                    P.add("tensor", f)
                    tpm = bump("tensor", tp)

                    def f(s, mt=mt, tc=tc, bank=bank, tpm=tpm):
                        s.wait_ge(tp.h, tpm)
                        store(mt, tc, pp[bank], s)
                    P.add("scalar", f)
                    pr_sph[ipr] = bump("scalar", sp)

        wait("tensor", ld, wk_done)

        def store_k(mt, tc, ps, s):
            s.activation(out=kabuf[:, mt * TH + tc * 512:mt * TH + (tc + 1) * 512],
                         in_=ps[:, :], func=AT.Relu)
        proj(0, store_k, mixb)
        sp_k = sp.n
        tp_k = tp.n

        wait("gpsimd", ld, ld.n)
        for kt in range(NKT):
            dma("gpsimd", wfbuf[1][:, kt * 1024:(kt + 1) * 1024],
                wr_t[kt * 128:(kt + 1) * 128, :], ld)
        wr_done = ld.n

        wait("vector", sp, sp_k)

        def f(v):
            for kt in range(NKT):
                v.tensor_tensor(kabuf[:, kt * TH:(kt + 1) * TH], kabuf[:, kt * TH:(kt + 1) * TH],
                                kabuf[:, kt * TH:(kt + 1) * TH], OP.mult)
        P.add("vector", f)
        wait("vector", tp, tp_k)
        mix(mu_r_sb, mu_r1_sb)

        wait("gpsimd", tp, tp_k)
        wait("gpsimd", ld, ld.n)
        for kt in range(NKT):
            dma("gpsimd", wfbuf[0][:, kt * 1024:(kt + 1) * 1024],
                wv_t[kt * 128:(kt + 1) * 128, :], ld)
        wv_done = ld.n
        wait("tensor", ld, wr_done)
        wait("tensor", ld, wv_done)
        wait("tensor", vp, vp.n)
        vp_hist = {}
        itf = 0
        for mt in range(NKT):
            for tc in range(4):
                itf += 1
                rb = (2 * itf) % 4
                vb = (2 * itf + 1) % 4

                def f(t, mt=mt, tc=tc, rb=rb, vb=vb, g=vp_hist.get(itf - 2, 0)):
                    if g:
                        t.wait_ge(vp.h, g)
                    for kt in range(NKT):
                        t.matmul(pp[rb][:, :],
                                 wfbuf[1][:, (kt * NKT + mt) * 128:(kt * NKT + mt + 1) * 128],
                                 mixb[:, kt * TH + tc * 512:kt * TH + (tc + 1) * 512],
                                 start=(kt == 0), stop=(kt == NKT - 1))
                    for kt in range(NKT):
                        t.matmul(pp[vb][:, :],
                                 wfbuf[0][:, (kt * NKT + mt) * 128:(kt * NKT + mt + 1) * 128],
                                 kabuf[:, kt * TH + tc * 512:kt * TH + (tc + 1) * 512],
                                 start=(kt == 0), stop=(kt == NKT - 1))
                P.add("tensor", f)
                tpm = bump("tensor", tp)

                def f(s, rb=rb, tpm=tpm, g=vp_hist.get(itf - 1, 0)):
                    s.wait_ge(tp.h, tpm)
                    if g:
                        s.wait_ge(vp.h, g)
                    s.activation(out=rbuf, in_=pp[rb][:, :], func=AT.Sigmoid)
                P.add("scalar", f)
                spm = bump("scalar", sp)

                def f(v, mt=mt, tc=tc, vb=vb, spm=spm):
                    v.wait_ge(sp.h, spm)
                    v.tensor_tensor(vstage, pp[vb][:, :], rbuf, OP.mult)
                    v.drain()
                    v.scalar_tensor_tensor(xview(mt, 1 + tc * 512, 1 + (tc + 1) * 512),
                                           vstage, 0.5,
                                           xview(mt, 1 + tc * 512, 1 + (tc + 1) * 512),
                                           OP.mult, OP.add)
                P.add("vector", f)
                vp_hist[itf] = bump("vector", vp)
        return bump("vector", vp)

    LN1_CH = [(0, 512), (512, 1024), (1024, 1536), (1536, 2048), (2048, 2049)]
    LN2_CH = [(1, 513), (513, 1025), (1025, 1537), (1537, 2049)]

    stA = ExitStack()
    eA = stA.enter_context
    lnb = eA(nc.sbuf_tensor("lnb", [128, NKT * XOW], BF16))

    def lnv(kt, c0, c1):
        return lnb[:, kt * XOW + c0: kt * XOW + c1]

    ffnsm = eA(nc.sbuf_tensor("ffnsm", [128, 1024], BF16))

    stG = ExitStack()
    g_sb = stG.enter_context(nc.sbuf_tensor("g_sb", [16, 7680], BF16))
    wstage = stG.enter_context(nc.sbuf_tensor("wstage", [128, 2048], BF16))
    go = {}
    _off = 0
    for nm, wd in [("g1in", 1024), ("g2in", 2048), ("g1x", 128), ("g2x", 768),
                   ("g1dt", 128), ("g2dt", 512), ("g1out", 512), ("g2out", 2048)]:
        go[nm] = (_off, _off + wd)
        _off += wd

    def gvw(nm):
        a, b_ = go[nm]
        return g_sb[:, a:b_]

    for nm, gsrc in [("g1in", g1in), ("g2in", g2in), ("g1x", g1x), ("g2x", g2x),
                     ("g1dt", g1dt), ("g2dt", g2dt), ("g1out", g1out), ("g2out", g2out)]:
        dma("gpsimd", gvw(nm), gsrc, ld)
    ld_g = ld.n
    wait("tensor", ld, ld_g)

    phase("ln1")
    emit_ln(xov, lnv, LN1_CH)

    def build_wtmp(g1name, g2name, npt, fchunks, rowbase, dst_hbm):
        for pt in range(npt):
            for ci, (c0, c1) in enumerate(fchunks):
                wd = c1 - c0
                bank = (pt * len(fchunks) + ci) % 2

                def f(t, pt=pt, c0=c0, wd=wd, bank=bank, g1name=g1name, g2name=g2name):
                    t.matmul(pp[4 + bank][:, 0:wd], gvw(g1name)[:, pt * 128:(pt + 1) * 128],
                             gvw(g2name)[:, c0:c0 + wd], start=True, stop=True)
                P.add("tensor", f)
                tpm = bump("tensor", tp)

                def f(s, tpm=tpm, wd=wd, bank=bank):
                    s.wait_ge(tp.h, tpm)
                    s.copy(wstage[:, bank * 512:bank * 512 + wd], pp[4 + bank][:, 0:wd])
                P.add("scalar", f)
                spm = bump("scalar", sp)
                wait("tensor", sp, spm)
                wait("gpsimd", sp, spm)
                wdm = dma("gpsimd", dst_hbm[rowbase + pt * 128: rowbase + (pt + 1) * 128, c0:c0 + wd],
                          wstage[:, bank * 512:bank * 512 + wd], d_wb[bank])
                wait("scalar", d_wb[bank], wdm)

    phase("wbuild")
    F4 = [(0, 512), (512, 1024), (1024, 1536), (1536, 2048)]
    build_wtmp("g1in", "g2in", 8, F4, 0, h_wtmp)
    build_wtmp("g1out", "g2out", 4, F4, 1024, h_wtmp)
    build_wtmp("g1x", "g2x", 1, [(0, 512), (512, 768)], 0, h_wtmpx)
    build_wtmp("g1dt", "g2dt", 1, [(0, 512)], 0, h_wtmpdt)
    barrier()
    stG.close()

    stF = ExitStack()
    mixb = stF.enter_context(nc.sbuf_tensor("mixb", [128, NKT * TH], BF16))
    kabuf = stF.enter_context(nc.sbuf_tensor("kabuf", [128, NKT * TH], BF16))
    wfbuf = [stF.enter_context(nc.sbuf_tensor(f"wf{i}", [128, NKT * NKT * 128], BF16)) for i in range(2)]
    rbuf = ffnsm[:, 0:512]
    vstage = ffnsm[:, 512:1024]

    phase("ffn1")
    emit_ffn(lnv, (mixb, rbuf, kabuf, wfbuf, vstage), xov)
    barrier()
    stF.close()

    phase("ln2")
    emit_ln(xov, lnv, LN2_CH)
    wait("sync", vp, vp.n)
    for kt in range(NKT):
        dma("sync", h_ln2own[kt * 128:(kt + 1) * 128, :], lnv(kt, 1, TH + 1), lx)

    barrier()
    stA.close()

    phase("e1")
    for kt in range(NKT):
        dma("gpsimd", h_xsp[kt * 128:(kt + 1) * 128, :], xov(kt, 1, TH + 1), ld)

    def f(gp):
        gp.collective_compute(
            "AllGather", OP.bypass,
            replica_groups=[[0, 1], [2, 3], [4, 5], [6, 7]],
            ins=[h_ln2own], outs=[h_ln2full],
        ).then_inc(ccs.h, 1)
    P.add("gpsimd", f)
    cc1 = ccs.res()
    wait("gpsimd", ccs, cc1)
    barrier()
    stX.close()

    phase("front")
    stB = ExitStack()
    eB = stB.enter_context
    wi_sb = eB(nc.sbuf_tensor("wi_sb", [128, NKT * 2048], BF16))
    wx_sb = eB(nc.sbuf_tensor("wx_sb", [128, NKT * 96], BF16))
    slab = [eB(nc.sbuf_tensor(f"slab{i}", [128, NKT * 512], BF16)) for i in range(2)]
    XCW = TF + 3
    xcp = eB(nc.sbuf_tensor("xcp", [128, XCW], BF16))
    zb = eB(nc.sbuf_tensor("zb", [128, TF], BF16))
    cacc = eB(nc.sbuf_tensor("cacc", [128, TF], BF16))
    ctmp = eB(nc.sbuf_tensor("ctmp", [128, TF], BF16))
    convd = eB(nc.sbuf_tensor("convd", [128, TF], BF16))
    ytile = eB(nc.sbuf_tensor("ytile", [128, TF], BF16))
    cf_sb = eB(nc.sbuf_tensor("cf_sb", [128, NDT * 4], F32))
    cb_sb = eB(nc.sbuf_tensor("cb_sb", [128, NDT], F32))
    dtbc_sb = eB(nc.sbuf_tensor("dtbc_sb", [96, TF], BF16))

    for kt in range(NKT):
        for ai in range(4):
            a = 4 * kt + ai
            srcv = h_wtmp[a * 32: a * 32 + 32, :].rearrange("i (b j) -> b i j", j=64)
            dstv = wi_sb[ai * 32:(ai + 1) * 32, kt * 2048:(kt + 1) * 2048].rearrange(
                "b (i j) -> b i j", j=64)
            dma("gpsimd", dstv, srcv, ld)
    for kt in range(NKT):
        for ai in range(2):
            a_loc = 2 * kt + ai
            srcv = h_wtmpx[a_loc * 8: a_loc * 8 + 8, :].rearrange("i (b j) -> b i j", j=12)
            dstv = wx_sb[ai * 64:(ai + 1) * 64, kt * 96:(kt + 1) * 96].rearrange(
                "b (i j) -> b i j", j=12)
            dma("gpsimd", dstv, srcv, ld)
    dma("gpsimd", cf_sb[:, :].rearrange("p (a b) -> p a b", b=4), cfin.rearrange("(a p) b -> p a b", p=128), ld, slow=True)
    ld_wi = dma("gpsimd", cb_sb[:, :].rearrange("p (a b) -> p a b", b=1), cbin.rearrange("(a p) b -> p a b", p=128), ld, slow=True)
    wait("tensor", ld, ld_wi)

    vp_y_prev = 0
    ld_y_prev = 0
    ip_sph = {}
    ip_tph = {}
    for j in range(NDT):
        for tc in range(NCH):
            sl = tc % 2
            gi = j * NCH + tc
            if gi >= 2:
                wait("sync", tp, ip_tph[gi - 2])
            src = h_ln2full.rearrange("(s p) t -> s p t", s=2)[tc // 4, :, (tc % 4) * 512:(tc % 4 + 1) * 512]
            srcv = src.rearrange("(a p) t -> p a t", p=128)
            lxm = dma("sync", slab[sl][:, :].rearrange("p (a t) -> p a t", a=NKT), srcv, d_slab[sl])
            for which in range(2):
                mcol = (0 if which == 0 else 1024) + j * 128
                iw = gi * 2 + which
                bank = iw % 4
                if iw >= 4:
                    wait("tensor", sp, ip_sph[iw - 4])

                def f(t, mcol=mcol, bank=bank, sl=sl, lxm=lxm):
                    t.wait_ge(d_slab[sl].h, lxm)
                    for kt in range(NKT):
                        t.matmul(pp[bank][:, :],
                                 wi_sb[:, kt * 2048 + mcol: kt * 2048 + mcol + 128],
                                 slab[sl][:, kt * 512:(kt + 1) * 512],
                                 start=(kt == 0), stop=(kt == NKT - 1))
                P.add("tensor", f)
                tpm = bump("tensor", tp)

                def f(s, which=which, tc=tc, bank=bank, tpm=tpm, vp_y_prev=vp_y_prev):
                    s.wait_ge(tp.h, tpm)
                    if tc == 0 and which == 0 and vp_y_prev:
                        s.wait_ge(vp.h, vp_y_prev)
                    if which == 0:
                        s.copy(xcp[:, 1 + tc * 512: 1 + (tc + 1) * 512], pp[bank][:, :])
                    else:
                        s.copy(zb[:, tc * 512:(tc + 1) * 512], pp[bank][:, :])
                P.add("scalar", f)
                ip_sph[iw] = bump("scalar", sp)
            ip_tph[gi] = tp.n
        sp_xz = sp.n

        def f(v, j=j, sp_xz=sp_xz, ld_y_prev=ld_y_prev):
            v.wait_ge(sp.h, sp_xz)
            if ld_y_prev:
                v.wait_ge(d_y.h, ld_y_prev)
            v.memset(xcp[:, 0:1], 0.0)
            v.memset(xcp[:, TF + 1:TF + 3], 0.0)
            v.drain()
            v.tensor_scalar(cacc[:, :], xcp[:, 0:TF], cf_sb[:, j * 4:j * 4 + 1], None, OP.mult)
            v.tensor_scalar(ctmp[:, :], xcp[:, 1:1 + TF], cf_sb[:, j * 4 + 1:j * 4 + 2], None, OP.mult)
            v.tensor_scalar(ytile[:, :], xcp[:, 2:2 + TF], cf_sb[:, j * 4 + 2:j * 4 + 3], None, OP.mult)
            v.tensor_scalar(convd[:, :], xcp[:, 3:3 + TF], cf_sb[:, j * 4 + 3:j * 4 + 4], None, OP.mult)
            v.drain()
            v.tensor_tensor(cacc[:, :], cacc[:, :], ctmp[:, :], OP.add)
            v.tensor_tensor(ytile[:, :], ytile[:, :], convd[:, :], OP.add)
            v.drain()
            v.tensor_tensor(cacc[:, :], cacc[:, :], ytile[:, :], OP.add)
        P.add("vector", f)
        vp_conv = bump("vector", vp)

        def f(s, j=j, vp_conv=vp_conv):
            s.wait_ge(vp.h, vp_conv)
            s.activation(out=cacc[:, :], in_=cacc[:, :], func=AT.Silu, bias=cb_sb[:, j:j + 1])
        P.add("scalar", f)
        sp_silu = bump("scalar", sp)

        def f(v, sp_silu=sp_silu):
            v.wait_ge(sp.h, sp_silu)
            v.tensor_tensor(ytile[:, :], cacc[:, :], zb[:, :], OP.mult)
        P.add("vector", f)
        vp_y = bump("vector", vp)
        wait("gpsimd", vp, vp_y)
        ld_y = dma("gpsimd", h_y[j * 128:(j + 1) * 128, :], ytile[:, :], d_y)
        vp_y_prev = vp_y
        ld_y_prev = ld_y

    phase("xproj")
    wait("sync", d_y, d_y.n)
    xp_sph = {}
    xp_tph = {}
    for tc in range(NCH):
        sl = tc % 2
        if tc >= 2:
            wait("sync", tp, xp_tph[tc - 2])
        lxm = dma("sync", slab[sl][:, :].rearrange("p (a t) -> p a t", a=NDT),
                  h_y.rearrange("(a p) t -> p a t", p=128)[:, :, tc * 512:(tc + 1) * 512], d_slab[sl])
        bank = tc % 4

        def f(t, bank=bank, sl=sl, lxm=lxm):
            t.wait_ge(d_slab[sl].h, lxm)
            for kt in range(NDT):
                t.matmul(pp[bank][0:96, :],
                         wx_sb[:, kt * 96:(kt + 1) * 96],
                         slab[sl][:, kt * 512:(kt + 1) * 512],
                         start=(kt == 0), stop=(kt == NDT - 1))
        P.add("tensor", f)
        tpm = bump("tensor", tp)

        def f(s, tc=tc, bank=bank, tpm=tpm):
            s.wait_ge(tp.h, tpm)
            s.copy(dtbc_sb[:, tc * 512:(tc + 1) * 512], pp[bank][0:96, :])
        P.add("scalar", f)
        spm = bump("scalar", sp)
        xp_sph[tc] = spm
        xp_tph[tc] = tp.n
        if tc >= 3:
            wait("tensor", sp, xp_sph[tc - 3])
    wait("gpsimd", sp, sp.n)
    dma("gpsimd", h_dtbc_in[:, :], dtbc_sb[:, :], ld)

    barrier()
    stB.close()

    phase("e2")
    def f(gp):
        gp.collective_compute(
            "AllReduce", OP.add,
            replica_groups=[[0, 1], [2, 3], [4, 5], [6, 7]],
            ins=[h_dtbc_in], outs=[h_dtbc],
        ).then_inc(ccs.h, 1)
    P.add("gpsimd", f)
    cc2 = ccs.res()
    wait("gpsimd", ccs, cc2)
    barrier()

    phase("scanprep")
    stC = ExitStack()
    eC = stC.enter_context
    bbc2 = [eC(nc.sbuf_tensor(f"bbc{i}", [128, NST * TC], BF16)) for i in range(2)]
    cbc2 = [eC(nc.sbuf_tensor(f"cbc{i}", [128, NST * TC], BF16)) for i in range(2)]
    dab2 = [eC(nc.sbuf_tensor(f"dab{i}", [128, NST * TC], BF16)) for i in range(2)]
    bsb = eC(nc.sbuf_tensor("bsb", [128, NST * TC], BF16))
    hsb2 = [eC(nc.sbuf_tensor(f"hsb{i}", [128, NST * TC], BF16)) for i in range(2)]
    dtun2 = [eC(nc.sbuf_tensor(f"dtun{i}", [64, TC], BF16)) for i in range(2)]
    wdt_sb = eC(nc.sbuf_tensor("wdt_sb", [64, DH], BF16))
    dts2 = [eC(nc.sbuf_tensor(f"dts{i}", [128, TC], BF16)) for i in range(2)]
    dus2 = [eC(nc.sbuf_tensor(f"dus{i}", [128, TC], BF16)) for i in range(2)]
    spf = eC(nc.sbuf_tensor("spf", [128, TC], F32))
    usb = [eC(nc.sbuf_tensor(f"usb{i}", [128, TC], BF16)) for i in range(3)]
    yssb = [eC(nc.sbuf_tensor(f"yssb{i}", [128, TC], BF16)) for i in range(2)]
    acol = eC(nc.sbuf_tensor("acol", [128, NDT * NST], F32))
    dcol = eC(nc.sbuf_tensor("dcol", [128, NDT], F32))
    carry = eC(nc.sbuf_tensor("carry", [128, NDT * NST], BF16))
    fold16 = eC(nc.sbuf_tensor("fold16", [128, NST], BF16))
    hsrow = eC(nc.sbuf_tensor("hsrow", [1, 512], F32))
    hspad = eC(nc.sbuf_tensor("hspad", [1, 1032], F32))

    stW = ExitStack()
    wo_sb = stW.enter_context(nc.sbuf_tensor("wo_sb", [128, NKT * 1024], BF16))
    for kt in range(NKT):
        for ai in range(2):
            a_loc = 2 * kt + ai
            srcv = h_wtmp[1024 + a_loc * 32: 1024 + a_loc * 32 + 32, :].rearrange(
                "i (b j) -> b i j", j=32)
            dstv = wo_sb[ai * 64:(ai + 1) * 64, kt * 1024:(kt + 1) * 1024].rearrange(
                "b (i j) -> b i j", j=32)
            dma("gpsimd", dstv, srcv, ld)
    ld_wo = ld.n

    dma("gpsimd", spf[:, 0:NDT * NST].rearrange("p (a n) -> p a n", a=NDT),
        alog.rearrange("(a p) n -> p a n", p=128), d_stat)
    dma("gpsimd", dcol[:, :].rearrange("p (a b) -> p a b", b=1),
        dvin.rearrange("(a p) b -> p a b", p=128), d_stat, slow=True)
    for a in range(8):
        srcv = h_wtmpdt[a * 16:a * 16 + 16, :].rearrange("i (b j) -> b i j", j=64)
        dstv = wdt_sb[a * 8:(a + 1) * 8, :].rearrange("b (i j) -> b i j", j=64)
        dma("gpsimd", dstv, srcv, d_stat)
    ds_all = d_stat.n

    NIT = NCH * NDT
    dt_ld = {}
    bc_ld = {}
    cc_ld = {}

    def bsrc_of(c):
        return h_dtbc[64:80, c * TC:(c + 1) * TC].unsqueeze(0).broadcast_to([128, NST, TC])

    def csrc_of(c):
        return h_dtbc[80:96, c * TC:(c + 1) * TC].unsqueeze(0).broadcast_to([128, NST, TC])

    dt_ld[0] = dma("gpsimd", dtun2[0][:, :], h_dtbc[0:64, 0:TC], d_dt)
    dt_ld[1] = dma("gpsimd", dtun2[1][:, :], h_dtbc[0:64, TC:2 * TC], d_dt)
    bc_ld[0] = dma("gpsimd", bbc2[0][:, :].rearrange("p (n t) -> p n t", n=NST), bsrc_of(0), d_bc)
    cc_ld[0] = dma("gpsimd", cbc2[0][:, :].rearrange("p (n t) -> p n t", n=NST), csrc_of(0), d_bc2)
    bc_ld[1] = dma("gpsimd", bbc2[1][:, :].rearrange("p (n t) -> p n t", n=NST), bsrc_of(1), d_bc)
    cc_ld[1] = dma("gpsimd", cbc2[1][:, :].rearrange("p (n t) -> p n t", n=NST), csrc_of(1), d_bc2)

    def f(s, g=ds_all):
        s.wait_ge(d_stat.h, g)
        s.activation(out=spf[:, 0:NDT * NST], in_=spf[:, 0:NDT * NST], func=AT.Exp)
    P.add("scalar", f)
    sp_aexp = bump("scalar", sp)

    def f(v, g=ds_all, gs=sp_aexp):
        v.wait_ge(d_stat.h, g)
        v.wait_ge(sp.h, gs)
        v.tensor_scalar(acol[:, :], spf[:, 0:NDT * NST], -1.0, None, OP.mult)
        v.memset(carry[:, :], 0.0)
        v.memset(hspad[0:1, :], 0.0)
    P.add("vector", f)
    vp_prep = bump("vector", vp)

    wait("tensor", ld, ld.n)
    wait("tensor", vp, vp.n)
    for ec in range(2):
        def f(t, ec=ec):
            for kt in range(NKT):
                t.matmul(pp[6][0:1, :], ones_c[:, :],
                         wo_sb[:, kt * 1024 + ec * 512: kt * 1024 + (ec + 1) * 512],
                         start=(kt == 0), stop=(kt == NKT - 1))
        P.add("tensor", f)
        tpm = bump("tensor", tp)

        def f(s, ec=ec, tpm=tpm):
            s.wait_ge(tp.h, tpm)
            s.copy(hspad[0:1, 8 + ec * 512: 8 + (ec + 1) * 512], pp[6][0:1, :])
        P.add("scalar", f)
        spm = bump("scalar", sp)
        wait("tensor", sp, spm)
    stW.close()

    phase("scan")
    sp_sp = {}
    sp_da = {}
    tp_hs = {}
    vp_du = {}
    vp_c = {}
    vp_scanh = {}
    vp_ysh = {}
    tp_dt = {}
    ysw_h = {}
    u_ld = {}

    def jj(k):
        return k % NDT

    def cof(k):
        return k // NDT

    def f(t, g=ds_all, gd=dt_ld[1]):
        t.wait_ge(d_stat.h, g)
        t.wait_ge(d_dt.h, gd)
        t.matmul(pp[0][:, :], wdt_sb[:, 0:128], dtun2[0][:, :], start=True, stop=True)
    P.add("tensor", f)
    tp_dt[0] = bump("tensor", tp)

    def f(s, g=tp_dt[0]):
        s.wait_ge(tp.h, g)
        s.activation(out=spf[:, :], in_=pp[0][:, :], func=AT.Exp)
        s.drain()
        s.activation(out=dts2[0][:, :], in_=spf[:, :], func=AT.Ln, bias=1.0)
    P.add("scalar", f)
    sp_sp[0] = bump("scalar", sp)

    def f(t):
        t.matmul(pp[1][:, :], wdt_sb[:, 128:256], dtun2[0][:, :], start=True, stop=True)
    P.add("tensor", f)
    tp_dt[1] = bump("tensor", tp)

    u_ld[0] = dma("sync", usb[0][:, :], h_y[0:128, 0:TC], d_u[0])
    u_ld[1] = dma("sync", usb[1][:, :], h_y[128:256, 0:TC], d_u[1])

    for i in range(NIT + 1):
        p, q = i % 2, (i - 1) % 2
        k = i - 1
        if i < NIT:
            def f(v, i=i, p=p, gs=sp_sp[i], gu=u_ld[i]):
                v.wait_ge(sp.h, gs)
                v.wait_ge(d_u[i % 3].h, gu)
                v.tensor_tensor(dus2[p][:, :], dts2[p][:, :], usb[i % 3][:, :], OP.mult)
            P.add("vector", f)
            vp_du[i] = bump("vector", vp)
        if i >= 1:
            def f(v, k=k, kp=q, gsp=sp_da[k], gtp=tp_hs.get(k - 2, 0),
                  gbc=bc_ld[cof(k)], j=jj(k)):
                v.wait_ge(sp.h, gsp)
                if gtp:
                    v.wait_ge(tp.h, gtp)
                v.wait_ge(d_bc.h, gbc)
                da0 = dab2[kp][:, :].rearrange("p (n t) -> p n t", n=NST)[:, :, 0]
                bb0 = bsb[:, :].rearrange("p (n t) -> p n t", n=NST)[:, :, 0]
                v.tensor_tensor(fold16[:, :], da0, carry[:, j * NST:(j + 1) * NST], OP.mult)
                v.tensor_tensor(bsb[:, :].rearrange("p (n t) -> p n t", n=NST),
                                dus2[kp][:, :].unsqueeze(1).broadcast_to([128, NST, TC]),
                                bbc2[cof(k) % 2][:, :].rearrange("p (n t) -> p n t", n=NST), OP.mult)
                v.drain()
                v.memset(da0, 0.0)
                v.drain()
                v.tensor_tensor(bb0, bb0, fold16[:, :], OP.add)
                v.drain()
                v.tensor_tensor_scan(hsb2[k % 2][:, :], dab2[kp][:, :], bsb[:, :], 0.0, OP.mult, OP.add)
            P.add("vector", f)
            vp_scanh[k] = bump("vector", vp)

            def f(v, k=k, kp=q, gcc=cc_ld[cof(k)], j=jj(k)):
                v.wait_ge(d_bc2.h, gcc)
                hl = hsb2[k % 2][:, :].rearrange("p (n t) -> p n t", n=NST)[:, :, TC - 1]
                v.tensor_tensor(bsb[:, :], hsb2[k % 2][:, :], cbc2[cof(k) % 2][:, :], OP.mult)
                v.tensor_copy(carry[:, j * NST:(j + 1) * NST], hl)
            P.add("vector", f)
            vp_c[k] = bump("vector", vp)

            def f(v, k=k, kp=q, gys=ysw_h.get(k - 2, 0), j=jj(k)):
                if gys:
                    v.wait_ge(d_ysw[kp].h, gys)
                v.tensor_tensor(bsb[:, 0:8 * TC], bsb[:, 0:8 * TC], bsb[:, 8 * TC:16 * TC], OP.add)
                v.drain()
                v.tensor_tensor(bsb[:, 0:4 * TC], bsb[:, 0:4 * TC], bsb[:, 4 * TC:8 * TC], OP.add)
                v.drain()
                v.tensor_tensor(bsb[:, 0:2 * TC], bsb[:, 0:2 * TC], bsb[:, 2 * TC:4 * TC], OP.add)
                v.drain()
                v.tensor_tensor(yssb[kp][:, :], bsb[:, 0:TC], bsb[:, TC:2 * TC], OP.add)
                v.drain()
                v.scalar_tensor_tensor(yssb[kp][:, :], usb[k % 3][:, :], dcol[:, j:j + 1],
                                       yssb[kp][:, :], OP.mult, OP.add)
            P.add("vector", f)
            vp_ysh[k] = bump("vector", vp)

        if i < NIT:
            if i + 1 < NIT:
                def f(s, b=q, gt=tp_dt[i + 1], gd=vp_du.get(i - 1, 0)):
                    s.wait_ge(tp.h, gt)
                    if gd:
                        s.wait_ge(vp.h, gd)
                    s.activation(out=spf[:, :], in_=pp[b][:, :], func=AT.Exp)
                    s.drain()
                    s.activation(out=dts2[b][:, :], in_=spf[:, :], func=AT.Ln, bias=1.0)
                P.add("scalar", f)
                sp_sp[i + 1] = bump("scalar", sp)

            gv = vp_prep if i == 0 else vp_scanh.get(i - 2, 0)

            def f(s, i=i, p=p, gv=gv, j=jj(i)):
                if gv:
                    s.wait_ge(vp.h, gv)
                s.drain()
                for n in range(NST):
                    s.activation(out=dab2[p][:, n * TC:(n + 1) * TC], in_=dts2[p][:, :],
                                 func=AT.Exp, scale=acol[:, j * NST + n: j * NST + n + 1])
            P.add("scalar", f)
            sp_da[i] = bump("scalar", sp)

        if i >= 1:
            def f(t, k=k, gv=vp_scanh[k]):
                t.wait_ge(vp.h, gv)
                for n in range(NST):
                    t.matmul(pp[5][0:1, :], ones_c[:, :], hsb2[k % 2][:, n * TC:(n + 1) * TC],
                             start=(k == 0 and n == 0), stop=(k == NIT - 1 and n == NST - 1))
            P.add("tensor", f)
            tp_hs[k] = bump("tensor", tp)

        if i + 2 < NIT:
            wait("tensor", sp, sp_sp[i])
            c2 = cof(i + 2)

            def f(t, i=i, c2=c2, gd=dt_ld[max(c2, 1)], j=jj(i + 2)):
                t.wait_ge(d_dt.h, gd)
                t.matmul(pp[i % 2][:, :], wdt_sb[:, j * 128:(j + 1) * 128],
                         dtun2[c2 % 2][:, :], start=True, stop=True)
            P.add("tensor", f)
            tp_dt[i + 2] = bump("tensor", tp)

        if 2 <= i + 2 < NIT:
            wait("sync", vp, vp_ysh.get(i - 1, 0))
            u_ld[i + 2] = dma("sync", usb[(i + 2) % 3][:, :],
                              h_y[jj(i + 2) * 128:(jj(i + 2) + 1) * 128,
                                  cof(i + 2) * TC:(cof(i + 2) + 1) * TC], d_u[(i + 2) % 3])

        if i >= 1:
            if i % NDT == 0 and i // NDT >= 1 and i < NIT:
                c = i // NDT
                if c + 1 < NCH:
                    wait("gpsimd", vp, vp_scanh[i - 1])
                    bc_ld[c + 1] = dma("gpsimd",
                                       bbc2[(c + 1) % 2][:, :].rearrange("p (n t) -> p n t", n=NST),
                                       bsrc_of(c + 1), d_bc)
                wait("gpsimd", vp, vp_ysh[i - 1])
                ysw_h[k] = dma("gpsimd", h_ys[jj(k) * 128:(jj(k) + 1) * 128,
                                              cof(k) * TC:(cof(k) + 1) * TC],
                               yssb[q][:, :], d_ysw[q])
                if c + 1 < NCH:
                    cc_ld[c + 1] = dma("gpsimd",
                                       cbc2[(c + 1) % 2][:, :].rearrange("p (n t) -> p n t", n=NST),
                                       csrc_of(c + 1), d_bc2)
                    wait("gpsimd", tp, tp_dt[c * NDT - 1])
                    dt_ld[c + 1] = dma("gpsimd", dtun2[(c + 1) % 2][:, :],
                                       h_dtbc[0:64, (c + 1) * TC:(c + 2) * TC], d_dt)
            else:
                wait("gpsimd", vp, vp_ysh[i - 1])
                ysw_h[k] = dma("gpsimd", h_ys[jj(k) * 128:(jj(k) + 1) * 128,
                                              cof(k) * TC:(cof(k) + 1) * TC],
                               yssb[q][:, :], d_ysw[q])


    phase("hsfin")
    def f(s, g=tp_hs[NIT - 1]):
        s.wait_ge(tp.h, g)
        s.copy(hsrow[0:1, :], pp[5][0:1, :])
    P.add("scalar", f)
    sp_hsr = bump("scalar", sp)

    def f(v, sp_hsr=sp_hsr):
        v.wait_ge(sp.h, sp_hsr)
        v.tensor_reduce(hspad[0:1, 0:1], hsrow[0:1, :], AX.X, OP.add)
    P.add("vector", f)
    vp_hsr = bump("vector", vp)
    wait("gpsimd", vp, vp_hsr)
    dma("gpsimd", h_hs_in[:, :], hspad[0:1, :], ld)

    barrier()
    stC.close()

    phase("e4")
    def f(gp):
        gp.collective_compute(
            "AllReduce", OP.add,
            replica_groups=[[0, 1, 2, 3, 4, 5, 6, 7]],
            ins=[h_hs_in], outs=[h_hs],
        ).then_inc(ccs.h, 1)
    P.add("gpsimd", f)
    cc4 = ccs.res()

    phase("outproj")
    stE = ExitStack()
    eE = stE.enter_context
    xo2 = eE(nc.sbuf_tensor("xo2", [128, NKT * XOW], F32))
    stW2 = ExitStack()
    wo_sb2 = stW2.enter_context(nc.sbuf_tensor("wo_sb2", [128, NKT * 1024], BF16))

    def xov2(kt, c0, c1):
        return xo2[:, kt * XOW + c0: kt * XOW + c1]

    stD = ExitStack()
    eD = stD.enter_context
    yslab = [eD(nc.sbuf_tensor(f"yslab{i}", [128, NKT * 128], BF16)) for i in range(2)]
    opst = [eD(nc.sbuf_tensor(f"opst{i}", [128, 512], BF16)) for i in range(2)]

    for kt in range(NKT):
        for ai in range(2):
            a_loc = 2 * kt + ai
            srcv = h_wtmp[1024 + a_loc * 32: 1024 + a_loc * 32 + 32, :].rearrange(
                "i (b j) -> b i j", j=32)
            dstv = wo_sb2[ai * 64:(ai + 1) * 64, kt * 1024:(kt + 1) * 1024].rearrange(
                "b (i j) -> b i j", j=32)
            dma("gpsimd", dstv, srcv, ld)
    ld_wo2 = ld.n
    wait("tensor", ld, ld_wo2)
    for kt in range(NKT):
        dma("scalar", xov2(kt, 1, TH + 1), h_xsp[kt * 128:(kt + 1) * 128, :], lx)
    xsp_ld = lx.n

    op_w_hist = {}
    op_sph = {}
    op_dwh = {}
    for tt in range(TF // 128):
        sl = tt % 2
        if tt >= 2:
            wait("sync", tp, op_w_hist[tt - 2])
        lxm = dma("sync", yslab[sl][:, :].rearrange("p (a t) -> p a t", a=NKT),
                  h_ys.rearrange("(a p) t -> p a t", p=128)[:, :, tt * 128:(tt + 1) * 128], d_slab[sl])
        for ec in range(2):
            bank = (tt * 2 + ec) % 4

            def f(t, tt=tt, ec=ec, bank=bank, sl=sl, lxm=lxm):
                t.wait_ge(d_slab[sl].h, lxm)
                for kt in range(NKT):
                    t.matmul(pp[bank][:, :],
                             yslab[sl][:, kt * 128:(kt + 1) * 128],
                             wo_sb2[:, kt * 1024 + ec * 512:(kt) * 1024 + (ec + 1) * 512],
                             start=(kt == 0), stop=(kt == NKT - 1))
            P.add("tensor", f)
            tpm = bump("tensor", tp)
            if ec == 1:
                op_w_hist[tt] = tpm

            def f(s, ec=ec, bank=bank, tpm=tpm, st_i=(tt * 2 + ec) % 2,
                  g=op_dwh.get(tt * 2 + ec - 2, 0), gs=(tt * 2 + ec) % 2):
                s.wait_ge(tp.h, tpm)
                if g:
                    s.wait_ge(d_op[gs].h, g)
                s.copy(opst[st_i][:, :], pp[bank][:, :])
            P.add("scalar", f)
            spm = bump("scalar", sp)
            op_sph[tt * 2 + ec] = spm
            if tt * 2 + ec >= 3:
                wait("tensor", sp, op_sph[tt * 2 + ec - 3])
            wait("gpsimd", sp, spm)
            stb = (tt * 2 + ec) % 2
            opw = dma("gpsimd", h_op[tt * 128:(tt + 1) * 128, ec * 512:(ec + 1) * 512],
                      opst[stb][:, :], d_op[stb])
            op_dwh[tt * 2 + ec] = opw

    barrier()

    phase("rs")
    def f(gp):
        gp.collective_compute(
            "ReduceScatter", OP.add,
            replica_groups=[[0, 1], [2, 3], [4, 5], [6, 7]],
            ins=[h_op], outs=[h_od],
        ).then_inc(ccs.h, 1)
    P.add("gpsimd", f)
    cc_rs = ccs.res()
    wait("gpsimd", ccs, cc_rs)
    wait("gpsimd", ccs, cc4)
    barrier()
    stD.close()
    stW2.close()

    phase("grpE_prep")
    lnb3 = eE(nc.sbuf_tensor("lnb3", [128, NKT * XOW], BF16))

    def lnv3(kt, c0, c1):
        return lnb3[:, kt * XOW + c0: kt * XOW + c1]

    ffnsm3 = eE(nc.sbuf_tensor("ffnsm3", [128, 1024], BF16))
    halc = eE(nc.sbuf_tensor("halc", [128, NKT], F32))
    stT = ExitStack()
    hrow = stT.enter_context(nc.sbuf_tensor("hrow", [1, 1032], F32))
    mrow = stT.enter_context(nc.sbuf_tensor("mrow", [1, E], F32))
    mbc = stT.enter_context(nc.sbuf_tensor("mbc", [128, NKT], F32))
    dodb = [stT.enter_context(nc.sbuf_tensor(f"dodb{i}", [128, TH], BF16)) for i in range(2)]

    ld_hrow = dma("gpsimd", hrow[0:1, :], h_hs[0:1, :], ld)

    def f(v, ld_hrow=ld_hrow):
        v.wait_ge(ld.h, ld_hrow)
        v.tensor_scalar(hrow[0:1, 0:1], hrow[0:1, 0:1], 1e-8 / (4.0 * HS_COUNT), None, OP.mult)
        v.drain()
        v.tensor_scalar(mrow[0:1, :], hrow[0:1, 8:8 + E], hrow[0:1, 0:1], None, OP.mult)
    P.add("vector", f)
    vp_mb = bump("vector", vp)
    wait("gpsimd", vp, vp_mb)
    wmb = dma("gpsimd", h_mb[:, :], mrow[0:1, :], ld)
    wait("gpsimd", ld, wmb)
    for kt in range(NKT):
        dma("gpsimd", mbc[:, kt:kt + 1],
            h_mb[0:1, kt * 128:(kt + 1) * 128].rearrange("a b -> b a"), ld, slow=True)
    ld_mbc = ld.n

    dod_hist = {}
    for et in range(NKT):
        sl = et % 2
        if et >= 2:
            wait("sync", vp, dod_hist[et - 2])
        v_tr = d_dod[sl].res()

        def f(eng, sl=sl, et=et):
            eng.dma_start_transpose(dodb[sl][:, :],
                                    h_od[:, et * 128:(et + 1) * 128]).then_inc(d_dod[sl].h, 16)
        P.add("sync", f)

        def f(v, sl=sl, et=et, v_tr=v_tr, ld_mbc=ld_mbc):
            v.wait_ge(d_dod[sl].h, v_tr)
            if et == 0:
                v.wait_ge(ld.h, ld_mbc)
                v.wait_ge(lx.h, xsp_ld)
            v.scalar_tensor_tensor(xov2(et, 1, TH + 1), dodb[sl][:, :],
                                   mbc[:, et:et + 1], xov2(et, 1, TH + 1),
                                   OP.add, OP.add)
        P.add("vector", f)
        dod_hist[et] = bump("vector", vp)
    barrier()
    stT.close()

    phase("ln3")
    emit_ln(xov2, lnv3, LN2_CH)

    phase("e5")
    def f(v):
        for kt in range(NKT):
            v.tensor_scalar(halc[:, kt:kt + 1], lnv3(kt, TH, TH + 1),
                            mskp_sb[:, kt:kt + 1], None, OP.mult)
    P.add("vector", f)
    vp_pl = bump("vector", vp)
    wait("gpsimd", vp, vp_pl)
    wait("gpsimd", ld, ld.n)
    wpl = dma("gpsimd", h_halo_in.rearrange("(a p) b -> p a b", p=128), halc[:, :].rearrange("p (a b) -> p a b", b=1), ld, slow=True)
    wait("gpsimd", ld, wpl)

    def f(gp):
        gp.collective_compute(
            "AllReduce", OP.add,
            replica_groups=[[0, 1], [2, 3], [4, 5], [6, 7]],
            ins=[h_halo_in], outs=[h_halo],
        ).then_inc(ccs.h, 1)
    P.add("gpsimd", f)
    cc5 = ccs.res()
    wait("gpsimd", ccs, cc5)
    ld_hal = dma("gpsimd", halc[:, :].rearrange("p (a b) -> p a b", b=1), h_halo.rearrange("(a p) b -> p a b", p=128), ld, slow=True)
    wait("vector", ld, ld_hal)

    def f(v):
        for kt in range(NKT):
            v.tensor_scalar(lnv3(kt, 0, 1), halc[:, kt:kt + 1],
                            msku_sb[:, kt:kt + 1], None, OP.mult)
    P.add("vector", f)
    bump("vector", vp)

    phase("ffn3")
    stF3 = ExitStack()
    mixb3 = stF3.enter_context(nc.sbuf_tensor("mixb3", [128, NKT * TH], BF16))
    kabuf3 = stF3.enter_context(nc.sbuf_tensor("kabuf3", [128, NKT * TH], BF16))
    wfbuf3 = [stF3.enter_context(nc.sbuf_tensor(f"wf3{i}", [128, NKT * NKT * 128], BF16)) for i in range(2)]
    rbuf3 = ffnsm3[:, 0:512]
    vstage3 = ffnsm3[:, 512:1024]
    emit_ffn(lnv3, (mixb3, rbuf3, kabuf3, wfbuf3, vstage3), xov2)
    stF3.close()

    phase("outstore")
    wait("sync", vp, vp.n)
    for kt in range(NKT):
        dma("sync", out_t[kt * 128:(kt + 1) * 128, :], xov2(kt, 1, TH + 1), lx)
    lx_fin = lx.n
    wait("sync", lx, lx_fin)
    barrier()
    stE.close()

    phase_end()
    block = e(nc.Block())
    P.emit(block)
    st.close()
    return nc


def _host_maps(inputs):
    x = np.asarray(inputs["L0_attn_x"])
    in_g1 = np.asarray(inputs["L0_attn_in_g1"], np.float32)
    in_g2 = np.asarray(inputs["L0_attn_in_g2"], np.float32)
    x_g1 = np.asarray(inputs["L0_attn_x_g1"], np.float32)
    x_g2 = np.asarray(inputs["L0_attn_x_g2"], np.float32)
    dt_g1 = np.asarray(inputs["L0_attn_dt_g1"], np.float32)
    dt_g2 = np.asarray(inputs["L0_attn_dt_g2"], np.float32)
    out_g1 = np.asarray(inputs["L0_attn_out_g1"], np.float32)
    out_g2 = np.asarray(inputs["L0_attn_out_g2"], np.float32)
    cf = np.asarray(inputs["L0_attn_conv_filter"], np.float32)
    cb = np.asarray(inputs["L0_attn_conv_bias"], np.float32)
    alog_a = np.asarray(inputs["L0_attn_A_log"], np.float32)
    dv = np.asarray(inputs["L0_attn_D"], np.float32)
    wr = np.asarray(inputs["L0_ffn_W_r"], np.float32)
    wk = np.asarray(inputs["L0_ffn_W_k"], np.float32)
    wv = np.asarray(inputs["L0_ffn_W_v"], np.float32)
    mur = np.asarray(inputs["L0_ffn_mu_r"], np.float32)
    muk = np.asarray(inputs["L0_ffn_mu_k"], np.float32)

    C = np.ascontiguousarray
    wr_t = C(wr.T)
    wk_t = C(wk.T)
    wv_t = C(wv.T)
    g2in_m = C(in_g2.reshape(16, 2048))
    g2x_m = C(x_g2.reshape(16, 768))
    g2dt_m = C(dt_g2.reshape(16, 512))
    g2out_m = C(out_g2.reshape(16, 2048))

    maps = []
    for c in range(8):
        b, h = c // 2, c % 2
        xs = x[b, h * TH:(h + 1) * TH, :].T.astype(np.float32)
        halo = (x[b, h * TH - 1:h * TH, :].T.astype(np.float32) if h == 1
                else np.zeros((E, 1), np.float32))
        xin_m = C(np.concatenate([halo, xs], axis=1))

        iperm = list(range(h * 16, (h + 1) * 16)) + list(range(32 + h * 16, 32 + (h + 1) * 16))
        g1in_m = C(in_g1[:, iperm, :].reshape(32 * 32, 16).T)
        aperm = list(range(h * 16, (h + 1) * 16))
        g1x_m = C(x_g1[aperm].reshape(16 * 8, 16).T)
        ipermd = list(range(h * 16, (h + 1) * 16))
        g1dt_m = C(dt_g1[:, ipermd, :].reshape(8 * 16, 16).T)
        g1out_m = C(out_g1[aperm].reshape(16 * 32, 16).T)

        dh = slice(h * DH, (h + 1) * DH)
        maps.append(dict(
            xin=xin_m,
            wr_t=wr_t, wk_t=wk_t, wv_t=wv_t,
            mu_r=C(mur.reshape(E, 1)), mu_k=C(muk.reshape(E, 1)),
            g1in=g1in_m, g2in=g2in_m,
            g1x=g1x_m, g2x=g2x_m,
            g1dt=g1dt_m, g2dt=g2dt_m,
            g1out=g1out_m, g2out=g2out_m,
            cfin=C(cf[:, 0, dh].T),
            cbin=C(cb[dh].reshape(DH, 1)),
            alog=C(alog_a[dh]),
            dvin=C(dv[dh].reshape(DH, 1)),
            maskp=np.full((E, 1), 1.0 - h, np.float32),
            masku=np.full((E, 1), float(h), np.float32),
        ))
    return maps


def kernel(**inputs):
    maps = _host_maps(inputs)
    nc = build_nc()
    res = run_bass_kernel_spmd(nc, maps, core_ids=list(range(8)))
    out = np.empty((4, TF, E), np.float32)
    for c in range(8):
        b, h = c // 2, c % 2
        out[b, h * TH:(h + 1) * TH, :] = res.results[c]["out"].T
    return out


# version 14
# speedup vs baseline: 1.4580x; 1.0460x over previous
import numpy as np
from contextlib import ExitStack

import concourse.bass as bass
import concourse.mybir as mybir
from concourse.bass_utils import run_bass_kernel_spmd
from concourse.aot_env import allow_ici_all_to_all

F32 = mybir.dt.float32
BF16 = mybir.dt.bfloat16
AT = mybir.ActivationFunctionType
OP = mybir.AluOpType
AX = mybir.AxisListType

E = 1024
DIN = 2048
DH = 1024
NST = 16
TF = 4096
TH = 2048
TC = 512
NCH = TF // TC
NDT = DH // 128
NKT = E // 128
XOW = TH + 1
HS_COUNT = 4.0 * TF * DIN * NST

ENGS = ["sync", "gpsimd", "vector", "scalar", "tensor"]


class Cnt:
    def __init__(self, h, step=1):
        self.h = h
        self.n = 0
        self.step = step

    def res(self, k=None):
        self.n += self.step if k is None else k
        return self.n


class Prog:
    def __init__(self):
        self.q = {en: [] for en in ENGS}

    def add(self, en, f):
        self.q[en].append(f)

    def emit(self, block):
        for en in ENGS:
            fs = self.q[en]

            def runner(eng, fs=fs):
                for f in fs:
                    f(eng)
            getattr(block, en)(runner)


def build_nc():
    nc = bass.Bass("TRN2", target_bir_lowering=False)

    xin = nc.dram_tensor("xin", [E, XOW], F32, kind="ExternalInput").ap()
    wr_t = nc.dram_tensor("wr_t", [E, E], F32, kind="ExternalInput").ap()
    wk_t = nc.dram_tensor("wk_t", [E, E], F32, kind="ExternalInput").ap()
    wv_t = nc.dram_tensor("wv_t", [E, E], F32, kind="ExternalInput").ap()
    mu_r = nc.dram_tensor("mu_r", [E, 1], F32, kind="ExternalInput").ap()
    mu_k = nc.dram_tensor("mu_k", [E, 1], F32, kind="ExternalInput").ap()
    g1in = nc.dram_tensor("g1in", [16, 1024], F32, kind="ExternalInput").ap()
    g2in = nc.dram_tensor("g2in", [16, 2048], F32, kind="ExternalInput").ap()
    g1x = nc.dram_tensor("g1x", [16, 128], F32, kind="ExternalInput").ap()
    g2x = nc.dram_tensor("g2x", [16, 768], F32, kind="ExternalInput").ap()
    g1dt = nc.dram_tensor("g1dt", [16, 128], F32, kind="ExternalInput").ap()
    g2dt = nc.dram_tensor("g2dt", [16, 512], F32, kind="ExternalInput").ap()
    g1out = nc.dram_tensor("g1out", [16, 512], F32, kind="ExternalInput").ap()
    g2out = nc.dram_tensor("g2out", [16, 2048], F32, kind="ExternalInput").ap()
    cfin = nc.dram_tensor("cfin", [DH, 4], F32, kind="ExternalInput").ap()
    cbin = nc.dram_tensor("cbin", [DH, 1], F32, kind="ExternalInput").ap()
    alog = nc.dram_tensor("alog", [DH, NST], F32, kind="ExternalInput").ap()
    dvin = nc.dram_tensor("dvin", [DH, 1], F32, kind="ExternalInput").ap()
    maskp = nc.dram_tensor("maskp", [E, 1], F32, kind="ExternalInput").ap()
    masku = nc.dram_tensor("masku", [E, 1], F32, kind="ExternalInput").ap()
    out_t = nc.dram_tensor("out", [E, TH], F32, kind="ExternalOutput").ap()

    h_wtmp = nc.dram_tensor("h_wtmp", [2048, 2048], BF16).ap()
    h_wtmpx = nc.dram_tensor("h_wtmpx", [128, 768], BF16).ap()
    h_wtmpdt = nc.dram_tensor("h_wtmpdt", [128, 512], BF16).ap()
    h_ln2own = nc.dram_tensor("h_ln2own", [E, TH], BF16).ap()
    h_ln2full = nc.dram_tensor("h_ln2full", [2 * E, TH], BF16).ap()
    h_y = nc.dram_tensor("h_y", [DH, TF], BF16).ap()
    h_dtbc_in = nc.dram_tensor("h_dtbc_in", [96, TF], BF16).ap()
    h_dtbc = nc.dram_tensor("h_dtbc", [96, TF], BF16).ap()
    h_ys = nc.dram_tensor("h_ys", [DH, TF], BF16).ap()
    h_op = nc.dram_tensor("h_op", [TF, E], BF16).ap()
    h_od = nc.dram_tensor("h_od", [TH, E], BF16).ap()
    h_hs_in = nc.dram_tensor("h_hs_in", [1, 1032], F32).ap()
    h_hs = nc.dram_tensor("h_hs", [1, 1032], F32).ap()
    h_halo_in = nc.dram_tensor("h_halo_in", [E, 1], F32).ap()
    h_halo = nc.dram_tensor("h_halo", [E, 1], F32).ap()
    h_stat = nc.dram_tensor("h_stat", [2, XOW], BF16).ap()
    h_mb = nc.dram_tensor("h_mb", [1, E], F32).ap()
    h_xsp = nc.dram_tensor("h_xsp", [E, TH], F32).ap()
    h_bcr = nc.dram_tensor("h_bcr", [NCH, 32 * TC], BF16).ap()

    st = ExitStack()
    e = st.enter_context

    BARs = e(nc.semaphore("BAR"))
    bar_n = [0]
    ld = Cnt(e(nc.semaphore("ld")), 16)
    lx = Cnt(e(nc.semaphore("lx")), 16)
    tp = Cnt(e(nc.semaphore("tp")), 1)
    sp = Cnt(e(nc.semaphore("sp")), 1)
    vp = Cnt(e(nc.semaphore("vp")), 1)
    ccs = Cnt(e(nc.semaphore("ccs")), 1)
    d_slab = [Cnt(e(nc.semaphore(f"dslab{i}")), 16) for i in range(2)]
    d_u = [Cnt(e(nc.semaphore(f"du{i}")), 16) for i in range(3)]
    d_ysw = [Cnt(e(nc.semaphore(f"dysw{i}")), 16) for i in range(2)]
    d_op = [Cnt(e(nc.semaphore(f"dop{i}")), 16) for i in range(2)]
    d_dod = [Cnt(e(nc.semaphore(f"ddod{i}")), 16) for i in range(2)]
    d_wb = [Cnt(e(nc.semaphore(f"dwb{i}")), 16) for i in range(4)]
    d_bc = Cnt(e(nc.semaphore("dbc")), 16)
    d_bc2 = Cnt(e(nc.semaphore("dbc2")), 16)
    d_dt = Cnt(e(nc.semaphore("ddt")), 16)
    d_y = Cnt(e(nc.semaphore("dy")), 16)
    d_stat = Cnt(e(nc.semaphore("dstat")), 16)

    SYNC_SEMS = [lx] + d_slab + d_u + d_dod
    GP_SEMS = [ld, d_bc, d_bc2, d_dt, d_y, d_stat] + d_ysw + d_op + d_wb

    P = Prog()

    def barrier():
        bar_n[0] += 1
        tgt = 5 * bar_n[0]
        waits = {"sync": [(c, c.n) for c in SYNC_SEMS],
                 "gpsimd": [(c, c.n) for c in GP_SEMS],
                 "vector": [(vp, vp.n)],
                 "scalar": [(sp, sp.n)],
                 "tensor": [(tp, tp.n)]}
        for en in ENGS:
            ws = waits[en]

            def f(eng, ws=ws, tgt=tgt):
                for c, v in ws:
                    if v > 0:
                        eng.wait_ge(c.h, v)
                eng.sem_inc(BARs, 1)
                eng.wait_ge(BARs, tgt)
            P.add(en, f)

    def dma(en, dst, src, cnt, slow=False):
        v = cnt.res()

        def f(eng, dst=dst, src=src, slow=slow):
            if slow:
                with nc.allow_non_contiguous_dma(reason="tiny gather"):
                    eng.dma_start(out=dst, in_=src).then_inc(cnt.h, cnt.step)
            else:
                eng.dma_start(out=dst, in_=src).then_inc(cnt.h, cnt.step)
        P.add(en, f)
        return v

    def wait(en, cnt, v):
        if v <= 0:
            return

        def f(eng, cnt=cnt, v=v):
            eng.wait_ge(cnt.h, v)
        P.add(en, f)

    cur_scope = {en: None for en in ENGS}

    def phase(name):
        for en in ENGS:
            def f(eng, en=en, name=name):
                if cur_scope[en] is not None:
                    pn, pid = cur_scope[en]
                    nc.leave_named_scope(pn, pid, False)
                sid, _ = nc.enter_named_scope(name, False)
                cur_scope[en] = (name, sid)
            P.add(en, f)

    def phase_end():
        for en in ENGS:
            def f(eng, en=en):
                if cur_scope[en] is not None:
                    pn, pid = cur_scope[en]
                    nc.leave_named_scope(pn, pid, False)
                    cur_scope[en] = None
            P.add(en, f)

    def bump(en, cnt):
        def f(eng, cnt=cnt):
            eng.maybe_drain_then_inc((cnt.h, cnt.step))
        P.add(en, f)
        return cnt.res()

    mu_r_sb = e(nc.sbuf_tensor("mu_r_sb", [128, NKT], F32))
    mu_k_sb = e(nc.sbuf_tensor("mu_k_sb", [128, NKT], F32))
    mu_r1_sb = e(nc.sbuf_tensor("mu_r1_sb", [128, NKT], F32))
    mu_k1_sb = e(nc.sbuf_tensor("mu_k1_sb", [128, NKT], F32))
    msku_sb = e(nc.sbuf_tensor("msku_sb", [128, NKT], F32))
    mskp_sb = e(nc.sbuf_tensor("mskp_sb", [128, NKT], F32))
    ones_c = e(nc.sbuf_tensor("ones_c", [128, 1], BF16))
    ones_f = e(nc.sbuf_tensor("ones_f", [128, 1], F32))
    pp = [e(nc.psum_tensor(f"pp{i}", [128, 512], F32)) for i in range(8)]
    spf = e(nc.sbuf_tensor("spf", [128, TC], F32))
    acol = e(nc.sbuf_tensor("acol", [128, NDT * NST], F32))
    dcol = e(nc.sbuf_tensor("dcol", [128, NDT], F32))
    carry = e(nc.sbuf_tensor("carry", [128, NDT * NST], BF16))

    stX = ExitStack()
    xo = stX.enter_context(nc.sbuf_tensor("xo", [128, NKT * XOW], F32))

    def xov(kt, c0, c1):
        return xo[:, kt * XOW + c0: kt * XOW + c1]

    phase("p0_load")
    for kt in range(NKT):
        dma("sync", xov(kt, 0, XOW), xin[kt * 128:(kt + 1) * 128, :], lx)
    dma("gpsimd", mu_r_sb[:, :].rearrange("p (a b) -> p a b", b=1), mu_r.rearrange("(a p) b -> p a b", p=128), ld, slow=True)
    dma("gpsimd", mu_k_sb[:, :].rearrange("p (a b) -> p a b", b=1), mu_k.rearrange("(a p) b -> p a b", p=128), ld, slow=True)
    dma("gpsimd", msku_sb[:, :].rearrange("p (a b) -> p a b", b=1), masku.rearrange("(a p) b -> p a b", p=128), ld, slow=True)
    dma("gpsimd", mskp_sb[:, :].rearrange("p (a b) -> p a b", b=1), maskp.rearrange("(a p) b -> p a b", p=128), ld, slow=True)

    dma("gpsimd", spf[:, 0:NDT * NST].rearrange("p (a n) -> p a n", a=NDT),
        alog.rearrange("(a p) n -> p a n", p=128), ld)
    dma("gpsimd", dcol[:, :].rearrange("p (a b) -> p a b", b=1),
        dvin.rearrange("(a p) b -> p a b", p=128), ld, slow=True)
    ld_mu = ld.n

    def f(s, ld_mu=ld_mu):
        s.wait_ge(ld.h, ld_mu)
        s.activation(out=spf[:, 0:NDT * NST], in_=spf[:, 0:NDT * NST], func=AT.Exp)
    P.add("scalar", f)
    sp_aexp0 = bump("scalar", sp)

    def f(v, ld_mu=ld_mu, sp_aexp0=sp_aexp0):
        v.memset(ones_c[:, :], 1.0)
        v.memset(ones_f[:, :], 1.0)
        v.memset(carry[:, :], 0.0)
        v.wait_ge(ld.h, ld_mu)
        v.tensor_scalar(mu_r1_sb[:, :], mu_r_sb[:, :], -1.0, 1.0, OP.mult, OP.add)
        v.tensor_scalar(mu_k1_sb[:, :], mu_k_sb[:, :], -1.0, 1.0, OP.mult, OP.add)
        v.wait_ge(sp.h, sp_aexp0)
        v.tensor_scalar(acol[:, :], spf[:, 0:NDT * NST], -1.0, None, OP.mult)
    P.add("vector", f)
    bump("vector", vp)

    barrier()

    def emit_ln(xview, lnview, chunks):
        stL = ExitStack()
        sqbuf = stL.enter_context(nc.sbuf_tensor(None, [128, 2 * 512], BF16))
        s_m = stL.enter_context(nc.sbuf_tensor(None, [1, XOW], F32))
        s_v = stL.enter_context(nc.sbuf_tensor(None, [1, XOW], F32))
        s_i = stL.enter_context(nc.sbuf_tensor(None, [1, XOW], F32))
        invb_ = stL.enter_context(nc.sbuf_tensor(None, [128, XOW], BF16))
        msbb_ = stL.enter_context(nc.sbuf_tensor(None, [128, XOW], BF16))
        c_lo = chunks[0][0]
        c_hi = chunks[-1][1]
        wdt = c_hi - c_lo
        wait("scalar", vp, vp.n)

        def f(s):
            for kt in range(NKT):
                s.activation(out=lnview(kt, c_lo, c_hi), in_=xview(kt, c_lo, c_hi), func=AT.Copy)
        P.add("scalar", f)
        sp_cast = bump("scalar", sp)
        wait("vector", sp, sp_cast)

        tp_pair = {}
        gix = 0
        for (c0, c1) in chunks:
            wd = c1 - c0
            for kt in range(NKT):
                gix += 1
                def f(v, kt=kt, c0=c0, c1=c1, wd=wd, tpg=tp_pair.get(gix - 2, 0)):
                    if tpg:
                        v.wait_ge(tp.h, tpg)
                    v.tensor_tensor(sqbuf[:, (gix % 2) * 512:(gix % 2) * 512 + wd] if False else
                                    sqbuf[:, (kt % 2) * 512:(kt % 2) * 512 + wd],
                                    lnview(kt, c0, c1), lnview(kt, c0, c1), OP.mult)
                P.add("vector", f)
                vpm = bump("vector", vp)

                def f(t, kt=kt, c0=c0, c1=c1, wd=wd, vpm=vpm):
                    t.wait_ge(vp.h, vpm)
                    t.matmul(pp[6][0:1, 0:wd], ones_c[:, :], lnview(kt, c0, c1),
                             start=(kt == 0), stop=(kt == NKT - 1))
                    t.matmul(pp[7][0:1, 0:wd], ones_c[:, :],
                             sqbuf[:, (kt % 2) * 512:(kt % 2) * 512 + wd],
                             start=(kt == 0), stop=(kt == NKT - 1))
                P.add("tensor", f)
                tp_pair[gix] = bump("tensor", tp)
            tpm = tp_pair[gix]

            def f(s, c0=c0, c1=c1, wd=wd, tpm=tpm):
                s.wait_ge(tp.h, tpm)
                s.copy(s_m[0:1, c0 - c_lo:c1 - c_lo], pp[6][0:1, 0:wd])
                s.copy(s_v[0:1, c0 - c_lo:c1 - c_lo], pp[7][0:1, 0:wd])
            P.add("scalar", f)
            spm = bump("scalar", sp)
            wait("tensor", sp, spm)
        sp_ev = sp.n

        def f(v, sp_ev=sp_ev, wdt=wdt):
            v.wait_ge(sp.h, sp_ev)
            v.tensor_scalar(s_m[0:1, 0:wdt], s_m[0:1, 0:wdt], 1.0 / E, None, OP.mult)
            v.tensor_scalar(s_v[0:1, 0:wdt], s_v[0:1, 0:wdt], 1.0 / E, None, OP.mult)
            v.drain()
            v.tensor_tensor(s_i[0:1, 0:wdt], s_m[0:1, 0:wdt], s_m[0:1, 0:wdt], OP.mult)
            v.drain()
            v.tensor_tensor(s_v[0:1, 0:wdt], s_v[0:1, 0:wdt], s_i[0:1, 0:wdt], OP.subtract)
            v.drain()
            v.tensor_scalar(s_v[0:1, 0:wdt], s_v[0:1, 0:wdt], 1e-5, None, OP.add)
        P.add("vector", f)
        vp_var = bump("vector", vp)

        def f(s, vp_var=vp_var, wdt=wdt):
            s.wait_ge(vp.h, vp_var)
            s.activation(out=s_i[0:1, 0:wdt], in_=s_v[0:1, 0:wdt], func=AT.Ln)
        P.add("scalar", f)
        sp_inv = bump("scalar", sp)

        def f(v, sp_inv=sp_inv, wdt=wdt):
            v.wait_ge(sp.h, sp_inv)
            v.tensor_scalar(s_i[0:1, 0:wdt], s_i[0:1, 0:wdt], -0.5, None, OP.mult)
        P.add("vector", f)
        vp_l2 = bump("vector", vp)

        def f(s, vp_l2=vp_l2, wdt=wdt):
            s.wait_ge(vp.h, vp_l2)
            s.activation(out=s_i[0:1, 0:wdt], in_=s_i[0:1, 0:wdt], func=AT.Exp)
        P.add("scalar", f)
        sp_inv2 = bump("scalar", sp)

        def f(v, sp_inv2=sp_inv2, wdt=wdt):
            v.wait_ge(sp.h, sp_inv2)
            v.tensor_tensor(s_v[0:1, 0:wdt], s_m[0:1, 0:wdt], s_i[0:1, 0:wdt], OP.mult)
        P.add("vector", f)
        vp_ms = bump("vector", vp)

        wait("gpsimd", vp, vp_ms)
        dma("gpsimd", h_stat[0:1, 0:wdt], s_i[0:1, 0:wdt], d_stat)
        w2 = dma("gpsimd", h_stat[1:2, 0:wdt], s_v[0:1, 0:wdt], d_stat)
        wait("gpsimd", d_stat, w2)
        dma("gpsimd", invb_[:, 0:wdt], h_stat[0:1, 0:wdt].broadcast_to([128, wdt]), d_stat)
        wbc = dma("gpsimd", msbb_[:, 0:wdt], h_stat[1:2, 0:wdt].broadcast_to([128, wdt]), d_stat)
        wait("vector", d_stat, wbc)

        def f(v):
            for kt in range(NKT):
                v.tensor_tensor(lnview(kt, c_lo, c_hi), lnview(kt, c_lo, c_hi), invb_[:, 0:wdt], OP.mult)
            v.drain()
            for kt in range(NKT):
                v.tensor_tensor(lnview(kt, c_lo, c_hi), lnview(kt, c_lo, c_hi), msbb_[:, 0:wdt], OP.subtract)
        P.add("vector", f)
        r = bump("vector", vp)
        barrier()
        stL.close()
        return r

    def emit_ffn(lnview, bufs, xview):
        (mixb, rbuf, kabuf, wfbuf, vstage) = bufs
        wait("gpsimd", ld, ld.n)
        for kt in range(NKT):
            dma("gpsimd", wfbuf[0][:, kt * 1024:(kt + 1) * 1024],
                wk_t[kt * 128:(kt + 1) * 128, :], ld)
        wk_done = ld.n

        def mix(mu_sb, mu1_sb):
            def f(v, mu_sb=mu_sb, mu1_sb=mu1_sb):
                for kt in range(NKT):
                    v.tensor_scalar(mixb[:, kt * TH:(kt + 1) * TH], lnview(kt, 1, TH + 1),
                                    mu_sb[:, kt:kt + 1], None, OP.mult)
                v.drain()
                for kt in range(NKT):
                    v.scalar_tensor_tensor(mixb[:, kt * TH:(kt + 1) * TH], lnview(kt, 0, TH),
                                           mu1_sb[:, kt:kt + 1], mixb[:, kt * TH:(kt + 1) * TH],
                                           OP.mult, OP.add)
            P.add("vector", f)
            return bump("vector", vp)

        mix(mu_k_sb, mu_k1_sb)

        def proj(wbuf_idx, store, rhs_buf):
            wait("tensor", vp, vp.n)
            pr_sph = {}
            for mt in range(NKT):
                for tc in range(4):
                    ipr = mt * 4 + tc
                    bank = ipr % 4
                    if ipr >= 4:
                        wait("tensor", sp, pr_sph[ipr - 4])

                    def f(t, mt=mt, tc=tc, bank=bank, wbuf_idx=wbuf_idx, rhs_buf=rhs_buf):
                        for kt in range(NKT):
                            t.matmul(pp[bank][:, :],
                                     wfbuf[wbuf_idx][:, (kt * NKT + mt) * 128:(kt * NKT + mt + 1) * 128],
                                     rhs_buf[:, kt * TH + tc * 512:kt * TH + (tc + 1) * 512],
                                     start=(kt == 0), stop=(kt == NKT - 1))
                    P.add("tensor", f)
                    tpm = bump("tensor", tp)

                    def f(s, mt=mt, tc=tc, bank=bank, tpm=tpm):
                        s.wait_ge(tp.h, tpm)
                        store(mt, tc, pp[bank], s)
                    P.add("scalar", f)
                    pr_sph[ipr] = bump("scalar", sp)

        wait("tensor", ld, wk_done)

        def store_k(mt, tc, ps, s):
            s.activation(out=kabuf[:, mt * TH + tc * 512:mt * TH + (tc + 1) * 512],
                         in_=ps[:, :], func=AT.Relu)
        proj(0, store_k, mixb)
        sp_k = sp.n
        tp_k = tp.n

        wait("gpsimd", ld, ld.n)
        for kt in range(NKT):
            dma("gpsimd", wfbuf[1][:, kt * 1024:(kt + 1) * 1024],
                wr_t[kt * 128:(kt + 1) * 128, :], ld)
        wr_done = ld.n

        wait("vector", sp, sp_k)

        def f(v):
            for kt in range(NKT):
                v.tensor_tensor(kabuf[:, kt * TH:(kt + 1) * TH], kabuf[:, kt * TH:(kt + 1) * TH],
                                kabuf[:, kt * TH:(kt + 1) * TH], OP.mult)
        P.add("vector", f)
        wait("vector", tp, tp_k)
        mix(mu_r_sb, mu_r1_sb)

        wait("gpsimd", tp, tp_k)
        wait("gpsimd", ld, ld.n)
        for kt in range(NKT):
            dma("gpsimd", wfbuf[0][:, kt * 1024:(kt + 1) * 1024],
                wv_t[kt * 128:(kt + 1) * 128, :], ld)
        wv_done = ld.n
        wait("tensor", ld, wr_done)
        wait("tensor", ld, wv_done)
        wait("tensor", vp, vp.n)
        vp_hist = {}
        itf = 0
        for mt in range(NKT):
            for tc in range(4):
                itf += 1
                rb = (2 * itf) % 4
                vb = (2 * itf + 1) % 4

                def f(t, mt=mt, tc=tc, rb=rb, vb=vb, g=vp_hist.get(itf - 2, 0)):
                    if g:
                        t.wait_ge(vp.h, g)
                    for kt in range(NKT):
                        t.matmul(pp[rb][:, :],
                                 wfbuf[1][:, (kt * NKT + mt) * 128:(kt * NKT + mt + 1) * 128],
                                 mixb[:, kt * TH + tc * 512:kt * TH + (tc + 1) * 512],
                                 start=(kt == 0), stop=(kt == NKT - 1))
                    for kt in range(NKT):
                        t.matmul(pp[vb][:, :],
                                 wfbuf[0][:, (kt * NKT + mt) * 128:(kt * NKT + mt + 1) * 128],
                                 kabuf[:, kt * TH + tc * 512:kt * TH + (tc + 1) * 512],
                                 start=(kt == 0), stop=(kt == NKT - 1))
                P.add("tensor", f)
                tpm = bump("tensor", tp)

                def f(s, rb=rb, tpm=tpm, g=vp_hist.get(itf - 1, 0)):
                    s.wait_ge(tp.h, tpm)
                    if g:
                        s.wait_ge(vp.h, g)
                    s.activation(out=rbuf, in_=pp[rb][:, :], func=AT.Sigmoid)
                P.add("scalar", f)
                spm = bump("scalar", sp)

                def f(v, mt=mt, tc=tc, vb=vb, spm=spm):
                    v.wait_ge(sp.h, spm)
                    v.tensor_tensor(vstage, pp[vb][:, :], rbuf, OP.mult)
                    v.drain()
                    v.scalar_tensor_tensor(xview(mt, 1 + tc * 512, 1 + (tc + 1) * 512),
                                           vstage, 0.5,
                                           xview(mt, 1 + tc * 512, 1 + (tc + 1) * 512),
                                           OP.mult, OP.add)
                P.add("vector", f)
                vp_hist[itf] = bump("vector", vp)
        bump("vector", vp)
        return vp_hist

    LN1_CH = [(0, 512), (512, 1024), (1024, 1536), (1536, 2048), (2048, 2049)]
    LN2_CH = [(1, 513), (513, 1025), (1025, 1537), (1537, 2049)]

    stA = ExitStack()
    eA = stA.enter_context
    lnb = eA(nc.sbuf_tensor("lnb", [128, NKT * XOW], BF16))

    def lnv(kt, c0, c1):
        return lnb[:, kt * XOW + c0: kt * XOW + c1]

    ffnsm = eA(nc.sbuf_tensor("ffnsm", [128, 1024], BF16))

    stG = ExitStack()
    g_sb = stG.enter_context(nc.sbuf_tensor("g_sb", [16, 7680], BF16))
    wstage = stG.enter_context(nc.sbuf_tensor("wstage", [128, 2048], BF16))
    go = {}
    _off = 0
    for nm, wd in [("g1in", 1024), ("g2in", 2048), ("g1x", 128), ("g2x", 768),
                   ("g1dt", 128), ("g2dt", 512), ("g1out", 512), ("g2out", 2048)]:
        go[nm] = (_off, _off + wd)
        _off += wd

    def gvw(nm):
        a, b_ = go[nm]
        return g_sb[:, a:b_]

    for nm, gsrc in [("g1in", g1in), ("g2in", g2in), ("g1x", g1x), ("g2x", g2x),
                     ("g1dt", g1dt), ("g2dt", g2dt), ("g1out", g1out), ("g2out", g2out)]:
        dma("gpsimd", gvw(nm), gsrc, ld)
    ld_g = ld.n
    wait("tensor", ld, ld_g)

    wcnt = [0]

    def build_wtmp(g1name, g2name, npt, fchunks, rowbase, dst_hbm):
        for pt in range(npt):
            for ci, (c0, c1) in enumerate(fchunks):
                wd = c1 - c0
                idx = wcnt[0]
                wcnt[0] += 1
                bank = idx % 2
                slot = idx % 4

                def f(t, pt=pt, c0=c0, wd=wd, bank=bank, g1name=g1name, g2name=g2name):
                    t.matmul(pp[4 + bank][:, 0:wd], gvw(g1name)[:, pt * 128:(pt + 1) * 128],
                             gvw(g2name)[:, c0:c0 + wd], start=True, stop=True)
                P.add("tensor", f)
                tpm = bump("tensor", tp)

                def f(s, tpm=tpm, wd=wd, slot=slot, bank=bank):
                    s.wait_ge(tp.h, tpm)
                    s.copy(wstage[:, slot * 512:slot * 512 + wd], pp[4 + bank][:, 0:wd])
                P.add("scalar", f)
                spm = bump("scalar", sp)
                wait("tensor", sp, spm)
                wait("gpsimd", sp, spm)
                wdm = dma("gpsimd", dst_hbm[rowbase + pt * 128: rowbase + (pt + 1) * 128, c0:c0 + wd],
                          wstage[:, slot * 512:slot * 512 + wd], d_wb[slot])
                wait("scalar", d_wb[slot], wdm)

    phase("wbuild")
    F4 = [(0, 512), (512, 1024), (1024, 1536), (1536, 2048)]
    build_wtmp("g1in", "g2in", 8, F4, 0, h_wtmp)
    build_wtmp("g1out", "g2out", 4, F4, 1024, h_wtmp)
    build_wtmp("g1x", "g2x", 1, [(0, 512), (512, 768)], 0, h_wtmpx)
    build_wtmp("g1dt", "g2dt", 1, [(0, 512)], 0, h_wtmpdt)

    phase("ln1")
    emit_ln(xov, lnv, LN1_CH)
    stG.close()

    stF = ExitStack()
    mixb = stF.enter_context(nc.sbuf_tensor("mixb", [128, NKT * TH], BF16))
    kabuf = stF.enter_context(nc.sbuf_tensor("kabuf", [128, NKT * TH], BF16))
    wfbuf = [stF.enter_context(nc.sbuf_tensor(f"wf{i}", [128, NKT * NKT * 128], BF16)) for i in range(2)]
    rbuf = ffnsm[:, 0:512]
    vstage = ffnsm[:, 512:1024]

    phase("ffn1")
    emit_ffn(lnv, (mixb, rbuf, kabuf, wfbuf, vstage), xov)
    barrier()
    stF.close()

    phase("ln2")
    emit_ln(xov, lnv, LN2_CH)
    wait("sync", vp, vp.n)
    for kt in range(NKT):
        dma("sync", h_ln2own[kt * 128:(kt + 1) * 128, :], lnv(kt, 1, TH + 1), lx)

    barrier()
    stA.close()

    phase("e1")
    for kt in range(NKT):
        dma("gpsimd", h_xsp[kt * 128:(kt + 1) * 128, :], xov(kt, 1, TH + 1), ld)

    def f(gp):
        gp.collective_compute(
            "AllGather", OP.bypass,
            replica_groups=[[0, 1], [2, 3], [4, 5], [6, 7]],
            ins=[h_ln2own], outs=[h_ln2full],
        ).then_inc(ccs.h, 1)
    P.add("gpsimd", f)
    cc1 = ccs.res()
    wait("gpsimd", ccs, cc1)
    barrier()
    stX.close()

    phase("front")
    stB = ExitStack()
    eB = stB.enter_context
    wi_sb = eB(nc.sbuf_tensor("wi_sb", [128, NKT * 2048], BF16))
    wx_sb = eB(nc.sbuf_tensor("wx_sb", [128, NKT * 96], BF16))
    slab = [eB(nc.sbuf_tensor(f"slab{i}", [128, NKT * 512], BF16)) for i in range(2)]
    XCW = TF + 3
    xcp = eB(nc.sbuf_tensor("xcp", [128, XCW], BF16))
    zb = eB(nc.sbuf_tensor("zb", [128, TF], BF16))
    cacc = eB(nc.sbuf_tensor("cacc", [128, TF], BF16))
    ctmp = eB(nc.sbuf_tensor("ctmp", [128, TF], BF16))
    convd = eB(nc.sbuf_tensor("convd", [128, TF], BF16))
    ytile = eB(nc.sbuf_tensor("ytile", [128, TF], BF16))
    cf_sb = eB(nc.sbuf_tensor("cf_sb", [128, NDT * 4], F32))
    cb_sb = eB(nc.sbuf_tensor("cb_sb", [128, NDT], F32))
    dtbc_sb = eB(nc.sbuf_tensor("dtbc_sb", [96, TF], BF16))

    for kt in range(NKT):
        for ai in range(4):
            a = 4 * kt + ai
            srcv = h_wtmp[a * 32: a * 32 + 32, :].rearrange("i (b j) -> b i j", j=64)
            dstv = wi_sb[ai * 32:(ai + 1) * 32, kt * 2048:(kt + 1) * 2048].rearrange(
                "b (i j) -> b i j", j=64)
            dma("gpsimd", dstv, srcv, ld)
    for kt in range(NKT):
        for ai in range(2):
            a_loc = 2 * kt + ai
            srcv = h_wtmpx[a_loc * 8: a_loc * 8 + 8, :].rearrange("i (b j) -> b i j", j=12)
            dstv = wx_sb[ai * 64:(ai + 1) * 64, kt * 96:(kt + 1) * 96].rearrange(
                "b (i j) -> b i j", j=12)
            dma("gpsimd", dstv, srcv, ld)
    dma("gpsimd", cf_sb[:, :].rearrange("p (a b) -> p a b", b=4), cfin.rearrange("(a p) b -> p a b", p=128), ld, slow=True)
    ld_wi = dma("gpsimd", cb_sb[:, :].rearrange("p (a b) -> p a b", b=1), cbin.rearrange("(a p) b -> p a b", p=128), ld, slow=True)
    wait("tensor", ld, ld_wi)

    vp_y_prev = 0
    ld_y_prev = 0
    ip_sph = {}
    ip_tph = {}
    for j in range(NDT):
        for tc in range(NCH):
            sl = tc % 2
            gi = j * NCH + tc
            if gi >= 2:
                wait("sync", tp, ip_tph[gi - 2])
            src = h_ln2full.rearrange("(s p) t -> s p t", s=2)[tc // 4, :, (tc % 4) * 512:(tc % 4 + 1) * 512]
            srcv = src.rearrange("(a p) t -> p a t", p=128)
            lxm = dma("sync", slab[sl][:, :].rearrange("p (a t) -> p a t", a=NKT), srcv, d_slab[sl])
            for which in range(2):
                mcol = (0 if which == 0 else 1024) + j * 128
                iw = gi * 2 + which
                bank = iw % 4
                if iw >= 4:
                    wait("tensor", sp, ip_sph[iw - 4])

                def f(t, mcol=mcol, bank=bank, sl=sl, lxm=lxm):
                    t.wait_ge(d_slab[sl].h, lxm)
                    for kt in range(NKT):
                        t.matmul(pp[bank][:, :],
                                 wi_sb[:, kt * 2048 + mcol: kt * 2048 + mcol + 128],
                                 slab[sl][:, kt * 512:(kt + 1) * 512],
                                 start=(kt == 0), stop=(kt == NKT - 1))
                P.add("tensor", f)
                tpm = bump("tensor", tp)

                def f(s, which=which, tc=tc, bank=bank, tpm=tpm, vp_y_prev=vp_y_prev):
                    s.wait_ge(tp.h, tpm)
                    if tc == 0 and which == 0 and vp_y_prev:
                        s.wait_ge(vp.h, vp_y_prev)
                    if which == 0:
                        s.copy(xcp[:, 1 + tc * 512: 1 + (tc + 1) * 512], pp[bank][:, :])
                    else:
                        s.copy(zb[:, tc * 512:(tc + 1) * 512], pp[bank][:, :])
                P.add("scalar", f)
                ip_sph[iw] = bump("scalar", sp)
            ip_tph[gi] = tp.n
        sp_xz = sp.n

        def f(v, j=j, sp_xz=sp_xz, ld_y_prev=ld_y_prev):
            v.wait_ge(sp.h, sp_xz)
            if ld_y_prev:
                v.wait_ge(d_y.h, ld_y_prev)
            v.memset(xcp[:, 0:1], 0.0)
            v.memset(xcp[:, TF + 1:TF + 3], 0.0)
            v.drain()
            v.tensor_scalar(cacc[:, :], xcp[:, 0:TF], cf_sb[:, j * 4:j * 4 + 1], None, OP.mult)
            v.tensor_scalar(ctmp[:, :], xcp[:, 1:1 + TF], cf_sb[:, j * 4 + 1:j * 4 + 2], None, OP.mult)
            v.tensor_scalar(ytile[:, :], xcp[:, 2:2 + TF], cf_sb[:, j * 4 + 2:j * 4 + 3], None, OP.mult)
            v.tensor_scalar(convd[:, :], xcp[:, 3:3 + TF], cf_sb[:, j * 4 + 3:j * 4 + 4], None, OP.mult)
            v.drain()
            v.tensor_tensor(cacc[:, :], cacc[:, :], ctmp[:, :], OP.add)
            v.tensor_tensor(ytile[:, :], ytile[:, :], convd[:, :], OP.add)
            v.drain()
            v.tensor_tensor(cacc[:, :], cacc[:, :], ytile[:, :], OP.add)
        P.add("vector", f)
        vp_conv = bump("vector", vp)

        def f(s, j=j, vp_conv=vp_conv):
            s.wait_ge(vp.h, vp_conv)
            s.activation(out=cacc[:, :], in_=cacc[:, :], func=AT.Silu, bias=cb_sb[:, j:j + 1])
        P.add("scalar", f)
        sp_silu = bump("scalar", sp)

        def f(v, sp_silu=sp_silu):
            v.wait_ge(sp.h, sp_silu)
            v.tensor_tensor(ytile[:, :], cacc[:, :], zb[:, :], OP.mult)
        P.add("vector", f)
        vp_y = bump("vector", vp)
        wait("gpsimd", vp, vp_y)
        ld_y = dma("gpsimd", h_y[j * 128:(j + 1) * 128, :], ytile[:, :], d_y)
        vp_y_prev = vp_y
        ld_y_prev = ld_y

    phase("xproj")
    wait("sync", d_y, d_y.n)
    xp_sph = {}
    xp_tph = {}
    for tc in range(NCH):
        sl = tc % 2
        if tc >= 2:
            wait("sync", tp, xp_tph[tc - 2])
        lxm = dma("sync", slab[sl][:, :].rearrange("p (a t) -> p a t", a=NDT),
                  h_y.rearrange("(a p) t -> p a t", p=128)[:, :, tc * 512:(tc + 1) * 512], d_slab[sl])
        bank = tc % 4

        def f(t, bank=bank, sl=sl, lxm=lxm):
            t.wait_ge(d_slab[sl].h, lxm)
            for kt in range(NDT):
                t.matmul(pp[bank][0:96, :],
                         wx_sb[:, kt * 96:(kt + 1) * 96],
                         slab[sl][:, kt * 512:(kt + 1) * 512],
                         start=(kt == 0), stop=(kt == NDT - 1))
        P.add("tensor", f)
        tpm = bump("tensor", tp)

        def f(s, tc=tc, bank=bank, tpm=tpm):
            s.wait_ge(tp.h, tpm)
            s.copy(dtbc_sb[:, tc * 512:(tc + 1) * 512], pp[bank][0:96, :])
        P.add("scalar", f)
        spm = bump("scalar", sp)
        xp_sph[tc] = spm
        xp_tph[tc] = tp.n
        if tc >= 3:
            wait("tensor", sp, xp_sph[tc - 3])
    wait("gpsimd", sp, sp.n)
    dma("gpsimd", h_dtbc_in[:, :], dtbc_sb[:, :], ld)

    barrier()
    stB.close()

    phase("e2")
    def f(gp):
        gp.collective_compute(
            "AllReduce", OP.add,
            replica_groups=[[0, 1], [2, 3], [4, 5], [6, 7]],
            ins=[h_dtbc_in], outs=[h_dtbc],
        ).then_inc(ccs.h, 1)
    P.add("gpsimd", f)
    cc2 = ccs.res()
    wait("gpsimd", ccs, cc2)
    bcr_done = dma("gpsimd", h_bcr.rearrange("c (r t) -> c r t", r=32),
                   h_dtbc[64:96, :].rearrange("r (c t) -> c r t", c=NCH), ld)
    barrier()

    phase("scanprep")
    stC = ExitStack()
    eC = stC.enter_context
    bbc2 = [eC(nc.sbuf_tensor(f"bbc{i}", [128, NST * TC], BF16)) for i in range(2)]
    cbc2 = [eC(nc.sbuf_tensor(f"cbc{i}", [128, NST * TC], BF16)) for i in range(2)]
    dab2 = [eC(nc.sbuf_tensor(f"dab{i}", [128, NST * TC], BF16)) for i in range(2)]
    bsb = eC(nc.sbuf_tensor("bsb", [128, NST * TC], BF16))
    hsb2 = [eC(nc.sbuf_tensor(f"hsb{i}", [128, NST * TC], BF16)) for i in range(2)]
    dtun2 = [eC(nc.sbuf_tensor(f"dtun{i}", [64, TC], BF16)) for i in range(2)]
    wdt_sb = eC(nc.sbuf_tensor("wdt_sb", [64, DH], BF16))
    dts2 = [eC(nc.sbuf_tensor(f"dts{i}", [128, TC], BF16)) for i in range(2)]
    dus2 = [eC(nc.sbuf_tensor(f"dus{i}", [128, TC], BF16)) for i in range(2)]
    usb = [eC(nc.sbuf_tensor(f"usb{i}", [128, TC], BF16)) for i in range(3)]
    yssb = [eC(nc.sbuf_tensor(f"yssb{i}", [128, TC], BF16)) for i in range(2)]
    fold16 = eC(nc.sbuf_tensor("fold16", [128, NST], BF16))
    hsrow = eC(nc.sbuf_tensor("hsrow", [1, 512], F32))
    hspad = eC(nc.sbuf_tensor("hspad", [1, 1032], F32))

    stW = ExitStack()
    wo_sb = stW.enter_context(nc.sbuf_tensor("wo_sb", [128, NKT * 1024], BF16))
    for kt in range(NKT):
        for ai in range(2):
            a_loc = 2 * kt + ai
            srcv = h_wtmp[1024 + a_loc * 32: 1024 + a_loc * 32 + 32, :].rearrange(
                "i (b j) -> b i j", j=32)
            dstv = wo_sb[ai * 64:(ai + 1) * 64, kt * 1024:(kt + 1) * 1024].rearrange(
                "b (i j) -> b i j", j=32)
            dma("gpsimd", dstv, srcv, ld)
    ld_wo = ld.n

    for a in range(8):
        srcv = h_wtmpdt[a * 16:a * 16 + 16, :].rearrange("i (b j) -> b i j", j=64)
        dstv = wdt_sb[a * 8:(a + 1) * 8, :].rearrange("b (i j) -> b i j", j=64)
        dma("gpsimd", dstv, srcv, d_stat)
    ds_all = d_stat.n

    NIT = NCH * NDT
    dt_ld = {}
    bc_ld = {}
    cc_ld = {}

    def bsrc_of(c):
        return h_bcr[c:c + 1, 0:NST * TC].broadcast_to([128, NST * TC])

    def csrc_of(c):
        return h_bcr[c:c + 1, NST * TC:2 * NST * TC].broadcast_to([128, NST * TC])

    dt_ld[0] = dma("gpsimd", dtun2[0][:, :], h_dtbc[0:64, 0:TC], d_dt)
    dt_ld[1] = dma("gpsimd", dtun2[1][:, :], h_dtbc[0:64, TC:2 * TC], d_dt)
    bc_ld[0] = dma("gpsimd", bbc2[0][:, :], bsrc_of(0), d_bc)
    cc_ld[0] = dma("gpsimd", cbc2[0][:, :], csrc_of(0), d_bc2)
    bc_ld[1] = dma("gpsimd", bbc2[1][:, :], bsrc_of(1), d_bc)
    cc_ld[1] = dma("gpsimd", cbc2[1][:, :], csrc_of(1), d_bc2)

    def f(v):
        v.memset(hspad[0:1, :], 0.0)
    P.add("vector", f)
    vp_prep = bump("vector", vp)

    wait("tensor", ld, ld.n)
    wait("tensor", vp, vp.n)
    for ec in range(2):
        def f(t, ec=ec):
            for kt in range(NKT):
                t.matmul(pp[6][0:1, :], ones_c[:, :],
                         wo_sb[:, kt * 1024 + ec * 512: kt * 1024 + (ec + 1) * 512],
                         start=(kt == 0), stop=(kt == NKT - 1))
        P.add("tensor", f)
        tpm = bump("tensor", tp)

        def f(s, ec=ec, tpm=tpm):
            s.wait_ge(tp.h, tpm)
            s.copy(hspad[0:1, 8 + ec * 512: 8 + (ec + 1) * 512], pp[6][0:1, :])
        P.add("scalar", f)
        spm = bump("scalar", sp)
        wait("tensor", sp, spm)
    stW.close()

    phase("scan")
    sp_sp = {}
    sp_da = {}
    tp_hs = {}
    vp_du = {}
    vp_c = {}
    vp_scanh = {}
    vp_ysh = {}
    tp_dt = {}
    ysw_h = {}
    u_ld = {}

    def jj(k):
        return k % NDT

    def cof(k):
        return k // NDT

    def f(t, g=ds_all, gd=dt_ld[1]):
        t.wait_ge(d_stat.h, g)
        t.wait_ge(d_dt.h, gd)
        t.matmul(pp[0][:, :], wdt_sb[:, 0:128], dtun2[0][:, :], start=True, stop=True)
    P.add("tensor", f)
    tp_dt[0] = bump("tensor", tp)

    def f(s, g=tp_dt[0]):
        s.wait_ge(tp.h, g)
        s.activation(out=spf[:, :], in_=pp[0][:, :], func=AT.Exp)
        s.drain()
        s.activation(out=dts2[0][:, :], in_=spf[:, :], func=AT.Ln, bias=1.0)
    P.add("scalar", f)
    sp_sp[0] = bump("scalar", sp)

    def f(t):
        t.matmul(pp[1][:, :], wdt_sb[:, 128:256], dtun2[0][:, :], start=True, stop=True)
    P.add("tensor", f)
    tp_dt[1] = bump("tensor", tp)

    u_ld[0] = dma("sync", usb[0][:, :], h_y[0:128, 0:TC], d_u[0])
    u_ld[1] = dma("sync", usb[1][:, :], h_y[128:256, 0:TC], d_u[1])

    for i in range(NIT + 1):
        p, q = i % 2, (i - 1) % 2
        k = i - 1
        if i < NIT:
            def f(v, i=i, p=p, gs=sp_sp[i], gu=u_ld[i]):
                v.wait_ge(sp.h, gs)
                v.wait_ge(d_u[i % 3].h, gu)
                v.tensor_tensor(dus2[p][:, :], dts2[p][:, :], usb[i % 3][:, :], OP.mult)
            P.add("vector", f)
            vp_du[i] = bump("vector", vp)
        if i >= 1:
            def f(v, k=k, kp=q, gsp=sp_da[k], gtp=tp_hs.get(k - 2, 0),
                  gbc=bc_ld[cof(k)], j=jj(k)):
                v.wait_ge(sp.h, gsp)
                if gtp:
                    v.wait_ge(tp.h, gtp)
                v.wait_ge(d_bc.h, gbc)
                da0 = dab2[kp][:, :].rearrange("p (n t) -> p n t", n=NST)[:, :, 0]
                bb0 = bsb[:, :].rearrange("p (n t) -> p n t", n=NST)[:, :, 0]
                v.tensor_tensor(fold16[:, :], da0, carry[:, j * NST:(j + 1) * NST], OP.mult)
                v.tensor_tensor(bsb[:, :].rearrange("p (n t) -> p n t", n=NST),
                                dus2[kp][:, :].unsqueeze(1).broadcast_to([128, NST, TC]),
                                bbc2[cof(k) % 2][:, :].rearrange("p (n t) -> p n t", n=NST), OP.mult)
                v.drain()
                v.memset(da0, 0.0)
                v.drain()
                v.tensor_tensor(bb0, bb0, fold16[:, :], OP.add)
                v.drain()
                v.tensor_tensor_scan(hsb2[k % 2][:, :], dab2[kp][:, :], bsb[:, :], 0.0, OP.mult, OP.add)
            P.add("vector", f)
            vp_scanh[k] = bump("vector", vp)

            def f(v, k=k, kp=q, gcc=cc_ld[cof(k)], j=jj(k)):
                v.wait_ge(d_bc2.h, gcc)
                hl = hsb2[k % 2][:, :].rearrange("p (n t) -> p n t", n=NST)[:, :, TC - 1]
                v.tensor_tensor(bsb[:, :], hsb2[k % 2][:, :], cbc2[cof(k) % 2][:, :], OP.mult)
                v.tensor_copy(carry[:, j * NST:(j + 1) * NST], hl)
            P.add("vector", f)
            vp_c[k] = bump("vector", vp)

            def f(v, k=k, kp=q, gys=ysw_h.get(k - 2, 0), j=jj(k)):
                if gys:
                    v.wait_ge(d_ysw[kp].h, gys)
                v.tensor_tensor(bsb[:, 0:8 * TC], bsb[:, 0:8 * TC], bsb[:, 8 * TC:16 * TC], OP.add)
                v.drain()
                v.tensor_tensor(bsb[:, 0:4 * TC], bsb[:, 0:4 * TC], bsb[:, 4 * TC:8 * TC], OP.add)
                v.drain()
                v.tensor_tensor(bsb[:, 0:2 * TC], bsb[:, 0:2 * TC], bsb[:, 2 * TC:4 * TC], OP.add)
                v.drain()
                v.tensor_tensor(yssb[kp][:, :], bsb[:, 0:TC], bsb[:, TC:2 * TC], OP.add)
                v.drain()
                v.scalar_tensor_tensor(yssb[kp][:, :], usb[k % 3][:, :], dcol[:, j:j + 1],
                                       yssb[kp][:, :], OP.mult, OP.add)
            P.add("vector", f)
            vp_ysh[k] = bump("vector", vp)

        if i < NIT:
            if i + 1 < NIT:
                def f(s, b=q, gt=tp_dt[i + 1], gd=vp_du.get(i - 1, 0)):
                    s.wait_ge(tp.h, gt)
                    if gd:
                        s.wait_ge(vp.h, gd)
                    s.activation(out=spf[:, :], in_=pp[b][:, :], func=AT.Exp)
                    s.drain()
                    s.activation(out=dts2[b][:, :], in_=spf[:, :], func=AT.Ln, bias=1.0)
                P.add("scalar", f)
                sp_sp[i + 1] = bump("scalar", sp)

            gv = vp_prep if i == 0 else vp_scanh.get(i - 2, 0)

            def f(s, i=i, p=p, gv=gv, j=jj(i)):
                if gv:
                    s.wait_ge(vp.h, gv)
                s.drain()
                for n in range(NST):
                    s.activation(out=dab2[p][:, n * TC:(n + 1) * TC], in_=dts2[p][:, :],
                                 func=AT.Exp, scale=acol[:, j * NST + n: j * NST + n + 1])
            P.add("scalar", f)
            sp_da[i] = bump("scalar", sp)

        if i >= 1:
            def f(t, k=k, gv=vp_scanh[k]):
                t.wait_ge(vp.h, gv)
                for n in range(NST):
                    t.matmul(pp[5][0:1, :], ones_c[:, :], hsb2[k % 2][:, n * TC:(n + 1) * TC],
                             start=(k == 0 and n == 0), stop=(k == NIT - 1 and n == NST - 1))
            P.add("tensor", f)
            tp_hs[k] = bump("tensor", tp)

        if i + 2 < NIT:
            wait("tensor", sp, sp_sp[i])
            c2 = cof(i + 2)

            def f(t, i=i, c2=c2, gd=dt_ld[max(c2, 1)], j=jj(i + 2)):
                t.wait_ge(d_dt.h, gd)
                t.matmul(pp[i % 2][:, :], wdt_sb[:, j * 128:(j + 1) * 128],
                         dtun2[c2 % 2][:, :], start=True, stop=True)
            P.add("tensor", f)
            tp_dt[i + 2] = bump("tensor", tp)

        if 2 <= i + 2 < NIT:
            wait("sync", vp, vp_ysh.get(i - 1, 0))
            u_ld[i + 2] = dma("sync", usb[(i + 2) % 3][:, :],
                              h_y[jj(i + 2) * 128:(jj(i + 2) + 1) * 128,
                                  cof(i + 2) * TC:(cof(i + 2) + 1) * TC], d_u[(i + 2) % 3])

        if i >= 1:
            if i % NDT == 0 and i // NDT >= 1 and i < NIT:
                c = i // NDT
                if c + 1 < NCH:
                    wait("gpsimd", vp, vp_scanh[i - 1])
                    bc_ld[c + 1] = dma("gpsimd", bbc2[(c + 1) % 2][:, :],
                                       bsrc_of(c + 1), d_bc)
                wait("gpsimd", vp, vp_ysh[i - 1])
                ysw_h[k] = dma("gpsimd", h_ys[jj(k) * 128:(jj(k) + 1) * 128,
                                              cof(k) * TC:(cof(k) + 1) * TC],
                               yssb[q][:, :], d_ysw[q])
                if c + 1 < NCH:
                    cc_ld[c + 1] = dma("gpsimd", cbc2[(c + 1) % 2][:, :],
                                       csrc_of(c + 1), d_bc2)
                    wait("gpsimd", tp, tp_dt[c * NDT - 1])
                    dt_ld[c + 1] = dma("gpsimd", dtun2[(c + 1) % 2][:, :],
                                       h_dtbc[0:64, (c + 1) * TC:(c + 2) * TC], d_dt)
            else:
                wait("gpsimd", vp, vp_ysh[i - 1])
                ysw_h[k] = dma("gpsimd", h_ys[jj(k) * 128:(jj(k) + 1) * 128,
                                              cof(k) * TC:(cof(k) + 1) * TC],
                               yssb[q][:, :], d_ysw[q])


    phase("hsfin")
    def f(s, g=tp_hs[NIT - 1]):
        s.wait_ge(tp.h, g)
        s.copy(hsrow[0:1, :], pp[5][0:1, :])
    P.add("scalar", f)
    sp_hsr = bump("scalar", sp)

    def f(v, sp_hsr=sp_hsr):
        v.wait_ge(sp.h, sp_hsr)
        v.tensor_reduce(hspad[0:1, 0:1], hsrow[0:1, :], AX.X, OP.add)
    P.add("vector", f)
    vp_hsr = bump("vector", vp)
    wait("gpsimd", vp, vp_hsr)
    dma("gpsimd", h_hs_in[:, :], hspad[0:1, :], ld)

    barrier()
    stC.close()

    phase("e4")
    def f(gp):
        gp.collective_compute(
            "AllReduce", OP.add,
            replica_groups=[[0, 1, 2, 3, 4, 5, 6, 7]],
            ins=[h_hs_in], outs=[h_hs],
        ).then_inc(ccs.h, 1)
    P.add("gpsimd", f)
    cc4 = ccs.res()

    phase("outproj")
    stE = ExitStack()
    eE = stE.enter_context
    xo2 = eE(nc.sbuf_tensor("xo2", [128, NKT * XOW], F32))
    stW2 = ExitStack()
    wo_sb2 = stW2.enter_context(nc.sbuf_tensor("wo_sb2", [128, NKT * 1024], BF16))

    def xov2(kt, c0, c1):
        return xo2[:, kt * XOW + c0: kt * XOW + c1]

    stD = ExitStack()
    eD = stD.enter_context
    yslab = [eD(nc.sbuf_tensor(f"yslab{i}", [128, NKT * 128], BF16)) for i in range(2)]
    opst = [eD(nc.sbuf_tensor(f"opst{i}", [128, 512], BF16)) for i in range(2)]

    for kt in range(NKT):
        for ai in range(2):
            a_loc = 2 * kt + ai
            srcv = h_wtmp[1024 + a_loc * 32: 1024 + a_loc * 32 + 32, :].rearrange(
                "i (b j) -> b i j", j=32)
            dstv = wo_sb2[ai * 64:(ai + 1) * 64, kt * 1024:(kt + 1) * 1024].rearrange(
                "b (i j) -> b i j", j=32)
            dma("gpsimd", dstv, srcv, ld)
    ld_wo2 = ld.n
    wait("tensor", ld, ld_wo2)
    for kt in range(NKT):
        dma("scalar", xov2(kt, 1, TH + 1), h_xsp[kt * 128:(kt + 1) * 128, :], lx)
    xsp_ld = lx.n

    op_w_hist = {}
    op_sph = {}
    op_dwh = {}
    for tt in range(TF // 128):
        sl = tt % 2
        if tt >= 2:
            wait("sync", tp, op_w_hist[tt - 2])
        lxm = dma("sync", yslab[sl][:, :].rearrange("p (a t) -> p a t", a=NKT),
                  h_ys.rearrange("(a p) t -> p a t", p=128)[:, :, tt * 128:(tt + 1) * 128], d_slab[sl])
        for ec in range(2):
            bank = (tt * 2 + ec) % 4

            def f(t, tt=tt, ec=ec, bank=bank, sl=sl, lxm=lxm):
                t.wait_ge(d_slab[sl].h, lxm)
                for kt in range(NKT):
                    t.matmul(pp[bank][:, :],
                             yslab[sl][:, kt * 128:(kt + 1) * 128],
                             wo_sb2[:, kt * 1024 + ec * 512:(kt) * 1024 + (ec + 1) * 512],
                             start=(kt == 0), stop=(kt == NKT - 1))
            P.add("tensor", f)
            tpm = bump("tensor", tp)
            if ec == 1:
                op_w_hist[tt] = tpm

            def f(s, ec=ec, bank=bank, tpm=tpm, st_i=(tt * 2 + ec) % 2,
                  g=op_dwh.get(tt * 2 + ec - 2, 0), gs=(tt * 2 + ec) % 2):
                s.wait_ge(tp.h, tpm)
                if g:
                    s.wait_ge(d_op[gs].h, g)
                s.copy(opst[st_i][:, :], pp[bank][:, :])
            P.add("scalar", f)
            spm = bump("scalar", sp)
            op_sph[tt * 2 + ec] = spm
            if tt * 2 + ec >= 3:
                wait("tensor", sp, op_sph[tt * 2 + ec - 3])
            wait("gpsimd", sp, spm)
            stb = (tt * 2 + ec) % 2
            opw = dma("gpsimd", h_op[tt * 128:(tt + 1) * 128, ec * 512:(ec + 1) * 512],
                      opst[stb][:, :], d_op[stb])
            op_dwh[tt * 2 + ec] = opw

    barrier()

    phase("rs")
    def f(gp):
        gp.collective_compute(
            "ReduceScatter", OP.add,
            replica_groups=[[0, 1], [2, 3], [4, 5], [6, 7]],
            ins=[h_op], outs=[h_od],
        ).then_inc(ccs.h, 1)
    P.add("gpsimd", f)
    cc_rs = ccs.res()
    wait("gpsimd", ccs, cc_rs)
    wait("gpsimd", ccs, cc4)
    barrier()
    stD.close()
    stW2.close()

    phase("grpE_prep")
    lnb3 = eE(nc.sbuf_tensor("lnb3", [128, NKT * XOW], BF16))

    def lnv3(kt, c0, c1):
        return lnb3[:, kt * XOW + c0: kt * XOW + c1]

    ffnsm3 = eE(nc.sbuf_tensor("ffnsm3", [128, 1024], BF16))
    halc = eE(nc.sbuf_tensor("halc", [128, NKT], F32))
    stT = ExitStack()
    hrow = stT.enter_context(nc.sbuf_tensor("hrow", [1, 1032], F32))
    mrow = stT.enter_context(nc.sbuf_tensor("mrow", [1, E], F32))
    mbc = stT.enter_context(nc.sbuf_tensor("mbc", [128, NKT], F32))
    dodb = [stT.enter_context(nc.sbuf_tensor(f"dodb{i}", [128, TH], BF16)) for i in range(2)]

    ld_hrow = dma("gpsimd", hrow[0:1, :], h_hs[0:1, :], ld)

    def f(v, ld_hrow=ld_hrow):
        v.wait_ge(ld.h, ld_hrow)
        v.tensor_scalar(hrow[0:1, 0:1], hrow[0:1, 0:1], 1e-8 / (4.0 * HS_COUNT), None, OP.mult)
        v.drain()
        v.tensor_scalar(mrow[0:1, :], hrow[0:1, 8:8 + E], hrow[0:1, 0:1], None, OP.mult)
    P.add("vector", f)
    vp_mb = bump("vector", vp)
    wait("gpsimd", vp, vp_mb)
    wmb = dma("gpsimd", h_mb[:, :], mrow[0:1, :], ld)
    wait("gpsimd", ld, wmb)
    for kt in range(NKT):
        dma("gpsimd", mbc[:, kt:kt + 1],
            h_mb[0:1, kt * 128:(kt + 1) * 128].rearrange("a b -> b a"), ld, slow=True)
    ld_mbc = ld.n

    dod_hist = {}
    for et in range(NKT):
        sl = et % 2
        if et >= 2:
            wait("sync", vp, dod_hist[et - 2])
        v_tr = d_dod[sl].res()

        def f(eng, sl=sl, et=et):
            eng.dma_start_transpose(dodb[sl][:, :],
                                    h_od[:, et * 128:(et + 1) * 128]).then_inc(d_dod[sl].h, 16)
        P.add("sync", f)

        def f(v, sl=sl, et=et, v_tr=v_tr, ld_mbc=ld_mbc):
            v.wait_ge(d_dod[sl].h, v_tr)
            if et == 0:
                v.wait_ge(ld.h, ld_mbc)
                v.wait_ge(lx.h, xsp_ld)
            v.scalar_tensor_tensor(xov2(et, 1, TH + 1), dodb[sl][:, :],
                                   mbc[:, et:et + 1], xov2(et, 1, TH + 1),
                                   OP.add, OP.add)
        P.add("vector", f)
        dod_hist[et] = bump("vector", vp)
    barrier()
    stT.close()

    phase("ln3")
    emit_ln(xov2, lnv3, LN2_CH)

    phase("e5")
    def f(v):
        for kt in range(NKT):
            v.tensor_scalar(halc[:, kt:kt + 1], lnv3(kt, TH, TH + 1),
                            mskp_sb[:, kt:kt + 1], None, OP.mult)
    P.add("vector", f)
    vp_pl = bump("vector", vp)
    wait("gpsimd", vp, vp_pl)
    wait("gpsimd", ld, ld.n)
    wpl = dma("gpsimd", h_halo_in.rearrange("(a p) b -> p a b", p=128), halc[:, :].rearrange("p (a b) -> p a b", b=1), ld, slow=True)
    wait("gpsimd", ld, wpl)

    def f(gp):
        gp.collective_compute(
            "AllReduce", OP.add,
            replica_groups=[[0, 1], [2, 3], [4, 5], [6, 7]],
            ins=[h_halo_in], outs=[h_halo],
        ).then_inc(ccs.h, 1)
    P.add("gpsimd", f)
    cc5 = ccs.res()
    wait("gpsimd", ccs, cc5)
    ld_hal = dma("gpsimd", halc[:, :].rearrange("p (a b) -> p a b", b=1), h_halo.rearrange("(a p) b -> p a b", p=128), ld, slow=True)
    wait("vector", ld, ld_hal)

    def f(v):
        for kt in range(NKT):
            v.tensor_scalar(lnv3(kt, 0, 1), halc[:, kt:kt + 1],
                            msku_sb[:, kt:kt + 1], None, OP.mult)
    P.add("vector", f)
    bump("vector", vp)

    phase("ffn3")
    stF3 = ExitStack()
    mixb3 = stF3.enter_context(nc.sbuf_tensor("mixb3", [128, NKT * TH], BF16))
    kabuf3 = stF3.enter_context(nc.sbuf_tensor("kabuf3", [128, NKT * TH], BF16))
    wfbuf3 = [stF3.enter_context(nc.sbuf_tensor(f"wf3{i}", [128, NKT * NKT * 128], BF16)) for i in range(2)]
    rbuf3 = ffnsm3[:, 0:512]
    vstage3 = ffnsm3[:, 512:1024]
    vph3 = emit_ffn(lnv3, (mixb3, rbuf3, kabuf3, wfbuf3, vstage3), xov2)
    stF3.close()

    phase("outstore")
    for kt in range(NKT):
        en = "sync" if kt % 2 == 0 else "gpsimd"
        cnt = lx if kt % 2 == 0 else ld
        wait(en, vp, vph3[kt * 4 + 4])
        dma(en, out_t[kt * 128:(kt + 1) * 128, :], xov2(kt, 1, TH + 1), cnt)
    wait("sync", lx, lx.n)
    wait("gpsimd", ld, ld.n)
    barrier()
    stE.close()

    phase_end()
    block = e(nc.Block())
    P.emit(block)
    st.close()
    return nc


def _host_maps(inputs):
    x = np.asarray(inputs["L0_attn_x"])
    in_g1 = np.asarray(inputs["L0_attn_in_g1"], np.float32)
    in_g2 = np.asarray(inputs["L0_attn_in_g2"], np.float32)
    x_g1 = np.asarray(inputs["L0_attn_x_g1"], np.float32)
    x_g2 = np.asarray(inputs["L0_attn_x_g2"], np.float32)
    dt_g1 = np.asarray(inputs["L0_attn_dt_g1"], np.float32)
    dt_g2 = np.asarray(inputs["L0_attn_dt_g2"], np.float32)
    out_g1 = np.asarray(inputs["L0_attn_out_g1"], np.float32)
    out_g2 = np.asarray(inputs["L0_attn_out_g2"], np.float32)
    cf = np.asarray(inputs["L0_attn_conv_filter"], np.float32)
    cb = np.asarray(inputs["L0_attn_conv_bias"], np.float32)
    alog_a = np.asarray(inputs["L0_attn_A_log"], np.float32)
    dv = np.asarray(inputs["L0_attn_D"], np.float32)
    wr = np.asarray(inputs["L0_ffn_W_r"], np.float32)
    wk = np.asarray(inputs["L0_ffn_W_k"], np.float32)
    wv = np.asarray(inputs["L0_ffn_W_v"], np.float32)
    mur = np.asarray(inputs["L0_ffn_mu_r"], np.float32)
    muk = np.asarray(inputs["L0_ffn_mu_k"], np.float32)

    C = np.ascontiguousarray
    wr_t = C(wr.T)
    wk_t = C(wk.T)
    wv_t = C(wv.T)
    g2in_m = C(in_g2.reshape(16, 2048))
    g2x_m = C(x_g2.reshape(16, 768))
    g2dt_m = C(dt_g2.reshape(16, 512))
    g2out_m = C(out_g2.reshape(16, 2048))

    maps = []
    for c in range(8):
        b, h = c // 2, c % 2
        xs = x[b, h * TH:(h + 1) * TH, :].T.astype(np.float32)
        halo = (x[b, h * TH - 1:h * TH, :].T.astype(np.float32) if h == 1
                else np.zeros((E, 1), np.float32))
        xin_m = C(np.concatenate([halo, xs], axis=1))

        iperm = list(range(h * 16, (h + 1) * 16)) + list(range(32 + h * 16, 32 + (h + 1) * 16))
        g1in_m = C(in_g1[:, iperm, :].reshape(32 * 32, 16).T)
        aperm = list(range(h * 16, (h + 1) * 16))
        g1x_m = C(x_g1[aperm].reshape(16 * 8, 16).T)
        ipermd = list(range(h * 16, (h + 1) * 16))
        g1dt_m = C(dt_g1[:, ipermd, :].reshape(8 * 16, 16).T)
        g1out_m = C(out_g1[aperm].reshape(16 * 32, 16).T)

        dh = slice(h * DH, (h + 1) * DH)
        maps.append(dict(
            xin=xin_m,
            wr_t=wr_t, wk_t=wk_t, wv_t=wv_t,
            mu_r=C(mur.reshape(E, 1)), mu_k=C(muk.reshape(E, 1)),
            g1in=g1in_m, g2in=g2in_m,
            g1x=g1x_m, g2x=g2x_m,
            g1dt=g1dt_m, g2dt=g2dt_m,
            g1out=g1out_m, g2out=g2out_m,
            cfin=C(cf[:, 0, dh].T),
            cbin=C(cb[dh].reshape(DH, 1)),
            alog=C(alog_a[dh]),
            dvin=C(dv[dh].reshape(DH, 1)),
            maskp=np.full((E, 1), 1.0 - h, np.float32),
            masku=np.full((E, 1), float(h), np.float32),
        ))
    return maps


def kernel(**inputs):
    maps = _host_maps(inputs)
    nc = build_nc()
    res = run_bass_kernel_spmd(nc, maps, core_ids=list(range(8)))
    out = np.empty((4, TF, E), np.float32)
    for c in range(8):
        b, h = c // 2, c % 2
        out[b, h * TH:(h + 1) * TH, :] = res.results[c]["out"].T
    return out


# version 15
# speedup vs baseline: 1.4905x; 1.0223x over previous
import numpy as np
from contextlib import ExitStack

import concourse.bass as bass
import concourse.mybir as mybir
from concourse.bass_utils import run_bass_kernel_spmd
from concourse.aot_env import allow_ici_all_to_all

F32 = mybir.dt.float32
BF16 = mybir.dt.bfloat16
AT = mybir.ActivationFunctionType
OP = mybir.AluOpType
AX = mybir.AxisListType

E = 1024
DIN = 2048
DH = 1024
NST = 16
TF = 4096
TH = 2048
TC = 512
NCH = TF // TC
NDT = DH // 128
NKT = E // 128
XOW = TH + 1
HS_COUNT = 4.0 * TF * DIN * NST

ENGS = ["sync", "gpsimd", "vector", "scalar", "tensor"]


class Cnt:
    def __init__(self, h, step=1):
        self.h = h
        self.n = 0
        self.step = step

    def res(self, k=None):
        self.n += self.step if k is None else k
        return self.n


class Prog:
    def __init__(self):
        self.q = {en: [] for en in ENGS}

    def add(self, en, f):
        self.q[en].append(f)

    def emit(self, block):
        for en in ENGS:
            fs = self.q[en]

            def runner(eng, fs=fs):
                for f in fs:
                    f(eng)
            getattr(block, en)(runner)


def build_nc():
    nc = bass.Bass("TRN2", target_bir_lowering=False)

    xin = nc.dram_tensor("xin", [E, XOW], F32, kind="ExternalInput").ap()
    wr_t = nc.dram_tensor("wr_t", [E, E], F32, kind="ExternalInput").ap()
    wk_t = nc.dram_tensor("wk_t", [E, E], F32, kind="ExternalInput").ap()
    wv_t = nc.dram_tensor("wv_t", [E, E], F32, kind="ExternalInput").ap()
    mu_r = nc.dram_tensor("mu_r", [E, 1], F32, kind="ExternalInput").ap()
    mu_k = nc.dram_tensor("mu_k", [E, 1], F32, kind="ExternalInput").ap()
    g1in = nc.dram_tensor("g1in", [16, 1024], F32, kind="ExternalInput").ap()
    g2in = nc.dram_tensor("g2in", [16, 2048], F32, kind="ExternalInput").ap()
    g1x = nc.dram_tensor("g1x", [16, 128], F32, kind="ExternalInput").ap()
    g2x = nc.dram_tensor("g2x", [16, 768], F32, kind="ExternalInput").ap()
    g1dt = nc.dram_tensor("g1dt", [16, 128], F32, kind="ExternalInput").ap()
    g2dt = nc.dram_tensor("g2dt", [16, 512], F32, kind="ExternalInput").ap()
    g1out = nc.dram_tensor("g1out", [16, 512], F32, kind="ExternalInput").ap()
    g2out = nc.dram_tensor("g2out", [16, 2048], F32, kind="ExternalInput").ap()
    cfin = nc.dram_tensor("cfin", [DH, 4], F32, kind="ExternalInput").ap()
    cbin = nc.dram_tensor("cbin", [DH, 1], F32, kind="ExternalInput").ap()
    alog = nc.dram_tensor("alog", [DH, NST], F32, kind="ExternalInput").ap()
    dvin = nc.dram_tensor("dvin", [DH, 1], F32, kind="ExternalInput").ap()
    maskp = nc.dram_tensor("maskp", [E, 1], F32, kind="ExternalInput").ap()
    masku = nc.dram_tensor("masku", [E, 1], F32, kind="ExternalInput").ap()
    out_t = nc.dram_tensor("out", [E, TH], F32, kind="ExternalOutput").ap()

    h_wtmp = nc.dram_tensor("h_wtmp", [2048, 2048], BF16).ap()
    h_wtmpx = nc.dram_tensor("h_wtmpx", [128, 768], BF16).ap()
    h_wtmpdt = nc.dram_tensor("h_wtmpdt", [128, 512], BF16).ap()
    h_ln2own = nc.dram_tensor("h_ln2own", [E, TH], BF16).ap()
    h_ln2full = nc.dram_tensor("h_ln2full", [2 * E, TH], BF16).ap()
    h_y = nc.dram_tensor("h_y", [DH, TF], BF16).ap()
    h_dtbc_in = nc.dram_tensor("h_dtbc_in", [96, TF], BF16).ap()
    h_dtbc = nc.dram_tensor("h_dtbc", [96, TF], BF16).ap()
    h_ys = nc.dram_tensor("h_ys", [DH, TF], BF16).ap()
    h_op = nc.dram_tensor("h_op", [TF, E], BF16).ap()
    h_od = nc.dram_tensor("h_od", [TH, E], BF16).ap()
    h_hs_in = nc.dram_tensor("h_hs_in", [1, 1032], F32).ap()
    h_hs = nc.dram_tensor("h_hs", [1, 1032], F32).ap()
    h_halo_in = nc.dram_tensor("h_halo_in", [E, 1], F32).ap()
    h_halo = nc.dram_tensor("h_halo", [E, 1], F32).ap()
    h_stat = nc.dram_tensor("h_stat", [2, XOW], BF16).ap()
    h_mb = nc.dram_tensor("h_mb", [1, E], F32).ap()
    h_xsp = nc.dram_tensor("h_xsp", [E, TH], F32).ap()
    h_bcr = nc.dram_tensor("h_bcr", [NCH, 32 * TC], BF16).ap()

    st = ExitStack()
    e = st.enter_context

    BARs = e(nc.semaphore("BAR"))
    bar_n = [0]
    ld = Cnt(e(nc.semaphore("ld")), 16)
    lx = Cnt(e(nc.semaphore("lx")), 16)
    tp = Cnt(e(nc.semaphore("tp")), 1)
    sp = Cnt(e(nc.semaphore("sp")), 1)
    vp = Cnt(e(nc.semaphore("vp")), 1)
    ccs = Cnt(e(nc.semaphore("ccs")), 1)
    d_slab = [Cnt(e(nc.semaphore(f"dslab{i}")), 16) for i in range(2)]
    d_u = [Cnt(e(nc.semaphore(f"du{i}")), 16) for i in range(3)]
    d_ysw = [Cnt(e(nc.semaphore(f"dysw{i}")), 16) for i in range(2)]
    d_op = [Cnt(e(nc.semaphore(f"dop{i}")), 16) for i in range(2)]
    d_dod = [Cnt(e(nc.semaphore(f"ddod{i}")), 16) for i in range(2)]
    d_wb = [Cnt(e(nc.semaphore(f"dwb{i}")), 16) for i in range(4)]
    d_bc = Cnt(e(nc.semaphore("dbc")), 16)
    d_bc2 = Cnt(e(nc.semaphore("dbc2")), 16)
    d_dt = Cnt(e(nc.semaphore("ddt")), 16)
    d_y = Cnt(e(nc.semaphore("dy")), 16)
    d_stat = Cnt(e(nc.semaphore("dstat")), 16)

    SYNC_SEMS = [lx] + d_slab + d_u + d_dod
    GP_SEMS = [ld, d_bc, d_bc2, d_dt, d_y, d_stat] + d_ysw + d_op + d_wb

    P = Prog()

    def barrier():
        bar_n[0] += 1
        tgt = 5 * bar_n[0]
        waits = {"sync": [(c, c.n) for c in SYNC_SEMS],
                 "gpsimd": [(c, c.n) for c in GP_SEMS],
                 "vector": [(vp, vp.n)],
                 "scalar": [(sp, sp.n)],
                 "tensor": [(tp, tp.n)]}
        for en in ENGS:
            ws = waits[en]

            def f(eng, ws=ws, tgt=tgt):
                for c, v in ws:
                    if v > 0:
                        eng.wait_ge(c.h, v)
                eng.sem_inc(BARs, 1)
                eng.wait_ge(BARs, tgt)
            P.add(en, f)

    def dma(en, dst, src, cnt, slow=False):
        v = cnt.res()

        def f(eng, dst=dst, src=src, slow=slow):
            if slow:
                with nc.allow_non_contiguous_dma(reason="tiny gather"):
                    eng.dma_start(out=dst, in_=src).then_inc(cnt.h, cnt.step)
            else:
                eng.dma_start(out=dst, in_=src).then_inc(cnt.h, cnt.step)
        P.add(en, f)
        return v

    def wait(en, cnt, v):
        if v <= 0:
            return

        def f(eng, cnt=cnt, v=v):
            eng.wait_ge(cnt.h, v)
        P.add(en, f)

    cur_scope = {en: None for en in ENGS}

    def phase(name):
        for en in ENGS:
            def f(eng, en=en, name=name):
                if cur_scope[en] is not None:
                    pn, pid = cur_scope[en]
                    nc.leave_named_scope(pn, pid, False)
                sid, _ = nc.enter_named_scope(name, False)
                cur_scope[en] = (name, sid)
            P.add(en, f)

    def phase_end():
        for en in ENGS:
            def f(eng, en=en):
                if cur_scope[en] is not None:
                    pn, pid = cur_scope[en]
                    nc.leave_named_scope(pn, pid, False)
                    cur_scope[en] = None
            P.add(en, f)

    def bump(en, cnt):
        def f(eng, cnt=cnt):
            eng.maybe_drain_then_inc((cnt.h, cnt.step))
        P.add(en, f)
        return cnt.res()

    mu_r_sb = e(nc.sbuf_tensor("mu_r_sb", [128, NKT], F32))
    mu_k_sb = e(nc.sbuf_tensor("mu_k_sb", [128, NKT], F32))
    mu_r1_sb = e(nc.sbuf_tensor("mu_r1_sb", [128, NKT], F32))
    mu_k1_sb = e(nc.sbuf_tensor("mu_k1_sb", [128, NKT], F32))
    msku_sb = e(nc.sbuf_tensor("msku_sb", [128, NKT], F32))
    mskp_sb = e(nc.sbuf_tensor("mskp_sb", [128, NKT], F32))
    ones_c = e(nc.sbuf_tensor("ones_c", [128, 1], BF16))
    ones_f = e(nc.sbuf_tensor("ones_f", [128, 1], F32))
    pp = [e(nc.psum_tensor(f"pp{i}", [128, 512], F32)) for i in range(8)]
    spf = e(nc.sbuf_tensor("spf", [128, TC], F32))
    acol = e(nc.sbuf_tensor("acol", [128, NDT * NST], F32))
    dcol = e(nc.sbuf_tensor("dcol", [128, NDT], F32))
    carry = e(nc.sbuf_tensor("carry", [128, NDT * NST], BF16))

    stX = ExitStack()
    xo = stX.enter_context(nc.sbuf_tensor("xo", [128, NKT * XOW], F32))

    def xov(kt, c0, c1):
        return xo[:, kt * XOW + c0: kt * XOW + c1]

    phase("p0_load")
    for kt in range(NKT):
        dma("sync", xov(kt, 0, XOW), xin[kt * 128:(kt + 1) * 128, :], lx)
    dma("gpsimd", mu_r_sb[:, :].rearrange("p (a b) -> p a b", b=1), mu_r.rearrange("(a p) b -> p a b", p=128), ld, slow=True)
    dma("gpsimd", mu_k_sb[:, :].rearrange("p (a b) -> p a b", b=1), mu_k.rearrange("(a p) b -> p a b", p=128), ld, slow=True)
    dma("gpsimd", msku_sb[:, :].rearrange("p (a b) -> p a b", b=1), masku.rearrange("(a p) b -> p a b", p=128), ld, slow=True)
    dma("gpsimd", mskp_sb[:, :].rearrange("p (a b) -> p a b", b=1), maskp.rearrange("(a p) b -> p a b", p=128), ld, slow=True)

    dma("gpsimd", spf[:, 0:NDT * NST].rearrange("p (a n) -> p a n", a=NDT),
        alog.rearrange("(a p) n -> p a n", p=128), ld)
    dma("gpsimd", dcol[:, :].rearrange("p (a b) -> p a b", b=1),
        dvin.rearrange("(a p) b -> p a b", p=128), ld, slow=True)
    ld_mu = ld.n

    def f(s, ld_mu=ld_mu):
        s.wait_ge(ld.h, ld_mu)
        s.activation(out=spf[:, 0:NDT * NST], in_=spf[:, 0:NDT * NST], func=AT.Exp)
    P.add("scalar", f)
    sp_aexp0 = bump("scalar", sp)

    def f(v, ld_mu=ld_mu, sp_aexp0=sp_aexp0):
        v.memset(ones_c[:, :], 1.0)
        v.memset(ones_f[:, :], 1.0)
        v.memset(carry[:, :], 0.0)
        v.wait_ge(ld.h, ld_mu)
        v.tensor_scalar(mu_r1_sb[:, :], mu_r_sb[:, :], -1.0, 1.0, OP.mult, OP.add)
        v.tensor_scalar(mu_k1_sb[:, :], mu_k_sb[:, :], -1.0, 1.0, OP.mult, OP.add)
        v.wait_ge(sp.h, sp_aexp0)
        v.tensor_scalar(acol[:, :], spf[:, 0:NDT * NST], -1.0, None, OP.mult)
    P.add("vector", f)
    bump("vector", vp)

    barrier()

    def emit_ln(xview, lnview, chunks):
        stL = ExitStack()
        sqbuf = stL.enter_context(nc.sbuf_tensor(None, [128, 2 * 512], BF16))
        s_m = stL.enter_context(nc.sbuf_tensor(None, [1, XOW], F32))
        s_v = stL.enter_context(nc.sbuf_tensor(None, [1, XOW], F32))
        s_i = stL.enter_context(nc.sbuf_tensor(None, [1, XOW], F32))
        invb_ = stL.enter_context(nc.sbuf_tensor(None, [128, XOW], BF16))
        msbb_ = stL.enter_context(nc.sbuf_tensor(None, [128, XOW], BF16))
        c_lo = chunks[0][0]
        c_hi = chunks[-1][1]
        wdt = c_hi - c_lo
        wait("scalar", vp, vp.n)

        def f(s):
            for kt in range(NKT):
                s.activation(out=lnview(kt, c_lo, c_hi), in_=xview(kt, c_lo, c_hi), func=AT.Copy)
        P.add("scalar", f)
        sp_cast = bump("scalar", sp)
        wait("vector", sp, sp_cast)

        tp_pair = {}
        gix = 0
        for (c0, c1) in chunks:
            wd = c1 - c0
            for kt in range(NKT):
                gix += 1
                def f(v, kt=kt, c0=c0, c1=c1, wd=wd, tpg=tp_pair.get(gix - 2, 0)):
                    if tpg:
                        v.wait_ge(tp.h, tpg)
                    v.tensor_tensor(sqbuf[:, (gix % 2) * 512:(gix % 2) * 512 + wd] if False else
                                    sqbuf[:, (kt % 2) * 512:(kt % 2) * 512 + wd],
                                    lnview(kt, c0, c1), lnview(kt, c0, c1), OP.mult)
                P.add("vector", f)
                vpm = bump("vector", vp)

                def f(t, kt=kt, c0=c0, c1=c1, wd=wd, vpm=vpm):
                    t.wait_ge(vp.h, vpm)
                    t.matmul(pp[6][0:1, 0:wd], ones_c[:, :], lnview(kt, c0, c1),
                             start=(kt == 0), stop=(kt == NKT - 1))
                    t.matmul(pp[7][0:1, 0:wd], ones_c[:, :],
                             sqbuf[:, (kt % 2) * 512:(kt % 2) * 512 + wd],
                             start=(kt == 0), stop=(kt == NKT - 1))
                P.add("tensor", f)
                tp_pair[gix] = bump("tensor", tp)
            tpm = tp_pair[gix]

            def f(s, c0=c0, c1=c1, wd=wd, tpm=tpm):
                s.wait_ge(tp.h, tpm)
                s.copy(s_m[0:1, c0 - c_lo:c1 - c_lo], pp[6][0:1, 0:wd])
                s.copy(s_v[0:1, c0 - c_lo:c1 - c_lo], pp[7][0:1, 0:wd])
            P.add("scalar", f)
            spm = bump("scalar", sp)
            wait("tensor", sp, spm)
        sp_ev = sp.n

        def f(v, sp_ev=sp_ev, wdt=wdt):
            v.wait_ge(sp.h, sp_ev)
            v.tensor_scalar(s_m[0:1, 0:wdt], s_m[0:1, 0:wdt], 1.0 / E, None, OP.mult)
            v.tensor_scalar(s_v[0:1, 0:wdt], s_v[0:1, 0:wdt], 1.0 / E, None, OP.mult)
            v.drain()
            v.tensor_tensor(s_i[0:1, 0:wdt], s_m[0:1, 0:wdt], s_m[0:1, 0:wdt], OP.mult)
            v.drain()
            v.tensor_tensor(s_v[0:1, 0:wdt], s_v[0:1, 0:wdt], s_i[0:1, 0:wdt], OP.subtract)
            v.drain()
            v.tensor_scalar(s_v[0:1, 0:wdt], s_v[0:1, 0:wdt], 1e-5, None, OP.add)
        P.add("vector", f)
        vp_var = bump("vector", vp)

        def f(s, vp_var=vp_var, wdt=wdt):
            s.wait_ge(vp.h, vp_var)
            s.activation(out=s_i[0:1, 0:wdt], in_=s_v[0:1, 0:wdt], func=AT.Ln)
        P.add("scalar", f)
        sp_inv = bump("scalar", sp)

        def f(v, sp_inv=sp_inv, wdt=wdt):
            v.wait_ge(sp.h, sp_inv)
            v.tensor_scalar(s_i[0:1, 0:wdt], s_i[0:1, 0:wdt], -0.5, None, OP.mult)
        P.add("vector", f)
        vp_l2 = bump("vector", vp)

        def f(s, vp_l2=vp_l2, wdt=wdt):
            s.wait_ge(vp.h, vp_l2)
            s.activation(out=s_i[0:1, 0:wdt], in_=s_i[0:1, 0:wdt], func=AT.Exp)
        P.add("scalar", f)
        sp_inv2 = bump("scalar", sp)

        def f(v, sp_inv2=sp_inv2, wdt=wdt):
            v.wait_ge(sp.h, sp_inv2)
            v.tensor_tensor(s_v[0:1, 0:wdt], s_m[0:1, 0:wdt], s_i[0:1, 0:wdt], OP.mult)
        P.add("vector", f)
        vp_ms = bump("vector", vp)

        wait("gpsimd", vp, vp_ms)
        dma("gpsimd", h_stat[0:1, 0:wdt], s_i[0:1, 0:wdt], d_stat)
        w2 = dma("gpsimd", h_stat[1:2, 0:wdt], s_v[0:1, 0:wdt], d_stat)
        wait("gpsimd", d_stat, w2)
        dma("gpsimd", invb_[:, 0:wdt], h_stat[0:1, 0:wdt].broadcast_to([128, wdt]), d_stat)
        wbc = dma("gpsimd", msbb_[:, 0:wdt], h_stat[1:2, 0:wdt].broadcast_to([128, wdt]), d_stat)
        wait("vector", d_stat, wbc)

        def f(v):
            for kt in range(NKT):
                v.tensor_tensor(lnview(kt, c_lo, c_hi), lnview(kt, c_lo, c_hi), invb_[:, 0:wdt], OP.mult)
            v.drain()
            for kt in range(NKT):
                v.tensor_tensor(lnview(kt, c_lo, c_hi), lnview(kt, c_lo, c_hi), msbb_[:, 0:wdt], OP.subtract)
        P.add("vector", f)
        r = bump("vector", vp)
        barrier()
        stL.close()
        return r

    def emit_ffn(lnview, bufs, xview):
        (mixb, rbuf, kabuf, wfbuf, vstage) = bufs
        wait("gpsimd", ld, ld.n)
        for kt in range(NKT):
            dma("gpsimd", wfbuf[0][:, kt * 1024:(kt + 1) * 1024],
                wk_t[kt * 128:(kt + 1) * 128, :], ld)
        wk_done = ld.n

        def mix(mu_sb, mu1_sb):
            def f(v, mu_sb=mu_sb, mu1_sb=mu1_sb):
                for kt in range(NKT):
                    v.tensor_scalar(mixb[:, kt * TH:(kt + 1) * TH], lnview(kt, 1, TH + 1),
                                    mu_sb[:, kt:kt + 1], None, OP.mult)
                v.drain()
                for kt in range(NKT):
                    v.scalar_tensor_tensor(mixb[:, kt * TH:(kt + 1) * TH], lnview(kt, 0, TH),
                                           mu1_sb[:, kt:kt + 1], mixb[:, kt * TH:(kt + 1) * TH],
                                           OP.mult, OP.add)
            P.add("vector", f)
            return bump("vector", vp)

        mix(mu_k_sb, mu_k1_sb)

        def proj(wbuf_idx, store, rhs_buf):
            wait("tensor", vp, vp.n)
            pr_sph = {}
            for mt in range(NKT):
                for tc in range(4):
                    ipr = mt * 4 + tc
                    bank = ipr % 4
                    if ipr >= 4:
                        wait("tensor", sp, pr_sph[ipr - 4])

                    def f(t, mt=mt, tc=tc, bank=bank, wbuf_idx=wbuf_idx, rhs_buf=rhs_buf):
                        for kt in range(NKT):
                            t.matmul(pp[bank][:, :],
                                     wfbuf[wbuf_idx][:, (kt * NKT + mt) * 128:(kt * NKT + mt + 1) * 128],
                                     rhs_buf[:, kt * TH + tc * 512:kt * TH + (tc + 1) * 512],
                                     start=(kt == 0), stop=(kt == NKT - 1))
                    P.add("tensor", f)
                    tpm = bump("tensor", tp)

                    def f(s, mt=mt, tc=tc, bank=bank, tpm=tpm):
                        s.wait_ge(tp.h, tpm)
                        store(mt, tc, pp[bank], s)
                    P.add("scalar", f)
                    pr_sph[ipr] = bump("scalar", sp)

        wait("tensor", ld, wk_done)

        def store_k(mt, tc, ps, s):
            s.activation(out=kabuf[:, mt * TH + tc * 512:mt * TH + (tc + 1) * 512],
                         in_=ps[:, :], func=AT.Relu)
        proj(0, store_k, mixb)
        sp_k = sp.n
        tp_k = tp.n

        wait("gpsimd", ld, ld.n)
        for kt in range(NKT):
            dma("gpsimd", wfbuf[1][:, kt * 1024:(kt + 1) * 1024],
                wr_t[kt * 128:(kt + 1) * 128, :], ld)
        wr_done = ld.n

        wait("vector", sp, sp_k)

        def f(v):
            for kt in range(NKT):
                v.tensor_tensor(kabuf[:, kt * TH:(kt + 1) * TH], kabuf[:, kt * TH:(kt + 1) * TH],
                                kabuf[:, kt * TH:(kt + 1) * TH], OP.mult)
        P.add("vector", f)
        wait("vector", tp, tp_k)
        mix(mu_r_sb, mu_r1_sb)

        wait("gpsimd", tp, tp_k)
        wait("gpsimd", ld, ld.n)
        for kt in range(NKT):
            dma("gpsimd", wfbuf[0][:, kt * 1024:(kt + 1) * 1024],
                wv_t[kt * 128:(kt + 1) * 128, :], ld)
        wv_done = ld.n
        wait("tensor", ld, wr_done)
        wait("tensor", ld, wv_done)
        wait("tensor", vp, vp.n)
        vp_hist = {}
        itf = 0
        for mt in range(NKT):
            for tc in range(4):
                itf += 1
                rb = (2 * itf) % 4
                vb = (2 * itf + 1) % 4

                def f(t, mt=mt, tc=tc, rb=rb, vb=vb, g=vp_hist.get(itf - 2, 0)):
                    if g:
                        t.wait_ge(vp.h, g)
                    for kt in range(NKT):
                        t.matmul(pp[rb][:, :],
                                 wfbuf[1][:, (kt * NKT + mt) * 128:(kt * NKT + mt + 1) * 128],
                                 mixb[:, kt * TH + tc * 512:kt * TH + (tc + 1) * 512],
                                 start=(kt == 0), stop=(kt == NKT - 1))
                    for kt in range(NKT):
                        t.matmul(pp[vb][:, :],
                                 wfbuf[0][:, (kt * NKT + mt) * 128:(kt * NKT + mt + 1) * 128],
                                 kabuf[:, kt * TH + tc * 512:kt * TH + (tc + 1) * 512],
                                 start=(kt == 0), stop=(kt == NKT - 1))
                P.add("tensor", f)
                tpm = bump("tensor", tp)

                def f(s, rb=rb, tpm=tpm, g=vp_hist.get(itf - 1, 0)):
                    s.wait_ge(tp.h, tpm)
                    if g:
                        s.wait_ge(vp.h, g)
                    s.activation(out=rbuf, in_=pp[rb][:, :], func=AT.Sigmoid)
                P.add("scalar", f)
                spm = bump("scalar", sp)

                def f(v, mt=mt, tc=tc, vb=vb, spm=spm):
                    v.wait_ge(sp.h, spm)
                    v.tensor_tensor(vstage, pp[vb][:, :], rbuf, OP.mult)
                    v.drain()
                    v.scalar_tensor_tensor(xview(mt, 1 + tc * 512, 1 + (tc + 1) * 512),
                                           vstage, 0.5,
                                           xview(mt, 1 + tc * 512, 1 + (tc + 1) * 512),
                                           OP.mult, OP.add)
                P.add("vector", f)
                vp_hist[itf] = bump("vector", vp)
        bump("vector", vp)
        return vp_hist

    LN1_CH = [(0, 512), (512, 1024), (1024, 1536), (1536, 2048), (2048, 2049)]
    LN2_CH = [(1, 513), (513, 1025), (1025, 1537), (1537, 2049)]

    stA = ExitStack()
    eA = stA.enter_context
    lnb = eA(nc.sbuf_tensor("lnb", [128, NKT * XOW], BF16))

    def lnv(kt, c0, c1):
        return lnb[:, kt * XOW + c0: kt * XOW + c1]

    ffnsm = eA(nc.sbuf_tensor("ffnsm", [128, 1024], BF16))

    stG = ExitStack()
    g_sb = stG.enter_context(nc.sbuf_tensor("g_sb", [16, 7680], BF16))
    wstage = stG.enter_context(nc.sbuf_tensor("wstage", [128, 2048], BF16))
    go = {}
    _off = 0
    for nm, wd in [("g1in", 1024), ("g2in", 2048), ("g1x", 128), ("g2x", 768),
                   ("g1dt", 128), ("g2dt", 512), ("g1out", 512), ("g2out", 2048)]:
        go[nm] = (_off, _off + wd)
        _off += wd

    def gvw(nm):
        a, b_ = go[nm]
        return g_sb[:, a:b_]

    for nm, gsrc in [("g1in", g1in), ("g2in", g2in), ("g1x", g1x), ("g2x", g2x),
                     ("g1dt", g1dt), ("g2dt", g2dt), ("g1out", g1out), ("g2out", g2out)]:
        dma("gpsimd", gvw(nm), gsrc, ld)
    ld_g = ld.n
    wait("tensor", ld, ld_g)

    wq = []

    def build_wtmp(g1name, g2name, npt, fchunks, rowbase, dst_hbm):
        for pt in range(npt):
            for (c0, c1) in fchunks:
                wq.append((g1name, pt, g2name, c0, c1 - c0, rowbase, dst_hbm))

    phase("wbuild")
    F4 = [(0, 512), (512, 1024), (1024, 1536), (1536, 2048)]
    build_wtmp("g1in", "g2in", 8, F4, 0, h_wtmp)
    build_wtmp("g1out", "g2out", 4, F4, 1024, h_wtmp)
    build_wtmp("g1x", "g2x", 1, [(0, 512), (512, 768)], 0, h_wtmpx)
    build_wtmp("g1dt", "g2dt", 1, [(0, 512)], 0, h_wtmpdt)

    WG = 3
    wb_sph = {}
    wb_slot_last = [0, 0, 0, 0]
    for gidx in range(0, len(wq), WG):
        grp = list(enumerate(wq[gidx:gidx + WG], start=gidx))
        gno = gidx // WG

        def f(t, grp=grp, gsp=wb_sph.get(gidx // WG - 2, 0)):
            if gsp:
                t.wait_ge(sp.h, gsp)
            for idx, (g1, pt, g2, c0, wd, rb, dh) in grp:
                t.matmul(pp[idx % 6][:, 0:wd], gvw(g1)[:, pt * 128:(pt + 1) * 128],
                         gvw(g2)[:, c0:c0 + wd], start=True, stop=True)
        P.add("tensor", f)
        tpm = bump("tensor", tp)

        guards = [(idx % 4, wb_slot_last[idx % 4]) for idx, _ in grp]

        def f(s, grp=grp, tpm=tpm, guards=guards):
            s.wait_ge(tp.h, tpm)
            for (slot, gv), (idx, (g1, pt, g2, c0, wd, rb, dh)) in zip(guards, grp):
                if gv:
                    s.wait_ge(d_wb[slot].h, gv)
                s.copy(wstage[:, slot * 512:slot * 512 + wd], pp[idx % 6][:, 0:wd])
        P.add("scalar", f)
        spm = bump("scalar", sp)
        wb_sph[gno] = spm
        wait("gpsimd", sp, spm)
        for idx, (g1, pt, g2, c0, wd, rb, dh) in grp:
            slot = idx % 4
            wb_slot_last[slot] = dma(
                "gpsimd", dh[rb + pt * 128: rb + (pt + 1) * 128, c0:c0 + wd],
                wstage[:, slot * 512:slot * 512 + wd], d_wb[slot])

    phase("ln1")
    emit_ln(xov, lnv, LN1_CH)
    stG.close()

    stF = ExitStack()
    mixb = stF.enter_context(nc.sbuf_tensor("mixb", [128, NKT * TH], BF16))
    kabuf = stF.enter_context(nc.sbuf_tensor("kabuf", [128, NKT * TH], BF16))
    wfbuf = [stF.enter_context(nc.sbuf_tensor(f"wf{i}", [128, NKT * NKT * 128], BF16)) for i in range(2)]
    rbuf = ffnsm[:, 0:512]
    vstage = ffnsm[:, 512:1024]

    phase("ffn1")
    emit_ffn(lnv, (mixb, rbuf, kabuf, wfbuf, vstage), xov)
    barrier()
    stF.close()

    phase("ln2")
    emit_ln(xov, lnv, LN2_CH)
    wait("sync", vp, vp.n)
    for kt in range(NKT):
        dma("sync", h_ln2own[kt * 128:(kt + 1) * 128, :], lnv(kt, 1, TH + 1), lx)

    barrier()
    stA.close()

    phase("e1")
    for kt in range(NKT):
        dma("gpsimd", h_xsp[kt * 128:(kt + 1) * 128, :], xov(kt, 1, TH + 1), ld)

    def f(gp):
        gp.collective_compute(
            "AllGather", OP.bypass,
            replica_groups=[[0, 1], [2, 3], [4, 5], [6, 7]],
            ins=[h_ln2own], outs=[h_ln2full],
        ).then_inc(ccs.h, 1)
    P.add("gpsimd", f)
    cc1 = ccs.res()
    wait("gpsimd", ccs, cc1)
    barrier()
    stX.close()

    phase("front")
    stB = ExitStack()
    eB = stB.enter_context
    wi_sb = eB(nc.sbuf_tensor("wi_sb", [128, NKT * 2048], BF16))
    wx_sb = eB(nc.sbuf_tensor("wx_sb", [128, NKT * 96], BF16))
    slab = [eB(nc.sbuf_tensor(f"slab{i}", [128, NKT * 512], BF16)) for i in range(2)]
    XCW = TF + 3
    xcp = eB(nc.sbuf_tensor("xcp", [128, XCW], BF16))
    zb = eB(nc.sbuf_tensor("zb", [128, TF], BF16))
    cacc = eB(nc.sbuf_tensor("cacc", [128, TF], BF16))
    ctmp = eB(nc.sbuf_tensor("ctmp", [128, TF], BF16))
    convd = eB(nc.sbuf_tensor("convd", [128, TF], BF16))
    ytile = eB(nc.sbuf_tensor("ytile", [128, TF], BF16))
    cf_sb = eB(nc.sbuf_tensor("cf_sb", [128, NDT * 4], F32))
    cb_sb = eB(nc.sbuf_tensor("cb_sb", [128, NDT], F32))
    dtbc_sb = eB(nc.sbuf_tensor("dtbc_sb", [96, TF], BF16))

    for kt in range(NKT):
        for ai in range(4):
            a = 4 * kt + ai
            srcv = h_wtmp[a * 32: a * 32 + 32, :].rearrange("i (b j) -> b i j", j=64)
            dstv = wi_sb[ai * 32:(ai + 1) * 32, kt * 2048:(kt + 1) * 2048].rearrange(
                "b (i j) -> b i j", j=64)
            dma("gpsimd", dstv, srcv, ld)
    for kt in range(NKT):
        for ai in range(2):
            a_loc = 2 * kt + ai
            srcv = h_wtmpx[a_loc * 8: a_loc * 8 + 8, :].rearrange("i (b j) -> b i j", j=12)
            dstv = wx_sb[ai * 64:(ai + 1) * 64, kt * 96:(kt + 1) * 96].rearrange(
                "b (i j) -> b i j", j=12)
            dma("gpsimd", dstv, srcv, ld)
    dma("gpsimd", cf_sb[:, :].rearrange("p (a b) -> p a b", b=4), cfin.rearrange("(a p) b -> p a b", p=128), ld, slow=True)
    ld_wi = dma("gpsimd", cb_sb[:, :].rearrange("p (a b) -> p a b", b=1), cbin.rearrange("(a p) b -> p a b", p=128), ld, slow=True)
    wait("tensor", ld, ld_wi)

    vp_y_prev = 0
    ld_y_prev = 0
    ip_sph = {}
    ip_tph = {}
    for j in range(NDT):
        for tc in range(NCH):
            sl = tc % 2
            gi = j * NCH + tc
            if gi >= 2:
                wait("sync", tp, ip_tph[gi - 2])
            src = h_ln2full.rearrange("(s p) t -> s p t", s=2)[tc // 4, :, (tc % 4) * 512:(tc % 4 + 1) * 512]
            srcv = src.rearrange("(a p) t -> p a t", p=128)
            lxm = dma("sync", slab[sl][:, :].rearrange("p (a t) -> p a t", a=NKT), srcv, d_slab[sl])
            for which in range(2):
                mcol = (0 if which == 0 else 1024) + j * 128
                iw = gi * 2 + which
                bank = iw % 4
                if iw >= 4:
                    wait("tensor", sp, ip_sph[iw - 4])

                def f(t, mcol=mcol, bank=bank, sl=sl, lxm=lxm):
                    t.wait_ge(d_slab[sl].h, lxm)
                    for kt in range(NKT):
                        t.matmul(pp[bank][:, :],
                                 wi_sb[:, kt * 2048 + mcol: kt * 2048 + mcol + 128],
                                 slab[sl][:, kt * 512:(kt + 1) * 512],
                                 start=(kt == 0), stop=(kt == NKT - 1))
                P.add("tensor", f)
                tpm = bump("tensor", tp)

                def f(s, which=which, tc=tc, bank=bank, tpm=tpm, vp_y_prev=vp_y_prev):
                    s.wait_ge(tp.h, tpm)
                    if tc == 0 and which == 0 and vp_y_prev:
                        s.wait_ge(vp.h, vp_y_prev)
                    if which == 0:
                        s.copy(xcp[:, 1 + tc * 512: 1 + (tc + 1) * 512], pp[bank][:, :])
                    else:
                        s.copy(zb[:, tc * 512:(tc + 1) * 512], pp[bank][:, :])
                P.add("scalar", f)
                ip_sph[iw] = bump("scalar", sp)
            ip_tph[gi] = tp.n
        sp_xz = sp.n

        def f(v, j=j, sp_xz=sp_xz, ld_y_prev=ld_y_prev):
            v.wait_ge(sp.h, sp_xz)
            if ld_y_prev:
                v.wait_ge(d_y.h, ld_y_prev)
            v.memset(xcp[:, 0:1], 0.0)
            v.memset(xcp[:, TF + 1:TF + 3], 0.0)
            v.drain()
            v.tensor_scalar(cacc[:, :], xcp[:, 0:TF], cf_sb[:, j * 4:j * 4 + 1], None, OP.mult)
            v.tensor_scalar(ctmp[:, :], xcp[:, 1:1 + TF], cf_sb[:, j * 4 + 1:j * 4 + 2], None, OP.mult)
            v.tensor_scalar(ytile[:, :], xcp[:, 2:2 + TF], cf_sb[:, j * 4 + 2:j * 4 + 3], None, OP.mult)
            v.tensor_scalar(convd[:, :], xcp[:, 3:3 + TF], cf_sb[:, j * 4 + 3:j * 4 + 4], None, OP.mult)
            v.drain()
            v.tensor_tensor(cacc[:, :], cacc[:, :], ctmp[:, :], OP.add)
            v.tensor_tensor(ytile[:, :], ytile[:, :], convd[:, :], OP.add)
            v.drain()
            v.tensor_tensor(cacc[:, :], cacc[:, :], ytile[:, :], OP.add)
        P.add("vector", f)
        vp_conv = bump("vector", vp)

        def f(s, j=j, vp_conv=vp_conv):
            s.wait_ge(vp.h, vp_conv)
            s.activation(out=cacc[:, :], in_=cacc[:, :], func=AT.Silu, bias=cb_sb[:, j:j + 1])
        P.add("scalar", f)
        sp_silu = bump("scalar", sp)

        def f(v, sp_silu=sp_silu):
            v.wait_ge(sp.h, sp_silu)
            v.tensor_tensor(ytile[:, :], cacc[:, :], zb[:, :], OP.mult)
        P.add("vector", f)
        vp_y = bump("vector", vp)
        wait("gpsimd", vp, vp_y)
        ld_y = dma("gpsimd", h_y[j * 128:(j + 1) * 128, :], ytile[:, :], d_y)
        vp_y_prev = vp_y
        ld_y_prev = ld_y

    phase("xproj")
    wait("sync", d_y, d_y.n)
    xp_sph = {}
    xp_tph = {}
    for tc in range(NCH):
        sl = tc % 2
        if tc >= 2:
            wait("sync", tp, xp_tph[tc - 2])
        lxm = dma("sync", slab[sl][:, :].rearrange("p (a t) -> p a t", a=NDT),
                  h_y.rearrange("(a p) t -> p a t", p=128)[:, :, tc * 512:(tc + 1) * 512], d_slab[sl])
        bank = tc % 4

        def f(t, bank=bank, sl=sl, lxm=lxm):
            t.wait_ge(d_slab[sl].h, lxm)
            for kt in range(NDT):
                t.matmul(pp[bank][0:96, :],
                         wx_sb[:, kt * 96:(kt + 1) * 96],
                         slab[sl][:, kt * 512:(kt + 1) * 512],
                         start=(kt == 0), stop=(kt == NDT - 1))
        P.add("tensor", f)
        tpm = bump("tensor", tp)

        def f(s, tc=tc, bank=bank, tpm=tpm):
            s.wait_ge(tp.h, tpm)
            s.copy(dtbc_sb[:, tc * 512:(tc + 1) * 512], pp[bank][0:96, :])
        P.add("scalar", f)
        spm = bump("scalar", sp)
        xp_sph[tc] = spm
        xp_tph[tc] = tp.n
        if tc >= 3:
            wait("tensor", sp, xp_sph[tc - 3])
    wait("gpsimd", sp, sp.n)
    dma("gpsimd", h_dtbc_in[:, :], dtbc_sb[:, :], ld)

    barrier()
    stB.close()

    phase("e2")
    def f(gp):
        gp.collective_compute(
            "AllReduce", OP.add,
            replica_groups=[[0, 1], [2, 3], [4, 5], [6, 7]],
            ins=[h_dtbc_in], outs=[h_dtbc],
        ).then_inc(ccs.h, 1)
    P.add("gpsimd", f)
    cc2 = ccs.res()
    wait("gpsimd", ccs, cc2)
    bcr_done = dma("gpsimd", h_bcr.rearrange("c (r t) -> c r t", r=32),
                   h_dtbc[64:96, :].rearrange("r (c t) -> c r t", c=NCH), ld)
    barrier()

    phase("scanprep")
    stC = ExitStack()
    eC = stC.enter_context
    bbc2 = [eC(nc.sbuf_tensor(f"bbc{i}", [128, NST * TC], BF16)) for i in range(2)]
    cbc2 = [eC(nc.sbuf_tensor(f"cbc{i}", [128, NST * TC], BF16)) for i in range(2)]
    dab2 = [eC(nc.sbuf_tensor(f"dab{i}", [128, NST * TC], BF16)) for i in range(2)]
    bsb = eC(nc.sbuf_tensor("bsb", [128, NST * TC], BF16))
    hsb2 = [eC(nc.sbuf_tensor(f"hsb{i}", [128, NST * TC], BF16)) for i in range(2)]
    dtun2 = [eC(nc.sbuf_tensor(f"dtun{i}", [64, TC], BF16)) for i in range(2)]
    wdt_sb = eC(nc.sbuf_tensor("wdt_sb", [64, DH], BF16))
    dts2 = [eC(nc.sbuf_tensor(f"dts{i}", [128, TC], BF16)) for i in range(2)]
    dus2 = [eC(nc.sbuf_tensor(f"dus{i}", [128, TC], BF16)) for i in range(2)]
    usb = [eC(nc.sbuf_tensor(f"usb{i}", [128, TC], BF16)) for i in range(3)]
    yssb = [eC(nc.sbuf_tensor(f"yssb{i}", [128, TC], BF16)) for i in range(2)]
    fold16 = eC(nc.sbuf_tensor("fold16", [128, NST], BF16))
    hsrow = eC(nc.sbuf_tensor("hsrow", [1, 512], F32))
    hspad = eC(nc.sbuf_tensor("hspad", [1, 1032], F32))

    stW = ExitStack()
    wo_sb = stW.enter_context(nc.sbuf_tensor("wo_sb", [128, NKT * 1024], BF16))
    for kt in range(NKT):
        for ai in range(2):
            a_loc = 2 * kt + ai
            srcv = h_wtmp[1024 + a_loc * 32: 1024 + a_loc * 32 + 32, :].rearrange(
                "i (b j) -> b i j", j=32)
            dstv = wo_sb[ai * 64:(ai + 1) * 64, kt * 1024:(kt + 1) * 1024].rearrange(
                "b (i j) -> b i j", j=32)
            dma("gpsimd", dstv, srcv, ld)
    ld_wo = ld.n

    for a in range(8):
        srcv = h_wtmpdt[a * 16:a * 16 + 16, :].rearrange("i (b j) -> b i j", j=64)
        dstv = wdt_sb[a * 8:(a + 1) * 8, :].rearrange("b (i j) -> b i j", j=64)
        dma("gpsimd", dstv, srcv, d_stat)
    ds_all = d_stat.n

    NIT = NCH * NDT
    dt_ld = {}
    bc_ld = {}
    cc_ld = {}

    def bsrc_of(c):
        return h_bcr[c:c + 1, 0:NST * TC].broadcast_to([128, NST * TC])

    def csrc_of(c):
        return h_bcr[c:c + 1, NST * TC:2 * NST * TC].broadcast_to([128, NST * TC])

    dt_ld[0] = dma("gpsimd", dtun2[0][:, :], h_dtbc[0:64, 0:TC], d_dt)
    dt_ld[1] = dma("gpsimd", dtun2[1][:, :], h_dtbc[0:64, TC:2 * TC], d_dt)
    bc_ld[0] = dma("gpsimd", bbc2[0][:, :], bsrc_of(0), d_bc)
    cc_ld[0] = dma("gpsimd", cbc2[0][:, :], csrc_of(0), d_bc2)
    bc_ld[1] = dma("gpsimd", bbc2[1][:, :], bsrc_of(1), d_bc)
    cc_ld[1] = dma("gpsimd", cbc2[1][:, :], csrc_of(1), d_bc2)

    def f(v):
        v.memset(hspad[0:1, :], 0.0)
    P.add("vector", f)
    vp_prep = bump("vector", vp)

    wait("tensor", ld, ld.n)
    wait("tensor", vp, vp.n)
    for ec in range(2):
        def f(t, ec=ec):
            for kt in range(NKT):
                t.matmul(pp[6][0:1, :], ones_c[:, :],
                         wo_sb[:, kt * 1024 + ec * 512: kt * 1024 + (ec + 1) * 512],
                         start=(kt == 0), stop=(kt == NKT - 1))
        P.add("tensor", f)
        tpm = bump("tensor", tp)

        def f(s, ec=ec, tpm=tpm):
            s.wait_ge(tp.h, tpm)
            s.copy(hspad[0:1, 8 + ec * 512: 8 + (ec + 1) * 512], pp[6][0:1, :])
        P.add("scalar", f)
        spm = bump("scalar", sp)
        wait("tensor", sp, spm)
    stW.close()

    phase("scan")
    sp_sp = {}
    sp_da = {}
    tp_hs = {}
    vp_du = {}
    vp_c = {}
    vp_scanh = {}
    vp_ysh = {}
    tp_dt = {}
    ysw_h = {}
    u_ld = {}

    def jj(k):
        return k % NDT

    def cof(k):
        return k // NDT

    def f(t, g=ds_all, gd=dt_ld[1]):
        t.wait_ge(d_stat.h, g)
        t.wait_ge(d_dt.h, gd)
        t.matmul(pp[0][:, :], wdt_sb[:, 0:128], dtun2[0][:, :], start=True, stop=True)
    P.add("tensor", f)
    tp_dt[0] = bump("tensor", tp)

    def f(s, g=tp_dt[0]):
        s.wait_ge(tp.h, g)
        s.activation(out=spf[:, :], in_=pp[0][:, :], func=AT.Exp)
        s.drain()
        s.activation(out=dts2[0][:, :], in_=spf[:, :], func=AT.Ln, bias=1.0)
    P.add("scalar", f)
    sp_sp[0] = bump("scalar", sp)

    def f(t):
        t.matmul(pp[1][:, :], wdt_sb[:, 128:256], dtun2[0][:, :], start=True, stop=True)
    P.add("tensor", f)
    tp_dt[1] = bump("tensor", tp)

    u_ld[0] = dma("sync", usb[0][:, :], h_y[0:128, 0:TC], d_u[0])
    u_ld[1] = dma("sync", usb[1][:, :], h_y[128:256, 0:TC], d_u[1])

    for i in range(NIT + 1):
        p, q = i % 2, (i - 1) % 2
        k = i - 1
        if i < NIT:
            def f(v, i=i, p=p, gs=sp_sp[i], gu=u_ld[i]):
                v.wait_ge(sp.h, gs)
                v.wait_ge(d_u[i % 3].h, gu)
                v.tensor_tensor(dus2[p][:, :], dts2[p][:, :], usb[i % 3][:, :], OP.mult)
            P.add("vector", f)
            vp_du[i] = bump("vector", vp)
        if i >= 1:
            def f(v, k=k, kp=q, gsp=sp_da[k], gtp=tp_hs.get(k - 2, 0),
                  gbc=bc_ld[cof(k)], j=jj(k)):
                v.wait_ge(sp.h, gsp)
                if gtp:
                    v.wait_ge(tp.h, gtp)
                v.wait_ge(d_bc.h, gbc)
                da0 = dab2[kp][:, :].rearrange("p (n t) -> p n t", n=NST)[:, :, 0]
                bb0 = bsb[:, :].rearrange("p (n t) -> p n t", n=NST)[:, :, 0]
                v.tensor_tensor(fold16[:, :], da0, carry[:, j * NST:(j + 1) * NST], OP.mult)
                v.tensor_tensor(bsb[:, :].rearrange("p (n t) -> p n t", n=NST),
                                dus2[kp][:, :].unsqueeze(1).broadcast_to([128, NST, TC]),
                                bbc2[cof(k) % 2][:, :].rearrange("p (n t) -> p n t", n=NST), OP.mult)
                v.drain()
                v.memset(da0, 0.0)
                v.drain()
                v.tensor_tensor(bb0, bb0, fold16[:, :], OP.add)
                v.drain()
                v.tensor_tensor_scan(hsb2[k % 2][:, :], dab2[kp][:, :], bsb[:, :], 0.0, OP.mult, OP.add)
            P.add("vector", f)
            vp_scanh[k] = bump("vector", vp)

            def f(v, k=k, kp=q, gcc=cc_ld[cof(k)], j=jj(k)):
                v.wait_ge(d_bc2.h, gcc)
                hl = hsb2[k % 2][:, :].rearrange("p (n t) -> p n t", n=NST)[:, :, TC - 1]
                v.tensor_tensor(bsb[:, :], hsb2[k % 2][:, :], cbc2[cof(k) % 2][:, :], OP.mult)
                v.tensor_copy(carry[:, j * NST:(j + 1) * NST], hl)
            P.add("vector", f)
            vp_c[k] = bump("vector", vp)

            def f(v, k=k, kp=q, gys=ysw_h.get(k - 2, 0), j=jj(k)):
                if gys:
                    v.wait_ge(d_ysw[kp].h, gys)
                v.tensor_tensor(bsb[:, 0:8 * TC], bsb[:, 0:8 * TC], bsb[:, 8 * TC:16 * TC], OP.add)
                v.drain()
                v.tensor_tensor(bsb[:, 0:4 * TC], bsb[:, 0:4 * TC], bsb[:, 4 * TC:8 * TC], OP.add)
                v.drain()
                v.tensor_tensor(bsb[:, 0:2 * TC], bsb[:, 0:2 * TC], bsb[:, 2 * TC:4 * TC], OP.add)
                v.drain()
                v.tensor_tensor(yssb[kp][:, :], bsb[:, 0:TC], bsb[:, TC:2 * TC], OP.add)
                v.drain()
                v.scalar_tensor_tensor(yssb[kp][:, :], usb[k % 3][:, :], dcol[:, j:j + 1],
                                       yssb[kp][:, :], OP.mult, OP.add)
            P.add("vector", f)
            vp_ysh[k] = bump("vector", vp)

        if i < NIT:
            if i + 1 < NIT:
                def f(s, b=q, gt=tp_dt[i + 1], gd=vp_du.get(i - 1, 0)):
                    s.wait_ge(tp.h, gt)
                    if gd:
                        s.wait_ge(vp.h, gd)
                    s.activation(out=spf[:, :], in_=pp[b][:, :], func=AT.Exp)
                    s.drain()
                    s.activation(out=dts2[b][:, :], in_=spf[:, :], func=AT.Ln, bias=1.0)
                P.add("scalar", f)
                sp_sp[i + 1] = bump("scalar", sp)

            gv = vp_prep if i == 0 else vp_scanh.get(i - 2, 0)

            def f(s, i=i, p=p, gv=gv, j=jj(i)):
                if gv:
                    s.wait_ge(vp.h, gv)
                s.drain()
                for n in range(NST):
                    s.activation(out=dab2[p][:, n * TC:(n + 1) * TC], in_=dts2[p][:, :],
                                 func=AT.Exp, scale=acol[:, j * NST + n: j * NST + n + 1])
            P.add("scalar", f)
            sp_da[i] = bump("scalar", sp)

        if i >= 1:
            def f(t, k=k, gv=vp_scanh[k]):
                t.wait_ge(vp.h, gv)
                for n in range(NST):
                    t.matmul(pp[5][0:1, :], ones_c[:, :], hsb2[k % 2][:, n * TC:(n + 1) * TC],
                             start=(k == 0 and n == 0), stop=(k == NIT - 1 and n == NST - 1))
            P.add("tensor", f)
            tp_hs[k] = bump("tensor", tp)

        if i + 2 < NIT:
            wait("tensor", sp, sp_sp[i])
            c2 = cof(i + 2)

            def f(t, i=i, c2=c2, gd=dt_ld[max(c2, 1)], j=jj(i + 2)):
                t.wait_ge(d_dt.h, gd)
                t.matmul(pp[i % 2][:, :], wdt_sb[:, j * 128:(j + 1) * 128],
                         dtun2[c2 % 2][:, :], start=True, stop=True)
            P.add("tensor", f)
            tp_dt[i + 2] = bump("tensor", tp)

        if 2 <= i + 2 < NIT:
            wait("sync", vp, vp_ysh.get(i - 1, 0))
            u_ld[i + 2] = dma("sync", usb[(i + 2) % 3][:, :],
                              h_y[jj(i + 2) * 128:(jj(i + 2) + 1) * 128,
                                  cof(i + 2) * TC:(cof(i + 2) + 1) * TC], d_u[(i + 2) % 3])

        if i >= 1:
            if i % NDT == 0 and i // NDT >= 1 and i < NIT:
                c = i // NDT
                if c + 1 < NCH:
                    wait("gpsimd", vp, vp_scanh[i - 1])
                    bc_ld[c + 1] = dma("gpsimd", bbc2[(c + 1) % 2][:, :],
                                       bsrc_of(c + 1), d_bc)
                wait("gpsimd", vp, vp_ysh[i - 1])
                ysw_h[k] = dma("gpsimd", h_ys[jj(k) * 128:(jj(k) + 1) * 128,
                                              cof(k) * TC:(cof(k) + 1) * TC],
                               yssb[q][:, :], d_ysw[q])
                if c + 1 < NCH:
                    cc_ld[c + 1] = dma("gpsimd", cbc2[(c + 1) % 2][:, :],
                                       csrc_of(c + 1), d_bc2)
                    wait("gpsimd", tp, tp_dt[c * NDT - 1])
                    dt_ld[c + 1] = dma("gpsimd", dtun2[(c + 1) % 2][:, :],
                                       h_dtbc[0:64, (c + 1) * TC:(c + 2) * TC], d_dt)
            else:
                wait("gpsimd", vp, vp_ysh[i - 1])
                ysw_h[k] = dma("gpsimd", h_ys[jj(k) * 128:(jj(k) + 1) * 128,
                                              cof(k) * TC:(cof(k) + 1) * TC],
                               yssb[q][:, :], d_ysw[q])


    phase("hsfin")
    def f(s, g=tp_hs[NIT - 1]):
        s.wait_ge(tp.h, g)
        s.copy(hsrow[0:1, :], pp[5][0:1, :])
    P.add("scalar", f)
    sp_hsr = bump("scalar", sp)

    def f(v, sp_hsr=sp_hsr):
        v.wait_ge(sp.h, sp_hsr)
        v.tensor_reduce(hspad[0:1, 0:1], hsrow[0:1, :], AX.X, OP.add)
    P.add("vector", f)
    vp_hsr = bump("vector", vp)
    wait("gpsimd", vp, vp_hsr)
    dma("gpsimd", h_hs_in[:, :], hspad[0:1, :], ld)

    barrier()
    stC.close()

    phase("e4")
    def f(gp):
        gp.collective_compute(
            "AllReduce", OP.add,
            replica_groups=[[0, 1, 2, 3, 4, 5, 6, 7]],
            ins=[h_hs_in], outs=[h_hs],
        ).then_inc(ccs.h, 1)
    P.add("gpsimd", f)
    cc4 = ccs.res()

    phase("outproj")
    stE = ExitStack()
    eE = stE.enter_context
    xo2 = eE(nc.sbuf_tensor("xo2", [128, NKT * XOW], F32))
    stW2 = ExitStack()
    wo_sb2 = stW2.enter_context(nc.sbuf_tensor("wo_sb2", [128, NKT * 1024], BF16))

    def xov2(kt, c0, c1):
        return xo2[:, kt * XOW + c0: kt * XOW + c1]

    stD = ExitStack()
    eD = stD.enter_context
    yslab = [eD(nc.sbuf_tensor(f"yslab{i}", [128, NKT * 128], BF16)) for i in range(2)]
    opst = [eD(nc.sbuf_tensor(f"opst{i}", [128, 512], BF16)) for i in range(2)]

    for kt in range(NKT):
        for ai in range(2):
            a_loc = 2 * kt + ai
            srcv = h_wtmp[1024 + a_loc * 32: 1024 + a_loc * 32 + 32, :].rearrange(
                "i (b j) -> b i j", j=32)
            dstv = wo_sb2[ai * 64:(ai + 1) * 64, kt * 1024:(kt + 1) * 1024].rearrange(
                "b (i j) -> b i j", j=32)
            dma("gpsimd", dstv, srcv, ld)
    ld_wo2 = ld.n
    wait("tensor", ld, ld_wo2)
    for kt in range(NKT):
        dma("scalar", xov2(kt, 1, TH + 1), h_xsp[kt * 128:(kt + 1) * 128, :], lx)
    xsp_ld = lx.n

    op_w_hist = {}
    op_sph = {}
    op_dwh = {}
    for tt in range(TF // 128):
        sl = tt % 2
        if tt >= 2:
            wait("sync", tp, op_w_hist[tt - 2])
        lxm = dma("sync", yslab[sl][:, :].rearrange("p (a t) -> p a t", a=NKT),
                  h_ys.rearrange("(a p) t -> p a t", p=128)[:, :, tt * 128:(tt + 1) * 128], d_slab[sl])
        for ec in range(2):
            bank = (tt * 2 + ec) % 4

            def f(t, tt=tt, ec=ec, bank=bank, sl=sl, lxm=lxm):
                t.wait_ge(d_slab[sl].h, lxm)
                for kt in range(NKT):
                    t.matmul(pp[bank][:, :],
                             yslab[sl][:, kt * 128:(kt + 1) * 128],
                             wo_sb2[:, kt * 1024 + ec * 512:(kt) * 1024 + (ec + 1) * 512],
                             start=(kt == 0), stop=(kt == NKT - 1))
            P.add("tensor", f)
            tpm = bump("tensor", tp)
            if ec == 1:
                op_w_hist[tt] = tpm

            def f(s, ec=ec, bank=bank, tpm=tpm, st_i=(tt * 2 + ec) % 2,
                  g=op_dwh.get(tt * 2 + ec - 2, 0), gs=(tt * 2 + ec) % 2):
                s.wait_ge(tp.h, tpm)
                if g:
                    s.wait_ge(d_op[gs].h, g)
                s.copy(opst[st_i][:, :], pp[bank][:, :])
            P.add("scalar", f)
            spm = bump("scalar", sp)
            op_sph[tt * 2 + ec] = spm
            if tt * 2 + ec >= 3:
                wait("tensor", sp, op_sph[tt * 2 + ec - 3])
            wait("gpsimd", sp, spm)
            stb = (tt * 2 + ec) % 2
            opw = dma("gpsimd", h_op[tt * 128:(tt + 1) * 128, ec * 512:(ec + 1) * 512],
                      opst[stb][:, :], d_op[stb])
            op_dwh[tt * 2 + ec] = opw

    barrier()

    phase("rs")
    def f(gp):
        gp.collective_compute(
            "ReduceScatter", OP.add,
            replica_groups=[[0, 1], [2, 3], [4, 5], [6, 7]],
            ins=[h_op], outs=[h_od],
        ).then_inc(ccs.h, 1)
    P.add("gpsimd", f)
    cc_rs = ccs.res()
    wait("gpsimd", ccs, cc_rs)
    wait("gpsimd", ccs, cc4)
    barrier()
    stD.close()
    stW2.close()

    phase("grpE_prep")
    lnb3 = eE(nc.sbuf_tensor("lnb3", [128, NKT * XOW], BF16))

    def lnv3(kt, c0, c1):
        return lnb3[:, kt * XOW + c0: kt * XOW + c1]

    ffnsm3 = eE(nc.sbuf_tensor("ffnsm3", [128, 1024], BF16))
    halc = eE(nc.sbuf_tensor("halc", [128, NKT], F32))
    stT = ExitStack()
    hrow = stT.enter_context(nc.sbuf_tensor("hrow", [1, 1032], F32))
    mrow = stT.enter_context(nc.sbuf_tensor("mrow", [1, E], F32))
    mbc = stT.enter_context(nc.sbuf_tensor("mbc", [128, NKT], F32))
    dodb = [stT.enter_context(nc.sbuf_tensor(f"dodb{i}", [128, TH], BF16)) for i in range(2)]

    ld_hrow = dma("gpsimd", hrow[0:1, :], h_hs[0:1, :], ld)

    def f(v, ld_hrow=ld_hrow):
        v.wait_ge(ld.h, ld_hrow)
        v.tensor_scalar(hrow[0:1, 0:1], hrow[0:1, 0:1], 1e-8 / (4.0 * HS_COUNT), None, OP.mult)
        v.drain()
        v.tensor_scalar(mrow[0:1, :], hrow[0:1, 8:8 + E], hrow[0:1, 0:1], None, OP.mult)
    P.add("vector", f)
    vp_mb = bump("vector", vp)
    wait("gpsimd", vp, vp_mb)
    wmb = dma("gpsimd", h_mb[:, :], mrow[0:1, :], ld)
    wait("gpsimd", ld, wmb)
    for kt in range(NKT):
        dma("gpsimd", mbc[:, kt:kt + 1],
            h_mb[0:1, kt * 128:(kt + 1) * 128].rearrange("a b -> b a"), ld, slow=True)
    ld_mbc = ld.n

    dod_hist = {}
    for et in range(NKT):
        sl = et % 2
        if et >= 2:
            wait("sync", vp, dod_hist[et - 2])
        v_tr = d_dod[sl].res()

        def f(eng, sl=sl, et=et):
            eng.dma_start_transpose(dodb[sl][:, :],
                                    h_od[:, et * 128:(et + 1) * 128]).then_inc(d_dod[sl].h, 16)
        P.add("sync", f)

        def f(v, sl=sl, et=et, v_tr=v_tr, ld_mbc=ld_mbc):
            v.wait_ge(d_dod[sl].h, v_tr)
            if et == 0:
                v.wait_ge(ld.h, ld_mbc)
                v.wait_ge(lx.h, xsp_ld)
            v.scalar_tensor_tensor(xov2(et, 1, TH + 1), dodb[sl][:, :],
                                   mbc[:, et:et + 1], xov2(et, 1, TH + 1),
                                   OP.add, OP.add)
        P.add("vector", f)
        dod_hist[et] = bump("vector", vp)
    barrier()
    stT.close()

    phase("ln3")
    emit_ln(xov2, lnv3, LN2_CH)

    phase("e5")
    def f(v):
        for kt in range(NKT):
            v.tensor_scalar(halc[:, kt:kt + 1], lnv3(kt, TH, TH + 1),
                            mskp_sb[:, kt:kt + 1], None, OP.mult)
    P.add("vector", f)
    vp_pl = bump("vector", vp)
    wait("gpsimd", vp, vp_pl)
    wait("gpsimd", ld, ld.n)
    wpl = dma("gpsimd", h_halo_in.rearrange("(a p) b -> p a b", p=128), halc[:, :].rearrange("p (a b) -> p a b", b=1), ld, slow=True)
    wait("gpsimd", ld, wpl)

    def f(gp):
        gp.collective_compute(
            "AllReduce", OP.add,
            replica_groups=[[0, 1], [2, 3], [4, 5], [6, 7]],
            ins=[h_halo_in], outs=[h_halo],
        ).then_inc(ccs.h, 1)
    P.add("gpsimd", f)
    cc5 = ccs.res()
    wait("gpsimd", ccs, cc5)
    ld_hal = dma("gpsimd", halc[:, :].rearrange("p (a b) -> p a b", b=1), h_halo.rearrange("(a p) b -> p a b", p=128), ld, slow=True)
    wait("vector", ld, ld_hal)

    def f(v):
        for kt in range(NKT):
            v.tensor_scalar(lnv3(kt, 0, 1), halc[:, kt:kt + 1],
                            msku_sb[:, kt:kt + 1], None, OP.mult)
    P.add("vector", f)
    bump("vector", vp)

    phase("ffn3")
    stF3 = ExitStack()
    mixb3 = stF3.enter_context(nc.sbuf_tensor("mixb3", [128, NKT * TH], BF16))
    kabuf3 = stF3.enter_context(nc.sbuf_tensor("kabuf3", [128, NKT * TH], BF16))
    wfbuf3 = [stF3.enter_context(nc.sbuf_tensor(f"wf3{i}", [128, NKT * NKT * 128], BF16)) for i in range(2)]
    rbuf3 = ffnsm3[:, 0:512]
    vstage3 = ffnsm3[:, 512:1024]
    vph3 = emit_ffn(lnv3, (mixb3, rbuf3, kabuf3, wfbuf3, vstage3), xov2)
    stF3.close()

    phase("outstore")
    for kt in range(NKT):
        en = "sync" if kt % 2 == 0 else "gpsimd"
        cnt = lx if kt % 2 == 0 else ld
        wait(en, vp, vph3[kt * 4 + 4])
        dma(en, out_t[kt * 128:(kt + 1) * 128, :], xov2(kt, 1, TH + 1), cnt)
    wait("sync", lx, lx.n)
    wait("gpsimd", ld, ld.n)
    barrier()
    stE.close()

    phase_end()
    block = e(nc.Block())
    P.emit(block)
    st.close()
    return nc


def _host_maps(inputs):
    x = np.asarray(inputs["L0_attn_x"])
    in_g1 = np.asarray(inputs["L0_attn_in_g1"], np.float32)
    in_g2 = np.asarray(inputs["L0_attn_in_g2"], np.float32)
    x_g1 = np.asarray(inputs["L0_attn_x_g1"], np.float32)
    x_g2 = np.asarray(inputs["L0_attn_x_g2"], np.float32)
    dt_g1 = np.asarray(inputs["L0_attn_dt_g1"], np.float32)
    dt_g2 = np.asarray(inputs["L0_attn_dt_g2"], np.float32)
    out_g1 = np.asarray(inputs["L0_attn_out_g1"], np.float32)
    out_g2 = np.asarray(inputs["L0_attn_out_g2"], np.float32)
    cf = np.asarray(inputs["L0_attn_conv_filter"], np.float32)
    cb = np.asarray(inputs["L0_attn_conv_bias"], np.float32)
    alog_a = np.asarray(inputs["L0_attn_A_log"], np.float32)
    dv = np.asarray(inputs["L0_attn_D"], np.float32)
    wr = np.asarray(inputs["L0_ffn_W_r"], np.float32)
    wk = np.asarray(inputs["L0_ffn_W_k"], np.float32)
    wv = np.asarray(inputs["L0_ffn_W_v"], np.float32)
    mur = np.asarray(inputs["L0_ffn_mu_r"], np.float32)
    muk = np.asarray(inputs["L0_ffn_mu_k"], np.float32)

    C = np.ascontiguousarray
    wr_t = C(wr.T)
    wk_t = C(wk.T)
    wv_t = C(wv.T)
    g2in_m = C(in_g2.reshape(16, 2048))
    g2x_m = C(x_g2.reshape(16, 768))
    g2dt_m = C(dt_g2.reshape(16, 512))
    g2out_m = C(out_g2.reshape(16, 2048))

    maps = []
    for c in range(8):
        b, h = c // 2, c % 2
        xs = x[b, h * TH:(h + 1) * TH, :].T.astype(np.float32)
        halo = (x[b, h * TH - 1:h * TH, :].T.astype(np.float32) if h == 1
                else np.zeros((E, 1), np.float32))
        xin_m = C(np.concatenate([halo, xs], axis=1))

        iperm = list(range(h * 16, (h + 1) * 16)) + list(range(32 + h * 16, 32 + (h + 1) * 16))
        g1in_m = C(in_g1[:, iperm, :].reshape(32 * 32, 16).T)
        aperm = list(range(h * 16, (h + 1) * 16))
        g1x_m = C(x_g1[aperm].reshape(16 * 8, 16).T)
        ipermd = list(range(h * 16, (h + 1) * 16))
        g1dt_m = C(dt_g1[:, ipermd, :].reshape(8 * 16, 16).T)
        g1out_m = C(out_g1[aperm].reshape(16 * 32, 16).T)

        dh = slice(h * DH, (h + 1) * DH)
        maps.append(dict(
            xin=xin_m,
            wr_t=wr_t, wk_t=wk_t, wv_t=wv_t,
            mu_r=C(mur.reshape(E, 1)), mu_k=C(muk.reshape(E, 1)),
            g1in=g1in_m, g2in=g2in_m,
            g1x=g1x_m, g2x=g2x_m,
            g1dt=g1dt_m, g2dt=g2dt_m,
            g1out=g1out_m, g2out=g2out_m,
            cfin=C(cf[:, 0, dh].T),
            cbin=C(cb[dh].reshape(DH, 1)),
            alog=C(alog_a[dh]),
            dvin=C(dv[dh].reshape(DH, 1)),
            maskp=np.full((E, 1), 1.0 - h, np.float32),
            masku=np.full((E, 1), float(h), np.float32),
        ))
    return maps


def kernel(**inputs):
    maps = _host_maps(inputs)
    nc = build_nc()
    res = run_bass_kernel_spmd(nc, maps, core_ids=list(range(8)))
    out = np.empty((4, TF, E), np.float32)
    for c in range(8):
        b, h = c // 2, c % 2
        out[b, h * TH:(h + 1) * TH, :] = res.results[c]["out"].T
    return out
